# version 02
# baseline (speedup 1.0000x reference)
import numpy as np
import ml_dtypes
import concourse.bass as bass
import concourse.mybir as mybir
from concourse.bass_utils import run_bass_kernel_spmd
from contextlib import ExitStack

F32 = mybir.dt.float32
BF16 = mybir.dt.bfloat16
AF = mybir.ActivationFunctionType
ALU = mybir.AluOpType
AX = mybir.AxisListType
BF = ml_dtypes.bfloat16

NCORES = 8
B, T, DM = 4, 4096, 1024
TL = 2048
NCH = DM // 128
DS, DI = 16, 2048
NDI = DI // 128
CL = 128
NSC = TL // CL
CW = 2056

RO_MEAN, RO_RSTD, RO_MRS = 0, 2048, 4096
RO_OV = 6144
RO_DROW = 6656
RO_DELTA, RO_CD = 0, 2048
RO_CDREV, RO_CDC, RO_NEGCD, RO_CTOT = 4096, 4224, 4352, 4480
RO_SP1 = 4608


class Res:
    __slots__ = ("name", "w", "r")

    def __init__(self, name=""):
        self.name = name
        self.w = None
        self.r = []


class Prog:
    ENGS = ("pe", "act", "dve", "pool", "sp")

    def __init__(self, nc, ctx):
        self.nc = nc
        self.eng = {"pe": nc.tensor, "act": nc.scalar, "dve": nc.vector,
                    "pool": nc.gpsimd, "sp": nc.sync}
        self.sem = {e: ctx.enter_context(nc.semaphore(f"sem_{e}")) for e in self.ENGS}
        self.pos = {e: 0 for e in self.ENGS}
        self.known = {e: {e2: 0 for e2 in self.ENGS} for e in self.ENGS}
        self.strict_same_engine = True

    def emit(self, e, fn, reads=(), writes=(), inc=1, dma=False):
        need = {}

        def _need(dep):
            e2, v, d2 = dep
            if e2 != e or dma or d2 or self.strict_same_engine:
                if v > need.get(e2, 0):
                    need[e2] = v

        for r in reads:
            if r.w:
                _need(r.w)
        for w in writes:
            if w.w:
                _need(w.w)
            for rr in w.r:
                _need(rr)
        eng = self.eng[e]
        for e2 in sorted(need):
            v = need[e2]
            if self.known[e][e2] < v:
                eng.wait_ge(self.sem[e2], v)
                self.known[e][e2] = v
        last = fn(eng)
        last.then_inc(self.sem[e], inc)
        self.pos[e] += inc
        v = self.pos[e]
        for w in writes:
            w.w = (e, v, dma)
            w.r = []
        for r in reads:
            r.r.append((e, v, dma))
        return v

    def dma(self, q, out, in_, reads=(), writes=(), **kw):
        return self.emit(q, lambda g: g.dma_start(out=out, in_=in_, **kw),
                         reads=reads, writes=writes, inc=16, dma=True)

    def finish(self):
        eng = self.eng["sp"]
        for e2 in self.ENGS:
            if e2 != "sp" and self.known["sp"][e2] < self.pos[e2]:
                eng.wait_ge(self.sem[e2], self.pos[e2])
                self.known["sp"][e2] = self.pos[e2]
        eng.wait_ge(self.sem["sp"], self.pos["sp"])


def collective_barrier(p, d, scr_ap, r_scr, out_res):
    rb = Res("cbar")
    p.emit("pool", lambda e: e.collective_compute(
        "AllReduce", ALU.add, replica_groups=[[2 * i, 2 * i + 1] for i in range(4)],
        ins=[d["bar1"].opt()], outs=[d["bar2"].opt()]),
        reads=[out_res], writes=[rb], inc=1)
    p.dma("sp", scr_ap, d["bar2"][:, :], reads=[rb, r_scr], writes=[r_scr])
    out_res.w = r_scr.w


def sbuf(nc, ctx, name, shape, dt):
    return ctx.enter_context(nc.sbuf_tensor(name, shape, dt))


def psumt(nc, ctx, name, shape, dt=F32):
    return ctx.enter_context(nc.psum_tensor(name, shape, dt))


class Consts:
    def __init__(self, p, nc, ctx):
        self.onesrow_b = sbuf(nc, ctx, "c_onesrow_b", [1, 128], BF16)
        self.onesrow_f = sbuf(nc, ctx, "c_onesrow_f", [1, 128], F32)
        self.onescol_b = sbuf(nc, ctx, "c_onescol_b", [128, 1], BF16)
        self.onescol_f = sbuf(nc, ctx, "c_onescol_f", [128, 1], F32)
        self.ident = sbuf(nc, ctx, "c_ident", [128, 128], F32)
        self.minf = sbuf(nc, ctx, "c_minf", [128, 128], F32)
        self.eps1 = sbuf(nc, ctx, "c_eps1", [128, 1], F32)
        self.eps2 = sbuf(nc, ctx, "c_eps2", [128, 1], F32)
        self.r = {k: Res("c_" + k) for k in
                  ["onesrow_b", "onesrow_f", "onescol_b", "onescol_f", "ident", "minf",
                   "eps1", "eps2"]}
        r = self.r
        p.emit("dve", lambda e: e.memset(self.onesrow_b[:, :], 1.0), writes=[r["onesrow_b"]])
        p.emit("dve", lambda e: e.memset(self.onesrow_f[:, :], 1.0), writes=[r["onesrow_f"]])
        p.emit("dve", lambda e: e.memset(self.onescol_b[:, :], 1.0), writes=[r["onescol_b"]])
        p.emit("dve", lambda e: e.memset(self.onescol_f[:, :], 1.0), writes=[r["onescol_f"]])
        p.emit("dve", lambda e: e.memset(self.ident[:, :], 1.0), writes=[r["ident"]])
        p.emit("pool", lambda e: e.affine_select(self.ident[:, :], self.ident[:, :],
                                                 [[1, 128]], ALU.is_equal, 0.0,
                                                 base=0, channel_multiplier=-1),
               reads=[r["ident"]], writes=[r["ident"]])
        p.emit("dve", lambda e: e.memset(self.minf[:, :], 1.0e30), writes=[r["minf"]])
        p.emit("pool", lambda e: e.affine_select(self.minf[:, :], self.minf[:, :],
                                                 [[-1, 128]], ALU.is_ge, 0.0,
                                                 base=-1, channel_multiplier=1),
               reads=[r["minf"]], writes=[r["minf"]])
        p.emit("dve", lambda e: e.memset(self.eps1[:, :], 1e-5), writes=[r["eps1"]])
        p.emit("dve", lambda e: e.memset(self.eps2[:, :], 1.1920929e-07), writes=[r["eps2"]])


def bcast_row(p, C, pbank, r_pbank, row_ap, r_row, ncols):
    p.emit("pe", lambda e: e.matmul(pbank[:, 0:ncols], C.onesrow_f[0:1, :], row_ap,
                                    start=True, stop=True),
           reads=[r_row, C.r["onesrow_f"]], writes=[r_pbank])


def fence_res(p, name):
    r = Res(name)
    r.r = [(e, p.pos[e], True) for e in p.ENGS if p.pos[e] > 0]
    return r


def build_l0_attn(p, nc, ctx, C, xres, r_xres, d, first=True, dbg=None):
    dbg = dbg or {}
    rows = sbuf(nc, ctx, "l0_rows", [1, 8704], F32)
    scr = sbuf(nc, ctx, "l0_scr", [128, 512], F32)
    scrb = sbuf(nc, ctx, "l0_scrb", [128, 512], BF16)
    wtile = sbuf(nc, ctx, "l0_wtile", [128, 2 * 1024], BF16)
    xdbl = sbuf(nc, ctx, "l0_xdbl", [128, 2048], F32)
    dbc = sbuf(nc, ctx, "l0_dbc", [128, 2048], F32)
    szt = sbuf(nc, ctx, "l0_szt", [128, 2048], BF16)
    arow = sbuf(nc, ctx, "l0_arow", [1, 16], F32)
    abc = sbuf(nc, ctx, "l0_abc", [128, 16], F32)
    flagc = sbuf(nc, ctx, "l0_flagc", [16, 1], F32)

    pb = [psumt(nc, ctx, f"l0_pb{i}", [128, 512]) for i in range(8)]
    rpb = [Res(f"pb{i}") for i in range(8)]
    R = {k: Res(k) for k in ["rows", "scr", "scrb", "xdbl", "dbc", "szt",
                             "arow", "abc", "flagc",
                             "d_xs", "d_y", "d_sz", "d_ccin", "d_ccout"]}
    R["wtile"] = [Res("wtile0"), Res("wtile1")]

    cd = rows[0:1, RO_CD:RO_CD + 2048]
    dl = rows[0:1, RO_DELTA:RO_DELTA + 2048]

    if first:
        for c in range(NCH):
            p.dma("sp", xres[:, c * 2048:(c + 1) * 2048],
                  d["xT"].rearrange("(c p) t -> c p t", p=128)[c], writes=[r_xres])
    p.dma("sp", arow[:, :], d["Alog"][:, :], writes=[R["arow"]])
    p.emit("act", lambda e: e.activation(arow[:, :], arow[:, :], AF.Exp),
           reads=[R["arow"]], writes=[R["arow"]])
    p.emit("dve", lambda e: e.tensor_scalar(arow[:, :], arow[:, :], -1.0, None, ALU.mult),
           reads=[R["arow"]], writes=[R["arow"]])
    p.emit("pe", lambda e: e.matmul(pb[7][0:128, 0:16], C.onesrow_f[0:1, :], arow[:, :],
                                    start=True, stop=True),
           reads=[R["arow"], C.r["onesrow_f"]], writes=[rpb[7]])
    p.emit("dve", lambda e: e.tensor_copy(abc[:, :], pb[7][0:128, 0:16]),
           reads=[rpb[7]], writes=[R["abc"]])
    p.dma("sp", rows[0:1, RO_OV + 16:RO_OV + 17], d["flag"][:, :], writes=[R["rows"]])
    p.emit("pe", lambda e: e.matmul(pb[7][0:16, 0:1], C.onesrow_f[0:1, 0:16],
                                    rows[0:1, RO_OV + 16:RO_OV + 17], start=True, stop=True),
           reads=[R["rows"], C.r["onesrow_f"]], writes=[rpb[7]])
    p.emit("dve", lambda e: e.tensor_copy(flagc[:, :], pb[7][0:16, 0:1]),
           reads=[rpb[7]], writes=[R["flagc"]])
    p.dma("sp", rows[0:1, RO_DROW:RO_DROW + 2048], d["Drow"][0:1, :], writes=[R["rows"]])
    for tb in range(4):
        bcast_row(p, C, pb[6], rpb[6],
                  rows[0:1, RO_DROW + tb * 512:RO_DROW + (tb + 1) * 512], R["rows"], 512)
        p.emit("dve", lambda e, tb=tb: e.tensor_copy(dbc[:, tb * 512:(tb + 1) * 512],
                                                     pb[6][:, 0:512]),
               reads=[rpb[6]], writes=[R["dbc"]])

    s12 = ExitStack()
    xn = s12.enter_context(nc.sbuf_tensor("l0_xn", [128, NCH * CW], BF16))
    xcs = s12.enter_context(nc.sbuf_tensor("l0_xcs", [128, 2052], F32))
    xconv = s12.enter_context(nc.sbuf_tensor("l0_xconv", [128, 2048], F32))
    xscur = s12.enter_context(nc.sbuf_tensor("l0_xscur", [128, 2048], BF16))
    wz = s12.enter_context(nc.sbuf_tensor("l0_wz", [128, 8 * 1024], BF16))
    wxp = s12.enter_context(nc.sbuf_tensor("l0_wxp", [128, NDI * 80], BF16))
    convw = s12.enter_context(nc.sbuf_tensor("l0_convw", [128, 64], F32))
    convb = s12.enter_context(nc.sbuf_tensor("l0_convb", [128, 16], F32))
    ovf = s12.enter_context(nc.sbuf_tensor("l0_ovf", [128, NCH * 3], F32))
    for k in ["xn", "xcs", "xconv", "xscur", "wz", "wxp", "convw", "convb", "ovf"]:
        R[k] = fence_res(p, k)

    for c in range(NCH):
        p.emit("dve", lambda e, c=c: e.tensor_copy(
            xn[:, c * CW + 3:c * CW + 3 + 2048], xres[:, c * 2048:(c + 1) * 2048]),
            reads=[r_xres], writes=[R["xn"]])
    for c in range(NCH):
        p.dma("sp", ovf[:, c * 3:(c + 1) * 3],
              d["xTov"].rearrange("(c p) t -> c p t", p=128)[c], writes=[R["ovf"]])
    for c in range(NCH):
        p.emit("dve", lambda e, c=c: e.tensor_copy(xn[:, c * CW:c * CW + 3],
                                                   ovf[:, c * 3:(c + 1) * 3]),
               reads=[R["ovf"]], writes=[R["xn"]])

    def ln_block(tok0, n, roff):
        mu = rows[0:1, roff:roff + n]
        rs = rows[0:1, roff + n:roff + 2 * n]
        mr = rows[0:1, roff + 2 * n:roff + 3 * n]
        for s0 in range(0, n, 512):
            w = min(512, n - s0)

            def mmsum(e, s0=s0, w=w):
                last = None
                for c in range(NCH):
                    last = e.matmul(pb[5][0:1, 0:w], C.onescol_b[:, :],
                                    xn[:, c * CW + tok0 + s0:c * CW + tok0 + s0 + w],
                                    start=(c == 0), stop=(c == NCH - 1))
                return last
            p.emit("pe", mmsum, reads=[R["xn"], C.r["onescol_b"]], writes=[rpb[5]])
            for c in range(NCH):
                p.emit("act", lambda e, c=c, s0=s0, w=w: e.activation(
                    scrb[:, 0:w], xn[:, c * CW + tok0 + s0:c * CW + tok0 + s0 + w],
                    AF.Square),
                    reads=[R["xn"]], writes=[R["scrb"]])
                p.emit("pe", lambda e, c=c, w=w: e.matmul(
                    pb[4][0:1, 0:w], C.onescol_b[:, :], scrb[:, 0:w],
                    start=(c == 0), stop=(c == NCH - 1)),
                    reads=[R["scrb"], C.r["onescol_b"]], writes=[rpb[4]])
            p.emit("dve", lambda e, s0=s0, w=w: e.tensor_scalar(
                mu[0:1, s0:s0 + w], pb[5][0:1, 0:w], 1.0 / DM, None, ALU.mult),
                reads=[rpb[5]], writes=[R["rows"]])
            p.emit("dve", lambda e, s0=s0, w=w: e.tensor_scalar(
                rs[0:1, s0:s0 + w], pb[4][0:1, 0:w], 1.0 / DM, None, ALU.mult),
                reads=[rpb[4]], writes=[R["rows"]])
        musq = rows[0:1, RO_DROW:RO_DROW + n]
        p.emit("act", lambda e: e.activation(musq, mu, AF.Square),
               reads=[R["rows"]], writes=[R["rows"]])
        p.emit("dve", lambda e: e.tensor_sub(rs, rs, musq),
               reads=[R["rows"]], writes=[R["rows"]])
        p.emit("act", lambda e: e.activation(rs, rs, AF.Ln, bias=C.eps1[0:1, 0:1]),
               reads=[R["rows"], C.r["eps1"]], writes=[R["rows"]])
        p.emit("act", lambda e: e.activation(rs, rs, AF.Exp, bias=0.0, scale=-0.5),
               reads=[R["rows"]], writes=[R["rows"]])
        p.emit("dve", lambda e: e.tensor_mul(mr, mu, rs),
               reads=[R["rows"]], writes=[R["rows"]])
        nb = (n + 511) // 512
        for tb in range(nb):
            s0, s1 = tb * 512, min((tb + 1) * 512, n)
            w = s1 - s0
            bcast_row(p, C, pb[5], rpb[5], rows[0:1, roff + n + s0:roff + n + s1],
                      R["rows"], w)
            bcast_row(p, C, pb[4], rpb[4], rows[0:1, roff + 2 * n + s0:roff + 2 * n + s1],
                      R["rows"], w)
            for c in range(NCH):
                if tok0 == 0:
                    src_f, r_src = ovf[:, c * 3:(c + 1) * 3], R["ovf"]
                else:
                    src_f, r_src = xres[:, c * 2048 + s0:c * 2048 + s1], r_xres
                dst = xn[:, c * CW + tok0 + s0:c * CW + tok0 + s1]
                p.emit("dve", lambda e, src_f=src_f, w=w: e.tensor_mul(
                    scr[:, 0:w], src_f, pb[5][:, 0:w]),
                    reads=[r_src, rpb[5]], writes=[R["scr"]])
                p.emit("dve", lambda e, dst=dst, w=w: e.tensor_sub(dst, scr[:, 0:w],
                                                                   pb[4][:, 0:w]),
                       reads=[R["scr"], rpb[4]], writes=[R["xn"]])

    ln_block(3, 2048, RO_MEAN)
    ln_block(0, 3, RO_OV)

    if "xn" in dbg:
        p.dma("pool", dbg["xn"][:, :], xn[:, :], reads=[R["xn"]], writes=[Res("dbgxn")])

    for kc in range(NDI):
        p.dma("pool", wxp.rearrange("p (k n) -> p k n", n=80)[:, kc],
              d["Wxp"].rearrange("(k p) n -> k p n", p=128)[kc], writes=[R["wxp"]])
    p.dma("sp", convw[:, :], d["convw"][:, :], writes=[R["convw"]])
    p.dma("sp", convb[:, :], d["convb"][:, :], writes=[R["convb"]])

    for co in range(NDI):
        wt = wtile[:, (co % 2) * 1024:(co % 2) * 1024 + 1024]
        r_wt = R["wtile"][co % 2]
        p.dma("pool", wt.rearrange("p (k n) -> p k n", n=128),
              d["Win"].rearrange("(k p) n -> p k n", p=128)[:, :, co * 128:(co + 1) * 128],
              reads=[r_wt], writes=[r_wt])

        def mm_small(e, wt=wt):
            last = None
            for kc in range(NCH):
                last = e.matmul(pb[0][:, 0:3], wt[:, kc * 128:(kc + 1) * 128],
                                xn[:, kc * CW:kc * CW + 3],
                                start=(kc == 0), stop=(kc == NCH - 1))
            return last
        p.emit("pe", mm_small, reads=[r_wt, R["xn"]], writes=[rpb[0]])
        p.emit("dve", lambda e: e.tensor_copy(xcs[:, 0:3], pb[0][:, 0:3]),
               reads=[rpb[0]], writes=[R["xcs"]])
        for tb in range(4):
            s0 = 3 + tb * 512
            bk = tb % 2

            def mm_xc(e, wt=wt, s0=s0, bk=bk):
                last = None
                for kc in range(NCH):
                    last = e.matmul(pb[bk][:, 0:512], wt[:, kc * 128:(kc + 1) * 128],
                                    xn[:, kc * CW + s0:kc * CW + s0 + 512],
                                    start=(kc == 0), stop=(kc == NCH - 1))
                return last
            p.emit("pe", mm_xc, reads=[r_wt, R["xn"]], writes=[rpb[bk]])
            p.emit("dve", lambda e, s0=s0, bk=bk: e.tensor_copy(
                xcs[:, s0:s0 + 512], pb[bk][:, 0:512]),
                reads=[rpb[bk]], writes=[R["xcs"]])
        p.emit("dve", lambda e, co=co: e.tensor_scalar(
            xconv[:, :], xcs[:, 0:2048], convw[:, co * 4:co * 4 + 1], None, ALU.mult),
            reads=[R["xcs"], R["convw"]], writes=[R["xconv"]])
        for j in range(1, 4):
            p.emit("dve", lambda e, co=co, j=j: e.scalar_tensor_tensor(
                xconv[:, :], xcs[:, j:j + 2048], convw[:, co * 4 + j:co * 4 + j + 1],
                xconv[:, :], ALU.mult, ALU.add),
                reads=[R["xcs"], R["convw"], R["xconv"]], writes=[R["xconv"]])
        if co == 0 and "xcs" in dbg:
            p.dma("pool", dbg["xcs"][:, :], xcs[:, :], reads=[R["xcs"]], writes=[Res("dxcs")])
            p.dma("pool", dbg["xconv"][:, :], xconv[:, :], reads=[R["xconv"]], writes=[Res("dxcv")])
        p.emit("act", lambda e, co=co: e.activation(
            xscur[:, :], xconv[:, :], AF.Silu, bias=convb[:, co:co + 1]),
            reads=[R["xconv"], R["convb"]], writes=[R["xscur"]])
        p.dma("sp", d["xs"][co * 128:(co + 1) * 128, :], xscur[:, :],
              reads=[R["xscur"]], writes=[R["d_xs"]])
        for tb in range(4):
            p.emit("pe", lambda e, co=co, tb=tb: e.matmul(
                pb[2 + tb][0:80, 0:512], wxp[:, co * 80:(co + 1) * 80],
                xscur[:, tb * 512:(tb + 1) * 512],
                start=(co == 0), stop=(co == NDI - 1)),
                reads=[R["wxp"], R["xscur"]], writes=[rpb[2 + tb]])
    for tb in range(4):
        p.emit("dve", lambda e, tb=tb: e.tensor_copy(
            xdbl[0:80, tb * 512:(tb + 1) * 512], pb[2 + tb][0:80, 0:512]),
            reads=[rpb[2 + tb]], writes=[R["xdbl"]])

    for zh in range(2):
        for kc in range(NCH):
            p.dma("pool", wz.rearrange("p (k n) -> p k n", n=1024)[:, kc],
                  d["Win"].rearrange("(k p) n -> k p n", p=128)[kc][:, 2048 + zh * 1024:
                                                                   2048 + (zh + 1) * 1024],
                  reads=[R["wz"]], writes=[R["wz"]])
        for tc in range(NSC):
            for zb in range(2):
                bk = zb % 2

                def mm_z(e, tc=tc, zb=zb, bk=bk):
                    last = None
                    for kc in range(NCH):
                        last = e.matmul(pb[bk][:, 0:512],
                                        xn[:, kc * CW + 3 + tc * 128:kc * CW + 3 + (tc + 1) * 128],
                                        wz[:, kc * 1024 + zb * 512:kc * 1024 + (zb + 1) * 512],
                                        start=(kc == 0), stop=(kc == NCH - 1))
                    return last
                p.emit("pe", mm_z, reads=[R["xn"], R["wz"]], writes=[rpb[bk]])
                p.emit("act", lambda e, zh=zh, zb=zb, bk=bk: e.activation(
                    szt[:, zh * 1024 + zb * 512:zh * 1024 + (zb + 1) * 512],
                    pb[bk][:, 0:512], AF.Silu),
                    reads=[rpb[bk]], writes=[R["szt"]])
            p.dma("sp", d["sz"][tc * 128:(tc + 1) * 128, zh * 1024:(zh + 1) * 1024],
                  szt[:, zh * 1024:(zh + 1) * 1024], reads=[R["szt"]], writes=[R["d_sz"]])

    s12.close()

    s36 = ExitStack()
    hrun = s36.enter_context(nc.sbuf_tensor("l0_hrun", [16, 2048], F32))
    hin = s36.enter_context(nc.sbuf_tensor("l0_hin", [16, 2048], F32))
    htmp = s36.enter_context(nc.sbuf_tensor("l0_htmp", [16, 2048], BF16))
    ccs = s36.enter_context(nc.sbuf_tensor("l0_ccs", [16, 2048], F32))
    dpc = s36.enter_context(nc.sbuf_tensor("l0_dpc", [16, 40], F32))
    mtf = s36.enter_context(nc.sbuf_tensor("l0_mtf", [128, 128], F32))
    macc = s36.enter_context(nc.sbuf_tensor("l0_macc", [128, 128], F32))
    mtb = s36.enter_context(nc.sbuf_tensor("l0_mtb", [128, 128], BF16))
    esc = s36.enter_context(nc.sbuf_tensor("l0_esc", [128, 2 * 128], F32))
    qT = s36.enter_context(nc.sbuf_tensor("l0_qT", [16, 128], BF16))
    rtok = s36.enter_context(nc.sbuf_tensor("l0_rtok", [128, 16], BF16))
    bbtok = s36.enter_context(nc.sbuf_tensor("l0_bbtok", [128, 16], F32))
    xstok = s36.enter_context(nc.sbuf_tensor("l0_xstok", [128, 2 * 2048], BF16))
    ytok = s36.enter_context(nc.sbuf_tensor("l0_ytok", [128, 2048], BF16))
    cT = s36.enter_context(nc.sbuf_tensor("l0_cT", [16, 2048], F32))
    cTb = s36.enter_context(nc.sbuf_tensor("l0_cTb", [16, 2048], BF16))
    onehot = s36.enter_context(nc.sbuf_tensor("l0_onehot", [16, 16 * 128], BF16))
    bpT = s36.enter_context(nc.sbuf_tensor("l0_bpT", [16, 2048], F32))
    for k in ["hrun", "hin", "htmp", "ccs", "dpc", "mtf", "macc", "mtb", "esc",
              "qT", "rtok", "bbtok", "ytok", "cT", "cTb", "bpT", "onehot"]:
        R[k] = fence_res(p, k)
    p.emit("dve", lambda e: e.memset(onehot[:, :], 1.0),
           reads=[R["onehot"]], writes=[R["onehot"]])
    p.emit("pool", lambda e: e.affine_select(onehot[:, :], onehot[:, :],
                                             [[1, 16], [0, 128]], ALU.is_equal, 0.0,
                                             base=0, channel_multiplier=-1),
           reads=[R["onehot"]], writes=[R["onehot"]])
    R["xstok"] = [fence_res(p, "xstok0"), fence_res(p, "xstok1")]

    p.emit("dve", lambda e: e.tensor_copy(cT[:, :], xdbl[64:80, :]),
           reads=[R["xdbl"]], writes=[R["cT"]])
    p.emit("dve", lambda e: e.tensor_copy(bpT[:, :], xdbl[32:48, :]),
           reads=[R["xdbl"]], writes=[R["bpT"]])
    p.emit("dve", lambda e: e.tensor_copy(cTb[:, :], cT[:, :]),
           reads=[R["cT"]], writes=[R["cTb"]])
    sp1 = rows[0:1, RO_SP1:RO_SP1 + 2048]
    p.emit("act", lambda e: e.activation(sp1, xdbl[0:1, 0:2048], AF.Abs),
           reads=[R["xdbl"]], writes=[R["rows"]])
    p.emit("act", lambda e: e.activation(sp1, sp1, AF.Exp, bias=0.0, scale=-1.0),
           reads=[R["rows"]], writes=[R["rows"]])
    p.emit("act", lambda e: e.activation(sp1, sp1, AF.Ln, bias=1.0),
           reads=[R["rows"]], writes=[R["rows"]])
    p.emit("dve", lambda e: e.tensor_scalar(dl, xdbl[0:1, 0:2048], 0.0, None, ALU.max),
           reads=[R["xdbl"]], writes=[R["rows"]])
    p.emit("dve", lambda e: e.tensor_add(dl, dl, sp1),
           reads=[R["rows"]], writes=[R["rows"]])
    p.emit("dve", lambda e: e.tensor_tensor_scan(cd, dl, dl, 0.0, ALU.add, ALU.max),
           reads=[R["rows"]], writes=[R["rows"]])
    for tc in range(NSC):
        def mk(e, tc=tc):
            dst = rows[0:1, RO_CTOT + tc:RO_CTOT + tc + 1]
            if tc == 0:
                return e.tensor_copy(dst, cd[0:1, CL - 1:CL])
            return e.tensor_sub(dst, cd[0:1, (tc + 1) * CL - 1:(tc + 1) * CL],
                                cd[0:1, tc * CL - 1:tc * CL])
        p.emit("dve", mk, reads=[R["rows"]], writes=[R["rows"]])
    p.emit("pe", lambda e: e.matmul(pb[6][0:16, 0:16], arow[:, :],
                                    rows[0:1, RO_CTOT:RO_CTOT + 16], start=True, stop=True),
           reads=[R["arow"], R["rows"]], writes=[rpb[6]])
    p.emit("act", lambda e: e.activation(dpc[:, 0:16], pb[6][0:16, 0:16], AF.Exp),
           reads=[rpb[6]], writes=[R["dpc"]])
    PO = 20
    p.emit("dve", lambda e: e.memset(dpc[:, PO:PO + 1], 1.0),
           reads=[R["dpc"]], writes=[R["dpc"]])
    for tc in range(1, NSC):
        p.emit("dve", lambda e, tc=tc: e.tensor_mul(
            dpc[:, PO + tc:PO + tc + 1], dpc[:, PO + tc - 1:PO + tc], dpc[:, tc - 1:tc]),
            reads=[R["dpc"]], writes=[R["dpc"]])

    def chunk_prep(tc, xst, r_xst):
        p.dma("sp", xst, d["xs"][:, tc * 128:(tc + 1) * 128], transpose=True,
              reads=[R["d_xs"]], writes=[r_xst])
        p.emit("pe", lambda e, tc=tc: e.transpose(pb[6][0:128, 0:16],
                                                  bpT[:, tc * 128:(tc + 1) * 128],
                                                  C.ident[0:16, 0:16]),
               reads=[R["bpT"], C.r["ident"]], writes=[rpb[6]])
        p.emit("pe", lambda e, tc=tc: e.transpose(pb[7][0:128, 0:1],
                                                  dl[0:1, tc * 128:(tc + 1) * 128],
                                                  C.ident[0:1, 0:1]),
               reads=[R["rows"], C.r["ident"]], writes=[rpb[7]])
        p.emit("dve", lambda e: e.tensor_copy(scr[:, 0:1], pb[7][0:128, 0:1]),
               reads=[rpb[7]], writes=[R["scr"]])
        p.emit("dve", lambda e: e.tensor_scalar(bbtok[:, :], pb[6][0:128, 0:16],
                                                scr[:, 0:1], None, ALU.mult),
               reads=[rpb[6], R["scr"]], writes=[R["bbtok"]])

    def chunk_scan_step(tc, xst, r_xst):
        p.emit("dve", lambda e, tc=tc: e.tensor_scalar(
            rows[0:1, RO_CDREV:RO_CDREV + 128], cd[0:1, tc * 128:(tc + 1) * 128],
            -1.0, cd[0:1, (tc + 1) * 128 - 1:(tc + 1) * 128], ALU.mult, ALU.add),
            reads=[R["rows"]], writes=[R["rows"]])
        p.emit("pe", lambda e: e.matmul(pb[6][0:128, 0:16],
                                        rows[0:1, RO_CDREV:RO_CDREV + 128], arow[:, :],
                                        start=True, stop=True),
               reads=[R["rows"], R["arow"]], writes=[rpb[6]])
        p.emit("act", lambda e: e.activation(esc[:, 0:16], pb[6][0:128, 0:16], AF.Exp),
               reads=[rpb[6]], writes=[R["esc"]])
        p.emit("dve", lambda e: e.tensor_mul(rtok[:, :], esc[:, 0:16], bbtok[:, :]),
               reads=[R["esc"], R["bbtok"]], writes=[R["rtok"]])
        for dbk in range(4):
            p.emit("pe", lambda e, dbk=dbk: e.matmul(
                pb[2 + dbk][0:16, 0:512], rtok[:, :], xst[:, dbk * 512:(dbk + 1) * 512],
                start=True, stop=True),
                reads=[R["rtok"], r_xst], writes=[rpb[2 + dbk]])
        for dbk in range(4):
            p.emit("dve", lambda e, tc=tc, dbk=dbk: e.scalar_tensor_tensor(
                hrun[:, dbk * 512:(dbk + 1) * 512], hrun[:, dbk * 512:(dbk + 1) * 512],
                dpc[:, tc:tc + 1], pb[2 + dbk][0:16, 0:512], ALU.mult, ALU.add),
                reads=[R["hrun"], R["dpc"], rpb[2 + dbk]], writes=[R["hrun"]])

    p.emit("dve", lambda e: e.memset(hrun[:, :], 0.0),
           reads=[R["hrun"]], writes=[R["hrun"]])
    for tc in range(NSC):
        xst = xstok[:, (tc % 2) * 2048:(tc % 2) * 2048 + 2048]
        chunk_prep(tc, xst, R["xstok"][tc % 2])
        chunk_scan_step(tc, xst, R["xstok"][tc % 2])

    if "xdbl" in dbg:
        p.dma("sp", dbg["xdbl"][:, :], xdbl[:, :], reads=[R["xdbl"]], writes=[Res("dbg1")])
        p.dma("sp", dbg["rows"][:, :], rows[:, :], reads=[R["rows"]], writes=[Res("dbg2")])
        p.dma("sp", dbg["hrun"][:, :], hrun[:, :], reads=[R["hrun"]], writes=[Res("dbg3")])

    p.dma("sp", d["ccin"][0:16, :], hrun[:, :], reads=[R["hrun"]], writes=[R["d_ccin"]])
    p.emit("pool", lambda e: e.collective_compute(
        "AllGather", ALU.bypass, replica_groups=[[2 * i, 2 * i + 1] for i in range(4)],
        ins=[d["ccin"].opt()], outs=[d["ccout"].opt()]),
        reads=[R["d_ccin"]], writes=[R["d_ccout"]], inc=1)
    collective_barrier(p, d, scr[0:16, 0:16], R["scr"], R["d_ccout"])
    p.dma("sp", ccs[:, :], d["ccout"][0:16, :], reads=[R["d_ccout"]], writes=[R["ccs"]])
    p.emit("dve", lambda e: e.tensor_scalar(hin[:, :], ccs[:, :], flagc[:, :], None, ALU.mult),
           reads=[R["ccs"], R["flagc"]], writes=[R["hin"]])

    p.emit("dve", lambda e: e.memset(hrun[:, :], 0.0), reads=[R["hrun"]], writes=[R["hrun"]])
    for tc in range(NSC):
        xst = xstok[:, (tc % 2) * 2048:(tc % 2) * 2048 + 2048]
        r_xst = R["xstok"][tc % 2]
        chunk_prep(tc, xst, r_xst)
        p.dma("sp", szt[:, :], d["sz"][tc * 128:(tc + 1) * 128, :],
              reads=[R["d_sz"], R["szt"]], writes=[R["szt"]])
        p.emit("dve", lambda e, tc=tc: e.scalar_tensor_tensor(
            htmp[:, :], hin[:, :], dpc[:, PO + tc:PO + tc + 1], hrun[:, :],
            ALU.mult, ALU.add),
            reads=[R["hin"], R["dpc"], R["hrun"]], writes=[R["htmp"]])
        p.emit("dve", lambda e, tc=tc: e.tensor_scalar(
            rows[0:1, RO_NEGCD:RO_NEGCD + 128], cd[0:1, tc * 128:(tc + 1) * 128],
            -1.0, None, ALU.mult),
            reads=[R["rows"]], writes=[R["rows"]])

        def mmdt(e, tc=tc):
            e.matmul(pb[6][:, 0:128], C.onesrow_f[0:1, :], cd[0:1, tc * 128:(tc + 1) * 128],
                     start=True, stop=False)
            return e.matmul(pb[6][:, 0:128], rows[0:1, RO_NEGCD:RO_NEGCD + 128],
                            C.onesrow_f[0:1, :], start=False, stop=True)
        p.emit("pe", mmdt, reads=[R["rows"], C.r["onesrow_f"]], writes=[rpb[6]])
        p.emit("dve", lambda e: e.tensor_add(mtf[:, :], pb[6][:, 0:128], C.minf[:, :]),
               reads=[rpb[6], C.r["minf"]], writes=[R["mtf"]])
        for n in range(DS):
            eb = esc[:, (n % 2) * 128:(n % 2) * 128 + 128]
            bk = 6 + (n % 2)
            p.emit("pe", lambda e, n=n, tc=tc, bk=bk: e.matmul(
                pb[bk][:, 0:128], onehot[:, n * 128:(n + 1) * 128],
                cTb[:, tc * 128:(tc + 1) * 128], start=True, stop=True),
                reads=[R["onehot"], R["cTb"]], writes=[rpb[bk]])
            p.emit("act", lambda e, n=n, eb=eb: e.activation(
                eb, mtf[:, :], AF.Exp, bias=0.0, scale=abc[:, n:n + 1]),
                reads=[R["mtf"], R["abc"]], writes=[R["esc"]])
            p.emit("dve", lambda e, eb=eb, bk=bk: e.tensor_mul(eb, eb, pb[bk][:, 0:128]),
                   reads=[R["esc"], rpb[bk]], writes=[R["esc"]])
            if n == 0:
                p.emit("dve", lambda e, n=n, eb=eb: e.tensor_scalar(
                    macc[:, :], eb, bbtok[:, n:n + 1], None, ALU.mult),
                    reads=[R["esc"], R["bbtok"]], writes=[R["macc"]])
            elif n < DS - 1:
                p.emit("dve", lambda e, n=n, eb=eb: e.scalar_tensor_tensor(
                    macc[:, :], eb, bbtok[:, n:n + 1], macc[:, :], ALU.mult, ALU.add),
                    reads=[R["esc"], R["bbtok"], R["macc"]], writes=[R["macc"]])
            else:
                p.emit("dve", lambda e, n=n, eb=eb: e.scalar_tensor_tensor(
                    mtb[:, :], eb, bbtok[:, n:n + 1], macc[:, :], ALU.mult, ALU.add),
                    reads=[R["esc"], R["bbtok"], R["macc"]], writes=[R["mtb"]])

        def mkcdc(e, tc=tc):
            dst = rows[0:1, RO_CDC:RO_CDC + 128]
            if tc == 0:
                return e.tensor_copy(dst, cd[0:1, 0:128])
            return e.tensor_scalar(dst, cd[0:1, tc * 128:(tc + 1) * 128], 1.0,
                                   cd[0:1, tc * 128 - 1:tc * 128], ALU.mult, ALU.subtract)
        p.emit("dve", mkcdc, reads=[R["rows"]], writes=[R["rows"]])
        p.emit("pe", lambda e: e.matmul(pb[6][0:16, 0:128], arow[:, :],
                                        rows[0:1, RO_CDC:RO_CDC + 128], start=True, stop=True),
               reads=[R["arow"], R["rows"]], writes=[rpb[6]])
        p.emit("act", lambda e: e.activation(esc[0:16, 0:128], pb[6][0:16, 0:128], AF.Exp),
               reads=[rpb[6]], writes=[R["esc"]])
        p.emit("dve", lambda e, tc=tc: e.tensor_mul(
            qT[:, :], esc[0:16, 0:128], cT[:, tc * 128:(tc + 1) * 128]),
            reads=[R["esc"], R["cT"]], writes=[R["qT"]])
        for dbk in range(4):
            def mmy(e, dbk=dbk):
                e.matmul(pb[2 + dbk][:, 0:512], mtb[:, :], xst[:, dbk * 512:(dbk + 1) * 512],
                         start=True, stop=False)
                return e.matmul(pb[2 + dbk][:, 0:512], qT[:, :],
                                htmp[:, dbk * 512:(dbk + 1) * 512], start=False, stop=True)
            p.emit("pe", mmy, reads=[R["mtb"], R["qT"], r_xst, R["htmp"]],
                   writes=[rpb[2 + dbk]])
        for dbk in range(4):
            p.emit("dve", lambda e, dbk=dbk: e.tensor_mul(
                scr[:, 0:512], xst[:, dbk * 512:(dbk + 1) * 512],
                dbc[:, dbk * 512:(dbk + 1) * 512]),
                reads=[r_xst, R["dbc"]], writes=[R["scr"]])
            p.emit("dve", lambda e, dbk=dbk: e.tensor_add(
                scr[:, 0:512], scr[:, 0:512], pb[2 + dbk][:, 0:512]),
                reads=[R["scr"], rpb[2 + dbk]], writes=[R["scr"]])
            p.emit("dve", lambda e, dbk=dbk: e.tensor_mul(
                ytok[:, dbk * 512:(dbk + 1) * 512], scr[:, 0:512],
                szt[:, dbk * 512:(dbk + 1) * 512]),
                reads=[R["scr"], R["szt"]], writes=[R["ytok"]])
        p.dma("sp", d["y"][tc * 128:(tc + 1) * 128, :], ytok[:, :],
              reads=[R["ytok"]], writes=[R["d_y"]])
        chunk_scan_step(tc, xst, r_xst)

    s36.close()

    s7 = ExitStack()
    yT = s7.enter_context(nc.sbuf_tensor("l0_yT", [128, NDI * 2048], BF16))
    wv2 = s7.enter_context(nc.sbuf_tensor("l0_wv2", [128, 2 * 2048], BF16))
    R["yT"] = fence_res(p, "yT")
    R["wv2"] = [fence_res(p, "wv20"), fence_res(p, "wv21")]
    for kc in range(NDI):
        p.dma("sp", yT[:, kc * 2048:(kc + 1) * 2048],
              d["y"][:, kc * 128:(kc + 1) * 128], transpose=True,
              reads=[R["d_y"]], writes=[R["yT"]])
    for do in range(NCH):
        wv = wv2[:, (do % 2) * 2048:(do % 2) * 2048 + 2048]
        r_wv = R["wv2"][do % 2]
        p.dma("pool", wv.rearrange("p (k n) -> p k n", n=128),
              d["Wout"].rearrange("(k p) n -> p k n", p=128)[:, :, do * 128:(do + 1) * 128],
              reads=[r_wv], writes=[r_wv])
        for tb in range(4):
            bk = tb % 2

            def mmo(e, do=do, tb=tb, bk=bk, wv=wv):
                last = None
                for kc in range(NDI):
                    last = e.matmul(pb[bk][:, 0:512], wv[:, kc * 128:(kc + 1) * 128],
                                    yT[:, kc * 2048 + tb * 512:kc * 2048 + (tb + 1) * 512],
                                    start=(kc == 0), stop=(kc == NDI - 1))
                return last
            p.emit("pe", mmo, reads=[r_wv, R["yT"]], writes=[rpb[bk]])
            p.emit("dve", lambda e, do=do, tb=tb, bk=bk: e.tensor_add(
                xres[:, do * 2048 + tb * 512:do * 2048 + (tb + 1) * 512],
                xres[:, do * 2048 + tb * 512:do * 2048 + (tb + 1) * 512],
                pb[bk][:, 0:512]),
                reads=[rpb[bk], r_xres], writes=[r_xres])
    s7.close()



def feat_norm(p, nc, C, xres, r_xres, xnb, r_xnb, rows, r_rows, scr, r_scr,
              scrb, r_scrb, pb, rpb, mode, wcol=None, r_wcol=None):
    for c in range(NCH):
        p.emit("dve", lambda e, c=c: e.tensor_copy(
            xnb[:, c * 2048:(c + 1) * 2048], xres[:, c * 2048:(c + 1) * 2048]),
            reads=[r_xres], writes=[r_xnb])
    isln = mode in ("ln", "lnip")
    for s0 in range(0, 2048, 512):
        if isln:
            def mmsum(e, s0=s0):
                last = None
                for c in range(NCH):
                    last = e.matmul(pb[5][0:1, 0:512], C.onescol_b[:, :],
                                    xnb[:, c * 2048 + s0:c * 2048 + s0 + 512],
                                    start=(c == 0), stop=(c == NCH - 1))
                return last
            p.emit("pe", mmsum, reads=[r_xnb, C.r["onescol_b"]], writes=[rpb[5]])
        for c in range(NCH):
            p.emit("act", lambda e, c=c, s0=s0: e.activation(
                scrb[:, 0:512], xnb[:, c * 2048 + s0:c * 2048 + s0 + 512], AF.Square),
                reads=[r_xnb], writes=[r_scrb])
            p.emit("pe", lambda e, c=c: e.matmul(
                pb[4][0:1, 0:512], C.onescol_b[:, :], scrb[:, 0:512],
                start=(c == 0), stop=(c == NCH - 1)),
                reads=[r_scrb, C.r["onescol_b"]], writes=[rpb[4]])
        if isln:
            p.emit("dve", lambda e, s0=s0: e.tensor_scalar(
                rows[0:1, s0:s0 + 512], pb[5][0:1, 0:512], 1.0 / DM, None, ALU.mult),
                reads=[rpb[5]], writes=[r_rows])
        p.emit("dve", lambda e, s0=s0: e.tensor_scalar(
            rows[0:1, 2048 + s0:2048 + s0 + 512], pb[4][0:1, 0:512], 1.0 / DM, None,
            ALU.mult),
            reads=[rpb[4]], writes=[r_rows])
    mu = rows[0:1, 0:2048]
    rs = rows[0:1, 2048:4096]
    if isln:
        p.emit("act", lambda e: e.activation(rows[0:1, 4096:6144], mu, AF.Square),
               reads=[r_rows], writes=[r_rows])
        p.emit("dve", lambda e: e.tensor_sub(rs, rs, rows[0:1, 4096:6144]),
               reads=[r_rows], writes=[r_rows])
        epsap, repcs = C.eps1[0:1, 0:1], [C.r["eps1"]]
    else:
        epsap, repcs = C.eps2[0:1, 0:1], [C.r["eps2"]]
    p.emit("act", lambda e: e.activation(rs, rs, AF.Ln, bias=epsap),
           reads=[r_rows] + repcs, writes=[r_rows])
    p.emit("act", lambda e: e.activation(rs, rs, AF.Exp, bias=0.0, scale=-0.5),
           reads=[r_rows], writes=[r_rows])
    if isln:
        p.emit("dve", lambda e: e.tensor_mul(rows[0:1, 4096:6144], mu, rs),
               reads=[r_rows], writes=[r_rows])
    for s0 in range(0, 2048, 512):
        bcast_row(p, C, pb[5], rpb[5], rs[0:1, s0:s0 + 512], r_rows, 512)
        if isln:
            bcast_row(p, C, pb[4], rpb[4], rows[0:1, 4096 + s0:4096 + s0 + 512],
                      r_rows, 512)
        for c in range(NCH):
            if mode == "lnip":
                dst, r_dst = xres[:, c * 2048 + s0:c * 2048 + s0 + 512], r_xres
            else:
                dst, r_dst = xnb[:, c * 2048 + s0:c * 2048 + s0 + 512], r_xnb
            p.emit("dve", lambda e, c=c, s0=s0: e.tensor_mul(
                scr[:, 0:512], xres[:, c * 2048 + s0:c * 2048 + s0 + 512],
                pb[5][:, 0:512]),
                reads=[r_xres, rpb[5]], writes=[r_scr])
            if isln:
                p.emit("dve", lambda e, dst=dst: e.tensor_sub(dst, scr[:, 0:512],
                                                              pb[4][:, 0:512]),
                       reads=[r_scr, rpb[4]], writes=[r_dst])
            else:
                p.emit("dve", lambda e, c=c, dst=dst: e.tensor_scalar(
                    dst, scr[:, 0:512], wcol[:, c:c + 1], None, ALU.mult),
                    reads=[r_scr, r_wcol], writes=[r_dst])


def build_attn_block(p, nc, ctx, C, xres, r_xres, d, kind):
    import math as _m
    sa = ExitStack()
    rows = sa.enter_context(nc.sbuf_tensor("at_rows", [1, 2048], F32))
    scr = sa.enter_context(nc.sbuf_tensor("at_scr", [128, 512], F32))
    scrb = sa.enter_context(nc.sbuf_tensor("at_scrb", [128, 512], BF16))
    qw = sa.enter_context(nc.sbuf_tensor("at_qw", [128, 4096], BF16))
    kw = sa.enter_context(nc.sbuf_tensor("at_kw", [128, 4096], BF16))
    vw = sa.enter_context(nc.sbuf_tensor("at_vw", [128, 32 * 130], BF16))
    wt3 = sa.enter_context(nc.sbuf_tensor("at_wt3", [128, 3 * 1024], BF16))
    wcol = sa.enter_context(nc.sbuf_tensor("at_wcol", [128, 8], F32))
    bcols = sa.enter_context(nc.sbuf_tensor("at_bcols", [128, 16], F32))
    bvbc = sa.enter_context(nc.sbuf_tensor("at_bvbc", [128, 128], F32))
    nmx = sa.enter_context(nc.sbuf_tensor("at_nmx", [128, 32], F32))
    nmx32 = sa.enter_context(nc.sbuf_tensor("at_nmx32", [32, 128], F32))
    negrow = sa.enter_context(nc.sbuf_tensor("at_negrow", [1, 4096], BF16))
    abf = sa.enter_context(nc.sbuf_tensor("at_abf", [128, 2 * 512], BF16))
    rbc = sa.enter_context(nc.sbuf_tensor("at_rbc", [64, 512], F32))
    obf = sa.enter_context(nc.sbuf_tensor("at_obf", [64, 512], BF16))
    m512 = sa.enter_context(nc.sbuf_tensor("at_m512", [128, 896], BF16))
    fsel = sa.enter_context(nc.sbuf_tensor("at_fsel", [128, 2], F32))
    pb = [psumt(nc, sa, f"at_pb{i}", [128, 512]) for i in range(8)]
    rpb = [Res(f"at_pb{i}") for i in range(8)]
    N = ["rows", "scr", "scrb", "qw", "kw", "vw", "wcol", "bcols",
         "bvbc", "nmx", "nmx32", "negrow", "abf", "rbc", "obf", "m512", "fsel",
         "d_agi", "d_ago", "d_oag", "d_oago", "d_mx"]
    R = {k: fence_res(p, k) for k in N}
    R["wt3"] = [fence_res(p, f"wt3_{i}") for i in range(3)]

    p.emit("dve", lambda e: e.memset(m512[:, :], 1.0), writes=[R["m512"]])
    p.emit("pool", lambda e: e.affine_select(
        m512[:, :], m512[:, :], [[1, 896]], ALU.is_ge, 0.0,
        base=-384, channel_multiplier=-1),
        reads=[R["m512"]], writes=[R["m512"]])
    p.dma("sp", rows[0:1, 0:1], d["flag"][:, :], writes=[R["rows"]])
    p.emit("pe", lambda e: e.matmul(pb[7][0:128, 0:1], C.onesrow_f[0:1, :],
                                    rows[0:1, 0:1], start=True, stop=True),
           reads=[R["rows"], C.r["onesrow_f"]], writes=[rpb[7]])
    p.emit("dve", lambda e: e.tensor_copy(fsel[:, 1:2], pb[7][0:128, 0:1]),
           reads=[rpb[7]], writes=[R["fsel"]])
    p.emit("dve", lambda e: e.tensor_scalar(fsel[:, 0:1], fsel[:, 1:2], -1.0, 1.0,
                                            ALU.mult, ALU.add),
           reads=[R["fsel"]], writes=[R["fsel"]])

    pfx = "l1a" if kind == "l1attn" else "l1f"
    if kind == "l1ffn":
        p.dma("sp", wcol[:, :], d[pfx + "_n1w"][:, :], writes=[R["wcol"]])
        p.dma("sp", bcols[:, 0:1], d[pfx + "_qnw"][:, :], writes=[R["bcols"]])
        p.dma("sp", bcols[:, 1:2], d[pfx + "_knw"][:, :], writes=[R["bcols"]])
    else:
        p.dma("sp", bcols[:, 0:8], d[pfx + "_bqk"][:, :], writes=[R["bcols"]])
        p.dma("sp", bcols[:, 8:16], d[pfx + "_bo"][:, :], writes=[R["bcols"]])

    sn = ExitStack()
    xnb = sn.enter_context(nc.sbuf_tensor("at_xnb", [128, NCH * 2048], BF16))
    rows6 = sn.enter_context(nc.sbuf_tensor("at_rows6", [1, 6144], F32))
    R["xnb"] = fence_res(p, "xnb")
    R["rows6"] = fence_res(p, "rows6")
    if kind == "l1ffn":
        R["d_xsp"] = Res("d_xsp")
        for c in range(NCH):
            p.dma("sp", d["xsp"][c * 128:(c + 1) * 128, :],
                  xres[:, c * 2048:(c + 1) * 2048], reads=[r_xres], writes=[R["d_xsp"]])
        feat_norm(p, nc, C, xres, r_xres, xnb, R["xnb"], rows6, R["rows6"], scr,
                  R["scr"], scrb, R["scrb"], pb, rpb, "lnip")
    feat_norm(p, nc, C, xres, r_xres, xnb, R["xnb"], rows6, R["rows6"], scr, R["scr"],
              scrb, R["scrb"], pb, rpb,
              "ln" if kind == "l1attn" else "rms", wcol, R["wcol"])

    for c in range(NCH):
        p.dma("sp", d["agi"][c * 128:(c + 1) * 128, :], xnb[:, c * 2048:(c + 1) * 2048],
              reads=[R["xnb"]], writes=[R["d_agi"]])
    p.emit("pool", lambda e: e.collective_compute(
        "AllGather", ALU.bypass, replica_groups=[[2 * i, 2 * i + 1] for i in range(4)],
        ins=[d["agi"].opt()], outs=[d["ago"].opt()]),
        reads=[R["d_agi"]], writes=[R["d_ago"]], inc=1)
    collective_barrier(p, d, scr[0:16, 0:16], R["scr"], R["d_ago"])
    sn.close()
    xnf = sa.enter_context(nc.sbuf_tensor("at_xnf", [128, NCH * 4096], BF16))
    R["xnf"] = fence_res(p, "xnf")
    for c in range(NCH):
        for rr in range(2):
            p.dma("sp", xnf[:, c * 4096 + rr * 2048:c * 4096 + (rr + 1) * 2048],
                  d["ago"][rr * 1024 + c * 128:rr * 1024 + (c + 1) * 128, :],
                  reads=[R["d_ago"]], writes=[R["xnf"]])

    if kind == "l1attn":
        cosb = sa.enter_context(nc.sbuf_tensor("at_cos", [128, 4096], BF16))
        sinb = sa.enter_context(nc.sbuf_tensor("at_sin", [128, 4096], BF16))
        pmb = sa.enter_context(nc.sbuf_tensor("at_pmb", [128, 128], BF16))
        sr = ExitStack()
        angf = sr.enter_context(nc.sbuf_tensor("at_ang", [128, 512], F32))
        angi = sr.enter_context(nc.sbuf_tensor("at_angi", [128, 512], mybir.dt.int32))
        for k in ["cosb", "sinb", "angf", "angi", "pmb"]:
            R[k] = fence_res(p, k)
        p.dma("pool", pmb[:, :], d["ropePM"][:, :], writes=[R["pmb"]])
        p.dma("sp", rows[0:1, 1024:1536], d[pfx + "_bvrow"][0:1, :],
              reads=[R["rows"]], writes=[R["rows"]])
        p.emit("pool", lambda e: e.iota(rows[0:1, 1536:1600], [[1, 64]], base=0,
                                        channel_multiplier=0,
                                        allow_small_or_imprecise_dtypes=True),
               reads=[R["rows"]], writes=[R["rows"]])
        p.emit("dve", lambda e: e.tensor_scalar(rows[0:1, 1600:1664], rows[0:1, 1536:1600],
                                                32.0, None, ALU.is_ge),
               reads=[R["rows"]], writes=[R["rows"]])
        p.emit("dve", lambda e: e.scalar_tensor_tensor(rows[0:1, 1536:1600],
                                                       rows[0:1, 1600:1664], -32.0,
                                                       rows[0:1, 1536:1600], ALU.mult,
                                                       ALU.add),
               reads=[R["rows"]], writes=[R["rows"]])
        p.emit("act", lambda e: e.activation(rows[0:1, 1536:1600], rows[0:1, 1536:1600],
                                             AF.Exp, bias=0.0,
                                             scale=-2.0 * _m.log(10000.0) / 64.0),
               reads=[R["rows"]], writes=[R["rows"]])

        TWO_PI = 2.0 * _m.pi
        for half in range(2):
            dst, rdst = (sinb, R["sinb"]) if half == 0 else (cosb, R["cosb"])
            for tb in range(8):
                p.emit("pool", lambda e, tb=tb: e.iota(
                    rows[0:1, 0:512], [[1, 512]], base=tb * 512 + 1,
                    channel_multiplier=0, allow_small_or_imprecise_dtypes=True),
                    reads=[R["rows"]], writes=[R["rows"]])
                p.emit("pe", lambda e, tb=tb: e.matmul(
                    pb[6][0:64, 0:512], rows[0:1, 1536:1600], rows[0:1, 0:512],
                    start=True, stop=True),
                    reads=[R["rows"]], writes=[rpb[6]])
                p.emit("dve", lambda e, half=half: e.tensor_scalar(
                    angf[0:64, :], pb[6][0:64, 0:512], 1.0,
                    _m.pi / 2.0 if half else 0.0, ALU.mult, ALU.add),
                    reads=[rpb[6]], writes=[R["angf"]])
                p.emit("dve", lambda e: e.tensor_scalar(
                    scr[0:64, 0:512], pb[6][0:64, 0:512], 1.0 / TWO_PI,
                    0.25 if half else 0.0, ALU.mult, ALU.add),
                    reads=[rpb[6], R["scr"]], writes=[R["scr"]])
                p.emit("dve", lambda e: e.tensor_copy(angi[0:64, :], scr[0:64, 0:512]),
                       reads=[R["scr"]], writes=[R["angi"]])
                p.emit("dve", lambda e: e.tensor_copy(scr[0:64, 0:512], angi[0:64, :]),
                       reads=[R["angi"]], writes=[R["scr"]])
                p.emit("dve", lambda e: e.scalar_tensor_tensor(
                    angf[0:64, :], scr[0:64, 0:512], -TWO_PI, angf[0:64, :],
                    ALU.mult, ALU.add),
                    reads=[R["scr"], R["angf"]], writes=[R["angf"]])
                p.emit("act", lambda e, dst=dst, tb=tb: e.activation(
                    dst[0:64, tb * 512:(tb + 1) * 512], angf[0:64, :], AF.Sin),
                    reads=[R["angf"]], writes=[rdst])
            p.emit("dve", lambda e, dst=dst: e.tensor_copy(dst[64:128, :], dst[0:64, :]),
                   reads=[rdst], writes=[rdst])
        sr.close()
    else:
        hsel2 = sa.enter_context(nc.sbuf_tensor("at_hsel2", [128, 2], BF16))
        sel64 = sa.enter_context(nc.sbuf_tensor("at_sel64", [2, 128], F32))
        sqrow = sa.enter_context(nc.sbuf_tensor("at_sqrow", [2, 512], F32))
        for k in ["hsel2", "sel64", "sqrow"]:
            R[k] = fence_res(p, k)
        p.emit("dve", lambda e: e.memset(hsel2[:, :], 1.0), writes=[R["hsel2"]])
        p.emit("pool", lambda e: e.affine_select(hsel2[:, :], hsel2[:, :],
                                                 [[-64, 2]], ALU.is_ge, 0.0,
                                                 base=0, channel_multiplier=1),
               reads=[R["hsel2"]], writes=[R["hsel2"]])
        p.emit("pool", lambda e: e.affine_select(hsel2[:, :], hsel2[:, :],
                                                 [[64, 2]], ALU.is_ge, 0.0,
                                                 base=63, channel_multiplier=-1),
               reads=[R["hsel2"]], writes=[R["hsel2"]])
        p.emit("dve", lambda e: e.memset(sel64[:, :], 1.0), writes=[R["sel64"]])
        p.emit("pool", lambda e: e.affine_select(sel64[:, :], sel64[:, :],
                                                 [[1, 128]], ALU.is_ge, 0.0,
                                                 base=0, channel_multiplier=-64),
               reads=[R["sel64"]], writes=[R["sel64"]])
        p.emit("pool", lambda e: e.affine_select(sel64[:, :], sel64[:, :],
                                                 [[-1, 128]], ALU.is_ge, 0.0,
                                                 base=63, channel_multiplier=64),
               reads=[R["sel64"]], writes=[R["sel64"]])

    SCALE = 1.0 / 8.0
    for w in range(4):
        for i, nm in enumerate(["wq", "wk", "wv"]):
            p.dma("pool", wt3[:, i * 1024:(i + 1) * 1024].rearrange("p (k n) -> p k n", n=128),
                  d[pfx + "_" + nm].rearrange("(k p) n -> p k n", p=128)[:, :, w * 128:(w + 1) * 128],
                  reads=[R["wt3"][i]], writes=[R["wt3"][i]])
        for tb in range(8):
            for i, (dst, rdst) in enumerate([(qw, R["qw"]), (kw, R["kw"])]):
                bk = i % 2

                def mmqk(e, i=i, tb=tb, bk=bk):
                    last = None
                    for kc in range(NCH):
                        last = e.matmul(pb[bk][:, 0:512],
                                        wt3[:, i * 1024 + kc * 128:i * 1024 + (kc + 1) * 128],
                                        xnf[:, kc * 4096 + tb * 512:kc * 4096 + (tb + 1) * 512],
                                        start=(kc == 0), stop=(kc == NCH - 1))
                    return last
                p.emit("pe", mmqk, reads=[R["wt3"][i], R["xnf"]], writes=[rpb[bk]])
                if kind == "l1attn":
                    p.emit("dve", lambda e, dst=dst, tb=tb, bk=bk, i=i, w=w: e.tensor_scalar(
                        dst[:, tb * 512:(tb + 1) * 512], pb[bk][:, 0:512],
                        bcols[:, i * 4 + w:i * 4 + w + 1], None, ALU.add),
                        reads=[rpb[bk], R["bcols"]], writes=[rdst])
                else:
                    p.emit("act", lambda e, dst=dst, tb=tb, bk=bk: e.activation(
                        dst[:, tb * 512:(tb + 1) * 512], pb[bk][:, 0:512], AF.Copy),
                        reads=[rpb[bk]], writes=[rdst])
        p.emit("dve", lambda e: e.memset(vw[:, :], 1.0), reads=[R["vw"]], writes=[R["vw"]])
        if kind == "l1attn":
            bcast_row(p, C, pb[5], rpb[5], rows[0:1, 1024 + w * 128:1024 + (w + 1) * 128],
                      R["rows"], 128)
            p.emit("dve", lambda e: e.tensor_copy(bvbc[:, :], pb[5][:, 0:128]),
                   reads=[rpb[5]], writes=[R["bvbc"]])
        for kt in range(32):
            def mmv(e, kt=kt, w=w):
                last = None
                for kc in range(NCH):
                    last = e.matmul(pb[2][:, 0:128],
                                    xnf[:, kc * 4096 + kt * 128:kc * 4096 + (kt + 1) * 128],
                                    wt3[:, 2 * 1024 + kc * 128:2 * 1024 + (kc + 1) * 128],
                                    start=(kc == 0), stop=(kc == NCH - 1))
                return last
            p.emit("pe", mmv, reads=[R["wt3"][2], R["xnf"]], writes=[rpb[2]])
            if kind == "l1attn":
                p.emit("dve", lambda e, kt=kt: e.tensor_add(scrb[:, 0:128],
                                                            pb[2][:, 0:128], bvbc[:, :]),
                       reads=[rpb[2], R["bvbc"]], writes=[R["scrb"]])
            else:
                p.emit("dve", lambda e, kt=kt: e.tensor_copy(scrb[:, 0:128],
                                                             pb[2][:, 0:128]),
                       reads=[rpb[2]], writes=[R["scrb"]])
            for hw in range(2):
                p.emit("dve", lambda e, kt=kt, hw=hw: e.tensor_copy(
                    vw[:, kt * 130 + hw * 65:kt * 130 + hw * 65 + 64],
                    scrb[:, hw * 64:hw * 64 + 64]),
                    reads=[R["scrb"]], writes=[R["vw"]])
        if kind == "l1attn":
            for (tw, rtw) in [(qw, R["qw"]), (kw, R["kw"])]:
                for tb in range(8):
                    p.emit("pe", lambda e, tw=tw, tb=tb: e.matmul(
                        pb[3][:, 0:512], pmb[:, :], tw[:, tb * 512:(tb + 1) * 512],
                        start=True, stop=True),
                        reads=[R["pmb"], rtw], writes=[rpb[3]])
                    p.emit("dve", lambda e, tw=tw, tb=tb: e.tensor_mul(
                        scr[:, 0:512], tw[:, tb * 512:(tb + 1) * 512],
                        cosb[:, tb * 512:(tb + 1) * 512]),
                        reads=[rtw, R["cosb"]], writes=[R["scr"]])
                    p.emit("dve", lambda e, tb=tb: e.tensor_mul(
                        scrb[:, 0:512], pb[3][:, 0:512], sinb[:, tb * 512:(tb + 1) * 512]),
                        reads=[rpb[3], R["sinb"]], writes=[R["scrb"]])
                    p.emit("dve", lambda e, tw=tw, tb=tb: e.tensor_add(
                        tw[:, tb * 512:(tb + 1) * 512], scr[:, 0:512], scrb[:, 0:512]),
                        reads=[R["scr"], R["scrb"]], writes=[rtw])
        else:
            for (tw, rtw, wc) in [(qw, R["qw"], 0), (kw, R["kw"], 1)]:
                for tb in range(8):
                    p.emit("act", lambda e, tw=tw, tb=tb: e.activation(
                        scrb[:, 0:512], tw[:, tb * 512:(tb + 1) * 512], AF.Square),
                        reads=[rtw], writes=[R["scrb"]])
                    p.emit("pe", lambda e, tb=tb: e.matmul(
                        pb[3][0:2, 0:512], hsel2[:, :], scrb[:, 0:512],
                        start=True, stop=True),
                        reads=[R["scrb"], R["hsel2"]], writes=[rpb[3]])
                    p.emit("dve", lambda e: e.tensor_scalar(
                        sqrow[:, :], pb[3][0:2, 0:512], 1.0 / 64.0, None, ALU.mult),
                        reads=[rpb[3]], writes=[R["sqrow"]])
                    p.emit("act", lambda e: e.activation(sqrow[:, :], sqrow[:, :], AF.Ln,
                                                         bias=C.eps2[0:2, 0:1]),
                           reads=[R["sqrow"], C.r["eps2"]], writes=[R["sqrow"]])
                    p.emit("act", lambda e: e.activation(sqrow[:, :], sqrow[:, :], AF.Exp,
                                                         bias=0.0, scale=-0.5),
                           reads=[R["sqrow"]], writes=[R["sqrow"]])
                    p.emit("pe", lambda e, tb=tb: e.matmul(
                        pb[3][:, 0:512], sel64[:, :], sqrow[:, :],
                        start=True, stop=True),
                        reads=[R["sqrow"], R["sel64"]], writes=[rpb[3]])
                    p.emit("dve", lambda e, tw=tw, tb=tb: e.tensor_mul(
                        scr[:, 0:512], tw[:, tb * 512:(tb + 1) * 512], pb[3][:, 0:512]),
                        reads=[rtw, rpb[3]], writes=[R["scr"]])
                    p.emit("dve", lambda e, tw=tw, tb=tb, wc=wc: e.tensor_scalar(
                        tw[:, tb * 512:(tb + 1) * 512], scr[:, 0:512],
                        bcols[:, wc:wc + 1], None, ALU.mult),
                        reads=[R["scr"], R["bcols"]], writes=[rtw])
        if "dbg_qw" in d and w == 0:
            p.dma("pool", d["dbg_qw"][:, :], qw[:, :], reads=[R["qw"]], writes=[Res("dbq")])
            p.dma("pool", d["dbg_kw"][:, :], kw[:, :], reads=[R["kw"]], writes=[Res("dbk")])
            p.dma("pool", d["dbg_vw"][:, :], vw[:, :], reads=[R["vw"]], writes=[Res("dbv")])
        for hw in range(2):
            hb = hw * 64
            for qt in range(32):
                qe = (qt + 1) * 128
                nkb = (qe + 511) // 512
                for kb in range(nkb):
                    wdt = min(512, qe - kb * 512)
                    bk = kb % 2
                    p.emit("pe", lambda e, qt=qt, kb=kb, wdt=wdt, hb=hb, bk=bk: e.matmul(
                        pb[bk][:, 0:wdt], qw[hb:hb + 64, qt * 128:(qt + 1) * 128],
                        kw[hb:hb + 64, kb * 512:kb * 512 + wdt],
                        start=True, stop=True),
                        reads=[R["qw"], R["kw"]], writes=[rpb[bk]])
                    if kb == 0:
                        p.emit("dve", lambda e, qt=qt, wdt=wdt, bk=bk: e.reduce_max(
                            nmx[:, qt:qt + 1], pb[bk][:, 0:wdt], axis=AX.X, negate=True),
                            reads=[rpb[bk]], writes=[R["nmx"]])
                    else:
                        p.emit("dve", lambda e, wdt=wdt, bk=bk: e.reduce_max(
                            scr[:, 0:1], pb[bk][:, 0:wdt], axis=AX.X, negate=True),
                            reads=[rpb[bk]], writes=[R["scr"]])
                        p.emit("dve", lambda e, qt=qt: e.tensor_tensor(
                            nmx[:, qt:qt + 1], nmx[:, qt:qt + 1], scr[:, 0:1], ALU.min),
                            reads=[R["scr"], R["nmx"]], writes=[R["nmx"]])
            p.emit("pe", lambda e: e.transpose(pb[6][0:32, 0:128], nmx[:, :],
                                               C.ident[:, :]),
                   reads=[R["nmx"], C.r["ident"]], writes=[rpb[6]])
            p.emit("dve", lambda e: e.tensor_copy(nmx32[:, :], pb[6][0:32, 0:128]),
                   reads=[rpb[6]], writes=[R["nmx32"]])
            p.dma("sp", d["mx"].rearrange("o (a b) -> (o a) b", b=128), nmx32[:, :],
                  reads=[R["nmx32"]], writes=[R["d_mx"]])
            p.dma("pool", negrow[0:1, :], d["mx"][0:1, :],
                  reads=[R["d_mx"], R["negrow"]], writes=[R["negrow"]])
            if "dbg_nmx" in d and w == 0 and hw == 0:
                p.dma("pool", d["dbg_nmx"][:, :], negrow[:, :],
                      reads=[R["negrow"]], writes=[Res("dbn")])
                p.dma("pool", d["dbg_nmxraw"][:, :], nmx[:, :],
                      reads=[R["nmx"]], writes=[Res("dbn2")])
            for qb in range(8):
                nkc = (qb + 1) * 4
                for kc in range(nkc):
                    ab = abf[:, (kc % 2) * 512:(kc % 2) * 512 + 512]
                    bk = kc % 2

                    def mmst(e, qb=qb, kc=kc, hb=hb, bk=bk):
                        e.matmul(pb[bk][:, 0:512], kw[hb:hb + 64, kc * 128:(kc + 1) * 128],
                                 qw[hb:hb + 64, qb * 512:(qb + 1) * 512],
                                 start=True, stop=False)
                        return e.matmul(pb[bk][:, 0:512], C.onesrow_b[0:1, :],
                                        negrow[0:1, qb * 512:(qb + 1) * 512],
                                        start=False, stop=True)
                    p.emit("pe", mmst, reads=[R["qw"], R["kw"], R["negrow"],
                                              C.r["onesrow_b"]], writes=[rpb[bk]])
                    p.emit("act", lambda e, ab=ab, bk=bk: e.activation(
                        ab, pb[bk][:, 0:512], AF.Exp, bias=0.0, scale=SCALE),
                        reads=[rpb[bk]], writes=[R["abf"]])
                    dlt = kc - qb * 4
                    if dlt >= 0:
                        o0 = 384 - dlt * 128
                        p.emit("dve", lambda e, ab=ab, o0=o0: e.tensor_mul(
                            ab, ab, m512[:, o0:o0 + 512]),
                            reads=[R["abf"], R["m512"]], writes=[R["abf"]])
                    p.emit("pe", lambda e, ab=ab, kc=kc, hw=hw, nkc=nkc: e.matmul(
                        pb[4][0:65, 0:512], vw[:, kc * 130 + hw * 65:kc * 130 + (hw + 1) * 65],
                        ab, start=(kc == 0), stop=(kc == nkc - 1)),
                        reads=[R["vw"], R["abf"]], writes=[rpb[4]])
                p.emit("dve", lambda e: e.tensor_copy(rows[0:1, 0:512], pb[4][64:65, 0:512]),
                       reads=[rpb[4]], writes=[R["rows"]])
                p.emit("dve", lambda e: e.reciprocal(rows[0:1, 512:1024], rows[0:1, 0:512]),
                       reads=[R["rows"]], writes=[R["rows"]])
                p.emit("pe", lambda e: e.matmul(pb[5][0:64, 0:512], C.onesrow_f[0:1, 0:64],
                                                rows[0:1, 512:1024], start=True, stop=True),
                       reads=[R["rows"], C.r["onesrow_f"]], writes=[rpb[5]])
                p.emit("dve", lambda e: e.tensor_copy(rbc[:, :], pb[5][0:64, 0:512]),
                       reads=[rpb[5]], writes=[R["rbc"]])
                p.emit("dve", lambda e: e.tensor_mul(obf[:, :], pb[4][0:64, 0:512],
                                                     rbc[:, :]),
                       reads=[rpb[4], R["rbc"]], writes=[R["obf"]])
                p.dma("sp", d["oag"][w * 128 + hw * 64:w * 128 + hw * 64 + 64,
                                     qb * 512:(qb + 1) * 512], obf[:, :],
                      reads=[R["obf"]], writes=[R["d_oag"]])

    if "dbg_oag" in d:
        p.dma("pool", d["dbg_oag"][:, :], d["oag"][:, :],
              reads=[R["d_oag"]], writes=[Res("dbo")])
    p.emit("pool", lambda e: e.collective_compute(
        "AllGather", ALU.bypass, replica_groups=[[2 * i, 2 * i + 1] for i in range(4)],
        ins=[d["oag"].opt()], outs=[d["oago"].opt()]),
        reads=[R["d_oag"]], writes=[R["d_oago"]], inc=1)
    collective_barrier(p, d, scr[0:16, 0:16], R["scr"], R["d_oago"])
    osel, r_osel = xnf, R["xnf"]
    for c in range(NCH):
        p.dma("sp", qw[:, 0:2048], d["oago"][c * 128:(c + 1) * 128, 0:2048],
              reads=[R["d_oago"], R["qw"]], writes=[R["qw"]])
        p.dma("sp", kw[:, 0:2048], d["oago"][c * 128:(c + 1) * 128, 2048:4096],
              reads=[R["d_oago"], R["kw"]], writes=[R["kw"]])
        p.emit("dve", lambda e, c=c: e.tensor_scalar(
            osel[:, c * 2048:(c + 1) * 2048], qw[:, 0:2048], fsel[:, 0:1], None,
            ALU.mult),
            reads=[R["qw"], R["fsel"], r_osel], writes=[r_osel])
        p.emit("dve", lambda e, c=c: e.scalar_tensor_tensor(
            osel[:, c * 2048:(c + 1) * 2048], kw[:, 0:2048], fsel[:, 1:2],
            osel[:, c * 2048:(c + 1) * 2048], ALU.mult, ALU.add),
            reads=[R["kw"], R["fsel"], r_osel], writes=[r_osel])
    do_start = 0 if kind == "l1attn" else 2
    for do in range(do_start, NCH):
        i3 = do % 3
        p.dma("pool", wt3[:, i3 * 1024:(i3 + 1) * 1024].rearrange("p (k n) -> p k n", n=128),
              d[pfx + "_wo"].rearrange("(k p) n -> p k n", p=128)[:, :, do * 128:(do + 1) * 128],
              reads=[R["wt3"][i3]], writes=[R["wt3"][i3]])
        for tb in range(4):
            bk = tb % 2

            def mmo(e, do=do, tb=tb, bk=bk, i3=i3):
                last = None
                for kc in range(NCH):
                    last = e.matmul(pb[bk][:, 0:512],
                                    wt3[:, i3 * 1024 + kc * 128:i3 * 1024 + (kc + 1) * 128],
                                    osel[:, kc * 2048 + tb * 512:kc * 2048 + (tb + 1) * 512],
                                    start=(kc == 0), stop=(kc == NCH - 1))
                return last
            p.emit("pe", mmo, reads=[R["wt3"][i3], r_osel], writes=[rpb[bk]])
            if kind == "l1attn":
                p.emit("dve", lambda e, do=do, bk=bk: e.tensor_scalar(
                    scr[:, 0:512], pb[bk][:, 0:512],
                    bcols[:, 8 + do:8 + do + 1], None, ALU.add),
                    reads=[rpb[bk], R["bcols"]], writes=[R["scr"]])
                p.emit("dve", lambda e, do=do, tb=tb: e.tensor_add(
                    xres[:, do * 2048 + tb * 512:do * 2048 + (tb + 1) * 512],
                    xres[:, do * 2048 + tb * 512:do * 2048 + (tb + 1) * 512],
                    scr[:, 0:512]),
                    reads=[R["scr"], r_xres], writes=[r_xres])
            else:
                p.emit("dve", lambda e, do=do, tb=tb, bk=bk: e.tensor_add(
                    xres[:, do * 2048 + tb * 512:do * 2048 + (tb + 1) * 512],
                    xres[:, do * 2048 + tb * 512:do * 2048 + (tb + 1) * 512],
                    pb[bk][:, 0:512]),
                    reads=[rpb[bk], r_xres], writes=[r_xres])
    sa.close()



def build_l0_ffn(p, nc, ctx, C, xres, r_xres, d):
    sf = ExitStack()
    rows = sf.enter_context(nc.sbuf_tensor("f0_rows", [1, 8224], F32))
    rsbc = sf.enter_context(nc.sbuf_tensor("f0_rsbc", [128, 512], F32))
    mrbc = sf.enter_context(nc.sbuf_tensor("f0_mrbc", [128, 512], F32))
    scr = sf.enter_context(nc.sbuf_tensor("f0_scr", [128, 512], F32))
    scr2 = sf.enter_context(nc.sbuf_tensor("f0_scr2", [128, 512], F32))
    scrb = sf.enter_context(nc.sbuf_tensor("f0_scrb", [128, 512], BF16))
    xnb = sf.enter_context(nc.sbuf_tensor("f0_xnb", [128, NCH * 2048], BF16))
    wTb = sf.enter_context(nc.sbuf_tensor("f0_wTb", [16, 2048], BF16))
    onehot = sf.enter_context(nc.sbuf_tensor("f0_onehot", [16, 16 * 128], BF16))
    bcolt = sf.enter_context(nc.sbuf_tensor("f0_bcolt", [128, 64], F32))
    gbias = sf.enter_context(nc.sbuf_tensor("f0_gbias_s", [128, 16], F32))
    ceT = sf.enter_context(nc.sbuf_tensor("f0_ceT_s", [128, NCH * 16], BF16))
    rcebc = sf.enter_context(nc.sbuf_tensor("f0_rcebc", [128, 16], F32))
    wtile = sf.enter_context(nc.sbuf_tensor("f0_wtile", [128, 2 * 1024], BF16))
    pb = [psumt(nc, sf, f"f0_pb{i}", [128, 512]) for i in range(8)]
    rpb = [Res(f"f0_pb{i}") for i in range(8)]
    N = ["rows", "scr", "scr2", "scrb", "xnb", "wtok", "ltok", "onehot", "bcolt",
         "gbias", "ceT", "rcebc", "rsbc", "mrbc"]
    R = {k: fence_res(p, k) for k in N}
    R["wtile"] = [fence_res(p, "f0_wt0"), fence_res(p, "f0_wt1")]

    p.emit("dve", lambda e: e.memset(onehot[:, :], 1.0), writes=[R["onehot"]])
    p.emit("pool", lambda e: e.affine_select(onehot[:, :], onehot[:, :],
                                             [[1, 16], [0, 128]], ALU.is_equal, 0.0,
                                             base=0, channel_multiplier=-1),
           reads=[R["onehot"]], writes=[R["onehot"]])
    p.dma("sp", bcolt[:, :], d["f0_bcols"][:, :], writes=[R["bcolt"]])
    p.dma("sp", gbias[:, :], d["f0_gbias"][:, :], writes=[R["gbias"]])
    p.dma("pool", ceT.rearrange("p (k n) -> p k n", n=16),
          d["f0_ceT"].rearrange("(k p) n -> p k n", p=128), writes=[R["ceT"]])

    feat_norm(p, nc, C, xres, r_xres, xnb, R["xnb"], rows, R["rows"], scr, R["scr"],
              scrb, R["scrb"], pb, rpb, "ln")

    for s0 in range(0, 2048, 512):
        for c in range(NCH):
            p.emit("act", lambda e, c=c, s0=s0: e.activation(
                scrb[:, 0:512], xnb[:, c * 2048 + s0:c * 2048 + s0 + 512], AF.Square),
                reads=[R["xnb"]], writes=[R["scrb"]])
            p.emit("pe", lambda e, c=c: e.matmul(pb[4][0:1, 0:512], C.onescol_b[:, :],
                                                 scrb[:, 0:512], start=(c == 0),
                                                 stop=(c == NCH - 1)),
                   reads=[R["scrb"], C.r["onescol_b"]], writes=[rpb[4]])
        p.emit("dve", lambda e, s0=s0: e.tensor_copy(rows[0:1, 6144 + s0:6144 + s0 + 512],
                                                     pb[4][0:1, 0:512]),
               reads=[rpb[4]], writes=[R["rows"]])
    rx = rows[0:1, 6144:8192]
    p.emit("act", lambda e: e.activation(rx, rx, AF.Ln), reads=[R["rows"]], writes=[R["rows"]])
    p.emit("act", lambda e: e.activation(rx, rx, AF.Exp, bias=0.0, scale=-0.5),
           reads=[R["rows"]], writes=[R["rows"]])
    for c in range(NCH):
        p.emit("act", lambda e, c=c: e.activation(scrb[:, 0:16],
                                                  ceT[:, c * 16:(c + 1) * 16], AF.Square),
               reads=[R["ceT"]], writes=[R["scrb"]])
        p.emit("pe", lambda e, c=c: e.matmul(pb[4][0:1, 0:16], C.onescol_b[:, :],
                                             scrb[:, 0:16], start=(c == 0),
                                             stop=(c == NCH - 1)),
               reads=[R["scrb"], C.r["onescol_b"]], writes=[rpb[4]])
    rc = rows[0:1, 8192:8208]
    p.emit("dve", lambda e: e.tensor_copy(rc, pb[4][0:1, 0:16]),
           reads=[rpb[4]], writes=[R["rows"]])
    p.emit("act", lambda e: e.activation(rc, rc, AF.Ln), reads=[R["rows"]], writes=[R["rows"]])
    p.emit("act", lambda e: e.activation(rc, rc, AF.Exp, bias=0.0, scale=-0.5),
           reads=[R["rows"]], writes=[R["rows"]])
    bcast_row(p, C, pb[5], rpb[5], rc, R["rows"], 16)
    p.emit("dve", lambda e: e.tensor_copy(rcebc[:, :], pb[5][:, 0:16]),
           reads=[rpb[5]], writes=[R["rcebc"]])

    xn32 = sf.enter_context(nc.sbuf_tensor("f0_xn32", [128, 8 * 512], F32))
    sw = ExitStack()
    wtok = sw.enter_context(nc.sbuf_tensor("f0_wtok", [128, 16 * 16], F32))
    ltok = sw.enter_context(nc.sbuf_tensor("f0_ltok", [128, 16 * 16], F32))
    R["xn32"] = fence_res(p, "xn32")
    sg = ExitStack()
    hT = sg.enter_context(nc.sbuf_tensor("f0_hT", [128, 4 * 2048], F32))
    wt32 = sg.enter_context(nc.sbuf_tensor("f0_wt32", [128, 1024], F32))
    R["hT"] = fence_res(p, "hT")
    R["wt32"] = fence_res(p, "wt32")
    for tb in range(4):
        bcast_row(p, C, pb[4], rpb[4], rows[0:1, 2048 + tb * 512:2048 + (tb + 1) * 512],
                  R["rows"], 512)
        p.emit("dve", lambda e: e.tensor_copy(rsbc[:, :], pb[4][:, 0:512]),
               reads=[rpb[4]], writes=[R["rsbc"]])
        bcast_row(p, C, pb[4], rpb[4], rows[0:1, 4096 + tb * 512:4096 + (tb + 1) * 512],
                  R["rows"], 512)
        p.emit("dve", lambda e: e.tensor_copy(mrbc[:, :], pb[4][:, 0:512]),
               reads=[rpb[4]], writes=[R["mrbc"]])
        for kc in range(NCH):
            p.emit("dve", lambda e, kc=kc, tb=tb: e.tensor_mul(
                xn32[:, kc * 512:(kc + 1) * 512],
                xres[:, kc * 2048 + tb * 512:kc * 2048 + (tb + 1) * 512], rsbc[:, :]),
                reads=[r_xres, R["rsbc"], R["xn32"]], writes=[R["xn32"]])
            p.emit("dve", lambda e, kc=kc: e.tensor_sub(
                xn32[:, kc * 512:(kc + 1) * 512], xn32[:, kc * 512:(kc + 1) * 512],
                mrbc[:, :]),
                reads=[R["mrbc"], R["xn32"]], writes=[R["xn32"]])
        for ho in range(4):
            p.dma("sp", wt32.rearrange("p (k n) -> p k n", n=128),
                  d["f0_g1"].rearrange("(k p) n -> p k n", p=128)[:, :, ho * 128:(ho + 1) * 128],
                  reads=[R["wt32"]], writes=[R["wt32"]])
            bk = ho % 2

            def mmh(e, ho=ho, tb=tb, bk=bk):
                last = None
                for kc in range(NCH):
                    last = e.matmul(pb[bk][:, 0:512], wt32[:, kc * 128:(kc + 1) * 128],
                                    xn32[:, kc * 512:(kc + 1) * 512],
                                    start=(kc == 0), stop=(kc == NCH - 1))
                return last
            p.emit("pe", mmh, reads=[R["wt32"], R["xn32"]], writes=[rpb[bk]])
            p.emit("act", lambda e, ho=ho, tb=tb, bk=bk: e.activation(
                hT[:, ho * 2048 + tb * 512:ho * 2048 + (tb + 1) * 512], pb[bk][:, 0:512],
                AF.Gelu, bias=gbias[:, ho:ho + 1]),
                reads=[rpb[bk], R["gbias"]], writes=[R["hT"]])
    g2t = sg.enter_context(nc.sbuf_tensor("f0_g2t", [128, 64], F32))
    R["g2t"] = fence_res(p, "g2t")
    p.dma("sp", g2t.rearrange("p (k n) -> p k n", n=16),
          d["f0_g2"].rearrange("(k p) n -> p k n", p=128), writes=[R["g2t"]])
    p.dma("sp", rows[0:1, 8208:8224], d["f0_b2row"][0:1, :], reads=[R["rows"]],
          writes=[R["rows"]])
    bcast_row(p, C, pb[5], rpb[5], rows[0:1, 8208:8224], R["rows"], 16)
    b2bc = sg.enter_context(nc.sbuf_tensor("f0_b2bc", [128, 16], F32))
    R["b2bc"] = fence_res(p, "b2bc")
    p.emit("dve", lambda e: e.tensor_copy(b2bc[:, :], pb[5][:, 0:16]),
           reads=[rpb[5]], writes=[R["b2bc"]])

    for tt in range(16):
        def mmsim(e, tt=tt):
            last = None
            for kc in range(NCH):
                last = e.matmul(pb[6][:, 0:16],
                                xnb[:, kc * 2048 + tt * 128:kc * 2048 + (tt + 1) * 128],
                                ceT[:, kc * 16:(kc + 1) * 16],
                                start=(kc == 0), stop=(kc == NCH - 1))
            return last
        p.emit("pe", mmsim, reads=[R["xnb"], R["ceT"]], writes=[rpb[6]])
        def mmgv(e, tt=tt):
            last = None
            for kc in range(4):
                last = e.matmul(pb[7][:, 0:16],
                                hT[:, kc * 2048 + tt * 128:kc * 2048 + (tt + 1) * 128],
                                g2t[:, kc * 16:(kc + 1) * 16],
                                start=(kc == 0), stop=(kc == 3))
            return last
        p.emit("pe", mmgv, reads=[R["hT"], R["g2t"]], writes=[rpb[7]])
        p.emit("pe", lambda e, tt=tt: e.transpose(pb[5][0:128, 0:1],
                                                  rx[0:1, tt * 128:(tt + 1) * 128],
                                                  C.ident[0:1, 0:1]),
               reads=[R["rows"], C.r["ident"]], writes=[rpb[5]])
        p.emit("dve", lambda e: e.tensor_copy(scr2[:, 0:1], pb[5][0:128, 0:1]),
               reads=[rpb[5]], writes=[R["scr2"]])
        p.emit("dve", lambda e: e.tensor_scalar(scr[:, 0:16], pb[6][:, 0:16],
                                                scr2[:, 0:1], None, ALU.mult),
               reads=[rpb[6], R["scr2"]], writes=[R["scr"]])
        p.emit("dve", lambda e: e.tensor_mul(scr[:, 0:16], scr[:, 0:16], rcebc[:, :]),
               reads=[R["rcebc"], R["scr"]], writes=[R["scr"]])
        p.emit("dve", lambda e: e.tensor_add(scr2[:, 0:16], pb[7][:, 0:16], b2bc[:, :]),
               reads=[rpb[7], R["b2bc"]], writes=[R["scr2"]])
        p.emit("act", lambda e: e.activation(scr2[:, 0:16], scr2[:, 0:16], AF.Sigmoid),
               reads=[R["scr2"]], writes=[R["scr2"]])
        p.emit("dve", lambda e: e.tensor_add(scr[:, 0:16], scr[:, 0:16], scr2[:, 0:16]),
               reads=[R["scr"], R["scr2"]], writes=[R["scr"]])
        p.emit("dve", lambda e, tt=tt: e.tensor_copy(ltok[:, tt * 16:(tt + 1) * 16],
                                                     scr[:, 0:16]),
               reads=[R["scr"]], writes=[R["ltok"]])
        p.emit("dve", lambda e: e.reduce_max(scr2[:, 0:1], scr[:, 0:16], axis=AX.X,
                                             negate=True),
               reads=[R["scr"]], writes=[R["scr2"]])
        p.emit("act", lambda e: e.activation(scr2[:, 1:17], scr[:, 0:16], AF.Exp,
                                             bias=scr2[:, 0:1],
                                             accum_out=scr2[:, 17:18]),
               reads=[R["scr"], R["scr2"]], writes=[R["scr2"]])
        p.emit("dve", lambda e: e.reciprocal(scr2[:, 18:19], scr2[:, 17:18]),
               reads=[R["scr2"]], writes=[R["scr2"]])
        p.emit("dve", lambda e: e.tensor_scalar(scr2[:, 19:20], scr2[:, 0:1], -1.0,
                                                None, ALU.mult),
               reads=[R["scr2"]], writes=[R["scr2"]])
        p.emit("dve", lambda e: e.tensor_scalar(scr[:, 16:32], scr[:, 0:16],
                                                scr2[:, 19:20], None, ALU.is_equal),
               reads=[R["scr"], R["scr2"]], writes=[R["scr"]])
        p.emit("dve", lambda e: e.tensor_scalar(scr[:, 32:48], scr2[:, 1:17],
                                                scr2[:, 18:19], None, ALU.mult),
               reads=[R["scr2"], R["scr"]], writes=[R["scr"]])
        p.emit("dve", lambda e, tt=tt: e.tensor_mul(wtok[:, tt * 16:(tt + 1) * 16],
                                                    scr[:, 16:32], scr[:, 32:48]),
               reads=[R["scr"]], writes=[R["wtok"]])
    sg.close()
    if "dbg_wtok" in d:
        p.dma("sp", d["dbg_wtok"][:, :], wtok[:, :], reads=[R["wtok"]], writes=[Res("dw")])
        p.dma("sp", d["dbg_ltok"][:, :], ltok[:, :], reads=[R["ltok"]], writes=[Res("dl")])
    dump_y = "dbg_y" in d
    R["wTb"] = fence_res(p, "wTb")
    for tt in range(16):
        p.emit("pe", lambda e, tt=tt: e.transpose(pb[5][0:16, 0:128],
                                                  wtok[:, tt * 16:(tt + 1) * 16],
                                                  C.ident[:, :]),
               reads=[R["wtok"], C.r["ident"]], writes=[rpb[5]])
        p.emit("dve", lambda e, tt=tt: e.tensor_copy(wTb[:, tt * 128:(tt + 1) * 128],
                                                     pb[5][0:16, 0:128]),
               reads=[rpb[5]], writes=[R["wTb"]])
    sw.close()

    sc = ExitStack()
    xnlo = sc.enter_context(nc.sbuf_tensor("f0_xnlo", [128, 8 * 512], BF16))
    R["xnlo"] = fence_res(p, "xnlo")
    colsw = sc.enter_context(nc.sbuf_tensor("f0_colsw", [128, 4 * 512], F32))
    yacc = sc.enter_context(nc.sbuf_tensor("f0_yacc", [128, 8 * 512], F32))
    chl = sc.enter_context(nc.sbuf_tensor("f0_chl", [128, 2 * 512], BF16))
    R["colsw"] = fence_res(p, "colsw")
    R["yacc"] = fence_res(p, "yacc")
    R["chl"] = fence_res(p, "chl")
    R["d_f0y"] = Res("d_f0y")
    for tb in range(4):
        bcast_row(p, C, pb[4], rpb[4], rows[0:1, 2048 + tb * 512:2048 + (tb + 1) * 512],
                  R["rows"], 512)
        p.emit("dve", lambda e: e.tensor_copy(rsbc[:, :], pb[4][:, 0:512]),
               reads=[rpb[4]], writes=[R["rsbc"]])
        bcast_row(p, C, pb[4], rpb[4], rows[0:1, 4096 + tb * 512:4096 + (tb + 1) * 512],
                  R["rows"], 512)
        p.emit("dve", lambda e: e.tensor_copy(mrbc[:, :], pb[4][:, 0:512]),
               reads=[rpb[4]], writes=[R["mrbc"]])
        for kc in range(NCH):
            p.emit("dve", lambda e, kc=kc, tb=tb: e.tensor_mul(
                xn32[:, kc * 512:(kc + 1) * 512],
                xres[:, kc * 2048 + tb * 512:kc * 2048 + (tb + 1) * 512], rsbc[:, :]),
                reads=[r_xres, R["rsbc"], R["xn32"]], writes=[R["xn32"]])
            p.emit("dve", lambda e, kc=kc: e.tensor_sub(
                xn32[:, kc * 512:(kc + 1) * 512], xn32[:, kc * 512:(kc + 1) * 512],
                mrbc[:, :]),
                reads=[R["mrbc"], R["xn32"]], writes=[R["xn32"]])
            p.emit("dve", lambda e, kc=kc, tb=tb: e.tensor_sub(
                xnlo[:, kc * 512:(kc + 1) * 512], xn32[:, kc * 512:(kc + 1) * 512],
                xnb[:, kc * 2048 + tb * 512:kc * 2048 + (tb + 1) * 512]),
                reads=[R["xn32"], R["xnb"], R["xnlo"]], writes=[R["xnlo"]])
        for half in range(8):
            for ci in range(4):
                cn = half * 4 + ci
                wt = wtile[:, (cn % 2) * 1024:(cn % 2) * 1024 + 1024]
                r_wt = R["wtile"][cn % 2]
                p.dma("pool", wt.rearrange("p (k n) -> p k n", n=128),
                      d["f0_wcols"].rearrange("(k p) n -> p k n", p=128)[:, :, cn * 128:(cn + 1) * 128],
                      reads=[r_wt], writes=[r_wt])
                bk = cn % 2

                def mmc(e, cn=cn, tb=tb, bk=bk, wt=wt):
                    last = None
                    for kc in range(NCH):
                        e.matmul(pb[bk][:, 0:512], wt[:, kc * 128:(kc + 1) * 128],
                                 xnb[:, kc * 2048 + tb * 512:kc * 2048 + (tb + 1) * 512],
                                 start=(kc == 0), stop=False)
                        last = e.matmul(pb[bk][:, 0:512], wt[:, kc * 128:(kc + 1) * 128],
                                        xnlo[:, kc * 512:(kc + 1) * 512],
                                        start=False, stop=(kc == NCH - 1))
                    return last
                p.emit("pe", mmc, reads=[r_wt, R["xnb"], R["xnlo"]], writes=[rpb[bk]])
                p.emit("act", lambda e, cn=cn, bk=bk: e.activation(
                    scr[:, 0:512], pb[bk][:, 0:512], AF.Gelu, bias=bcolt[:, cn:cn + 1]),
                    reads=[rpb[bk], R["bcolt"]], writes=[R["scr"]])
                cix = cn // 2
                p.emit("pe", lambda e, cix=cix, tb=tb: e.matmul(
                    pb[5][:, 0:512], onehot[:, cix * 128:(cix + 1) * 128],
                    wTb[:, tb * 512:(tb + 1) * 512], start=True, stop=True),
                    reads=[R["onehot"], R["wTb"]], writes=[rpb[5]])
                p.emit("dve", lambda e, ci=ci: e.tensor_mul(
                    colsw[:, ci * 512:(ci + 1) * 512], scr[:, 0:512], pb[5][:, 0:512]),
                    reads=[R["scr"], rpb[5]], writes=[R["colsw"]])
            for ci in range(4):
                cn = half * 4 + ci
                wt = wtile[:, (cn % 2) * 1024:(cn % 2) * 1024 + 1024]
                r_wt = R["wtile"][cn % 2]
                p.dma("pool", wt[:, 0:1024],
                      d["f0_proj"].rearrange("(k p) n -> k p n", p=128)[cn],
                      reads=[r_wt], writes=[r_wt])
                p.emit("dve", lambda e, ci=ci: e.tensor_copy(
                    chl[:, 0:512], colsw[:, ci * 512:(ci + 1) * 512]),
                    reads=[R["colsw"], R["chl"]], writes=[R["chl"]])
                p.emit("dve", lambda e, ci=ci: e.tensor_sub(
                    chl[:, 512:1024], colsw[:, ci * 512:(ci + 1) * 512], chl[:, 0:512]),
                    reads=[R["colsw"], R["chl"]], writes=[R["chl"]])

                def mmp(e, cn=cn, half=half, ci=ci, wt=wt):
                    last = None
                    for do in range(NCH):
                        e.matmul(pb[do][:, 0:512], wt[:, do * 128:(do + 1) * 128],
                                 chl[:, 0:512],
                                 start=(ci == 0), stop=False)
                        last = e.matmul(pb[do][:, 0:512], wt[:, do * 128:(do + 1) * 128],
                                        chl[:, 512:1024], start=False,
                                        stop=(ci == 3))
                    return last
                p.emit("pe", mmp, reads=[r_wt, R["chl"]],
                       writes=[rpb[do2] for do2 in range(NCH)])
            for do in range(NCH):
                if half == 0:
                    p.emit("dve", lambda e, do=do: e.tensor_copy(
                        yacc[:, do * 512:(do + 1) * 512], pb[do][:, 0:512]),
                        reads=[rpb[do]], writes=[R["yacc"]])
                else:
                    p.emit("dve", lambda e, do=do: e.tensor_add(
                        yacc[:, do * 512:(do + 1) * 512],
                        yacc[:, do * 512:(do + 1) * 512], pb[do][:, 0:512]),
                        reads=[rpb[do], R["yacc"]], writes=[R["yacc"]])
        for do in range(NCH):
            p.emit("dve", lambda e, do=do: e.tensor_scalar(
                scr[:, 0:512], yacc[:, do * 512:(do + 1) * 512],
                bcolt[:, 32 + do:32 + do + 1], None, ALU.add),
                reads=[R["yacc"], R["bcolt"]], writes=[R["scr"]])
            p.emit("dve", lambda e, do=do, tb=tb: e.tensor_mul(
                scr2[:, 0:512], xres[:, do * 2048 + tb * 512:do * 2048 + (tb + 1) * 512],
                rsbc[:, :]),
                reads=[r_xres, R["rsbc"], R["scr2"]], writes=[R["scr2"]])
            p.emit("dve", lambda e: e.tensor_sub(scr2[:, 0:512], scr2[:, 0:512],
                                                 mrbc[:, :]),
                   reads=[R["scr2"], R["mrbc"]], writes=[R["scr2"]])
            p.emit("dve", lambda e: e.tensor_add(scr[:, 0:512], scr[:, 0:512],
                                                 scr2[:, 0:512]),
                   reads=[R["scr"], R["scr2"]], writes=[R["scr"]])
            p.dma("sp", d["f0y"][do * 128:(do + 1) * 128, tb * 512:(tb + 1) * 512],
                  scr[:, 0:512], reads=[R["scr"]], writes=[R["d_f0y"]])
            p.emit("pe", lambda e, do=do: e.matmul(pb[0][0:1, 0:512], C.onescol_f[:, :],
                                                   scr[:, 0:512], start=(do == 0),
                                                   stop=(do == NCH - 1)),
                   reads=[R["scr"], C.r["onescol_f"]], writes=[rpb[0]])
            p.emit("act", lambda e: e.activation(scr2[:, 0:512], scr[:, 0:512], AF.Square),
                   reads=[R["scr"]], writes=[R["scr2"]])
            p.emit("pe", lambda e, do=do: e.matmul(pb[1][0:1, 0:512], C.onescol_f[:, :],
                                                   scr2[:, 0:512], start=(do == 0),
                                                   stop=(do == NCH - 1)),
                   reads=[R["scr2"], C.r["onescol_f"]], writes=[rpb[1]])
        p.emit("dve", lambda e, tb=tb: e.tensor_scalar(
            rows[0:1, 2048 + tb * 512:2048 + (tb + 1) * 512], pb[0][0:1, 0:512],
            1.0 / DM, None, ALU.mult),
            reads=[rpb[0]], writes=[R["rows"]])
        p.emit("dve", lambda e, tb=tb: e.tensor_scalar(
            rows[0:1, 4096 + tb * 512:4096 + (tb + 1) * 512], pb[1][0:1, 0:512],
            1.0 / DM, None, ALU.mult),
            reads=[rpb[1]], writes=[R["rows"]])
    if "dbg_y" in d:
        p.dma("pool", d["dbg_y"][:, :], d["f0y"][:, :], reads=[R["d_f0y"]],
              writes=[Res("dy0")])
    muy = rows[0:1, 2048:4096]
    rsy = rows[0:1, 4096:6144]
    p.emit("act", lambda e: e.activation(rows[0:1, 0:2048], muy, AF.Square),
           reads=[R["rows"]], writes=[R["rows"]])
    p.emit("dve", lambda e: e.tensor_sub(rsy, rsy, rows[0:1, 0:2048]),
           reads=[R["rows"]], writes=[R["rows"]])
    p.emit("act", lambda e: e.activation(rsy, rsy, AF.Ln, bias=C.eps1[0:1, 0:1]),
           reads=[R["rows"], C.r["eps1"]], writes=[R["rows"]])
    p.emit("act", lambda e: e.activation(rsy, rsy, AF.Exp, bias=0.0, scale=-0.5),
           reads=[R["rows"]], writes=[R["rows"]])
    p.emit("dve", lambda e: e.tensor_mul(rows[0:1, 0:2048], muy, rsy),
           reads=[R["rows"]], writes=[R["rows"]])
    for s0 in range(0, 2048, 512):
        bcast_row(p, C, pb[5], rpb[5], rsy[0:1, s0:s0 + 512], R["rows"], 512)
        bcast_row(p, C, pb[4], rpb[4], rows[0:1, s0:s0 + 512], R["rows"], 512)
        for c in range(NCH):
            p.dma("sp", scr2[:, 0:512], d["f0y"][c * 128:(c + 1) * 128, s0:s0 + 512],
                  reads=[R["d_f0y"], R["scr2"]], writes=[R["scr2"]])
            p.emit("dve", lambda e, c=c, s0=s0: e.tensor_mul(
                scr[:, 0:512], scr2[:, 0:512], pb[5][:, 0:512]),
                reads=[R["scr2"], rpb[5]], writes=[R["scr"]])
            p.emit("dve", lambda e: e.tensor_sub(scr[:, 0:512], scr[:, 0:512],
                                                 pb[4][:, 0:512]),
                   reads=[R["scr"], rpb[4]], writes=[R["scr"]])
            p.emit("dve", lambda e, c=c: e.tensor_scalar(
                scr[:, 0:512], scr[:, 0:512], bcolt[:, 40 + c:40 + c + 1],
                bcolt[:, 48 + c:48 + c + 1], ALU.mult, ALU.add),
                reads=[R["scr"], R["bcolt"]], writes=[R["scr"]])
            p.emit("dve", lambda e, c=c, s0=s0: e.tensor_add(
                xres[:, c * 2048 + s0:c * 2048 + s0 + 512],
                xres[:, c * 2048 + s0:c * 2048 + s0 + 512], scr[:, 0:512]),
                reads=[R["scr"], r_xres], writes=[r_xres])
    sc.close()
    sf.close()


def build_l1_moe(p, nc, ctx, C, xres, r_xres, d):
    sf = ExitStack()
    rows = sf.enter_context(nc.sbuf_tensor("mo_rows", [1, 6144], F32))
    scr = sf.enter_context(nc.sbuf_tensor("mo_scr", [128, 512], F32))
    scr2 = sf.enter_context(nc.sbuf_tensor("mo_scr2", [128, 512], F32))
    scrb = sf.enter_context(nc.sbuf_tensor("mo_scrb", [128, 512], BF16))
    xf = sf.enter_context(nc.sbuf_tensor("mo_xf", [128, NCH * 2048], BF16))
    n2col = sf.enter_context(nc.sbuf_tensor("mo_n2col", [128, 8], F32))
    gwt = sf.enter_context(nc.sbuf_tensor("mo_gwt", [128, 32], BF16))
    wts = sf.enter_context(nc.sbuf_tensor("mo_wts", [128, 16 * 4], F32))
    wT4 = sf.enter_context(nc.sbuf_tensor("mo_wT4", [4, 2048], F32))
    oh4 = sf.enter_context(nc.sbuf_tensor("mo_oh4", [4, 4 * 128], F32))
    wtile = sf.enter_context(nc.sbuf_tensor("mo_wtile", [128, 2 * 1024], BF16))
    wbcs = sf.enter_context(nc.sbuf_tensor("mo_wbcs", [128, 512], F32))
    pb = [psumt(nc, sf, f"mo_pb{i}", [128, 512]) for i in range(8)]
    rpb = [Res(f"mo_pb{i}") for i in range(8)]
    N = ["rows", "scr", "scr2", "scrb", "xf", "n2col", "gwt", "wts", "wT4", "oh4",
         "wbcs"]
    R = {k: fence_res(p, k) for k in N}
    R["wtile"] = [fence_res(p, "mo_wt0"), fence_res(p, "mo_wt1")]

    p.dma("sp", n2col[:, :], d["l1f_n2w"][:, :], writes=[R["n2col"]])
    p.dma("pool", gwt.rearrange("p (k n) -> p k n", n=4),
          d["l1f_gwT"].rearrange("(k p) n -> p k n", p=128), writes=[R["gwt"]])
    p.emit("dve", lambda e: e.memset(oh4[:, :], 1.0), writes=[R["oh4"]])
    p.emit("pool", lambda e: e.affine_select(oh4[:, :], oh4[:, :],
                                             [[1, 4], [0, 128]], ALU.is_equal, 0.0,
                                             base=0, channel_multiplier=-1),
           reads=[R["oh4"]], writes=[R["oh4"]])

    feat_norm(p, nc, C, xres, r_xres, xf, R["xf"], rows, R["rows"], scr, R["scr"],
              scrb, R["scrb"], pb, rpb, "rms", n2col, R["n2col"])

    for tt in range(16):
        def mmg(e, tt=tt):
            last = None
            for kc in range(NCH):
                last = e.matmul(pb[6][:, 0:4],
                                xf[:, kc * 2048 + tt * 128:kc * 2048 + (tt + 1) * 128],
                                gwt[:, kc * 4:(kc + 1) * 4],
                                start=(kc == 0), stop=(kc == NCH - 1))
            return last
        p.emit("pe", mmg, reads=[R["xf"], R["gwt"]], writes=[rpb[6]])
        p.emit("dve", lambda e: e.tensor_copy(scr[:, 0:4], pb[6][:, 0:4]),
               reads=[rpb[6]], writes=[R["scr"]])
        p.emit("dve", lambda e: e.reduce_max(scr[:, 4:5], scr[:, 0:4], axis=AX.X,
                                             negate=True),
               reads=[R["scr"]], writes=[R["scr"]])
        p.emit("act", lambda e: e.activation(scr[:, 5:9], scr[:, 0:4], AF.Exp,
                                             bias=scr[:, 4:5], accum_out=scr[:, 9:10]),
               reads=[R["scr"]], writes=[R["scr"]])
        p.emit("dve", lambda e: e.tensor_scalar(scr[:, 10:11], scr[:, 4:5], -1.0,
                                                None, ALU.mult),
               reads=[R["scr"]], writes=[R["scr"]])
        p.emit("dve", lambda e: e.tensor_scalar(scr[:, 11:15], scr[:, 0:4],
                                                scr[:, 10:11], None, ALU.is_equal),
               reads=[R["scr"]], writes=[R["scr"]])
        p.emit("dve", lambda e: e.scalar_tensor_tensor(scr[:, 15:19], scr[:, 11:15],
                                                       -1.0e30, scr[:, 0:4],
                                                       ALU.mult, ALU.add),
               reads=[R["scr"]], writes=[R["scr"]])
        p.emit("dve", lambda e: e.reduce_max(scr[:, 19:20], scr[:, 15:19], axis=AX.X),
               reads=[R["scr"]], writes=[R["scr"]])
        p.emit("dve", lambda e: e.tensor_scalar(scr[:, 20:24], scr[:, 15:19],
                                                scr[:, 19:20], None, ALU.is_equal),
               reads=[R["scr"]], writes=[R["scr"]])
        p.emit("dve", lambda e: e.tensor_add(scr[:, 24:25], scr[:, 19:20], scr[:, 4:5]),
               reads=[R["scr"]], writes=[R["scr"]])
        p.emit("act", lambda e: e.activation(scr[:, 24:25], scr[:, 24:25], AF.Exp),
               reads=[R["scr"]], writes=[R["scr"]])
        p.emit("dve", lambda e: e.tensor_scalar(scr[:, 25:26], scr[:, 9:10], 1.0e-8,
                                                1.0, ALU.mult, ALU.add),
               reads=[R["scr"]], writes=[R["scr"]])
        p.emit("dve", lambda e: e.tensor_add(scr[:, 25:26], scr[:, 25:26], scr[:, 24:25]),
               reads=[R["scr"]], writes=[R["scr"]])
        p.emit("dve", lambda e: e.reciprocal(scr[:, 26:27], scr[:, 25:26]),
               reads=[R["scr"]], writes=[R["scr"]])
        p.emit("dve", lambda e: e.tensor_mul(scr[:, 27:28], scr[:, 24:25], scr[:, 26:27]),
               reads=[R["scr"]], writes=[R["scr"]])
        p.emit("dve", lambda e: e.tensor_scalar(scr[:, 28:32], scr[:, 11:15],
                                                scr[:, 26:27], None, ALU.mult),
               reads=[R["scr"]], writes=[R["scr"]])
        p.emit("dve", lambda e, tt=tt: e.scalar_tensor_tensor(
            wts[:, tt * 4:(tt + 1) * 4], scr[:, 20:24], scr[:, 27:28], scr[:, 28:32],
            ALU.mult, ALU.add),
            reads=[R["scr"]], writes=[R["wts"]])
    if "dbg_wts" in d:
        p.dma("sp", d["dbg_wts"][:, :], wts[:, :], reads=[R["wts"]], writes=[Res("dwts")])
    for tt in range(16):
        p.emit("pe", lambda e, tt=tt: e.transpose(pb[5][0:4, 0:128],
                                                  wts[:, tt * 4:(tt + 1) * 4],
                                                  C.ident[:, :]),
               reads=[R["wts"], C.r["ident"]], writes=[rpb[5]])
        p.emit("dve", lambda e, tt=tt: e.tensor_copy(wT4[:, tt * 128:(tt + 1) * 128],
                                                     pb[5][0:4, 0:128]),
               reads=[rpb[5]], writes=[R["wT4"]])

    se = ExitStack()
    h1 = se.enter_context(nc.sbuf_tensor("mo_h1", [128, 16 * 1024], BF16))
    h2 = se.enter_context(nc.sbuf_tensor("mo_h2", [128, 16 * 1024], BF16))
    R["h1"] = fence_res(p, "h1")
    R["h2"] = fence_res(p, "h2")

    def mm_layer(dst, r_dst, src, r_src, src_nch, wkey, nout_ch, tk, act, tb2n=2):
        nh = (src_nch + 7) // 8
        for ho in range(nout_ch):
            wt = wtile[:, (ho % 2) * 1024:(ho % 2) * 1024 + 1024]
            r_wt = R["wtile"][ho % 2]
            for half in range(nh):
                kn = min(8, src_nch - half * 8)
                p.dma("pool", wt.rearrange("p (k n) -> p k n", n=128)[:, 0:kn],
                      d[wkey].rearrange("(k p) n -> p k n", p=128)[:, half * 8:half * 8 + kn,
                                                                  ho * 128:(ho + 1) * 128],
                      reads=[r_wt], writes=[r_wt])
                for tb in range(tb2n):
                    bk = tb % 2

                    def mml(e, ho=ho, tb=tb, bk=bk, wt=wt, half=half, kn=kn):
                        last = None
                        for kc in range(kn):
                            kk = half * 8 + kc
                            if src is xf:
                                sl = xf[:, kk * 2048 + tk * 1024 + tb * 512:
                                        kk * 2048 + tk * 1024 + (tb + 1) * 512]
                            else:
                                sl = src[:, kk * 1024 + tb * 512:kk * 1024 + (tb + 1) * 512]
                            last = e.matmul(pb[bk][:, 0:512], wt[:, kc * 128:(kc + 1) * 128],
                                            sl, start=(half == 0 and kc == 0),
                                            stop=(half == nh - 1 and kc == kn - 1))
                        return last
                    p.emit("pe", mml, reads=[r_wt, r_src], writes=[rpb[bk]])
            fn = {"gelu": AF.Gelu, "silu": AF.Silu}.get(act, AF.Copy)
            for tb in range(tb2n):
                bk = tb % 2
                p.emit("act", lambda e, ho=ho, tb=tb, bk=bk, fn=fn: e.activation(
                    dst[:, ho * 1024 + tb * 512:ho * 1024 + (tb + 1) * 512],
                    pb[bk][:, 0:512], fn),
                    reads=[rpb[bk]], writes=[r_dst])

    def down_proj(src, r_src, src_nch, wkey, tk, eidx):
        for do in range(2, NCH):
            wt = wtile[:, (do % 2) * 1024:(do % 2) * 1024 + 1024]
            r_wt = R["wtile"][do % 2]
            bk2 = 6 + do % 2
            nh = (src_nch + 7) // 8
            for half in range(nh):
                kn = min(8, src_nch - half * 8)
                p.dma("pool", wt.rearrange("p (k n) -> p k n", n=128)[:, 0:kn],
                      d[wkey].rearrange("(k p) n -> p k n", p=128)[:, half * 8:half * 8 + kn,
                                                                  do * 128:(do + 1) * 128],
                      reads=[r_wt], writes=[r_wt])
                for tb in range(2):
                    def mmd(e, do=do, half=half, kn=kn, tb=tb, wt=wt):
                        last = None
                        for kc in range(kn):
                            last = e.matmul(
                                pb[4 + tb][:, 0:512],
                                wt[:, kc * 128:(kc + 1) * 128],
                                src[:, (half * 8 + kc) * 1024 + tb * 512:
                                    (half * 8 + kc) * 1024 + (tb + 1) * 512],
                                start=(half == 0 and kc == 0),
                                stop=(half == nh - 1 and kc == kn - 1))
                        return last
                    p.emit("pe", mmd, reads=[r_wt, r_src], writes=[rpb[4 + tb]])
            for tb in range(2):
                ts0 = tk * 1024 + tb * 512
                p.emit("pe", lambda e, eidx=eidx, ts0=ts0: e.matmul(
                    pb[6][:, 0:512], oh4[:, eidx * 128:(eidx + 1) * 128],
                    wT4[:, ts0:ts0 + 512], start=True, stop=True),
                    reads=[R["oh4"], R["wT4"]], writes=[rpb[6]])
                p.emit("dve", lambda e: e.tensor_copy(wbcs[:, 0:512], pb[6][:, 0:512]),
                       reads=[rpb[6]], writes=[R["wbcs"]])
                p.emit("dve", lambda e, tb=tb: e.tensor_mul(scr[:, 0:512],
                                                            pb[4 + tb][:, 0:512],
                                                            wbcs[:, 0:512]),
                       reads=[rpb[4 + tb], R["wbcs"]], writes=[R["scr"]])
                p.emit("dve", lambda e, do=do, ts0=ts0: e.tensor_add(
                    xres[:, do * 2048 + ts0:do * 2048 + ts0 + 512],
                    xres[:, do * 2048 + ts0:do * 2048 + ts0 + 512], scr[:, 0:512]),
                    reads=[R["scr"], r_xres], writes=[r_xres])

    for tk in range(2):
        mm_layer(h1, R["h1"], xf, R["xf"], NCH, "l1f_e0_up", 16, tk, "gelu")
        down_proj(h1, R["h1"], 16, "l1f_e0_down", tk, 0)
        mm_layer(h1, R["h1"], xf, R["xf"], NCH, "l1f_e1_w1", 16, tk, "silu")
        mm_layer(h2, R["h2"], xf, R["xf"], NCH, "l1f_e1_w2", 16, tk, "none")
        for hc in range(16):
            for tb in range(2):
                p.emit("dve", lambda e, hc=hc, tb=tb: e.tensor_mul(
                    h1[:, hc * 1024 + tb * 512:hc * 1024 + (tb + 1) * 512],
                    h1[:, hc * 1024 + tb * 512:hc * 1024 + (tb + 1) * 512],
                    h2[:, hc * 1024 + tb * 512:hc * 1024 + (tb + 1) * 512]),
                    reads=[R["h1"], R["h2"]], writes=[R["h1"]])
        down_proj(h1, R["h1"], 16, "l1f_e1_down", tk, 1)
        mm_layer(h1, R["h1"], xf, R["xf"], NCH, "l1f_e2_l1", 16, tk, "gelu")
        mm_layer(h2, R["h2"], h1, R["h1"], 16, "l1f_e2_l2", 16, tk, "gelu")
        mm_layer(h1, R["h1"], h2, R["h2"], 16, "l1f_e2_l3", 16, tk, "gelu")
        down_proj(h1, R["h1"], 16, "l1f_e2_l4", tk, 2)
        mm_layer(h1, R["h1"], xf, R["xf"], NCH, "l1f_e3_down", 8, tk, "gelu")
        mm_layer(h2, R["h2"], h1, R["h1"], 8, "l1f_e3_up", 16, tk, "gelu")
        down_proj(h2, R["h2"], 16, "l1f_e3_out", tk, 3)
    se.close()
    R["d_xsp2"] = Res("d_xsp2")
    R["d_xsp2"].w = None
    for c in range(NCH):
        for tb in range(4):
            p.dma("sp", scr2[:, 0:512], d["xsp"][c * 128:(c + 1) * 128, tb * 512:(tb + 1) * 512],
                  reads=[R["scr2"]], writes=[R["scr2"]])
            p.emit("dve", lambda e, c=c, tb=tb: e.tensor_add(
                xres[:, c * 2048 + tb * 512:c * 2048 + (tb + 1) * 512],
                xres[:, c * 2048 + tb * 512:c * 2048 + (tb + 1) * 512], scr2[:, 0:512]),
                reads=[R["scr2"], r_xres], writes=[r_xres])
    sf.close()


def _stage_l0_weights(inputs):
    w = {}
    w["Win"] = np.ascontiguousarray(np.asarray(inputs["L0_attn_W_in"], np.float32))
    cw = np.asarray(inputs["L0_attn_conv_w"], np.float32)[:, 0, :]
    w["convw"] = np.ascontiguousarray(cw.reshape(16, 128, 4).transpose(1, 0, 2).reshape(128, 64))
    cb = np.asarray(inputs["L0_attn_conv_b"], np.float32)
    w["convb"] = np.ascontiguousarray(cb.reshape(16, 128).T)
    wx = np.asarray(inputs["L0_attn_W_x"], np.float32)
    wxp = np.zeros((2048, 80), np.float32)
    wxp[:, 0] = wx[:, 0]
    wxp[:, 32:48] = wx[:, 1:17]
    wxp[:, 64:80] = wx[:, 17:33]
    w["Wxp"] = wxp
    w["Alog"] = np.ascontiguousarray(np.asarray(inputs["L0_attn_A_log"], np.float32).reshape(1, 16))
    w["Drow"] = np.ascontiguousarray(np.asarray(inputs["L0_attn_D"], np.float32).reshape(1, 2048))
    w["Wout"] = np.ascontiguousarray(np.asarray(inputs["L0_attn_W_out"], np.float32))
    return w


def declare_l0_inputs(nc, debug=False):
    dd = {}
    dd["xT"] = nc.dram_tensor("xT", [1024, 2048], F32, kind="ExternalInput").ap()
    dd["xTov"] = nc.dram_tensor("xTov", [1024, 3], F32, kind="ExternalInput").ap()
    dd["Win"] = nc.dram_tensor("Win", [1024, 4096], F32, kind="ExternalInput").ap()
    dd["convw"] = nc.dram_tensor("convw", [128, 64], F32, kind="ExternalInput").ap()
    dd["convb"] = nc.dram_tensor("convb", [128, 16], F32, kind="ExternalInput").ap()
    dd["Wxp"] = nc.dram_tensor("Wxp", [2048, 80], F32, kind="ExternalInput").ap()
    dd["Alog"] = nc.dram_tensor("Alog", [1, 16], F32, kind="ExternalInput").ap()
    dd["Drow"] = nc.dram_tensor("Drow", [1, 2048], F32, kind="ExternalInput").ap()
    dd["Wout"] = nc.dram_tensor("Wout", [2048, 1024], F32, kind="ExternalInput").ap()
    dd["flag"] = nc.dram_tensor("flag", [1, 1], F32, kind="ExternalInput").ap()
    kind = "ExternalOutput" if debug else "Internal"
    dd["xs"] = nc.dram_tensor("d_xs", [2048, 2048], BF16, kind=kind).ap()
    dd["y"] = nc.dram_tensor("d_y", [2048, 2048], BF16, kind=kind).ap()
    dd["sz"] = nc.dram_tensor("d_sz", [2048, 2048], BF16, kind=kind).ap()
    dd["ccin"] = nc.dram_tensor("d_ccin", [16, 2048], F32).ap()
    dd["ccout"] = nc.dram_tensor("d_ccout", [32, 2048], F32).ap()
    dd["bar1"] = nc.dram_tensor("d_bar1", [16, 16], F32).ap()
    dd["bar2"] = nc.dram_tensor("d_bar2", [16, 16], F32).ap()
    return dd


def build_l0_program(debug=False):
    nc = bass.Bass("TRN2", target_bir_lowering=False)
    dd = declare_l0_inputs(nc, debug)
    out = nc.dram_tensor("xout", [1024, 2048], F32, kind="ExternalOutput").ap()
    dbg = {}
    if debug:
        dbg["xdbl"] = nc.dram_tensor("dbg_xdbl", [128, 2048], F32, kind="ExternalOutput").ap()
        dbg["rows"] = nc.dram_tensor("dbg_rows", [1, 8704], F32, kind="ExternalOutput").ap()
        dbg["hrun"] = nc.dram_tensor("dbg_hrun", [16, 2048], F32, kind="ExternalOutput").ap()
        dbg["xn"] = nc.dram_tensor("dbg_xn", [128, 8 * 2056], F32, kind="ExternalOutput").ap()
        dbg["xcs"] = nc.dram_tensor("dbg_xcs", [128, 2052], F32, kind="ExternalOutput").ap()
        dbg["xconv"] = nc.dram_tensor("dbg_xconv", [128, 2048], F32, kind="ExternalOutput").ap()
    with ExitStack() as ctx:
        p = Prog(nc, ctx)
        C = Consts(p, nc, ctx)
        xres = sbuf(nc, ctx, "xres", [128, NCH * 2048], F32)
        r_xres = Res("xres")
        hooks = build_l0_attn(p, nc, ctx, C, xres, r_xres, dd, first=True, dbg=dbg)
        r_out = Res("out")
        for c in range(NCH):
            p.dma("sp", out.rearrange("(c p) t -> c p t", p=128)[c],
                  xres[:, c * 2048:(c + 1) * 2048], reads=[r_xres], writes=[r_out])
        p.finish()
    return nc


def l0_in_maps(inputs):
    w = _stage_l0_weights(inputs)
    x = np.asarray(inputs["L0_attn_x"], np.float32)
    maps = []
    for core in range(NCORES):
        b, h = core // 2, core % 2
        m = dict(w)
        m["xT"] = np.ascontiguousarray(x[b, h * 2048:(h + 1) * 2048, :].T)
        m["xTov"] = (np.zeros((1024, 3), np.float32) if h == 0 else
                     np.ascontiguousarray(x[b, h * 2048 - 3:h * 2048, :].T))
        m["flag"] = np.full((1, 1), float(h), np.float32)
        maps.append(m)
    return maps



def declare_attn_inputs(nc, kind, debug=False):
    dd = {}
    dd["flag"] = nc.dram_tensor("flag", [1, 1], F32, kind="ExternalInput").ap()
    dd["agi"] = nc.dram_tensor("d_agi", [1024, 2048], BF16).ap()
    dd["ago"] = nc.dram_tensor("d_ago", [2048, 2048], BF16).ap()
    dd["oag"] = nc.dram_tensor("d_oag", [512, 4096], BF16).ap()
    dd["oago"] = nc.dram_tensor("d_oago", [1024, 4096], BF16).ap()
    dd["mx"] = nc.dram_tensor("d_mx", [1, 4096], F32).ap()
    dd["bar1"] = nc.dram_tensor("d_bar1", [16, 16], F32).ap()
    dd["bar2"] = nc.dram_tensor("d_bar2", [16, 16], F32).ap()
    if kind == "l1attn":
        for nm in ["wq", "wk", "wv"]:
            dd["l1a_" + nm] = nc.dram_tensor("l1a_" + nm, [1024, 512], F32,
                                             kind="ExternalInput").ap()
        dd["l1a_wo"] = nc.dram_tensor("l1a_wo", [1024, 1024], F32, kind="ExternalInput").ap()
        dd["l1a_bqk"] = nc.dram_tensor("l1a_bqk", [128, 8], F32, kind="ExternalInput").ap()
        dd["l1a_bo"] = nc.dram_tensor("l1a_bo", [128, 8], F32, kind="ExternalInput").ap()
        dd["l1a_bvrow"] = nc.dram_tensor("l1a_bvrow", [1, 512], F32, kind="ExternalInput").ap()
        dd["ropePM"] = nc.dram_tensor("ropePM", [128, 128], F32, kind="ExternalInput").ap()
    else:
        for nm in ["wq", "wk", "wv"]:
            dd["l1f_" + nm] = nc.dram_tensor("l1f_" + nm, [1024, 512], F32,
                                             kind="ExternalInput").ap()
        dd["l1f_wo"] = nc.dram_tensor("l1f_wo", [1024, 1024], F32, kind="ExternalInput").ap()
        dd["l1f_n1w"] = nc.dram_tensor("l1f_n1w", [128, 8], F32, kind="ExternalInput").ap()
        dd["l1f_qnw"] = nc.dram_tensor("l1f_qnw", [128, 1], F32, kind="ExternalInput").ap()
        dd["l1f_knw"] = nc.dram_tensor("l1f_knw", [128, 1], F32, kind="ExternalInput").ap()
    return dd


def _rope_pm():
    pm = np.zeros((128, 128), np.float32)
    for blk in range(2):
        for i in range(32):
            d0 = blk * 64 + i
            pm[blk * 64 + i + 32, d0] = -1.0
            pm[d0, blk * 64 + i + 32] = 1.0
    return pm


def _stage_l1a_weights(inputs, half):
    h0 = half * 512
    w = {}
    for nm, key in [("wq", "L1_attn_Wq"), ("wk", "L1_attn_Wk"), ("wv", "L1_attn_Wv")]:
        W = np.asarray(inputs[key], np.float32)
        w["l1a_" + nm] = np.ascontiguousarray(W[h0:h0 + 512, :].T)
    w["l1a_wo"] = np.ascontiguousarray(np.asarray(inputs["L1_attn_Wo"], np.float32).T)
    bq = np.asarray(inputs["L1_attn_bq"], np.float32)[h0:h0 + 512].reshape(4, 128).T
    bk = np.asarray(inputs["L1_attn_bk"], np.float32)[h0:h0 + 512].reshape(4, 128).T
    w["l1a_bqk"] = np.ascontiguousarray(np.concatenate([bq, bk], 1))
    w["l1a_bo"] = np.ascontiguousarray(
        np.asarray(inputs["L1_attn_bo"], np.float32).reshape(8, 128).T)
    w["l1a_bvrow"] = np.ascontiguousarray(
        np.asarray(inputs["L1_attn_bv"], np.float32)[h0:h0 + 512].reshape(1, 512))
    w["ropePM"] = _rope_pm()
    return w


def build_l1a_program(debug=False):
    nc = bass.Bass("TRN2", target_bir_lowering=False)
    xTin = nc.dram_tensor("xT", [1024, 2048], F32, kind="ExternalInput").ap()
    dd = declare_attn_inputs(nc, "l1attn", debug)
    out = nc.dram_tensor("xout", [1024, 2048], F32, kind="ExternalOutput").ap()
    if debug:
        dd["dbg_qw"] = nc.dram_tensor("dbg_qw", [128, 4096], F32, kind="ExternalOutput").ap()
        dd["dbg_kw"] = nc.dram_tensor("dbg_kw", [128, 4096], F32, kind="ExternalOutput").ap()
        dd["dbg_vw"] = nc.dram_tensor("dbg_vw", [128, 32 * 130], F32, kind="ExternalOutput").ap()
        dd["dbg_nmx"] = nc.dram_tensor("dbg_nmx", [1, 4096], F32, kind="ExternalOutput").ap()
        dd["dbg_nmxraw"] = nc.dram_tensor("dbg_nmxraw", [128, 32], F32, kind="ExternalOutput").ap()
        dd["dbg_oag"] = nc.dram_tensor("dbg_oag", [512, 4096], BF16, kind="ExternalOutput").ap()
    with ExitStack() as ctx:
        p = Prog(nc, ctx)
        C = Consts(p, nc, ctx)
        xres = sbuf(nc, ctx, "xres", [128, NCH * 2048], F32)
        r_xres = Res("xres")
        for c in range(NCH):
            p.dma("sp", xres[:, c * 2048:(c + 1) * 2048],
                  xTin.rearrange("(c p) t -> c p t", p=128)[c], writes=[r_xres])
        build_attn_block(p, nc, ctx, C, xres, r_xres, dd, "l1attn")
        r_out = Res("out")
        for c in range(NCH):
            p.dma("sp", out.rearrange("(c p) t -> c p t", p=128)[c],
                  xres[:, c * 2048:(c + 1) * 2048], reads=[r_xres], writes=[r_out])
        p.finish()
    return nc


def run_l1a(inputs, x_in, debug=False):
    nc = build_l1a_program(debug)
    maps = []
    for core in range(NCORES):
        b, h = core // 2, core % 2
        m = _stage_l1a_weights(inputs, h)
        m["xT"] = np.ascontiguousarray(x_in[b, h * 2048:(h + 1) * 2048, :].T)
        m["flag"] = np.full((1, 1), float(h), np.float32)
        maps.append(m)
    res = run_bass_kernel_spmd(nc, maps, core_ids=list(range(NCORES)))
    out = np.zeros((4, 4096, 1024), np.float32)
    for core in range(NCORES):
        b, h = core // 2, core % 2
        out[b, h * 2048:(h + 1) * 2048, :] = np.asarray(res.results[core]["xout"]).T
    if debug:
        return out, res
    return out



def declare_l0f_inputs(nc):
    dd = {}
    dd["f0_bcols"] = nc.dram_tensor("f0_bcols", [128, 64], F32, kind="ExternalInput").ap()
    dd["f0_gbias"] = nc.dram_tensor("f0_gbias", [128, 16], F32, kind="ExternalInput").ap()
    dd["f0_b2row"] = nc.dram_tensor("f0_b2row", [1, 16], F32, kind="ExternalInput").ap()
    dd["f0_ceT"] = nc.dram_tensor("f0_ceT", [1024, 16], F32, kind="ExternalInput").ap()
    dd["f0_g1"] = nc.dram_tensor("f0_g1", [1024, 512], F32, kind="ExternalInput").ap()
    dd["f0_g2"] = nc.dram_tensor("f0_g2", [512, 16], F32, kind="ExternalInput").ap()
    dd["f0_wcols"] = nc.dram_tensor("f0_wcols", [1024, 4096], F32, kind="ExternalInput").ap()
    dd["f0_proj"] = nc.dram_tensor("f0_proj", [4096, 1024], F32, kind="ExternalInput").ap()
    dd["f0y"] = nc.dram_tensor("d_f0y", [1024, 2048], F32).ap()
    return dd


def _stage_l0f_weights(inputs):
    w = {}
    bcols = np.asarray(inputs["L0_ffn_b_cols"], np.float32).reshape(4096)
    projb = np.asarray(inputs["L0_ffn_proj_b"], np.float32)
    lng = np.asarray(inputs["L0_ffn_ln_g"], np.float32)
    lnb = np.asarray(inputs["L0_ffn_ln_b"], np.float32)
    out = np.zeros((128, 64), np.float32)
    for cn in range(32):
        out[:, cn] = bcols[cn * 128:(cn + 1) * 128]
    for c in range(8):
        out[:, 32 + c] = projb[c * 128:(c + 1) * 128]
        out[:, 40 + c] = lng[c * 128:(c + 1) * 128]
        out[:, 48 + c] = lnb[c * 128:(c + 1) * 128]
    w["f0_bcols"] = out
    w["f0_gbias"] = np.ascontiguousarray(
        np.concatenate([np.asarray(inputs["L0_ffn_gate_b1"], np.float32).reshape(4, 128).T,
                        np.zeros((128, 12), np.float32)], 1))
    w["f0_b2row"] = np.asarray(inputs["L0_ffn_gate_b2"], np.float32).reshape(1, 16)
    w["f0_ceT"] = np.ascontiguousarray(np.asarray(inputs["L0_ffn_col_emb"], np.float32).T)
    w["f0_g1"] = np.ascontiguousarray(np.asarray(inputs["L0_ffn_gate_w1"], np.float32))
    w["f0_g2"] = np.ascontiguousarray(np.asarray(inputs["L0_ffn_gate_w2"], np.float32))
    wc = np.asarray(inputs["L0_ffn_W_cols"], np.float32)
    w["f0_wcols"] = np.ascontiguousarray(wc.transpose(1, 0, 2).reshape(1024, 4096))
    w["f0_proj"] = np.ascontiguousarray(np.asarray(inputs["L0_ffn_proj_w"], np.float32))
    return w


def build_l0f_program(debug=False):
    nc = bass.Bass("TRN2", target_bir_lowering=False)
    xTin = nc.dram_tensor("xT", [1024, 2048], F32, kind="ExternalInput").ap()
    dd = declare_l0f_inputs(nc)
    out = nc.dram_tensor("xout", [1024, 2048], F32, kind="ExternalOutput").ap()
    if debug:
        dd["dbg_wtok"] = nc.dram_tensor("dbg_wtok", [128, 256], F32, kind="ExternalOutput").ap()
        dd["dbg_ltok"] = nc.dram_tensor("dbg_ltok", [128, 256], F32, kind="ExternalOutput").ap()
        dd["dbg_y"] = nc.dram_tensor("dbg_y", [1024, 2048], F32, kind="ExternalOutput").ap()
    with ExitStack() as ctx:
        p = Prog(nc, ctx)
        C = Consts(p, nc, ctx)
        xres = sbuf(nc, ctx, "xres", [128, NCH * 2048], F32)
        r_xres = Res("xres")
        for c in range(NCH):
            p.dma("sp", xres[:, c * 2048:(c + 1) * 2048],
                  xTin.rearrange("(c p) t -> c p t", p=128)[c], writes=[r_xres])
        build_l0_ffn(p, nc, ctx, C, xres, r_xres, dd)
        r_out = Res("out")
        for c in range(NCH):
            p.dma("sp", out.rearrange("(c p) t -> c p t", p=128)[c],
                  xres[:, c * 2048:(c + 1) * 2048], reads=[r_xres], writes=[r_out])
        p.finish()
    return nc


def run_l0f(inputs, x_in, debug=False):
    nc = build_l0f_program(debug)
    w = _stage_l0f_weights(inputs)
    maps = []
    for core in range(NCORES):
        b, h = core // 2, core % 2
        m = dict(w)
        m["xT"] = np.ascontiguousarray(x_in[b, h * 2048:(h + 1) * 2048, :].T)
        maps.append(m)
    res = run_bass_kernel_spmd(nc, maps, core_ids=list(range(NCORES)))
    out = np.zeros((4, 4096, 1024), np.float32)
    for core in range(NCORES):
        b, h = core // 2, core % 2
        out[b, h * 2048:(h + 1) * 2048, :] = np.asarray(res.results[core]["xout"]).T
    if debug:
        return out, res
    return out


def declare_l1f_inputs(nc):
    dd = declare_attn_inputs(nc, "l1ffn")
    dd["l1f_n2w"] = nc.dram_tensor("l1f_n2w", [128, 8], F32, kind="ExternalInput").ap()
    dd["xsp"] = nc.dram_tensor("d_xsp", [1024, 2048], F32).ap()
    dd["l1f_gwT"] = nc.dram_tensor("l1f_gwT", [1024, 4], F32, kind="ExternalInput").ap()
    for nm, shp in [("e0_up", (1024, 2048)), ("e0_down", (2048, 1024)),
                    ("e1_w1", (1024, 2048)), ("e1_w2", (1024, 2048)),
                    ("e1_down", (2048, 1024)), ("e2_l1", (1024, 2048)),
                    ("e2_l2", (2048, 2048)), ("e2_l3", (2048, 2048)),
                    ("e2_l4", (2048, 1024)), ("e3_down", (1024, 1024)),
                    ("e3_up", (1024, 2048)), ("e3_out", (2048, 1024))]:
        dd["l1f_" + nm] = nc.dram_tensor("l1f_" + nm, list(shp), F32,
                                         kind="ExternalInput").ap()
    return dd


def _stage_l1f_weights(inputs, half):
    h0 = half * 512
    w = {}
    for nm, key in [("wq", "L1_ffn_wq"), ("wk", "L1_ffn_wk"), ("wv", "L1_ffn_wv")]:
        W = np.asarray(inputs[key], np.float32)
        w["l1f_" + nm] = np.ascontiguousarray(W[h0:h0 + 512, :].T)
    w["l1f_wo"] = np.ascontiguousarray(np.asarray(inputs["L1_ffn_wo"], np.float32).T)
    w["l1f_n1w"] = np.ascontiguousarray(
        np.asarray(inputs["L1_ffn_n1_w"], np.float32).reshape(8, 128).T)
    w["l1f_n2w"] = np.ascontiguousarray(
        np.asarray(inputs["L1_ffn_n2_w"], np.float32).reshape(8, 128).T)
    w["l1f_qnw"] = np.ascontiguousarray(
        np.tile(np.asarray(inputs["L1_ffn_qn_w"], np.float32), 2).reshape(128, 1))
    w["l1f_knw"] = np.ascontiguousarray(
        np.tile(np.asarray(inputs["L1_ffn_kn_w"], np.float32), 2).reshape(128, 1))
    w["l1f_gwT"] = np.ascontiguousarray(np.asarray(inputs["L1_ffn_gate_w"], np.float32).T)
    for nm in ["e0_up", "e0_down", "e1_w1", "e1_w2", "e1_down", "e2_l1", "e2_l2",
               "e2_l3", "e2_l4", "e3_down", "e3_up", "e3_out"]:
        w["l1f_" + nm] = np.ascontiguousarray(
            np.asarray(inputs["L1_ffn_" + nm], np.float32).T)
    return w


def build_l1f_program(debug=False):
    nc = bass.Bass("TRN2", target_bir_lowering=False)
    xTin = nc.dram_tensor("xT", [1024, 2048], F32, kind="ExternalInput").ap()
    dd = declare_l1f_inputs(nc)
    out = nc.dram_tensor("xout", [1024, 2048], F32, kind="ExternalOutput").ap()
    dbgx = None
    if debug:
        dbgx = nc.dram_tensor("dbg_xmid", [1024, 2048], F32, kind="ExternalOutput").ap()
        dd["dbg_wts"] = nc.dram_tensor("dbg_wts", [128, 64], F32, kind="ExternalOutput").ap()
    with ExitStack() as ctx:
        p = Prog(nc, ctx)
        C = Consts(p, nc, ctx)
        xres = sbuf(nc, ctx, "xres", [128, NCH * 2048], F32)
        r_xres = Res("xres")
        for c in range(NCH):
            p.dma("sp", xres[:, c * 2048:(c + 1) * 2048],
                  xTin.rearrange("(c p) t -> c p t", p=128)[c], writes=[r_xres])
        build_attn_block(p, nc, ctx, C, xres, r_xres, dd, "l1ffn")
        if dbgx is not None:
            for c in range(NCH):
                p.dma("sp", dbgx.rearrange("(c p) t -> c p t", p=128)[c],
                      xres[:, c * 2048:(c + 1) * 2048], reads=[r_xres], writes=[Res("dx%d" % c)])
        build_l1_moe(p, nc, ctx, C, xres, r_xres, dd)
        r_out = Res("out")
        for c in range(NCH):
            p.dma("sp", out.rearrange("(c p) t -> c p t", p=128)[c],
                  xres[:, c * 2048:(c + 1) * 2048], reads=[r_xres], writes=[r_out])
        p.finish()
    return nc


def run_l1f(inputs, x_in, debug=False):
    nc = build_l1f_program(debug)
    maps = []
    for core in range(NCORES):
        b, h = core // 2, core % 2
        m = _stage_l1f_weights(inputs, h)
        m["xT"] = np.ascontiguousarray(x_in[b, h * 2048:(h + 1) * 2048, :].T)
        m["flag"] = np.full((1, 1), float(h), np.float32)
        maps.append(m)
    res = run_bass_kernel_spmd(nc, maps, core_ids=list(range(NCORES)))
    out = np.zeros((4, 4096, 1024), np.float32)
    for core in range(NCORES):
        b, h = core // 2, core % 2
        out[b, h * 2048:(h + 1) * 2048, :] = np.asarray(res.results[core]["xout"]).T
    if debug:
        return out, res
    return out


def kernel(**inputs):
    x = np.asarray(inputs["L0_attn_x"], np.float32)

    nc = build_l0_program()
    maps = l0_in_maps(inputs)
    res = run_bass_kernel_spmd(nc, maps, core_ids=list(range(NCORES)))
    x1 = np.zeros((4, 4096, 1024), np.float32)
    for core in range(NCORES):
        b, h = core // 2, core % 2
        x1[b, h * 2048:(h + 1) * 2048, :] = np.asarray(res.results[core]["xout"]).T

    x2 = run_l0f(inputs, x1)
    x3 = run_l1a(inputs, x2)
    x4 = run_l1f(inputs, x3)
    return x4


# version 04
# speedup vs baseline: 1.0098x; 1.0098x over previous
import numpy as np
import ml_dtypes
import concourse.bass as bass
import concourse.mybir as mybir
from concourse.bass_utils import run_bass_kernel_spmd
from contextlib import ExitStack

F32 = mybir.dt.float32
BF16 = mybir.dt.bfloat16
AF = mybir.ActivationFunctionType
ALU = mybir.AluOpType
AX = mybir.AxisListType
BF = ml_dtypes.bfloat16

NCORES = 8
B, T, DM = 4, 4096, 1024
TL = 2048
NCH = DM // 128
DS, DI = 16, 2048
NDI = DI // 128
CL = 128
NSC = TL // CL
CW = 2056

RO_MEAN, RO_RSTD, RO_MRS = 0, 2048, 4096
RO_OV = 6144
RO_DROW = 6656
RO_DELTA, RO_CD = 0, 2048
RO_CDREV, RO_CDC, RO_NEGCD, RO_CTOT = 4096, 4224, 4352, 4480
RO_SP1 = 4608


class Res:
    __slots__ = ("name", "w", "r")

    def __init__(self, name=""):
        self.name = name
        self.w = None
        self.r = []


class Prog:
    ENGS = ("pe", "act", "dve", "pool", "sp")

    def __init__(self, nc, ctx):
        self.nc = nc
        self.eng = {"pe": nc.tensor, "act": nc.scalar, "dve": nc.vector,
                    "pool": nc.gpsimd, "sp": nc.sync}
        self.sem = {e: ctx.enter_context(nc.semaphore(f"sem_{e}")) for e in self.ENGS}
        self.pos = {e: 0 for e in self.ENGS}
        self.known = {e: {e2: 0 for e2 in self.ENGS} for e in self.ENGS}
        self.strict_same_engine = True

    def emit(self, e, fn, reads=(), writes=(), inc=1, dma=False):
        need = {}

        def _need(dep):
            e2, v, d2 = dep
            if e2 != e or dma or d2 or self.strict_same_engine:
                if v > need.get(e2, 0):
                    need[e2] = v

        for r in reads:
            if r.w:
                _need(r.w)
        for w in writes:
            if w.w:
                _need(w.w)
            for rr in w.r:
                _need(rr)
        eng = self.eng[e]
        for e2 in sorted(need):
            v = need[e2]
            if self.known[e][e2] < v:
                eng.wait_ge(self.sem[e2], v)
                self.known[e][e2] = v
        last = fn(eng)
        last.then_inc(self.sem[e], inc)
        self.pos[e] += inc
        v = self.pos[e]
        for w in writes:
            w.w = (e, v, dma)
            w.r = []
        for r in reads:
            r.r.append((e, v, dma))
        return v

    def dma(self, q, out, in_, reads=(), writes=(), **kw):
        return self.emit(q, lambda g: g.dma_start(out=out, in_=in_, **kw),
                         reads=reads, writes=writes, inc=16, dma=True)

    def finish(self):
        eng = self.eng["sp"]
        for e2 in self.ENGS:
            if e2 != "sp" and self.known["sp"][e2] < self.pos[e2]:
                eng.wait_ge(self.sem[e2], self.pos[e2])
                self.known["sp"][e2] = self.pos[e2]
        eng.wait_ge(self.sem["sp"], self.pos["sp"])


def collective_barrier(p, d, scr_ap, r_scr, out_res):
    rb = Res("cbar")
    p.emit("pool", lambda e: e.collective_compute(
        "AllReduce", ALU.add, replica_groups=[[2 * i, 2 * i + 1] for i in range(4)],
        ins=[d["bar1"].opt()], outs=[d["bar2"].opt()]),
        reads=[out_res], writes=[rb], inc=1)
    p.dma("sp", scr_ap, d["bar2"][:, :], reads=[rb, r_scr], writes=[r_scr])
    out_res.w = r_scr.w


def sbuf(nc, ctx, name, shape, dt):
    return ctx.enter_context(nc.sbuf_tensor(name, shape, dt, allow_name_mangling=True))


def psumt(nc, ctx, name, shape, dt=F32):
    return ctx.enter_context(nc.psum_tensor(name, shape, dt, allow_name_mangling=True))


class Consts:
    def __init__(self, p, nc, ctx):
        self.onesrow_b = sbuf(nc, ctx, "c_onesrow_b", [1, 128], BF16)
        self.onesrow_f = sbuf(nc, ctx, "c_onesrow_f", [1, 128], F32)
        self.onescol_b = sbuf(nc, ctx, "c_onescol_b", [128, 1], BF16)
        self.onescol_f = sbuf(nc, ctx, "c_onescol_f", [128, 1], F32)
        self.ident = sbuf(nc, ctx, "c_ident", [128, 128], F32)
        self.minf = sbuf(nc, ctx, "c_minf", [128, 128], F32)
        self.eps1 = sbuf(nc, ctx, "c_eps1", [128, 1], F32)
        self.eps2 = sbuf(nc, ctx, "c_eps2", [128, 1], F32)
        self.r = {k: Res("c_" + k) for k in
                  ["onesrow_b", "onesrow_f", "onescol_b", "onescol_f", "ident", "minf",
                   "eps1", "eps2"]}
        r = self.r
        p.emit("dve", lambda e: e.memset(self.onesrow_b[:, :], 1.0), writes=[r["onesrow_b"]])
        p.emit("dve", lambda e: e.memset(self.onesrow_f[:, :], 1.0), writes=[r["onesrow_f"]])
        p.emit("dve", lambda e: e.memset(self.onescol_b[:, :], 1.0), writes=[r["onescol_b"]])
        p.emit("dve", lambda e: e.memset(self.onescol_f[:, :], 1.0), writes=[r["onescol_f"]])
        p.emit("dve", lambda e: e.memset(self.ident[:, :], 1.0), writes=[r["ident"]])
        p.emit("pool", lambda e: e.affine_select(self.ident[:, :], self.ident[:, :],
                                                 [[1, 128]], ALU.is_equal, 0.0,
                                                 base=0, channel_multiplier=-1),
               reads=[r["ident"]], writes=[r["ident"]])
        p.emit("dve", lambda e: e.memset(self.minf[:, :], 1.0e30), writes=[r["minf"]])
        p.emit("pool", lambda e: e.affine_select(self.minf[:, :], self.minf[:, :],
                                                 [[-1, 128]], ALU.is_ge, 0.0,
                                                 base=-1, channel_multiplier=1),
               reads=[r["minf"]], writes=[r["minf"]])
        p.emit("dve", lambda e: e.memset(self.eps1[:, :], 1e-5), writes=[r["eps1"]])
        p.emit("dve", lambda e: e.memset(self.eps2[:, :], 1.1920929e-07), writes=[r["eps2"]])


def bcast_row(p, C, pbank, r_pbank, row_ap, r_row, ncols):
    p.emit("pe", lambda e: e.matmul(pbank[:, 0:ncols], C.onesrow_f[0:1, :], row_ap,
                                    start=True, stop=True),
           reads=[r_row, C.r["onesrow_f"]], writes=[r_pbank])


def fence_res(p, name):
    r = Res(name)
    r.r = [(e, p.pos[e], True) for e in p.ENGS if p.pos[e] > 0]
    return r


def build_l0_attn(p, nc, ctx, C, xres, r_xres, d, first=True, dbg=None):
    dbg = dbg or {}
    ctx = ExitStack()
    rows = sbuf(nc, ctx, "l0_rows", [1, 8704], F32)
    scr = sbuf(nc, ctx, "l0_scr", [128, 512], F32)
    scrb = sbuf(nc, ctx, "l0_scrb", [128, 512], BF16)
    wtile = sbuf(nc, ctx, "l0_wtile", [128, 2 * 1024], BF16)
    xdbl = sbuf(nc, ctx, "l0_xdbl", [128, 2048], F32)
    dbc = sbuf(nc, ctx, "l0_dbc", [128, 2048], F32)
    szt = sbuf(nc, ctx, "l0_szt", [128, 2048], BF16)
    arow = sbuf(nc, ctx, "l0_arow", [1, 16], F32)
    abc = sbuf(nc, ctx, "l0_abc", [128, 16], F32)
    flagc = sbuf(nc, ctx, "l0_flagc", [16, 1], F32)

    pb = [psumt(nc, ctx, f"l0_pb{i}", [128, 512]) for i in range(8)]
    rpb = [Res(f"pb{i}") for i in range(8)]
    R = {k: Res(k) for k in ["rows", "scr", "scrb", "xdbl", "dbc", "szt",
                             "arow", "abc", "flagc",
                             "d_xs", "d_y", "d_sz", "d_ccin", "d_ccout"]}
    R["wtile"] = [Res("wtile0"), Res("wtile1")]

    cd = rows[0:1, RO_CD:RO_CD + 2048]
    dl = rows[0:1, RO_DELTA:RO_DELTA + 2048]

    if first:
        for c in range(NCH):
            p.dma("sp", xres[:, c * 2048:(c + 1) * 2048],
                  d["xT"].rearrange("(c p) t -> c p t", p=128)[c], writes=[r_xres])
    p.dma("sp", arow[:, :], d["Alog"][:, :], writes=[R["arow"]])
    p.emit("act", lambda e: e.activation(arow[:, :], arow[:, :], AF.Exp),
           reads=[R["arow"]], writes=[R["arow"]])
    p.emit("dve", lambda e: e.tensor_scalar(arow[:, :], arow[:, :], -1.0, None, ALU.mult),
           reads=[R["arow"]], writes=[R["arow"]])
    p.emit("pe", lambda e: e.matmul(pb[7][0:128, 0:16], C.onesrow_f[0:1, :], arow[:, :],
                                    start=True, stop=True),
           reads=[R["arow"], C.r["onesrow_f"]], writes=[rpb[7]])
    p.emit("dve", lambda e: e.tensor_copy(abc[:, :], pb[7][0:128, 0:16]),
           reads=[rpb[7]], writes=[R["abc"]])
    p.dma("sp", rows[0:1, RO_OV + 16:RO_OV + 17], d["flag"][:, :], writes=[R["rows"]])
    p.emit("pe", lambda e: e.matmul(pb[7][0:16, 0:1], C.onesrow_f[0:1, 0:16],
                                    rows[0:1, RO_OV + 16:RO_OV + 17], start=True, stop=True),
           reads=[R["rows"], C.r["onesrow_f"]], writes=[rpb[7]])
    p.emit("dve", lambda e: e.tensor_copy(flagc[:, :], pb[7][0:16, 0:1]),
           reads=[rpb[7]], writes=[R["flagc"]])
    p.dma("sp", rows[0:1, RO_DROW:RO_DROW + 2048], d["Drow"][0:1, :], writes=[R["rows"]])
    for tb in range(4):
        bcast_row(p, C, pb[6], rpb[6],
                  rows[0:1, RO_DROW + tb * 512:RO_DROW + (tb + 1) * 512], R["rows"], 512)
        p.emit("dve", lambda e, tb=tb: e.tensor_copy(dbc[:, tb * 512:(tb + 1) * 512],
                                                     pb[6][:, 0:512]),
               reads=[rpb[6]], writes=[R["dbc"]])

    s12 = ExitStack()
    xn = s12.enter_context(nc.sbuf_tensor("l0_xn", [128, NCH * CW], BF16, allow_name_mangling=True))
    xcs = s12.enter_context(nc.sbuf_tensor("l0_xcs", [128, 2052], F32, allow_name_mangling=True))
    xconv = s12.enter_context(nc.sbuf_tensor("l0_xconv", [128, 2048], F32, allow_name_mangling=True))
    xscur = s12.enter_context(nc.sbuf_tensor("l0_xscur", [128, 2048], BF16, allow_name_mangling=True))
    wz = s12.enter_context(nc.sbuf_tensor("l0_wz", [128, 8 * 1024], BF16, allow_name_mangling=True))
    wxp = s12.enter_context(nc.sbuf_tensor("l0_wxp", [128, NDI * 80], BF16, allow_name_mangling=True))
    convw = s12.enter_context(nc.sbuf_tensor("l0_convw", [128, 64], F32, allow_name_mangling=True))
    convb = s12.enter_context(nc.sbuf_tensor("l0_convb", [128, 16], F32, allow_name_mangling=True))
    ovf = s12.enter_context(nc.sbuf_tensor("l0_ovf", [128, NCH * 3], F32, allow_name_mangling=True))
    for k in ["xn", "xcs", "xconv", "xscur", "wz", "wxp", "convw", "convb", "ovf"]:
        R[k] = fence_res(p, k)

    for c in range(NCH):
        p.emit("dve", lambda e, c=c: e.tensor_copy(
            xn[:, c * CW + 3:c * CW + 3 + 2048], xres[:, c * 2048:(c + 1) * 2048]),
            reads=[r_xres], writes=[R["xn"]])
    for c in range(NCH):
        p.dma("sp", ovf[:, c * 3:(c + 1) * 3],
              d["xTov"].rearrange("(c p) t -> c p t", p=128)[c], writes=[R["ovf"]])
    for c in range(NCH):
        p.emit("dve", lambda e, c=c: e.tensor_copy(xn[:, c * CW:c * CW + 3],
                                                   ovf[:, c * 3:(c + 1) * 3]),
               reads=[R["ovf"]], writes=[R["xn"]])

    def ln_block(tok0, n, roff):
        mu = rows[0:1, roff:roff + n]
        rs = rows[0:1, roff + n:roff + 2 * n]
        mr = rows[0:1, roff + 2 * n:roff + 3 * n]
        for s0 in range(0, n, 512):
            w = min(512, n - s0)

            def mmsum(e, s0=s0, w=w):
                last = None
                for c in range(NCH):
                    last = e.matmul(pb[5][0:1, 0:w], C.onescol_b[:, :],
                                    xn[:, c * CW + tok0 + s0:c * CW + tok0 + s0 + w],
                                    start=(c == 0), stop=(c == NCH - 1))
                return last
            p.emit("pe", mmsum, reads=[R["xn"], C.r["onescol_b"]], writes=[rpb[5]])
            for c in range(NCH):
                p.emit("act", lambda e, c=c, s0=s0, w=w: e.activation(
                    scrb[:, 0:w], xn[:, c * CW + tok0 + s0:c * CW + tok0 + s0 + w],
                    AF.Square),
                    reads=[R["xn"]], writes=[R["scrb"]])
                p.emit("pe", lambda e, c=c, w=w: e.matmul(
                    pb[4][0:1, 0:w], C.onescol_b[:, :], scrb[:, 0:w],
                    start=(c == 0), stop=(c == NCH - 1)),
                    reads=[R["scrb"], C.r["onescol_b"]], writes=[rpb[4]])
            p.emit("dve", lambda e, s0=s0, w=w: e.tensor_scalar(
                mu[0:1, s0:s0 + w], pb[5][0:1, 0:w], 1.0 / DM, None, ALU.mult),
                reads=[rpb[5]], writes=[R["rows"]])
            p.emit("dve", lambda e, s0=s0, w=w: e.tensor_scalar(
                rs[0:1, s0:s0 + w], pb[4][0:1, 0:w], 1.0 / DM, None, ALU.mult),
                reads=[rpb[4]], writes=[R["rows"]])
        musq = rows[0:1, RO_DROW:RO_DROW + n]
        p.emit("act", lambda e: e.activation(musq, mu, AF.Square),
               reads=[R["rows"]], writes=[R["rows"]])
        p.emit("dve", lambda e: e.tensor_sub(rs, rs, musq),
               reads=[R["rows"]], writes=[R["rows"]])
        p.emit("act", lambda e: e.activation(rs, rs, AF.Ln, bias=C.eps1[0:1, 0:1]),
               reads=[R["rows"], C.r["eps1"]], writes=[R["rows"]])
        p.emit("act", lambda e: e.activation(rs, rs, AF.Exp, bias=0.0, scale=-0.5),
               reads=[R["rows"]], writes=[R["rows"]])
        p.emit("dve", lambda e: e.tensor_mul(mr, mu, rs),
               reads=[R["rows"]], writes=[R["rows"]])
        nb = (n + 511) // 512
        for tb in range(nb):
            s0, s1 = tb * 512, min((tb + 1) * 512, n)
            w = s1 - s0
            bcast_row(p, C, pb[5], rpb[5], rows[0:1, roff + n + s0:roff + n + s1],
                      R["rows"], w)
            bcast_row(p, C, pb[4], rpb[4], rows[0:1, roff + 2 * n + s0:roff + 2 * n + s1],
                      R["rows"], w)
            for c in range(NCH):
                if tok0 == 0:
                    src_f, r_src = ovf[:, c * 3:(c + 1) * 3], R["ovf"]
                else:
                    src_f, r_src = xres[:, c * 2048 + s0:c * 2048 + s1], r_xres
                dst = xn[:, c * CW + tok0 + s0:c * CW + tok0 + s1]
                p.emit("dve", lambda e, src_f=src_f, w=w: e.tensor_mul(
                    scr[:, 0:w], src_f, pb[5][:, 0:w]),
                    reads=[r_src, rpb[5]], writes=[R["scr"]])
                p.emit("dve", lambda e, dst=dst, w=w: e.tensor_sub(dst, scr[:, 0:w],
                                                                   pb[4][:, 0:w]),
                       reads=[R["scr"], rpb[4]], writes=[R["xn"]])

    ln_block(3, 2048, RO_MEAN)
    ln_block(0, 3, RO_OV)

    if "xn" in dbg:
        p.dma("pool", dbg["xn"][:, :], xn[:, :], reads=[R["xn"]], writes=[Res("dbgxn")])

    for kc in range(NDI):
        p.dma("pool", wxp.rearrange("p (k n) -> p k n", n=80)[:, kc],
              d["Wxp"].rearrange("(k p) n -> k p n", p=128)[kc], writes=[R["wxp"]])
    p.dma("sp", convw[:, :], d["convw"][:, :], writes=[R["convw"]])
    p.dma("sp", convb[:, :], d["convb"][:, :], writes=[R["convb"]])

    for co in range(NDI):
        wt = wtile[:, (co % 2) * 1024:(co % 2) * 1024 + 1024]
        r_wt = R["wtile"][co % 2]
        p.dma("pool", wt.rearrange("p (k n) -> p k n", n=128),
              d["Win"].rearrange("(k p) n -> p k n", p=128)[:, :, co * 128:(co + 1) * 128],
              reads=[r_wt], writes=[r_wt])

        def mm_small(e, wt=wt):
            last = None
            for kc in range(NCH):
                last = e.matmul(pb[0][:, 0:3], wt[:, kc * 128:(kc + 1) * 128],
                                xn[:, kc * CW:kc * CW + 3],
                                start=(kc == 0), stop=(kc == NCH - 1))
            return last
        p.emit("pe", mm_small, reads=[r_wt, R["xn"]], writes=[rpb[0]])
        p.emit("dve", lambda e: e.tensor_copy(xcs[:, 0:3], pb[0][:, 0:3]),
               reads=[rpb[0]], writes=[R["xcs"]])
        for tb in range(4):
            s0 = 3 + tb * 512
            bk = tb % 2

            def mm_xc(e, wt=wt, s0=s0, bk=bk):
                last = None
                for kc in range(NCH):
                    last = e.matmul(pb[bk][:, 0:512], wt[:, kc * 128:(kc + 1) * 128],
                                    xn[:, kc * CW + s0:kc * CW + s0 + 512],
                                    start=(kc == 0), stop=(kc == NCH - 1))
                return last
            p.emit("pe", mm_xc, reads=[r_wt, R["xn"]], writes=[rpb[bk]])
            p.emit("dve", lambda e, s0=s0, bk=bk: e.tensor_copy(
                xcs[:, s0:s0 + 512], pb[bk][:, 0:512]),
                reads=[rpb[bk]], writes=[R["xcs"]])
        p.emit("dve", lambda e, co=co: e.tensor_scalar(
            xconv[:, :], xcs[:, 0:2048], convw[:, co * 4:co * 4 + 1], None, ALU.mult),
            reads=[R["xcs"], R["convw"]], writes=[R["xconv"]])
        for j in range(1, 4):
            p.emit("dve", lambda e, co=co, j=j: e.scalar_tensor_tensor(
                xconv[:, :], xcs[:, j:j + 2048], convw[:, co * 4 + j:co * 4 + j + 1],
                xconv[:, :], ALU.mult, ALU.add),
                reads=[R["xcs"], R["convw"], R["xconv"]], writes=[R["xconv"]])
        if co == 0 and "xcs" in dbg:
            p.dma("pool", dbg["xcs"][:, :], xcs[:, :], reads=[R["xcs"]], writes=[Res("dxcs")])
            p.dma("pool", dbg["xconv"][:, :], xconv[:, :], reads=[R["xconv"]], writes=[Res("dxcv")])
        p.emit("act", lambda e, co=co: e.activation(
            xscur[:, :], xconv[:, :], AF.Silu, bias=convb[:, co:co + 1]),
            reads=[R["xconv"], R["convb"]], writes=[R["xscur"]])
        p.dma("sp", d["xs"][co * 128:(co + 1) * 128, :], xscur[:, :],
              reads=[R["xscur"]], writes=[R["d_xs"]])
        for tb in range(4):
            p.emit("pe", lambda e, co=co, tb=tb: e.matmul(
                pb[2 + tb][0:80, 0:512], wxp[:, co * 80:(co + 1) * 80],
                xscur[:, tb * 512:(tb + 1) * 512],
                start=(co == 0), stop=(co == NDI - 1)),
                reads=[R["wxp"], R["xscur"]], writes=[rpb[2 + tb]])
    for tb in range(4):
        p.emit("dve", lambda e, tb=tb: e.tensor_copy(
            xdbl[0:80, tb * 512:(tb + 1) * 512], pb[2 + tb][0:80, 0:512]),
            reads=[rpb[2 + tb]], writes=[R["xdbl"]])

    for zh in range(2):
        for kc in range(NCH):
            p.dma("pool", wz.rearrange("p (k n) -> p k n", n=1024)[:, kc],
                  d["Win"].rearrange("(k p) n -> k p n", p=128)[kc][:, 2048 + zh * 1024:
                                                                   2048 + (zh + 1) * 1024],
                  reads=[R["wz"]], writes=[R["wz"]])
        for tc in range(NSC):
            for zb in range(2):
                bk = zb % 2

                def mm_z(e, tc=tc, zb=zb, bk=bk):
                    last = None
                    for kc in range(NCH):
                        last = e.matmul(pb[bk][:, 0:512],
                                        xn[:, kc * CW + 3 + tc * 128:kc * CW + 3 + (tc + 1) * 128],
                                        wz[:, kc * 1024 + zb * 512:kc * 1024 + (zb + 1) * 512],
                                        start=(kc == 0), stop=(kc == NCH - 1))
                    return last
                p.emit("pe", mm_z, reads=[R["xn"], R["wz"]], writes=[rpb[bk]])
                p.emit("act", lambda e, zh=zh, zb=zb, bk=bk: e.activation(
                    szt[:, zh * 1024 + zb * 512:zh * 1024 + (zb + 1) * 512],
                    pb[bk][:, 0:512], AF.Silu),
                    reads=[rpb[bk]], writes=[R["szt"]])
            p.dma("sp", d["sz"][tc * 128:(tc + 1) * 128, zh * 1024:(zh + 1) * 1024],
                  szt[:, zh * 1024:(zh + 1) * 1024], reads=[R["szt"]], writes=[R["d_sz"]])

    s12.close()

    s36 = ExitStack()
    hrun = s36.enter_context(nc.sbuf_tensor("l0_hrun", [16, 2048], F32, allow_name_mangling=True))
    hin = s36.enter_context(nc.sbuf_tensor("l0_hin", [16, 2048], F32, allow_name_mangling=True))
    htmp = s36.enter_context(nc.sbuf_tensor("l0_htmp", [16, 2048], BF16, allow_name_mangling=True))
    ccs = s36.enter_context(nc.sbuf_tensor("l0_ccs", [16, 2048], F32, allow_name_mangling=True))
    dpc = s36.enter_context(nc.sbuf_tensor("l0_dpc", [16, 40], F32, allow_name_mangling=True))
    mtf = s36.enter_context(nc.sbuf_tensor("l0_mtf", [128, 128], F32, allow_name_mangling=True))
    macc = s36.enter_context(nc.sbuf_tensor("l0_macc", [128, 128], F32, allow_name_mangling=True))
    mtb = s36.enter_context(nc.sbuf_tensor("l0_mtb", [128, 128], BF16, allow_name_mangling=True))
    esc = s36.enter_context(nc.sbuf_tensor("l0_esc", [128, 2 * 128], F32, allow_name_mangling=True))
    qT = s36.enter_context(nc.sbuf_tensor("l0_qT", [16, 128], BF16))
    rtok = s36.enter_context(nc.sbuf_tensor("l0_rtok", [128, 16], BF16, allow_name_mangling=True))
    bbtok = s36.enter_context(nc.sbuf_tensor("l0_bbtok", [128, 16], F32, allow_name_mangling=True))
    xstok = s36.enter_context(nc.sbuf_tensor("l0_xstok", [128, 2 * 2048], BF16, allow_name_mangling=True))
    ytok = s36.enter_context(nc.sbuf_tensor("l0_ytok", [128, 2048], BF16, allow_name_mangling=True))
    cT = s36.enter_context(nc.sbuf_tensor("l0_cT", [16, 2048], F32))
    cTb = s36.enter_context(nc.sbuf_tensor("l0_cTb", [16, 2048], BF16))
    onehot = s36.enter_context(nc.sbuf_tensor("l0_onehot", [16, 16 * 128], BF16, allow_name_mangling=True))
    bpT = s36.enter_context(nc.sbuf_tensor("l0_bpT", [16, 2048], F32))
    for k in ["hrun", "hin", "htmp", "ccs", "dpc", "mtf", "macc", "mtb", "esc",
              "qT", "rtok", "bbtok", "ytok", "cT", "cTb", "bpT", "onehot"]:
        R[k] = fence_res(p, k)
    p.emit("dve", lambda e: e.memset(onehot[:, :], 1.0),
           reads=[R["onehot"]], writes=[R["onehot"]])
    p.emit("pool", lambda e: e.affine_select(onehot[:, :], onehot[:, :],
                                             [[1, 16], [0, 128]], ALU.is_equal, 0.0,
                                             base=0, channel_multiplier=-1),
           reads=[R["onehot"]], writes=[R["onehot"]])
    R["xstok"] = [fence_res(p, "xstok0"), fence_res(p, "xstok1")]

    p.emit("dve", lambda e: e.tensor_copy(cT[:, :], xdbl[64:80, :]),
           reads=[R["xdbl"]], writes=[R["cT"]])
    p.emit("dve", lambda e: e.tensor_copy(bpT[:, :], xdbl[32:48, :]),
           reads=[R["xdbl"]], writes=[R["bpT"]])
    p.emit("dve", lambda e: e.tensor_copy(cTb[:, :], cT[:, :]),
           reads=[R["cT"]], writes=[R["cTb"]])
    sp1 = rows[0:1, RO_SP1:RO_SP1 + 2048]
    p.emit("act", lambda e: e.activation(sp1, xdbl[0:1, 0:2048], AF.Abs),
           reads=[R["xdbl"]], writes=[R["rows"]])
    p.emit("act", lambda e: e.activation(sp1, sp1, AF.Exp, bias=0.0, scale=-1.0),
           reads=[R["rows"]], writes=[R["rows"]])
    p.emit("act", lambda e: e.activation(sp1, sp1, AF.Ln, bias=1.0),
           reads=[R["rows"]], writes=[R["rows"]])
    p.emit("dve", lambda e: e.tensor_scalar(dl, xdbl[0:1, 0:2048], 0.0, None, ALU.max),
           reads=[R["xdbl"]], writes=[R["rows"]])
    p.emit("dve", lambda e: e.tensor_add(dl, dl, sp1),
           reads=[R["rows"]], writes=[R["rows"]])
    p.emit("dve", lambda e: e.tensor_tensor_scan(cd, dl, dl, 0.0, ALU.add, ALU.max),
           reads=[R["rows"]], writes=[R["rows"]])
    for tc in range(NSC):
        def mk(e, tc=tc):
            dst = rows[0:1, RO_CTOT + tc:RO_CTOT + tc + 1]
            if tc == 0:
                return e.tensor_copy(dst, cd[0:1, CL - 1:CL])
            return e.tensor_sub(dst, cd[0:1, (tc + 1) * CL - 1:(tc + 1) * CL],
                                cd[0:1, tc * CL - 1:tc * CL])
        p.emit("dve", mk, reads=[R["rows"]], writes=[R["rows"]])
    p.emit("pe", lambda e: e.matmul(pb[6][0:16, 0:16], arow[:, :],
                                    rows[0:1, RO_CTOT:RO_CTOT + 16], start=True, stop=True),
           reads=[R["arow"], R["rows"]], writes=[rpb[6]])
    p.emit("act", lambda e: e.activation(dpc[:, 0:16], pb[6][0:16, 0:16], AF.Exp),
           reads=[rpb[6]], writes=[R["dpc"]])
    PO = 20
    p.emit("dve", lambda e: e.memset(dpc[:, PO:PO + 1], 1.0),
           reads=[R["dpc"]], writes=[R["dpc"]])
    for tc in range(1, NSC):
        p.emit("dve", lambda e, tc=tc: e.tensor_mul(
            dpc[:, PO + tc:PO + tc + 1], dpc[:, PO + tc - 1:PO + tc], dpc[:, tc - 1:tc]),
            reads=[R["dpc"]], writes=[R["dpc"]])

    def chunk_prep(tc, xst, r_xst):
        p.dma("sp", xst, d["xs"][:, tc * 128:(tc + 1) * 128], transpose=True,
              reads=[R["d_xs"]], writes=[r_xst])
        p.emit("pe", lambda e, tc=tc: e.transpose(pb[6][0:128, 0:16],
                                                  bpT[:, tc * 128:(tc + 1) * 128],
                                                  C.ident[0:16, 0:16]),
               reads=[R["bpT"], C.r["ident"]], writes=[rpb[6]])
        p.emit("pe", lambda e, tc=tc: e.transpose(pb[7][0:128, 0:1],
                                                  dl[0:1, tc * 128:(tc + 1) * 128],
                                                  C.ident[0:1, 0:1]),
               reads=[R["rows"], C.r["ident"]], writes=[rpb[7]])
        p.emit("dve", lambda e: e.tensor_copy(scr[:, 0:1], pb[7][0:128, 0:1]),
               reads=[rpb[7]], writes=[R["scr"]])
        p.emit("dve", lambda e: e.tensor_scalar(bbtok[:, :], pb[6][0:128, 0:16],
                                                scr[:, 0:1], None, ALU.mult),
               reads=[rpb[6], R["scr"]], writes=[R["bbtok"]])

    def chunk_scan_step(tc, xst, r_xst):
        p.emit("dve", lambda e, tc=tc: e.tensor_scalar(
            rows[0:1, RO_CDREV:RO_CDREV + 128], cd[0:1, tc * 128:(tc + 1) * 128],
            -1.0, cd[0:1, (tc + 1) * 128 - 1:(tc + 1) * 128], ALU.mult, ALU.add),
            reads=[R["rows"]], writes=[R["rows"]])
        p.emit("pe", lambda e: e.matmul(pb[6][0:128, 0:16],
                                        rows[0:1, RO_CDREV:RO_CDREV + 128], arow[:, :],
                                        start=True, stop=True),
               reads=[R["rows"], R["arow"]], writes=[rpb[6]])
        p.emit("act", lambda e: e.activation(esc[:, 0:16], pb[6][0:128, 0:16], AF.Exp),
               reads=[rpb[6]], writes=[R["esc"]])
        p.emit("dve", lambda e: e.tensor_mul(rtok[:, :], esc[:, 0:16], bbtok[:, :]),
               reads=[R["esc"], R["bbtok"]], writes=[R["rtok"]])
        for dbk in range(4):
            p.emit("pe", lambda e, dbk=dbk: e.matmul(
                pb[2 + dbk][0:16, 0:512], rtok[:, :], xst[:, dbk * 512:(dbk + 1) * 512],
                start=True, stop=True),
                reads=[R["rtok"], r_xst], writes=[rpb[2 + dbk]])
        for dbk in range(4):
            p.emit("dve", lambda e, tc=tc, dbk=dbk: e.scalar_tensor_tensor(
                hrun[:, dbk * 512:(dbk + 1) * 512], hrun[:, dbk * 512:(dbk + 1) * 512],
                dpc[:, tc:tc + 1], pb[2 + dbk][0:16, 0:512], ALU.mult, ALU.add),
                reads=[R["hrun"], R["dpc"], rpb[2 + dbk]], writes=[R["hrun"]])

    p.emit("dve", lambda e: e.memset(hrun[:, :], 0.0),
           reads=[R["hrun"]], writes=[R["hrun"]])
    for tc in range(NSC):
        xst = xstok[:, (tc % 2) * 2048:(tc % 2) * 2048 + 2048]
        chunk_prep(tc, xst, R["xstok"][tc % 2])
        chunk_scan_step(tc, xst, R["xstok"][tc % 2])

    if "xdbl" in dbg:
        p.dma("sp", dbg["xdbl"][:, :], xdbl[:, :], reads=[R["xdbl"]], writes=[Res("dbg1")])
        p.dma("sp", dbg["rows"][:, :], rows[:, :], reads=[R["rows"]], writes=[Res("dbg2")])
        p.dma("sp", dbg["hrun"][:, :], hrun[:, :], reads=[R["hrun"]], writes=[Res("dbg3")])

    p.dma("sp", d["ccin"][0:16, :], hrun[:, :], reads=[R["hrun"]], writes=[R["d_ccin"]])
    p.emit("pool", lambda e: e.collective_compute(
        "AllGather", ALU.bypass, replica_groups=[[2 * i, 2 * i + 1] for i in range(4)],
        ins=[d["ccin"].opt()], outs=[d["ccout"].opt()]),
        reads=[R["d_ccin"]], writes=[R["d_ccout"]], inc=1)
    collective_barrier(p, d, scr[0:16, 0:16], R["scr"], R["d_ccout"])
    p.dma("sp", ccs[:, :], d["ccout"][0:16, :], reads=[R["d_ccout"]], writes=[R["ccs"]])
    p.emit("dve", lambda e: e.tensor_scalar(hin[:, :], ccs[:, :], flagc[:, :], None, ALU.mult),
           reads=[R["ccs"], R["flagc"]], writes=[R["hin"]])

    p.emit("dve", lambda e: e.memset(hrun[:, :], 0.0), reads=[R["hrun"]], writes=[R["hrun"]])
    for tc in range(NSC):
        xst = xstok[:, (tc % 2) * 2048:(tc % 2) * 2048 + 2048]
        r_xst = R["xstok"][tc % 2]
        chunk_prep(tc, xst, r_xst)
        p.dma("sp", szt[:, :], d["sz"][tc * 128:(tc + 1) * 128, :],
              reads=[R["d_sz"], R["szt"]], writes=[R["szt"]])
        p.emit("dve", lambda e, tc=tc: e.scalar_tensor_tensor(
            htmp[:, :], hin[:, :], dpc[:, PO + tc:PO + tc + 1], hrun[:, :],
            ALU.mult, ALU.add),
            reads=[R["hin"], R["dpc"], R["hrun"]], writes=[R["htmp"]])
        p.emit("dve", lambda e, tc=tc: e.tensor_scalar(
            rows[0:1, RO_NEGCD:RO_NEGCD + 128], cd[0:1, tc * 128:(tc + 1) * 128],
            -1.0, None, ALU.mult),
            reads=[R["rows"]], writes=[R["rows"]])

        def mmdt(e, tc=tc):
            e.matmul(pb[6][:, 0:128], C.onesrow_f[0:1, :], cd[0:1, tc * 128:(tc + 1) * 128],
                     start=True, stop=False)
            return e.matmul(pb[6][:, 0:128], rows[0:1, RO_NEGCD:RO_NEGCD + 128],
                            C.onesrow_f[0:1, :], start=False, stop=True)
        p.emit("pe", mmdt, reads=[R["rows"], C.r["onesrow_f"]], writes=[rpb[6]])
        p.emit("dve", lambda e: e.tensor_add(mtf[:, :], pb[6][:, 0:128], C.minf[:, :]),
               reads=[rpb[6], C.r["minf"]], writes=[R["mtf"]])
        for n in range(DS):
            eb = esc[:, (n % 2) * 128:(n % 2) * 128 + 128]
            bk = 6 + (n % 2)
            p.emit("pe", lambda e, n=n, tc=tc, bk=bk: e.matmul(
                pb[bk][:, 0:128], onehot[:, n * 128:(n + 1) * 128],
                cTb[:, tc * 128:(tc + 1) * 128], start=True, stop=True),
                reads=[R["onehot"], R["cTb"]], writes=[rpb[bk]])
            p.emit("act", lambda e, n=n, eb=eb: e.activation(
                eb, mtf[:, :], AF.Exp, bias=0.0, scale=abc[:, n:n + 1]),
                reads=[R["mtf"], R["abc"]], writes=[R["esc"]])
            p.emit("dve", lambda e, eb=eb, bk=bk: e.tensor_mul(eb, eb, pb[bk][:, 0:128]),
                   reads=[R["esc"], rpb[bk]], writes=[R["esc"]])
            if n == 0:
                p.emit("dve", lambda e, n=n, eb=eb: e.tensor_scalar(
                    macc[:, :], eb, bbtok[:, n:n + 1], None, ALU.mult),
                    reads=[R["esc"], R["bbtok"]], writes=[R["macc"]])
            elif n < DS - 1:
                p.emit("dve", lambda e, n=n, eb=eb: e.scalar_tensor_tensor(
                    macc[:, :], eb, bbtok[:, n:n + 1], macc[:, :], ALU.mult, ALU.add),
                    reads=[R["esc"], R["bbtok"], R["macc"]], writes=[R["macc"]])
            else:
                p.emit("dve", lambda e, n=n, eb=eb: e.scalar_tensor_tensor(
                    mtb[:, :], eb, bbtok[:, n:n + 1], macc[:, :], ALU.mult, ALU.add),
                    reads=[R["esc"], R["bbtok"], R["macc"]], writes=[R["mtb"]])

        def mkcdc(e, tc=tc):
            dst = rows[0:1, RO_CDC:RO_CDC + 128]
            if tc == 0:
                return e.tensor_copy(dst, cd[0:1, 0:128])
            return e.tensor_scalar(dst, cd[0:1, tc * 128:(tc + 1) * 128], 1.0,
                                   cd[0:1, tc * 128 - 1:tc * 128], ALU.mult, ALU.subtract)
        p.emit("dve", mkcdc, reads=[R["rows"]], writes=[R["rows"]])
        p.emit("pe", lambda e: e.matmul(pb[6][0:16, 0:128], arow[:, :],
                                        rows[0:1, RO_CDC:RO_CDC + 128], start=True, stop=True),
               reads=[R["arow"], R["rows"]], writes=[rpb[6]])
        p.emit("act", lambda e: e.activation(esc[0:16, 0:128], pb[6][0:16, 0:128], AF.Exp),
               reads=[rpb[6]], writes=[R["esc"]])
        p.emit("dve", lambda e, tc=tc: e.tensor_mul(
            qT[:, :], esc[0:16, 0:128], cT[:, tc * 128:(tc + 1) * 128]),
            reads=[R["esc"], R["cT"]], writes=[R["qT"]])
        for dbk in range(4):
            def mmy(e, dbk=dbk):
                e.matmul(pb[2 + dbk][:, 0:512], mtb[:, :], xst[:, dbk * 512:(dbk + 1) * 512],
                         start=True, stop=False)
                return e.matmul(pb[2 + dbk][:, 0:512], qT[:, :],
                                htmp[:, dbk * 512:(dbk + 1) * 512], start=False, stop=True)
            p.emit("pe", mmy, reads=[R["mtb"], R["qT"], r_xst, R["htmp"]],
                   writes=[rpb[2 + dbk]])
        for dbk in range(4):
            p.emit("dve", lambda e, dbk=dbk: e.tensor_mul(
                scr[:, 0:512], xst[:, dbk * 512:(dbk + 1) * 512],
                dbc[:, dbk * 512:(dbk + 1) * 512]),
                reads=[r_xst, R["dbc"]], writes=[R["scr"]])
            p.emit("dve", lambda e, dbk=dbk: e.tensor_add(
                scr[:, 0:512], scr[:, 0:512], pb[2 + dbk][:, 0:512]),
                reads=[R["scr"], rpb[2 + dbk]], writes=[R["scr"]])
            p.emit("dve", lambda e, dbk=dbk: e.tensor_mul(
                ytok[:, dbk * 512:(dbk + 1) * 512], scr[:, 0:512],
                szt[:, dbk * 512:(dbk + 1) * 512]),
                reads=[R["scr"], R["szt"]], writes=[R["ytok"]])
        p.dma("sp", d["y"][tc * 128:(tc + 1) * 128, :], ytok[:, :],
              reads=[R["ytok"]], writes=[R["d_y"]])
        chunk_scan_step(tc, xst, r_xst)

    s36.close()

    s7 = ExitStack()
    yT = s7.enter_context(nc.sbuf_tensor("l0_yT", [128, NDI * 2048], BF16))
    wv2 = s7.enter_context(nc.sbuf_tensor("l0_wv2", [128, 2 * 2048], BF16, allow_name_mangling=True))
    R["yT"] = fence_res(p, "yT")
    R["wv2"] = [fence_res(p, "wv20"), fence_res(p, "wv21")]
    for kc in range(NDI):
        p.dma("sp", yT[:, kc * 2048:(kc + 1) * 2048],
              d["y"][:, kc * 128:(kc + 1) * 128], transpose=True,
              reads=[R["d_y"]], writes=[R["yT"]])
    for do in range(NCH):
        wv = wv2[:, (do % 2) * 2048:(do % 2) * 2048 + 2048]
        r_wv = R["wv2"][do % 2]
        p.dma("pool", wv.rearrange("p (k n) -> p k n", n=128),
              d["Wout"].rearrange("(k p) n -> p k n", p=128)[:, :, do * 128:(do + 1) * 128],
              reads=[r_wv], writes=[r_wv])
        for tb in range(4):
            bk = tb % 2

            def mmo(e, do=do, tb=tb, bk=bk, wv=wv):
                last = None
                for kc in range(NDI):
                    last = e.matmul(pb[bk][:, 0:512], wv[:, kc * 128:(kc + 1) * 128],
                                    yT[:, kc * 2048 + tb * 512:kc * 2048 + (tb + 1) * 512],
                                    start=(kc == 0), stop=(kc == NDI - 1))
                return last
            p.emit("pe", mmo, reads=[r_wv, R["yT"]], writes=[rpb[bk]])
            p.emit("dve", lambda e, do=do, tb=tb, bk=bk: e.tensor_add(
                xres[:, do * 2048 + tb * 512:do * 2048 + (tb + 1) * 512],
                xres[:, do * 2048 + tb * 512:do * 2048 + (tb + 1) * 512],
                pb[bk][:, 0:512]),
                reads=[rpb[bk], r_xres], writes=[r_xres])
    s7.close()
    ctx.close()



def feat_norm(p, nc, C, xres, r_xres, xnb, r_xnb, rows, r_rows, scr, r_scr,
              scrb, r_scrb, pb, rpb, mode, wcol=None, r_wcol=None):
    for c in range(NCH):
        p.emit("dve", lambda e, c=c: e.tensor_copy(
            xnb[:, c * 2048:(c + 1) * 2048], xres[:, c * 2048:(c + 1) * 2048]),
            reads=[r_xres], writes=[r_xnb])
    isln = mode in ("ln", "lnip")
    for s0 in range(0, 2048, 512):
        if isln:
            def mmsum(e, s0=s0):
                last = None
                for c in range(NCH):
                    last = e.matmul(pb[5][0:1, 0:512], C.onescol_b[:, :],
                                    xnb[:, c * 2048 + s0:c * 2048 + s0 + 512],
                                    start=(c == 0), stop=(c == NCH - 1))
                return last
            p.emit("pe", mmsum, reads=[r_xnb, C.r["onescol_b"]], writes=[rpb[5]])
        for c in range(NCH):
            p.emit("act", lambda e, c=c, s0=s0: e.activation(
                scrb[:, 0:512], xnb[:, c * 2048 + s0:c * 2048 + s0 + 512], AF.Square),
                reads=[r_xnb], writes=[r_scrb])
            p.emit("pe", lambda e, c=c: e.matmul(
                pb[4][0:1, 0:512], C.onescol_b[:, :], scrb[:, 0:512],
                start=(c == 0), stop=(c == NCH - 1)),
                reads=[r_scrb, C.r["onescol_b"]], writes=[rpb[4]])
        if isln:
            p.emit("dve", lambda e, s0=s0: e.tensor_scalar(
                rows[0:1, s0:s0 + 512], pb[5][0:1, 0:512], 1.0 / DM, None, ALU.mult),
                reads=[rpb[5]], writes=[r_rows])
        p.emit("dve", lambda e, s0=s0: e.tensor_scalar(
            rows[0:1, 2048 + s0:2048 + s0 + 512], pb[4][0:1, 0:512], 1.0 / DM, None,
            ALU.mult),
            reads=[rpb[4]], writes=[r_rows])
    mu = rows[0:1, 0:2048]
    rs = rows[0:1, 2048:4096]
    if isln:
        p.emit("act", lambda e: e.activation(rows[0:1, 4096:6144], mu, AF.Square),
               reads=[r_rows], writes=[r_rows])
        p.emit("dve", lambda e: e.tensor_sub(rs, rs, rows[0:1, 4096:6144]),
               reads=[r_rows], writes=[r_rows])
        epsap, repcs = C.eps1[0:1, 0:1], [C.r["eps1"]]
    else:
        epsap, repcs = C.eps2[0:1, 0:1], [C.r["eps2"]]
    p.emit("act", lambda e: e.activation(rs, rs, AF.Ln, bias=epsap),
           reads=[r_rows] + repcs, writes=[r_rows])
    p.emit("act", lambda e: e.activation(rs, rs, AF.Exp, bias=0.0, scale=-0.5),
           reads=[r_rows], writes=[r_rows])
    if isln:
        p.emit("dve", lambda e: e.tensor_mul(rows[0:1, 4096:6144], mu, rs),
               reads=[r_rows], writes=[r_rows])
    for s0 in range(0, 2048, 512):
        bcast_row(p, C, pb[5], rpb[5], rs[0:1, s0:s0 + 512], r_rows, 512)
        if isln:
            bcast_row(p, C, pb[4], rpb[4], rows[0:1, 4096 + s0:4096 + s0 + 512],
                      r_rows, 512)
        for c in range(NCH):
            if mode == "lnip":
                dst, r_dst = xres[:, c * 2048 + s0:c * 2048 + s0 + 512], r_xres
            else:
                dst, r_dst = xnb[:, c * 2048 + s0:c * 2048 + s0 + 512], r_xnb
            p.emit("dve", lambda e, c=c, s0=s0: e.tensor_mul(
                scr[:, 0:512], xres[:, c * 2048 + s0:c * 2048 + s0 + 512],
                pb[5][:, 0:512]),
                reads=[r_xres, rpb[5]], writes=[r_scr])
            if isln:
                p.emit("dve", lambda e, dst=dst: e.tensor_sub(dst, scr[:, 0:512],
                                                              pb[4][:, 0:512]),
                       reads=[r_scr, rpb[4]], writes=[r_dst])
            else:
                p.emit("dve", lambda e, c=c, dst=dst: e.tensor_scalar(
                    dst, scr[:, 0:512], wcol[:, c:c + 1], None, ALU.mult),
                    reads=[r_scr, r_wcol], writes=[r_dst])


def build_attn_block(p, nc, ctx, C, xres, r_xres, d, kind):
    import math as _m
    sa = ExitStack()
    rows = sa.enter_context(nc.sbuf_tensor("at_rows", [1, 2048], F32, allow_name_mangling=True))
    scr = sa.enter_context(nc.sbuf_tensor("at_scr", [128, 512], F32, allow_name_mangling=True))
    scrb = sa.enter_context(nc.sbuf_tensor("at_scrb", [128, 512], BF16, allow_name_mangling=True))
    qw = sa.enter_context(nc.sbuf_tensor("at_qw", [128, 4096], BF16, allow_name_mangling=True))
    kw = sa.enter_context(nc.sbuf_tensor("at_kw", [128, 4096], BF16, allow_name_mangling=True))
    vw = sa.enter_context(nc.sbuf_tensor("at_vw", [128, 32 * 130], BF16, allow_name_mangling=True))
    wt3 = sa.enter_context(nc.sbuf_tensor("at_wt3", [128, 3 * 1024], BF16, allow_name_mangling=True))
    wcol = sa.enter_context(nc.sbuf_tensor("at_wcol", [128, 8], F32, allow_name_mangling=True))
    bcols = sa.enter_context(nc.sbuf_tensor("at_bcols", [128, 16], F32, allow_name_mangling=True))
    bvbc = sa.enter_context(nc.sbuf_tensor("at_bvbc", [128, 128], F32, allow_name_mangling=True))
    nmx = sa.enter_context(nc.sbuf_tensor("at_nmx", [128, 32], F32, allow_name_mangling=True))
    nmx32 = sa.enter_context(nc.sbuf_tensor("at_nmx32", [32, 128], F32, allow_name_mangling=True))
    negrow = sa.enter_context(nc.sbuf_tensor("at_negrow", [1, 4096], BF16, allow_name_mangling=True))
    abf = sa.enter_context(nc.sbuf_tensor("at_abf", [128, 2 * 512], BF16, allow_name_mangling=True))
    rbc = sa.enter_context(nc.sbuf_tensor("at_rbc", [64, 512], F32, allow_name_mangling=True))
    obf = sa.enter_context(nc.sbuf_tensor("at_obf", [64, 512], BF16, allow_name_mangling=True))
    m512 = sa.enter_context(nc.sbuf_tensor("at_m512", [128, 896], BF16, allow_name_mangling=True))
    fsel = sa.enter_context(nc.sbuf_tensor("at_fsel", [128, 2], F32, allow_name_mangling=True))
    pb = [psumt(nc, sa, f"at_pb{i}", [128, 512]) for i in range(8)]
    rpb = [fence_res(p, f"at_pb{i}") for i in range(8)]
    N = ["rows", "scr", "scrb", "qw", "kw", "vw", "wcol", "bcols",
         "bvbc", "nmx", "nmx32", "negrow", "abf", "rbc", "obf", "m512", "fsel",
         "d_agi", "d_ago", "d_oag", "d_oago", "d_mx"]
    R = {k: fence_res(p, k) for k in N}
    R["wt3"] = [fence_res(p, f"wt3_{i}") for i in range(3)]

    p.emit("dve", lambda e: e.memset(m512[:, :], 1.0), writes=[R["m512"]])
    p.emit("pool", lambda e: e.affine_select(
        m512[:, :], m512[:, :], [[1, 896]], ALU.is_ge, 0.0,
        base=-384, channel_multiplier=-1),
        reads=[R["m512"]], writes=[R["m512"]])
    p.dma("sp", rows[0:1, 0:1], d["flag"][:, :], writes=[R["rows"]])
    p.emit("pe", lambda e: e.matmul(pb[7][0:128, 0:1], C.onesrow_f[0:1, :],
                                    rows[0:1, 0:1], start=True, stop=True),
           reads=[R["rows"], C.r["onesrow_f"]], writes=[rpb[7]])
    p.emit("dve", lambda e: e.tensor_copy(fsel[:, 1:2], pb[7][0:128, 0:1]),
           reads=[rpb[7]], writes=[R["fsel"]])
    p.emit("dve", lambda e: e.tensor_scalar(fsel[:, 0:1], fsel[:, 1:2], -1.0, 1.0,
                                            ALU.mult, ALU.add),
           reads=[R["fsel"]], writes=[R["fsel"]])

    pfx = "l1a" if kind == "l1attn" else "l1f"
    if kind == "l1ffn":
        p.dma("sp", wcol[:, :], d[pfx + "_n1w"][:, :], writes=[R["wcol"]])
        p.dma("sp", bcols[:, 0:1], d[pfx + "_qnw"][:, :], writes=[R["bcols"]])
        p.dma("sp", bcols[:, 1:2], d[pfx + "_knw"][:, :], writes=[R["bcols"]])
    else:
        p.dma("sp", bcols[:, 0:8], d[pfx + "_bqk"][:, :], writes=[R["bcols"]])
        p.dma("sp", bcols[:, 8:16], d[pfx + "_bo"][:, :], writes=[R["bcols"]])

    sn = ExitStack()
    xnb = sn.enter_context(nc.sbuf_tensor("at_xnb", [128, NCH * 2048], BF16, allow_name_mangling=True))
    rows6 = sn.enter_context(nc.sbuf_tensor("at_rows6", [1, 6144], F32, allow_name_mangling=True))
    R["xnb"] = fence_res(p, "xnb")
    R["rows6"] = fence_res(p, "rows6")
    if kind == "l1ffn":
        R["d_xsp"] = Res("d_xsp")
        for c in range(NCH):
            p.dma("sp", d["xsp"][c * 128:(c + 1) * 128, :],
                  xres[:, c * 2048:(c + 1) * 2048], reads=[r_xres], writes=[R["d_xsp"]])
        feat_norm(p, nc, C, xres, r_xres, xnb, R["xnb"], rows6, R["rows6"], scr,
                  R["scr"], scrb, R["scrb"], pb, rpb, "lnip")
    feat_norm(p, nc, C, xres, r_xres, xnb, R["xnb"], rows6, R["rows6"], scr, R["scr"],
              scrb, R["scrb"], pb, rpb,
              "ln" if kind == "l1attn" else "rms", wcol, R["wcol"])

    for c in range(NCH):
        p.dma("sp", d["agi"][c * 128:(c + 1) * 128, :], xnb[:, c * 2048:(c + 1) * 2048],
              reads=[R["xnb"]], writes=[R["d_agi"]])
    p.emit("pool", lambda e: e.collective_compute(
        "AllGather", ALU.bypass, replica_groups=[[2 * i, 2 * i + 1] for i in range(4)],
        ins=[d["agi"].opt()], outs=[d["ago"].opt()]),
        reads=[R["d_agi"]], writes=[R["d_ago"]], inc=1)
    collective_barrier(p, d, scr[0:16, 0:16], R["scr"], R["d_ago"])
    sn.close()
    xnf = sa.enter_context(nc.sbuf_tensor("at_xnf", [128, NCH * 4096], BF16, allow_name_mangling=True))
    R["xnf"] = fence_res(p, "xnf")
    for c in range(NCH):
        for rr in range(2):
            p.dma("sp", xnf[:, c * 4096 + rr * 2048:c * 4096 + (rr + 1) * 2048],
                  d["ago"][rr * 1024 + c * 128:rr * 1024 + (c + 1) * 128, :],
                  reads=[R["d_ago"]], writes=[R["xnf"]])

    if kind == "l1attn":
        cosb = sa.enter_context(nc.sbuf_tensor("at_cos", [128, 4096], BF16, allow_name_mangling=True))
        sinb = sa.enter_context(nc.sbuf_tensor("at_sin", [128, 4096], BF16, allow_name_mangling=True))
        pmb = sa.enter_context(nc.sbuf_tensor("at_pmb", [128, 128], BF16, allow_name_mangling=True))
        sr = ExitStack()
        angf = sr.enter_context(nc.sbuf_tensor("at_ang", [128, 512], F32, allow_name_mangling=True))
        angi = sr.enter_context(nc.sbuf_tensor("at_angi", [128, 512], mybir.dt.int32, allow_name_mangling=True))
        for k in ["cosb", "sinb", "angf", "angi", "pmb"]:
            R[k] = fence_res(p, k)
        p.dma("pool", pmb[:, :], d["ropePM"][:, :], writes=[R["pmb"]])
        p.dma("sp", rows[0:1, 1024:1536], d[pfx + "_bvrow"][0:1, :],
              reads=[R["rows"]], writes=[R["rows"]])
        p.emit("pool", lambda e: e.iota(rows[0:1, 1536:1600], [[1, 64]], base=0,
                                        channel_multiplier=0,
                                        allow_small_or_imprecise_dtypes=True),
               reads=[R["rows"]], writes=[R["rows"]])
        p.emit("dve", lambda e: e.tensor_scalar(rows[0:1, 1600:1664], rows[0:1, 1536:1600],
                                                32.0, None, ALU.is_ge),
               reads=[R["rows"]], writes=[R["rows"]])
        p.emit("dve", lambda e: e.scalar_tensor_tensor(rows[0:1, 1536:1600],
                                                       rows[0:1, 1600:1664], -32.0,
                                                       rows[0:1, 1536:1600], ALU.mult,
                                                       ALU.add),
               reads=[R["rows"]], writes=[R["rows"]])
        p.emit("act", lambda e: e.activation(rows[0:1, 1536:1600], rows[0:1, 1536:1600],
                                             AF.Exp, bias=0.0,
                                             scale=-2.0 * _m.log(10000.0) / 64.0),
               reads=[R["rows"]], writes=[R["rows"]])

        TWO_PI = 2.0 * _m.pi
        for half in range(2):
            dst, rdst = (sinb, R["sinb"]) if half == 0 else (cosb, R["cosb"])
            for tb in range(8):
                p.emit("pool", lambda e, tb=tb: e.iota(
                    rows[0:1, 0:512], [[1, 512]], base=tb * 512 + 1,
                    channel_multiplier=0, allow_small_or_imprecise_dtypes=True),
                    reads=[R["rows"]], writes=[R["rows"]])
                p.emit("pe", lambda e, tb=tb: e.matmul(
                    pb[6][0:64, 0:512], rows[0:1, 1536:1600], rows[0:1, 0:512],
                    start=True, stop=True),
                    reads=[R["rows"]], writes=[rpb[6]])
                p.emit("dve", lambda e, half=half: e.tensor_scalar(
                    angf[0:64, :], pb[6][0:64, 0:512], 1.0,
                    _m.pi / 2.0 if half else 0.0, ALU.mult, ALU.add),
                    reads=[rpb[6]], writes=[R["angf"]])
                p.emit("dve", lambda e: e.tensor_scalar(
                    scr[0:64, 0:512], pb[6][0:64, 0:512], 1.0 / TWO_PI,
                    0.25 if half else 0.0, ALU.mult, ALU.add),
                    reads=[rpb[6], R["scr"]], writes=[R["scr"]])
                p.emit("dve", lambda e: e.tensor_copy(angi[0:64, :], scr[0:64, 0:512]),
                       reads=[R["scr"]], writes=[R["angi"]])
                p.emit("dve", lambda e: e.tensor_copy(scr[0:64, 0:512], angi[0:64, :]),
                       reads=[R["angi"]], writes=[R["scr"]])
                p.emit("dve", lambda e: e.scalar_tensor_tensor(
                    angf[0:64, :], scr[0:64, 0:512], -TWO_PI, angf[0:64, :],
                    ALU.mult, ALU.add),
                    reads=[R["scr"], R["angf"]], writes=[R["angf"]])
                p.emit("act", lambda e, dst=dst, tb=tb: e.activation(
                    dst[0:64, tb * 512:(tb + 1) * 512], angf[0:64, :], AF.Sin),
                    reads=[R["angf"]], writes=[rdst])
            p.emit("dve", lambda e, dst=dst: e.tensor_copy(dst[64:128, :], dst[0:64, :]),
                   reads=[rdst], writes=[rdst])
        sr.close()
    else:
        hsel2 = sa.enter_context(nc.sbuf_tensor("at_hsel2", [128, 2], BF16, allow_name_mangling=True))
        sel64 = sa.enter_context(nc.sbuf_tensor("at_sel64", [2, 128], F32, allow_name_mangling=True))
        sqrow = sa.enter_context(nc.sbuf_tensor("at_sqrow", [2, 512], F32, allow_name_mangling=True))
        for k in ["hsel2", "sel64", "sqrow"]:
            R[k] = fence_res(p, k)
        p.emit("dve", lambda e: e.memset(hsel2[:, :], 1.0), writes=[R["hsel2"]])
        p.emit("pool", lambda e: e.affine_select(hsel2[:, :], hsel2[:, :],
                                                 [[-64, 2]], ALU.is_ge, 0.0,
                                                 base=0, channel_multiplier=1),
               reads=[R["hsel2"]], writes=[R["hsel2"]])
        p.emit("pool", lambda e: e.affine_select(hsel2[:, :], hsel2[:, :],
                                                 [[64, 2]], ALU.is_ge, 0.0,
                                                 base=63, channel_multiplier=-1),
               reads=[R["hsel2"]], writes=[R["hsel2"]])
        p.emit("dve", lambda e: e.memset(sel64[:, :], 1.0), writes=[R["sel64"]])
        p.emit("pool", lambda e: e.affine_select(sel64[:, :], sel64[:, :],
                                                 [[1, 128]], ALU.is_ge, 0.0,
                                                 base=0, channel_multiplier=-64),
               reads=[R["sel64"]], writes=[R["sel64"]])
        p.emit("pool", lambda e: e.affine_select(sel64[:, :], sel64[:, :],
                                                 [[-1, 128]], ALU.is_ge, 0.0,
                                                 base=63, channel_multiplier=64),
               reads=[R["sel64"]], writes=[R["sel64"]])

    SCALE = 1.0 / 8.0
    for w in range(4):
        for i, nm in enumerate(["wq", "wk", "wv"]):
            p.dma("pool", wt3[:, i * 1024:(i + 1) * 1024].rearrange("p (k n) -> p k n", n=128),
                  d[pfx + "_" + nm].rearrange("(k p) n -> p k n", p=128)[:, :, w * 128:(w + 1) * 128],
                  reads=[R["wt3"][i]], writes=[R["wt3"][i]])
        for tb in range(8):
            for i, (dst, rdst) in enumerate([(qw, R["qw"]), (kw, R["kw"])]):
                bk = i % 2

                def mmqk(e, i=i, tb=tb, bk=bk):
                    last = None
                    for kc in range(NCH):
                        last = e.matmul(pb[bk][:, 0:512],
                                        wt3[:, i * 1024 + kc * 128:i * 1024 + (kc + 1) * 128],
                                        xnf[:, kc * 4096 + tb * 512:kc * 4096 + (tb + 1) * 512],
                                        start=(kc == 0), stop=(kc == NCH - 1))
                    return last
                p.emit("pe", mmqk, reads=[R["wt3"][i], R["xnf"]], writes=[rpb[bk]])
                if kind == "l1attn":
                    p.emit("dve", lambda e, dst=dst, tb=tb, bk=bk, i=i, w=w: e.tensor_scalar(
                        dst[:, tb * 512:(tb + 1) * 512], pb[bk][:, 0:512],
                        bcols[:, i * 4 + w:i * 4 + w + 1], None, ALU.add),
                        reads=[rpb[bk], R["bcols"]], writes=[rdst])
                else:
                    p.emit("act", lambda e, dst=dst, tb=tb, bk=bk: e.activation(
                        dst[:, tb * 512:(tb + 1) * 512], pb[bk][:, 0:512], AF.Copy),
                        reads=[rpb[bk]], writes=[rdst])
        p.emit("dve", lambda e: e.memset(vw[:, :], 1.0), reads=[R["vw"]], writes=[R["vw"]])
        if kind == "l1attn":
            bcast_row(p, C, pb[5], rpb[5], rows[0:1, 1024 + w * 128:1024 + (w + 1) * 128],
                      R["rows"], 128)
            p.emit("dve", lambda e: e.tensor_copy(bvbc[:, :], pb[5][:, 0:128]),
                   reads=[rpb[5]], writes=[R["bvbc"]])
        for kt in range(32):
            def mmv(e, kt=kt, w=w):
                last = None
                for kc in range(NCH):
                    last = e.matmul(pb[2][:, 0:128],
                                    xnf[:, kc * 4096 + kt * 128:kc * 4096 + (kt + 1) * 128],
                                    wt3[:, 2 * 1024 + kc * 128:2 * 1024 + (kc + 1) * 128],
                                    start=(kc == 0), stop=(kc == NCH - 1))
                return last
            p.emit("pe", mmv, reads=[R["wt3"][2], R["xnf"]], writes=[rpb[2]])
            if kind == "l1attn":
                p.emit("dve", lambda e, kt=kt: e.tensor_add(scrb[:, 0:128],
                                                            pb[2][:, 0:128], bvbc[:, :]),
                       reads=[rpb[2], R["bvbc"]], writes=[R["scrb"]])
            else:
                p.emit("dve", lambda e, kt=kt: e.tensor_copy(scrb[:, 0:128],
                                                             pb[2][:, 0:128]),
                       reads=[rpb[2]], writes=[R["scrb"]])
            for hw in range(2):
                p.emit("dve", lambda e, kt=kt, hw=hw: e.tensor_copy(
                    vw[:, kt * 130 + hw * 65:kt * 130 + hw * 65 + 64],
                    scrb[:, hw * 64:hw * 64 + 64]),
                    reads=[R["scrb"]], writes=[R["vw"]])
        if kind == "l1attn":
            for (tw, rtw) in [(qw, R["qw"]), (kw, R["kw"])]:
                for tb in range(8):
                    p.emit("pe", lambda e, tw=tw, tb=tb: e.matmul(
                        pb[3][:, 0:512], pmb[:, :], tw[:, tb * 512:(tb + 1) * 512],
                        start=True, stop=True),
                        reads=[R["pmb"], rtw], writes=[rpb[3]])
                    p.emit("dve", lambda e, tw=tw, tb=tb: e.tensor_mul(
                        scr[:, 0:512], tw[:, tb * 512:(tb + 1) * 512],
                        cosb[:, tb * 512:(tb + 1) * 512]),
                        reads=[rtw, R["cosb"]], writes=[R["scr"]])
                    p.emit("dve", lambda e, tb=tb: e.tensor_mul(
                        scrb[:, 0:512], pb[3][:, 0:512], sinb[:, tb * 512:(tb + 1) * 512]),
                        reads=[rpb[3], R["sinb"]], writes=[R["scrb"]])
                    p.emit("dve", lambda e, tw=tw, tb=tb: e.tensor_add(
                        tw[:, tb * 512:(tb + 1) * 512], scr[:, 0:512], scrb[:, 0:512]),
                        reads=[R["scr"], R["scrb"]], writes=[rtw])
        else:
            for (tw, rtw, wc) in [(qw, R["qw"], 0), (kw, R["kw"], 1)]:
                for tb in range(8):
                    p.emit("act", lambda e, tw=tw, tb=tb: e.activation(
                        scrb[:, 0:512], tw[:, tb * 512:(tb + 1) * 512], AF.Square),
                        reads=[rtw], writes=[R["scrb"]])
                    p.emit("pe", lambda e, tb=tb: e.matmul(
                        pb[3][0:2, 0:512], hsel2[:, :], scrb[:, 0:512],
                        start=True, stop=True),
                        reads=[R["scrb"], R["hsel2"]], writes=[rpb[3]])
                    p.emit("dve", lambda e: e.tensor_scalar(
                        sqrow[:, :], pb[3][0:2, 0:512], 1.0 / 64.0, None, ALU.mult),
                        reads=[rpb[3]], writes=[R["sqrow"]])
                    p.emit("act", lambda e: e.activation(sqrow[:, :], sqrow[:, :], AF.Ln,
                                                         bias=C.eps2[0:2, 0:1]),
                           reads=[R["sqrow"], C.r["eps2"]], writes=[R["sqrow"]])
                    p.emit("act", lambda e: e.activation(sqrow[:, :], sqrow[:, :], AF.Exp,
                                                         bias=0.0, scale=-0.5),
                           reads=[R["sqrow"]], writes=[R["sqrow"]])
                    p.emit("pe", lambda e, tb=tb: e.matmul(
                        pb[3][:, 0:512], sel64[:, :], sqrow[:, :],
                        start=True, stop=True),
                        reads=[R["sqrow"], R["sel64"]], writes=[rpb[3]])
                    p.emit("dve", lambda e, tw=tw, tb=tb: e.tensor_mul(
                        scr[:, 0:512], tw[:, tb * 512:(tb + 1) * 512], pb[3][:, 0:512]),
                        reads=[rtw, rpb[3]], writes=[R["scr"]])
                    p.emit("dve", lambda e, tw=tw, tb=tb, wc=wc: e.tensor_scalar(
                        tw[:, tb * 512:(tb + 1) * 512], scr[:, 0:512],
                        bcols[:, wc:wc + 1], None, ALU.mult),
                        reads=[R["scr"], R["bcols"]], writes=[rtw])
        if "dbg_qw" in d and w == 0:
            p.dma("pool", d["dbg_qw"][:, :], qw[:, :], reads=[R["qw"]], writes=[Res("dbq")])
            p.dma("pool", d["dbg_kw"][:, :], kw[:, :], reads=[R["kw"]], writes=[Res("dbk")])
            p.dma("pool", d["dbg_vw"][:, :], vw[:, :], reads=[R["vw"]], writes=[Res("dbv")])
        for hw in range(2):
            hb = hw * 64
            for qt in range(32):
                qe = (qt + 1) * 128
                nkb = (qe + 511) // 512
                for kb in range(nkb):
                    wdt = min(512, qe - kb * 512)
                    bk = kb % 2
                    p.emit("pe", lambda e, qt=qt, kb=kb, wdt=wdt, hb=hb, bk=bk: e.matmul(
                        pb[bk][:, 0:wdt], qw[hb:hb + 64, qt * 128:(qt + 1) * 128],
                        kw[hb:hb + 64, kb * 512:kb * 512 + wdt],
                        start=True, stop=True),
                        reads=[R["qw"], R["kw"]], writes=[rpb[bk]])
                    if kb == 0:
                        p.emit("dve", lambda e, qt=qt, wdt=wdt, bk=bk: e.reduce_max(
                            nmx[:, qt:qt + 1], pb[bk][:, 0:wdt], axis=AX.X, negate=True),
                            reads=[rpb[bk]], writes=[R["nmx"]])
                    else:
                        p.emit("dve", lambda e, wdt=wdt, bk=bk: e.reduce_max(
                            scr[:, 0:1], pb[bk][:, 0:wdt], axis=AX.X, negate=True),
                            reads=[rpb[bk]], writes=[R["scr"]])
                        p.emit("dve", lambda e, qt=qt: e.tensor_tensor(
                            nmx[:, qt:qt + 1], nmx[:, qt:qt + 1], scr[:, 0:1], ALU.min),
                            reads=[R["scr"], R["nmx"]], writes=[R["nmx"]])
            p.emit("pe", lambda e: e.transpose(pb[6][0:32, 0:128], nmx[:, :],
                                               C.ident[:, :]),
                   reads=[R["nmx"], C.r["ident"]], writes=[rpb[6]])
            p.emit("dve", lambda e: e.tensor_copy(nmx32[:, :], pb[6][0:32, 0:128]),
                   reads=[rpb[6]], writes=[R["nmx32"]])
            p.dma("sp", d["mx"].rearrange("o (a b) -> (o a) b", b=128), nmx32[:, :],
                  reads=[R["nmx32"]], writes=[R["d_mx"]])
            p.dma("pool", negrow[0:1, :], d["mx"][0:1, :],
                  reads=[R["d_mx"], R["negrow"]], writes=[R["negrow"]])
            if "dbg_nmx" in d and w == 0 and hw == 0:
                p.dma("pool", d["dbg_nmx"][:, :], negrow[:, :],
                      reads=[R["negrow"]], writes=[Res("dbn")])
                p.dma("pool", d["dbg_nmxraw"][:, :], nmx[:, :],
                      reads=[R["nmx"]], writes=[Res("dbn2")])
            for qb in range(8):
                nkc = (qb + 1) * 4
                for kc in range(nkc):
                    ab = abf[:, (kc % 2) * 512:(kc % 2) * 512 + 512]
                    bk = kc % 2

                    def mmst(e, qb=qb, kc=kc, hb=hb, bk=bk):
                        e.matmul(pb[bk][:, 0:512], kw[hb:hb + 64, kc * 128:(kc + 1) * 128],
                                 qw[hb:hb + 64, qb * 512:(qb + 1) * 512],
                                 start=True, stop=False)
                        return e.matmul(pb[bk][:, 0:512], C.onesrow_b[0:1, :],
                                        negrow[0:1, qb * 512:(qb + 1) * 512],
                                        start=False, stop=True)
                    p.emit("pe", mmst, reads=[R["qw"], R["kw"], R["negrow"],
                                              C.r["onesrow_b"]], writes=[rpb[bk]])
                    p.emit("act", lambda e, ab=ab, bk=bk: e.activation(
                        ab, pb[bk][:, 0:512], AF.Exp, bias=0.0, scale=SCALE),
                        reads=[rpb[bk]], writes=[R["abf"]])
                    dlt = kc - qb * 4
                    if dlt >= 0:
                        o0 = 384 - dlt * 128
                        p.emit("dve", lambda e, ab=ab, o0=o0: e.tensor_mul(
                            ab, ab, m512[:, o0:o0 + 512]),
                            reads=[R["abf"], R["m512"]], writes=[R["abf"]])
                    p.emit("pe", lambda e, ab=ab, kc=kc, hw=hw, nkc=nkc: e.matmul(
                        pb[4][0:65, 0:512], vw[:, kc * 130 + hw * 65:kc * 130 + (hw + 1) * 65],
                        ab, start=(kc == 0), stop=(kc == nkc - 1)),
                        reads=[R["vw"], R["abf"]], writes=[rpb[4]])
                p.emit("dve", lambda e: e.tensor_copy(rows[0:1, 0:512], pb[4][64:65, 0:512]),
                       reads=[rpb[4]], writes=[R["rows"]])
                p.emit("dve", lambda e: e.reciprocal(rows[0:1, 512:1024], rows[0:1, 0:512]),
                       reads=[R["rows"]], writes=[R["rows"]])
                p.emit("pe", lambda e: e.matmul(pb[5][0:64, 0:512], C.onesrow_f[0:1, 0:64],
                                                rows[0:1, 512:1024], start=True, stop=True),
                       reads=[R["rows"], C.r["onesrow_f"]], writes=[rpb[5]])
                p.emit("dve", lambda e: e.tensor_copy(rbc[:, :], pb[5][0:64, 0:512]),
                       reads=[rpb[5]], writes=[R["rbc"]])
                p.emit("dve", lambda e: e.tensor_mul(obf[:, :], pb[4][0:64, 0:512],
                                                     rbc[:, :]),
                       reads=[rpb[4], R["rbc"]], writes=[R["obf"]])
                p.dma("sp", d["oag"][w * 128 + hw * 64:w * 128 + hw * 64 + 64,
                                     qb * 512:(qb + 1) * 512], obf[:, :],
                      reads=[R["obf"]], writes=[R["d_oag"]])

    if "dbg_oag" in d:
        p.dma("pool", d["dbg_oag"][:, :], d["oag"][:, :],
              reads=[R["d_oag"]], writes=[Res("dbo")])
    p.emit("pool", lambda e: e.collective_compute(
        "AllGather", ALU.bypass, replica_groups=[[2 * i, 2 * i + 1] for i in range(4)],
        ins=[d["oag"].opt()], outs=[d["oago"].opt()]),
        reads=[R["d_oag"]], writes=[R["d_oago"]], inc=1)
    collective_barrier(p, d, scr[0:16, 0:16], R["scr"], R["d_oago"])
    osel, r_osel = xnf, R["xnf"]
    for c in range(NCH):
        p.dma("sp", qw[:, 0:2048], d["oago"][c * 128:(c + 1) * 128, 0:2048],
              reads=[R["d_oago"], R["qw"]], writes=[R["qw"]])
        p.dma("sp", kw[:, 0:2048], d["oago"][c * 128:(c + 1) * 128, 2048:4096],
              reads=[R["d_oago"], R["kw"]], writes=[R["kw"]])
        p.emit("dve", lambda e, c=c: e.tensor_scalar(
            osel[:, c * 2048:(c + 1) * 2048], qw[:, 0:2048], fsel[:, 0:1], None,
            ALU.mult),
            reads=[R["qw"], R["fsel"], r_osel], writes=[r_osel])
        p.emit("dve", lambda e, c=c: e.scalar_tensor_tensor(
            osel[:, c * 2048:(c + 1) * 2048], kw[:, 0:2048], fsel[:, 1:2],
            osel[:, c * 2048:(c + 1) * 2048], ALU.mult, ALU.add),
            reads=[R["kw"], R["fsel"], r_osel], writes=[r_osel])
    do_start = 0 if kind == "l1attn" else 2
    for do in range(do_start, NCH):
        i3 = do % 3
        p.dma("pool", wt3[:, i3 * 1024:(i3 + 1) * 1024].rearrange("p (k n) -> p k n", n=128),
              d[pfx + "_wo"].rearrange("(k p) n -> p k n", p=128)[:, :, do * 128:(do + 1) * 128],
              reads=[R["wt3"][i3]], writes=[R["wt3"][i3]])
        for tb in range(4):
            bk = tb % 2

            def mmo(e, do=do, tb=tb, bk=bk, i3=i3):
                last = None
                for kc in range(NCH):
                    last = e.matmul(pb[bk][:, 0:512],
                                    wt3[:, i3 * 1024 + kc * 128:i3 * 1024 + (kc + 1) * 128],
                                    osel[:, kc * 2048 + tb * 512:kc * 2048 + (tb + 1) * 512],
                                    start=(kc == 0), stop=(kc == NCH - 1))
                return last
            p.emit("pe", mmo, reads=[R["wt3"][i3], r_osel], writes=[rpb[bk]])
            if kind == "l1attn":
                p.emit("dve", lambda e, do=do, bk=bk: e.tensor_scalar(
                    scr[:, 0:512], pb[bk][:, 0:512],
                    bcols[:, 8 + do:8 + do + 1], None, ALU.add),
                    reads=[rpb[bk], R["bcols"]], writes=[R["scr"]])
                p.emit("dve", lambda e, do=do, tb=tb: e.tensor_add(
                    xres[:, do * 2048 + tb * 512:do * 2048 + (tb + 1) * 512],
                    xres[:, do * 2048 + tb * 512:do * 2048 + (tb + 1) * 512],
                    scr[:, 0:512]),
                    reads=[R["scr"], r_xres], writes=[r_xres])
            else:
                p.emit("dve", lambda e, do=do, tb=tb, bk=bk: e.tensor_add(
                    xres[:, do * 2048 + tb * 512:do * 2048 + (tb + 1) * 512],
                    xres[:, do * 2048 + tb * 512:do * 2048 + (tb + 1) * 512],
                    pb[bk][:, 0:512]),
                    reads=[rpb[bk], r_xres], writes=[r_xres])
    sa.close()



def build_l0_ffn(p, nc, ctx, C, xres, r_xres, d):
    sf = ExitStack()
    rows = sf.enter_context(nc.sbuf_tensor("f0_rows", [1, 8224], F32, allow_name_mangling=True))
    rsbc = sf.enter_context(nc.sbuf_tensor("f0_rsbc", [128, 512], F32, allow_name_mangling=True))
    mrbc = sf.enter_context(nc.sbuf_tensor("f0_mrbc", [128, 512], F32, allow_name_mangling=True))
    scr = sf.enter_context(nc.sbuf_tensor("f0_scr", [128, 512], F32, allow_name_mangling=True))
    scr2 = sf.enter_context(nc.sbuf_tensor("f0_scr2", [128, 512], F32, allow_name_mangling=True))
    scrb = sf.enter_context(nc.sbuf_tensor("f0_scrb", [128, 512], BF16, allow_name_mangling=True))
    xnb = sf.enter_context(nc.sbuf_tensor("f0_xnb", [128, NCH * 2048], BF16, allow_name_mangling=True))
    wTb = sf.enter_context(nc.sbuf_tensor("f0_wTb", [16, 2048], BF16))
    onehot = sf.enter_context(nc.sbuf_tensor("f0_onehot", [16, 16 * 128], BF16, allow_name_mangling=True))
    bcolt = sf.enter_context(nc.sbuf_tensor("f0_bcolt", [128, 64], F32, allow_name_mangling=True))
    gbias = sf.enter_context(nc.sbuf_tensor("f0_gbias_s", [128, 16], F32, allow_name_mangling=True))
    ceT = sf.enter_context(nc.sbuf_tensor("f0_ceT_s", [128, NCH * 16], BF16))
    rcebc = sf.enter_context(nc.sbuf_tensor("f0_rcebc", [128, 16], F32, allow_name_mangling=True))
    wtile = sf.enter_context(nc.sbuf_tensor("f0_wtile", [128, 2 * 1024], BF16, allow_name_mangling=True))
    pb = [psumt(nc, sf, f"f0_pb{i}", [128, 512]) for i in range(8)]
    rpb = [fence_res(p, f"f0_pb{i}") for i in range(8)]
    N = ["rows", "scr", "scr2", "scrb", "xnb", "wtok", "ltok", "onehot", "bcolt",
         "gbias", "ceT", "rcebc", "rsbc", "mrbc"]
    R = {k: fence_res(p, k) for k in N}
    R["wtile"] = [fence_res(p, "f0_wt0"), fence_res(p, "f0_wt1")]

    p.emit("dve", lambda e: e.memset(onehot[:, :], 1.0), writes=[R["onehot"]])
    p.emit("pool", lambda e: e.affine_select(onehot[:, :], onehot[:, :],
                                             [[1, 16], [0, 128]], ALU.is_equal, 0.0,
                                             base=0, channel_multiplier=-1),
           reads=[R["onehot"]], writes=[R["onehot"]])
    p.dma("sp", bcolt[:, :], d["f0_bcols"][:, :], writes=[R["bcolt"]])
    p.dma("sp", gbias[:, :], d["f0_gbias"][:, :], writes=[R["gbias"]])
    p.dma("pool", ceT.rearrange("p (k n) -> p k n", n=16),
          d["f0_ceT"].rearrange("(k p) n -> p k n", p=128), writes=[R["ceT"]])

    feat_norm(p, nc, C, xres, r_xres, xnb, R["xnb"], rows, R["rows"], scr, R["scr"],
              scrb, R["scrb"], pb, rpb, "ln")

    for s0 in range(0, 2048, 512):
        for c in range(NCH):
            p.emit("act", lambda e, c=c, s0=s0: e.activation(
                scrb[:, 0:512], xnb[:, c * 2048 + s0:c * 2048 + s0 + 512], AF.Square),
                reads=[R["xnb"]], writes=[R["scrb"]])
            p.emit("pe", lambda e, c=c: e.matmul(pb[4][0:1, 0:512], C.onescol_b[:, :],
                                                 scrb[:, 0:512], start=(c == 0),
                                                 stop=(c == NCH - 1)),
                   reads=[R["scrb"], C.r["onescol_b"]], writes=[rpb[4]])
        p.emit("dve", lambda e, s0=s0: e.tensor_copy(rows[0:1, 6144 + s0:6144 + s0 + 512],
                                                     pb[4][0:1, 0:512]),
               reads=[rpb[4]], writes=[R["rows"]])
    rx = rows[0:1, 6144:8192]
    p.emit("act", lambda e: e.activation(rx, rx, AF.Ln), reads=[R["rows"]], writes=[R["rows"]])
    p.emit("act", lambda e: e.activation(rx, rx, AF.Exp, bias=0.0, scale=-0.5),
           reads=[R["rows"]], writes=[R["rows"]])
    for c in range(NCH):
        p.emit("act", lambda e, c=c: e.activation(scrb[:, 0:16],
                                                  ceT[:, c * 16:(c + 1) * 16], AF.Square),
               reads=[R["ceT"]], writes=[R["scrb"]])
        p.emit("pe", lambda e, c=c: e.matmul(pb[4][0:1, 0:16], C.onescol_b[:, :],
                                             scrb[:, 0:16], start=(c == 0),
                                             stop=(c == NCH - 1)),
               reads=[R["scrb"], C.r["onescol_b"]], writes=[rpb[4]])
    rc = rows[0:1, 8192:8208]
    p.emit("dve", lambda e: e.tensor_copy(rc, pb[4][0:1, 0:16]),
           reads=[rpb[4]], writes=[R["rows"]])
    p.emit("act", lambda e: e.activation(rc, rc, AF.Ln), reads=[R["rows"]], writes=[R["rows"]])
    p.emit("act", lambda e: e.activation(rc, rc, AF.Exp, bias=0.0, scale=-0.5),
           reads=[R["rows"]], writes=[R["rows"]])
    bcast_row(p, C, pb[5], rpb[5], rc, R["rows"], 16)
    p.emit("dve", lambda e: e.tensor_copy(rcebc[:, :], pb[5][:, 0:16]),
           reads=[rpb[5]], writes=[R["rcebc"]])

    xn32 = sf.enter_context(nc.sbuf_tensor("f0_xn32", [128, 8 * 512], F32, allow_name_mangling=True))
    sw = ExitStack()
    wtok = sw.enter_context(nc.sbuf_tensor("f0_wtok", [128, 16 * 16], F32, allow_name_mangling=True))
    ltok = sw.enter_context(nc.sbuf_tensor("f0_ltok", [128, 16 * 16], F32, allow_name_mangling=True))
    R["xn32"] = fence_res(p, "xn32")
    sg = ExitStack()
    hT = sg.enter_context(nc.sbuf_tensor("f0_hT", [128, 4 * 2048], F32))
    wt32 = sg.enter_context(nc.sbuf_tensor("f0_wt32", [128, 1024], F32, allow_name_mangling=True))
    R["hT"] = fence_res(p, "hT")
    R["wt32"] = fence_res(p, "wt32")
    for tb in range(4):
        bcast_row(p, C, pb[4], rpb[4], rows[0:1, 2048 + tb * 512:2048 + (tb + 1) * 512],
                  R["rows"], 512)
        p.emit("dve", lambda e: e.tensor_copy(rsbc[:, :], pb[4][:, 0:512]),
               reads=[rpb[4]], writes=[R["rsbc"]])
        bcast_row(p, C, pb[4], rpb[4], rows[0:1, 4096 + tb * 512:4096 + (tb + 1) * 512],
                  R["rows"], 512)
        p.emit("dve", lambda e: e.tensor_copy(mrbc[:, :], pb[4][:, 0:512]),
               reads=[rpb[4]], writes=[R["mrbc"]])
        for kc in range(NCH):
            p.emit("dve", lambda e, kc=kc, tb=tb: e.tensor_mul(
                xn32[:, kc * 512:(kc + 1) * 512],
                xres[:, kc * 2048 + tb * 512:kc * 2048 + (tb + 1) * 512], rsbc[:, :]),
                reads=[r_xres, R["rsbc"], R["xn32"]], writes=[R["xn32"]])
            p.emit("dve", lambda e, kc=kc: e.tensor_sub(
                xn32[:, kc * 512:(kc + 1) * 512], xn32[:, kc * 512:(kc + 1) * 512],
                mrbc[:, :]),
                reads=[R["mrbc"], R["xn32"]], writes=[R["xn32"]])
        for ho in range(4):
            p.dma("sp", wt32.rearrange("p (k n) -> p k n", n=128),
                  d["f0_g1"].rearrange("(k p) n -> p k n", p=128)[:, :, ho * 128:(ho + 1) * 128],
                  reads=[R["wt32"]], writes=[R["wt32"]])
            bk = ho % 2

            def mmh(e, ho=ho, tb=tb, bk=bk):
                last = None
                for kc in range(NCH):
                    last = e.matmul(pb[bk][:, 0:512], wt32[:, kc * 128:(kc + 1) * 128],
                                    xn32[:, kc * 512:(kc + 1) * 512],
                                    start=(kc == 0), stop=(kc == NCH - 1))
                return last
            p.emit("pe", mmh, reads=[R["wt32"], R["xn32"]], writes=[rpb[bk]])
            p.emit("act", lambda e, ho=ho, tb=tb, bk=bk: e.activation(
                hT[:, ho * 2048 + tb * 512:ho * 2048 + (tb + 1) * 512], pb[bk][:, 0:512],
                AF.Gelu, bias=gbias[:, ho:ho + 1]),
                reads=[rpb[bk], R["gbias"]], writes=[R["hT"]])
    g2t = sg.enter_context(nc.sbuf_tensor("f0_g2t", [128, 64], F32, allow_name_mangling=True))
    R["g2t"] = fence_res(p, "g2t")
    p.dma("sp", g2t.rearrange("p (k n) -> p k n", n=16),
          d["f0_g2"].rearrange("(k p) n -> p k n", p=128), writes=[R["g2t"]])
    p.dma("sp", rows[0:1, 8208:8224], d["f0_b2row"][0:1, :], reads=[R["rows"]],
          writes=[R["rows"]])
    bcast_row(p, C, pb[5], rpb[5], rows[0:1, 8208:8224], R["rows"], 16)
    b2bc = sg.enter_context(nc.sbuf_tensor("f0_b2bc", [128, 16], F32, allow_name_mangling=True))
    R["b2bc"] = fence_res(p, "b2bc")
    p.emit("dve", lambda e: e.tensor_copy(b2bc[:, :], pb[5][:, 0:16]),
           reads=[rpb[5]], writes=[R["b2bc"]])

    for tt in range(16):
        def mmsim(e, tt=tt):
            last = None
            for kc in range(NCH):
                last = e.matmul(pb[6][:, 0:16],
                                xnb[:, kc * 2048 + tt * 128:kc * 2048 + (tt + 1) * 128],
                                ceT[:, kc * 16:(kc + 1) * 16],
                                start=(kc == 0), stop=(kc == NCH - 1))
            return last
        p.emit("pe", mmsim, reads=[R["xnb"], R["ceT"]], writes=[rpb[6]])
        def mmgv(e, tt=tt):
            last = None
            for kc in range(4):
                last = e.matmul(pb[7][:, 0:16],
                                hT[:, kc * 2048 + tt * 128:kc * 2048 + (tt + 1) * 128],
                                g2t[:, kc * 16:(kc + 1) * 16],
                                start=(kc == 0), stop=(kc == 3))
            return last
        p.emit("pe", mmgv, reads=[R["hT"], R["g2t"]], writes=[rpb[7]])
        p.emit("pe", lambda e, tt=tt: e.transpose(pb[5][0:128, 0:1],
                                                  rx[0:1, tt * 128:(tt + 1) * 128],
                                                  C.ident[0:1, 0:1]),
               reads=[R["rows"], C.r["ident"]], writes=[rpb[5]])
        p.emit("dve", lambda e: e.tensor_copy(scr2[:, 0:1], pb[5][0:128, 0:1]),
               reads=[rpb[5]], writes=[R["scr2"]])
        p.emit("dve", lambda e: e.tensor_scalar(scr[:, 0:16], pb[6][:, 0:16],
                                                scr2[:, 0:1], None, ALU.mult),
               reads=[rpb[6], R["scr2"]], writes=[R["scr"]])
        p.emit("dve", lambda e: e.tensor_mul(scr[:, 0:16], scr[:, 0:16], rcebc[:, :]),
               reads=[R["rcebc"], R["scr"]], writes=[R["scr"]])
        p.emit("dve", lambda e: e.tensor_add(scr2[:, 0:16], pb[7][:, 0:16], b2bc[:, :]),
               reads=[rpb[7], R["b2bc"]], writes=[R["scr2"]])
        p.emit("act", lambda e: e.activation(scr2[:, 0:16], scr2[:, 0:16], AF.Sigmoid),
               reads=[R["scr2"]], writes=[R["scr2"]])
        p.emit("dve", lambda e: e.tensor_add(scr[:, 0:16], scr[:, 0:16], scr2[:, 0:16]),
               reads=[R["scr"], R["scr2"]], writes=[R["scr"]])
        p.emit("dve", lambda e, tt=tt: e.tensor_copy(ltok[:, tt * 16:(tt + 1) * 16],
                                                     scr[:, 0:16]),
               reads=[R["scr"]], writes=[R["ltok"]])
        p.emit("dve", lambda e: e.reduce_max(scr2[:, 0:1], scr[:, 0:16], axis=AX.X,
                                             negate=True),
               reads=[R["scr"]], writes=[R["scr2"]])
        p.emit("act", lambda e: e.activation(scr2[:, 1:17], scr[:, 0:16], AF.Exp,
                                             bias=scr2[:, 0:1],
                                             accum_out=scr2[:, 17:18]),
               reads=[R["scr"], R["scr2"]], writes=[R["scr2"]])
        p.emit("dve", lambda e: e.reciprocal(scr2[:, 18:19], scr2[:, 17:18]),
               reads=[R["scr2"]], writes=[R["scr2"]])
        p.emit("dve", lambda e: e.tensor_scalar(scr2[:, 19:20], scr2[:, 0:1], -1.0,
                                                None, ALU.mult),
               reads=[R["scr2"]], writes=[R["scr2"]])
        p.emit("dve", lambda e: e.tensor_scalar(scr[:, 16:32], scr[:, 0:16],
                                                scr2[:, 19:20], None, ALU.is_equal),
               reads=[R["scr"], R["scr2"]], writes=[R["scr"]])
        p.emit("dve", lambda e: e.tensor_scalar(scr[:, 32:48], scr2[:, 1:17],
                                                scr2[:, 18:19], None, ALU.mult),
               reads=[R["scr2"], R["scr"]], writes=[R["scr"]])
        p.emit("dve", lambda e, tt=tt: e.tensor_mul(wtok[:, tt * 16:(tt + 1) * 16],
                                                    scr[:, 16:32], scr[:, 32:48]),
               reads=[R["scr"]], writes=[R["wtok"]])
    sg.close()
    if "dbg_wtok" in d:
        p.dma("sp", d["dbg_wtok"][:, :], wtok[:, :], reads=[R["wtok"]], writes=[Res("dw")])
        p.dma("sp", d["dbg_ltok"][:, :], ltok[:, :], reads=[R["ltok"]], writes=[Res("dl")])
    dump_y = "dbg_y" in d
    R["wTb"] = fence_res(p, "wTb")
    for tt in range(16):
        p.emit("pe", lambda e, tt=tt: e.transpose(pb[5][0:16, 0:128],
                                                  wtok[:, tt * 16:(tt + 1) * 16],
                                                  C.ident[:, :]),
               reads=[R["wtok"], C.r["ident"]], writes=[rpb[5]])
        p.emit("dve", lambda e, tt=tt: e.tensor_copy(wTb[:, tt * 128:(tt + 1) * 128],
                                                     pb[5][0:16, 0:128]),
               reads=[rpb[5]], writes=[R["wTb"]])
    sw.close()

    sc = ExitStack()
    xnlo = sc.enter_context(nc.sbuf_tensor("f0_xnlo", [128, 8 * 512], BF16, allow_name_mangling=True))
    R["xnlo"] = fence_res(p, "xnlo")
    colsw = sc.enter_context(nc.sbuf_tensor("f0_colsw", [128, 4 * 512], F32, allow_name_mangling=True))
    yacc = sc.enter_context(nc.sbuf_tensor("f0_yacc", [128, 8 * 512], F32, allow_name_mangling=True))
    chl = sc.enter_context(nc.sbuf_tensor("f0_chl", [128, 2 * 512], BF16, allow_name_mangling=True))
    R["colsw"] = fence_res(p, "colsw")
    R["yacc"] = fence_res(p, "yacc")
    R["chl"] = fence_res(p, "chl")
    R["d_f0y"] = Res("d_f0y")
    for tb in range(4):
        bcast_row(p, C, pb[4], rpb[4], rows[0:1, 2048 + tb * 512:2048 + (tb + 1) * 512],
                  R["rows"], 512)
        p.emit("dve", lambda e: e.tensor_copy(rsbc[:, :], pb[4][:, 0:512]),
               reads=[rpb[4]], writes=[R["rsbc"]])
        bcast_row(p, C, pb[4], rpb[4], rows[0:1, 4096 + tb * 512:4096 + (tb + 1) * 512],
                  R["rows"], 512)
        p.emit("dve", lambda e: e.tensor_copy(mrbc[:, :], pb[4][:, 0:512]),
               reads=[rpb[4]], writes=[R["mrbc"]])
        for kc in range(NCH):
            p.emit("dve", lambda e, kc=kc, tb=tb: e.tensor_mul(
                xn32[:, kc * 512:(kc + 1) * 512],
                xres[:, kc * 2048 + tb * 512:kc * 2048 + (tb + 1) * 512], rsbc[:, :]),
                reads=[r_xres, R["rsbc"], R["xn32"]], writes=[R["xn32"]])
            p.emit("dve", lambda e, kc=kc: e.tensor_sub(
                xn32[:, kc * 512:(kc + 1) * 512], xn32[:, kc * 512:(kc + 1) * 512],
                mrbc[:, :]),
                reads=[R["mrbc"], R["xn32"]], writes=[R["xn32"]])
            p.emit("dve", lambda e, kc=kc, tb=tb: e.tensor_sub(
                xnlo[:, kc * 512:(kc + 1) * 512], xn32[:, kc * 512:(kc + 1) * 512],
                xnb[:, kc * 2048 + tb * 512:kc * 2048 + (tb + 1) * 512]),
                reads=[R["xn32"], R["xnb"], R["xnlo"]], writes=[R["xnlo"]])
        for half in range(8):
            for ci in range(4):
                cn = half * 4 + ci
                wt = wtile[:, (cn % 2) * 1024:(cn % 2) * 1024 + 1024]
                r_wt = R["wtile"][cn % 2]
                p.dma("pool", wt.rearrange("p (k n) -> p k n", n=128),
                      d["f0_wcols"].rearrange("(k p) n -> p k n", p=128)[:, :, cn * 128:(cn + 1) * 128],
                      reads=[r_wt], writes=[r_wt])
                bk = cn % 2

                def mmc(e, cn=cn, tb=tb, bk=bk, wt=wt):
                    last = None
                    for kc in range(NCH):
                        e.matmul(pb[bk][:, 0:512], wt[:, kc * 128:(kc + 1) * 128],
                                 xnb[:, kc * 2048 + tb * 512:kc * 2048 + (tb + 1) * 512],
                                 start=(kc == 0), stop=False)
                        last = e.matmul(pb[bk][:, 0:512], wt[:, kc * 128:(kc + 1) * 128],
                                        xnlo[:, kc * 512:(kc + 1) * 512],
                                        start=False, stop=(kc == NCH - 1))
                    return last
                p.emit("pe", mmc, reads=[r_wt, R["xnb"], R["xnlo"]], writes=[rpb[bk]])
                p.emit("act", lambda e, cn=cn, bk=bk: e.activation(
                    scr[:, 0:512], pb[bk][:, 0:512], AF.Gelu, bias=bcolt[:, cn:cn + 1]),
                    reads=[rpb[bk], R["bcolt"]], writes=[R["scr"]])
                cix = cn // 2
                p.emit("pe", lambda e, cix=cix, tb=tb: e.matmul(
                    pb[5][:, 0:512], onehot[:, cix * 128:(cix + 1) * 128],
                    wTb[:, tb * 512:(tb + 1) * 512], start=True, stop=True),
                    reads=[R["onehot"], R["wTb"]], writes=[rpb[5]])
                p.emit("dve", lambda e, ci=ci: e.tensor_mul(
                    colsw[:, ci * 512:(ci + 1) * 512], scr[:, 0:512], pb[5][:, 0:512]),
                    reads=[R["scr"], rpb[5]], writes=[R["colsw"]])
            for ci in range(4):
                cn = half * 4 + ci
                wt = wtile[:, (cn % 2) * 1024:(cn % 2) * 1024 + 1024]
                r_wt = R["wtile"][cn % 2]
                p.dma("pool", wt[:, 0:1024],
                      d["f0_proj"].rearrange("(k p) n -> k p n", p=128)[cn],
                      reads=[r_wt], writes=[r_wt])
                p.emit("dve", lambda e, ci=ci: e.tensor_copy(
                    chl[:, 0:512], colsw[:, ci * 512:(ci + 1) * 512]),
                    reads=[R["colsw"], R["chl"]], writes=[R["chl"]])
                p.emit("dve", lambda e, ci=ci: e.tensor_sub(
                    chl[:, 512:1024], colsw[:, ci * 512:(ci + 1) * 512], chl[:, 0:512]),
                    reads=[R["colsw"], R["chl"]], writes=[R["chl"]])

                def mmp(e, cn=cn, half=half, ci=ci, wt=wt):
                    last = None
                    for do in range(NCH):
                        e.matmul(pb[do][:, 0:512], wt[:, do * 128:(do + 1) * 128],
                                 chl[:, 0:512],
                                 start=(ci == 0), stop=False)
                        last = e.matmul(pb[do][:, 0:512], wt[:, do * 128:(do + 1) * 128],
                                        chl[:, 512:1024], start=False,
                                        stop=(ci == 3))
                    return last
                p.emit("pe", mmp, reads=[r_wt, R["chl"]],
                       writes=[rpb[do2] for do2 in range(NCH)])
            for do in range(NCH):
                if half == 0:
                    p.emit("dve", lambda e, do=do: e.tensor_copy(
                        yacc[:, do * 512:(do + 1) * 512], pb[do][:, 0:512]),
                        reads=[rpb[do]], writes=[R["yacc"]])
                else:
                    p.emit("dve", lambda e, do=do: e.tensor_add(
                        yacc[:, do * 512:(do + 1) * 512],
                        yacc[:, do * 512:(do + 1) * 512], pb[do][:, 0:512]),
                        reads=[rpb[do], R["yacc"]], writes=[R["yacc"]])
        for do in range(NCH):
            p.emit("dve", lambda e, do=do: e.tensor_scalar(
                scr[:, 0:512], yacc[:, do * 512:(do + 1) * 512],
                bcolt[:, 32 + do:32 + do + 1], None, ALU.add),
                reads=[R["yacc"], R["bcolt"]], writes=[R["scr"]])
            p.emit("dve", lambda e, do=do, tb=tb: e.tensor_mul(
                scr2[:, 0:512], xres[:, do * 2048 + tb * 512:do * 2048 + (tb + 1) * 512],
                rsbc[:, :]),
                reads=[r_xres, R["rsbc"], R["scr2"]], writes=[R["scr2"]])
            p.emit("dve", lambda e: e.tensor_sub(scr2[:, 0:512], scr2[:, 0:512],
                                                 mrbc[:, :]),
                   reads=[R["scr2"], R["mrbc"]], writes=[R["scr2"]])
            p.emit("dve", lambda e: e.tensor_add(scr[:, 0:512], scr[:, 0:512],
                                                 scr2[:, 0:512]),
                   reads=[R["scr"], R["scr2"]], writes=[R["scr"]])
            p.dma("sp", d["f0y"][do * 128:(do + 1) * 128, tb * 512:(tb + 1) * 512],
                  scr[:, 0:512], reads=[R["scr"]], writes=[R["d_f0y"]])
            p.emit("pe", lambda e, do=do: e.matmul(pb[0][0:1, 0:512], C.onescol_f[:, :],
                                                   scr[:, 0:512], start=(do == 0),
                                                   stop=(do == NCH - 1)),
                   reads=[R["scr"], C.r["onescol_f"]], writes=[rpb[0]])
            p.emit("act", lambda e: e.activation(scr2[:, 0:512], scr[:, 0:512], AF.Square),
                   reads=[R["scr"]], writes=[R["scr2"]])
            p.emit("pe", lambda e, do=do: e.matmul(pb[1][0:1, 0:512], C.onescol_f[:, :],
                                                   scr2[:, 0:512], start=(do == 0),
                                                   stop=(do == NCH - 1)),
                   reads=[R["scr2"], C.r["onescol_f"]], writes=[rpb[1]])
        p.emit("dve", lambda e, tb=tb: e.tensor_scalar(
            rows[0:1, 2048 + tb * 512:2048 + (tb + 1) * 512], pb[0][0:1, 0:512],
            1.0 / DM, None, ALU.mult),
            reads=[rpb[0]], writes=[R["rows"]])
        p.emit("dve", lambda e, tb=tb: e.tensor_scalar(
            rows[0:1, 4096 + tb * 512:4096 + (tb + 1) * 512], pb[1][0:1, 0:512],
            1.0 / DM, None, ALU.mult),
            reads=[rpb[1]], writes=[R["rows"]])
    if "dbg_y" in d:
        p.dma("pool", d["dbg_y"][:, :], d["f0y"][:, :], reads=[R["d_f0y"]],
              writes=[Res("dy0")])
    muy = rows[0:1, 2048:4096]
    rsy = rows[0:1, 4096:6144]
    p.emit("act", lambda e: e.activation(rows[0:1, 0:2048], muy, AF.Square),
           reads=[R["rows"]], writes=[R["rows"]])
    p.emit("dve", lambda e: e.tensor_sub(rsy, rsy, rows[0:1, 0:2048]),
           reads=[R["rows"]], writes=[R["rows"]])
    p.emit("act", lambda e: e.activation(rsy, rsy, AF.Ln, bias=C.eps1[0:1, 0:1]),
           reads=[R["rows"], C.r["eps1"]], writes=[R["rows"]])
    p.emit("act", lambda e: e.activation(rsy, rsy, AF.Exp, bias=0.0, scale=-0.5),
           reads=[R["rows"]], writes=[R["rows"]])
    p.emit("dve", lambda e: e.tensor_mul(rows[0:1, 0:2048], muy, rsy),
           reads=[R["rows"]], writes=[R["rows"]])
    for s0 in range(0, 2048, 512):
        bcast_row(p, C, pb[5], rpb[5], rsy[0:1, s0:s0 + 512], R["rows"], 512)
        bcast_row(p, C, pb[4], rpb[4], rows[0:1, s0:s0 + 512], R["rows"], 512)
        for c in range(NCH):
            p.dma("sp", scr2[:, 0:512], d["f0y"][c * 128:(c + 1) * 128, s0:s0 + 512],
                  reads=[R["d_f0y"], R["scr2"]], writes=[R["scr2"]])
            p.emit("dve", lambda e, c=c, s0=s0: e.tensor_mul(
                scr[:, 0:512], scr2[:, 0:512], pb[5][:, 0:512]),
                reads=[R["scr2"], rpb[5]], writes=[R["scr"]])
            p.emit("dve", lambda e: e.tensor_sub(scr[:, 0:512], scr[:, 0:512],
                                                 pb[4][:, 0:512]),
                   reads=[R["scr"], rpb[4]], writes=[R["scr"]])
            p.emit("dve", lambda e, c=c: e.tensor_scalar(
                scr[:, 0:512], scr[:, 0:512], bcolt[:, 40 + c:40 + c + 1],
                bcolt[:, 48 + c:48 + c + 1], ALU.mult, ALU.add),
                reads=[R["scr"], R["bcolt"]], writes=[R["scr"]])
            p.emit("dve", lambda e, c=c, s0=s0: e.tensor_add(
                xres[:, c * 2048 + s0:c * 2048 + s0 + 512],
                xres[:, c * 2048 + s0:c * 2048 + s0 + 512], scr[:, 0:512]),
                reads=[R["scr"], r_xres], writes=[r_xres])
    sc.close()
    sf.close()


def build_l1_moe(p, nc, ctx, C, xres, r_xres, d):
    sf = ExitStack()
    rows = sf.enter_context(nc.sbuf_tensor("mo_rows", [1, 6144], F32, allow_name_mangling=True))
    scr = sf.enter_context(nc.sbuf_tensor("mo_scr", [128, 512], F32, allow_name_mangling=True))
    scr2 = sf.enter_context(nc.sbuf_tensor("mo_scr2", [128, 512], F32, allow_name_mangling=True))
    scrb = sf.enter_context(nc.sbuf_tensor("mo_scrb", [128, 512], BF16, allow_name_mangling=True))
    xf = sf.enter_context(nc.sbuf_tensor("mo_xf", [128, NCH * 2048], BF16, allow_name_mangling=True))
    n2col = sf.enter_context(nc.sbuf_tensor("mo_n2col", [128, 8], F32, allow_name_mangling=True))
    gwt = sf.enter_context(nc.sbuf_tensor("mo_gwt", [128, 32], BF16, allow_name_mangling=True))
    wts = sf.enter_context(nc.sbuf_tensor("mo_wts", [128, 16 * 4], F32, allow_name_mangling=True))
    wT4 = sf.enter_context(nc.sbuf_tensor("mo_wT4", [4, 2048], F32))
    oh4 = sf.enter_context(nc.sbuf_tensor("mo_oh4", [4, 4 * 128], F32, allow_name_mangling=True))
    wtile = sf.enter_context(nc.sbuf_tensor("mo_wtile", [128, 2 * 1024], BF16, allow_name_mangling=True))
    wbcs = sf.enter_context(nc.sbuf_tensor("mo_wbcs", [128, 512], F32, allow_name_mangling=True))
    pb = [psumt(nc, sf, f"mo_pb{i}", [128, 512]) for i in range(8)]
    rpb = [fence_res(p, f"mo_pb{i}") for i in range(8)]
    N = ["rows", "scr", "scr2", "scrb", "xf", "n2col", "gwt", "wts", "wT4", "oh4",
         "wbcs"]
    R = {k: fence_res(p, k) for k in N}
    R["wtile"] = [fence_res(p, "mo_wt0"), fence_res(p, "mo_wt1")]

    p.dma("sp", n2col[:, :], d["l1f_n2w"][:, :], writes=[R["n2col"]])
    p.dma("pool", gwt.rearrange("p (k n) -> p k n", n=4),
          d["l1f_gwT"].rearrange("(k p) n -> p k n", p=128), writes=[R["gwt"]])
    p.emit("dve", lambda e: e.memset(oh4[:, :], 1.0), writes=[R["oh4"]])
    p.emit("pool", lambda e: e.affine_select(oh4[:, :], oh4[:, :],
                                             [[1, 4], [0, 128]], ALU.is_equal, 0.0,
                                             base=0, channel_multiplier=-1),
           reads=[R["oh4"]], writes=[R["oh4"]])

    feat_norm(p, nc, C, xres, r_xres, xf, R["xf"], rows, R["rows"], scr, R["scr"],
              scrb, R["scrb"], pb, rpb, "rms", n2col, R["n2col"])

    for tt in range(16):
        def mmg(e, tt=tt):
            last = None
            for kc in range(NCH):
                last = e.matmul(pb[6][:, 0:4],
                                xf[:, kc * 2048 + tt * 128:kc * 2048 + (tt + 1) * 128],
                                gwt[:, kc * 4:(kc + 1) * 4],
                                start=(kc == 0), stop=(kc == NCH - 1))
            return last
        p.emit("pe", mmg, reads=[R["xf"], R["gwt"]], writes=[rpb[6]])
        p.emit("dve", lambda e: e.tensor_copy(scr[:, 0:4], pb[6][:, 0:4]),
               reads=[rpb[6]], writes=[R["scr"]])
        p.emit("dve", lambda e: e.reduce_max(scr[:, 4:5], scr[:, 0:4], axis=AX.X,
                                             negate=True),
               reads=[R["scr"]], writes=[R["scr"]])
        p.emit("act", lambda e: e.activation(scr[:, 5:9], scr[:, 0:4], AF.Exp,
                                             bias=scr[:, 4:5], accum_out=scr[:, 9:10]),
               reads=[R["scr"]], writes=[R["scr"]])
        p.emit("dve", lambda e: e.tensor_scalar(scr[:, 10:11], scr[:, 4:5], -1.0,
                                                None, ALU.mult),
               reads=[R["scr"]], writes=[R["scr"]])
        p.emit("dve", lambda e: e.tensor_scalar(scr[:, 11:15], scr[:, 0:4],
                                                scr[:, 10:11], None, ALU.is_equal),
               reads=[R["scr"]], writes=[R["scr"]])
        p.emit("dve", lambda e: e.scalar_tensor_tensor(scr[:, 15:19], scr[:, 11:15],
                                                       -1.0e30, scr[:, 0:4],
                                                       ALU.mult, ALU.add),
               reads=[R["scr"]], writes=[R["scr"]])
        p.emit("dve", lambda e: e.reduce_max(scr[:, 19:20], scr[:, 15:19], axis=AX.X),
               reads=[R["scr"]], writes=[R["scr"]])
        p.emit("dve", lambda e: e.tensor_scalar(scr[:, 20:24], scr[:, 15:19],
                                                scr[:, 19:20], None, ALU.is_equal),
               reads=[R["scr"]], writes=[R["scr"]])
        p.emit("dve", lambda e: e.tensor_add(scr[:, 24:25], scr[:, 19:20], scr[:, 4:5]),
               reads=[R["scr"]], writes=[R["scr"]])
        p.emit("act", lambda e: e.activation(scr[:, 24:25], scr[:, 24:25], AF.Exp),
               reads=[R["scr"]], writes=[R["scr"]])
        p.emit("dve", lambda e: e.tensor_scalar(scr[:, 25:26], scr[:, 9:10], 1.0e-8,
                                                1.0, ALU.mult, ALU.add),
               reads=[R["scr"]], writes=[R["scr"]])
        p.emit("dve", lambda e: e.tensor_add(scr[:, 25:26], scr[:, 25:26], scr[:, 24:25]),
               reads=[R["scr"]], writes=[R["scr"]])
        p.emit("dve", lambda e: e.reciprocal(scr[:, 26:27], scr[:, 25:26]),
               reads=[R["scr"]], writes=[R["scr"]])
        p.emit("dve", lambda e: e.tensor_mul(scr[:, 27:28], scr[:, 24:25], scr[:, 26:27]),
               reads=[R["scr"]], writes=[R["scr"]])
        p.emit("dve", lambda e: e.tensor_scalar(scr[:, 28:32], scr[:, 11:15],
                                                scr[:, 26:27], None, ALU.mult),
               reads=[R["scr"]], writes=[R["scr"]])
        p.emit("dve", lambda e, tt=tt: e.scalar_tensor_tensor(
            wts[:, tt * 4:(tt + 1) * 4], scr[:, 20:24], scr[:, 27:28], scr[:, 28:32],
            ALU.mult, ALU.add),
            reads=[R["scr"]], writes=[R["wts"]])
    if "dbg_wts" in d:
        p.dma("sp", d["dbg_wts"][:, :], wts[:, :], reads=[R["wts"]], writes=[Res("dwts")])
    for tt in range(16):
        p.emit("pe", lambda e, tt=tt: e.transpose(pb[5][0:4, 0:128],
                                                  wts[:, tt * 4:(tt + 1) * 4],
                                                  C.ident[:, :]),
               reads=[R["wts"], C.r["ident"]], writes=[rpb[5]])
        p.emit("dve", lambda e, tt=tt: e.tensor_copy(wT4[:, tt * 128:(tt + 1) * 128],
                                                     pb[5][0:4, 0:128]),
               reads=[rpb[5]], writes=[R["wT4"]])

    se = ExitStack()
    h1 = se.enter_context(nc.sbuf_tensor("mo_h1", [128, 16 * 1024], BF16, allow_name_mangling=True))
    h2 = se.enter_context(nc.sbuf_tensor("mo_h2", [128, 16 * 1024], BF16, allow_name_mangling=True))
    R["h1"] = fence_res(p, "h1")
    R["h2"] = fence_res(p, "h2")

    def mm_layer(dst, r_dst, src, r_src, src_nch, wkey, nout_ch, tk, act, tb2n=2):
        nh = (src_nch + 7) // 8
        for ho in range(nout_ch):
            wt = wtile[:, (ho % 2) * 1024:(ho % 2) * 1024 + 1024]
            r_wt = R["wtile"][ho % 2]
            for half in range(nh):
                kn = min(8, src_nch - half * 8)
                p.dma("pool", wt.rearrange("p (k n) -> p k n", n=128)[:, 0:kn],
                      d[wkey].rearrange("(k p) n -> p k n", p=128)[:, half * 8:half * 8 + kn,
                                                                  ho * 128:(ho + 1) * 128],
                      reads=[r_wt], writes=[r_wt])
                for tb in range(tb2n):
                    bk = tb % 2

                    def mml(e, ho=ho, tb=tb, bk=bk, wt=wt, half=half, kn=kn):
                        last = None
                        for kc in range(kn):
                            kk = half * 8 + kc
                            if src is xf:
                                sl = xf[:, kk * 2048 + tk * 1024 + tb * 512:
                                        kk * 2048 + tk * 1024 + (tb + 1) * 512]
                            else:
                                sl = src[:, kk * 1024 + tb * 512:kk * 1024 + (tb + 1) * 512]
                            last = e.matmul(pb[bk][:, 0:512], wt[:, kc * 128:(kc + 1) * 128],
                                            sl, start=(half == 0 and kc == 0),
                                            stop=(half == nh - 1 and kc == kn - 1))
                        return last
                    p.emit("pe", mml, reads=[r_wt, r_src], writes=[rpb[bk]])
            fn = {"gelu": AF.Gelu, "silu": AF.Silu}.get(act, AF.Copy)
            for tb in range(tb2n):
                bk = tb % 2
                p.emit("act", lambda e, ho=ho, tb=tb, bk=bk, fn=fn: e.activation(
                    dst[:, ho * 1024 + tb * 512:ho * 1024 + (tb + 1) * 512],
                    pb[bk][:, 0:512], fn),
                    reads=[rpb[bk]], writes=[r_dst])

    def down_proj(src, r_src, src_nch, wkey, tk, eidx):
        for do in range(2, NCH):
            wt = wtile[:, (do % 2) * 1024:(do % 2) * 1024 + 1024]
            r_wt = R["wtile"][do % 2]
            bk2 = 6 + do % 2
            nh = (src_nch + 7) // 8
            for half in range(nh):
                kn = min(8, src_nch - half * 8)
                p.dma("pool", wt.rearrange("p (k n) -> p k n", n=128)[:, 0:kn],
                      d[wkey].rearrange("(k p) n -> p k n", p=128)[:, half * 8:half * 8 + kn,
                                                                  do * 128:(do + 1) * 128],
                      reads=[r_wt], writes=[r_wt])
                for tb in range(2):
                    def mmd(e, do=do, half=half, kn=kn, tb=tb, wt=wt):
                        last = None
                        for kc in range(kn):
                            last = e.matmul(
                                pb[4 + tb][:, 0:512],
                                wt[:, kc * 128:(kc + 1) * 128],
                                src[:, (half * 8 + kc) * 1024 + tb * 512:
                                    (half * 8 + kc) * 1024 + (tb + 1) * 512],
                                start=(half == 0 and kc == 0),
                                stop=(half == nh - 1 and kc == kn - 1))
                        return last
                    p.emit("pe", mmd, reads=[r_wt, r_src], writes=[rpb[4 + tb]])
            for tb in range(2):
                ts0 = tk * 1024 + tb * 512
                p.emit("pe", lambda e, eidx=eidx, ts0=ts0: e.matmul(
                    pb[6][:, 0:512], oh4[:, eidx * 128:(eidx + 1) * 128],
                    wT4[:, ts0:ts0 + 512], start=True, stop=True),
                    reads=[R["oh4"], R["wT4"]], writes=[rpb[6]])
                p.emit("dve", lambda e: e.tensor_copy(wbcs[:, 0:512], pb[6][:, 0:512]),
                       reads=[rpb[6]], writes=[R["wbcs"]])
                p.emit("dve", lambda e, tb=tb: e.tensor_mul(scr[:, 0:512],
                                                            pb[4 + tb][:, 0:512],
                                                            wbcs[:, 0:512]),
                       reads=[rpb[4 + tb], R["wbcs"]], writes=[R["scr"]])
                p.emit("dve", lambda e, do=do, ts0=ts0: e.tensor_add(
                    xres[:, do * 2048 + ts0:do * 2048 + ts0 + 512],
                    xres[:, do * 2048 + ts0:do * 2048 + ts0 + 512], scr[:, 0:512]),
                    reads=[R["scr"], r_xres], writes=[r_xres])

    for tk in range(2):
        mm_layer(h1, R["h1"], xf, R["xf"], NCH, "l1f_e0_up", 16, tk, "gelu")
        down_proj(h1, R["h1"], 16, "l1f_e0_down", tk, 0)
        mm_layer(h1, R["h1"], xf, R["xf"], NCH, "l1f_e1_w1", 16, tk, "silu")
        mm_layer(h2, R["h2"], xf, R["xf"], NCH, "l1f_e1_w2", 16, tk, "none")
        for hc in range(16):
            for tb in range(2):
                p.emit("dve", lambda e, hc=hc, tb=tb: e.tensor_mul(
                    h1[:, hc * 1024 + tb * 512:hc * 1024 + (tb + 1) * 512],
                    h1[:, hc * 1024 + tb * 512:hc * 1024 + (tb + 1) * 512],
                    h2[:, hc * 1024 + tb * 512:hc * 1024 + (tb + 1) * 512]),
                    reads=[R["h1"], R["h2"]], writes=[R["h1"]])
        down_proj(h1, R["h1"], 16, "l1f_e1_down", tk, 1)
        mm_layer(h1, R["h1"], xf, R["xf"], NCH, "l1f_e2_l1", 16, tk, "gelu")
        mm_layer(h2, R["h2"], h1, R["h1"], 16, "l1f_e2_l2", 16, tk, "gelu")
        mm_layer(h1, R["h1"], h2, R["h2"], 16, "l1f_e2_l3", 16, tk, "gelu")
        down_proj(h1, R["h1"], 16, "l1f_e2_l4", tk, 2)
        mm_layer(h1, R["h1"], xf, R["xf"], NCH, "l1f_e3_down", 8, tk, "gelu")
        mm_layer(h2, R["h2"], h1, R["h1"], 8, "l1f_e3_up", 16, tk, "gelu")
        down_proj(h2, R["h2"], 16, "l1f_e3_out", tk, 3)
    se.close()
    R["d_xsp2"] = Res("d_xsp2")
    R["d_xsp2"].w = None
    for c in range(NCH):
        for tb in range(4):
            p.dma("sp", scr2[:, 0:512], d["xsp"][c * 128:(c + 1) * 128, tb * 512:(tb + 1) * 512],
                  reads=[R["scr2"]], writes=[R["scr2"]])
            p.emit("dve", lambda e, c=c, tb=tb: e.tensor_add(
                xres[:, c * 2048 + tb * 512:c * 2048 + (tb + 1) * 512],
                xres[:, c * 2048 + tb * 512:c * 2048 + (tb + 1) * 512], scr2[:, 0:512]),
                reads=[R["scr2"], r_xres], writes=[r_xres])
    sf.close()


def _stage_l0_weights(inputs):
    w = {}
    w["Win"] = np.ascontiguousarray(np.asarray(inputs["L0_attn_W_in"], np.float32))
    cw = np.asarray(inputs["L0_attn_conv_w"], np.float32)[:, 0, :]
    w["convw"] = np.ascontiguousarray(cw.reshape(16, 128, 4).transpose(1, 0, 2).reshape(128, 64))
    cb = np.asarray(inputs["L0_attn_conv_b"], np.float32)
    w["convb"] = np.ascontiguousarray(cb.reshape(16, 128).T)
    wx = np.asarray(inputs["L0_attn_W_x"], np.float32)
    wxp = np.zeros((2048, 80), np.float32)
    wxp[:, 0] = wx[:, 0]
    wxp[:, 32:48] = wx[:, 1:17]
    wxp[:, 64:80] = wx[:, 17:33]
    w["Wxp"] = wxp
    w["Alog"] = np.ascontiguousarray(np.asarray(inputs["L0_attn_A_log"], np.float32).reshape(1, 16))
    w["Drow"] = np.ascontiguousarray(np.asarray(inputs["L0_attn_D"], np.float32).reshape(1, 2048))
    w["Wout"] = np.ascontiguousarray(np.asarray(inputs["L0_attn_W_out"], np.float32))
    return w


def declare_l0_inputs(nc, debug=False):
    dd = {}
    dd["xT"] = nc.dram_tensor("xT", [1024, 2048], F32, kind="ExternalInput").ap()
    dd["xTov"] = nc.dram_tensor("xTov", [1024, 3], F32, kind="ExternalInput").ap()
    dd["Win"] = nc.dram_tensor("Win", [1024, 4096], F32, kind="ExternalInput").ap()
    dd["convw"] = nc.dram_tensor("convw", [128, 64], F32, kind="ExternalInput").ap()
    dd["convb"] = nc.dram_tensor("convb", [128, 16], F32, kind="ExternalInput").ap()
    dd["Wxp"] = nc.dram_tensor("Wxp", [2048, 80], F32, kind="ExternalInput").ap()
    dd["Alog"] = nc.dram_tensor("Alog", [1, 16], F32, kind="ExternalInput").ap()
    dd["Drow"] = nc.dram_tensor("Drow", [1, 2048], F32, kind="ExternalInput").ap()
    dd["Wout"] = nc.dram_tensor("Wout", [2048, 1024], F32, kind="ExternalInput").ap()
    dd["flag"] = nc.dram_tensor("flag", [1, 1], F32, kind="ExternalInput").ap()
    kind = "ExternalOutput" if debug else "Internal"
    dd["xs"] = nc.dram_tensor("d_xs", [2048, 2048], BF16, kind=kind).ap()
    dd["y"] = nc.dram_tensor("d_y", [2048, 2048], BF16, kind=kind).ap()
    dd["sz"] = nc.dram_tensor("d_sz", [2048, 2048], BF16, kind=kind).ap()
    dd["ccin"] = nc.dram_tensor("d_ccin", [16, 2048], F32).ap()
    dd["ccout"] = nc.dram_tensor("d_ccout", [32, 2048], F32).ap()
    dd["bar1"] = nc.dram_tensor("d_bar1", [16, 16], F32).ap()
    dd["bar2"] = nc.dram_tensor("d_bar2", [16, 16], F32).ap()
    return dd


def build_l0_program(debug=False):
    nc = bass.Bass("TRN2", target_bir_lowering=False)
    dd = declare_l0_inputs(nc, debug)
    out = nc.dram_tensor("xout", [1024, 2048], F32, kind="ExternalOutput").ap()
    dbg = {}
    if debug:
        dbg["xdbl"] = nc.dram_tensor("dbg_xdbl", [128, 2048], F32, kind="ExternalOutput").ap()
        dbg["rows"] = nc.dram_tensor("dbg_rows", [1, 8704], F32, kind="ExternalOutput").ap()
        dbg["hrun"] = nc.dram_tensor("dbg_hrun", [16, 2048], F32, kind="ExternalOutput").ap()
        dbg["xn"] = nc.dram_tensor("dbg_xn", [128, 8 * 2056], F32, kind="ExternalOutput").ap()
        dbg["xcs"] = nc.dram_tensor("dbg_xcs", [128, 2052], F32, kind="ExternalOutput").ap()
        dbg["xconv"] = nc.dram_tensor("dbg_xconv", [128, 2048], F32, kind="ExternalOutput").ap()
    with ExitStack() as ctx:
        p = Prog(nc, ctx)
        C = Consts(p, nc, ctx)
        xres = sbuf(nc, ctx, "xres", [128, NCH * 2048], F32)
        r_xres = Res("xres")
        hooks = build_l0_attn(p, nc, ctx, C, xres, r_xres, dd, first=True, dbg=dbg)
        r_out = Res("out")
        for c in range(NCH):
            p.dma("sp", out.rearrange("(c p) t -> c p t", p=128)[c],
                  xres[:, c * 2048:(c + 1) * 2048], reads=[r_xres], writes=[r_out])
        p.finish()
    return nc


def l0_in_maps(inputs):
    w = _stage_l0_weights(inputs)
    x = np.asarray(inputs["L0_attn_x"], np.float32)
    maps = []
    for core in range(NCORES):
        b, h = core // 2, core % 2
        m = dict(w)
        m["xT"] = np.ascontiguousarray(x[b, h * 2048:(h + 1) * 2048, :].T)
        m["xTov"] = (np.zeros((1024, 3), np.float32) if h == 0 else
                     np.ascontiguousarray(x[b, h * 2048 - 3:h * 2048, :].T))
        m["flag"] = np.full((1, 1), float(h), np.float32)
        maps.append(m)
    return maps



def declare_attn_inputs(nc, kind, debug=False):
    dd = {}
    dd["flag"] = nc.dram_tensor("flag", [1, 1], F32, kind="ExternalInput").ap()
    dd["agi"] = nc.dram_tensor("d_agi", [1024, 2048], BF16).ap()
    dd["ago"] = nc.dram_tensor("d_ago", [2048, 2048], BF16).ap()
    dd["oag"] = nc.dram_tensor("d_oag", [512, 4096], BF16).ap()
    dd["oago"] = nc.dram_tensor("d_oago", [1024, 4096], BF16).ap()
    dd["mx"] = nc.dram_tensor("d_mx", [1, 4096], F32).ap()
    dd["bar1"] = nc.dram_tensor("d_bar1", [16, 16], F32).ap()
    dd["bar2"] = nc.dram_tensor("d_bar2", [16, 16], F32).ap()
    if kind == "l1attn":
        for nm in ["wq", "wk", "wv"]:
            dd["l1a_" + nm] = nc.dram_tensor("l1a_" + nm, [1024, 512], F32,
                                             kind="ExternalInput").ap()
        dd["l1a_wo"] = nc.dram_tensor("l1a_wo", [1024, 1024], F32, kind="ExternalInput").ap()
        dd["l1a_bqk"] = nc.dram_tensor("l1a_bqk", [128, 8], F32, kind="ExternalInput").ap()
        dd["l1a_bo"] = nc.dram_tensor("l1a_bo", [128, 8], F32, kind="ExternalInput").ap()
        dd["l1a_bvrow"] = nc.dram_tensor("l1a_bvrow", [1, 512], F32, kind="ExternalInput").ap()
        dd["ropePM"] = nc.dram_tensor("ropePM", [128, 128], F32, kind="ExternalInput").ap()
    else:
        for nm in ["wq", "wk", "wv"]:
            dd["l1f_" + nm] = nc.dram_tensor("l1f_" + nm, [1024, 512], F32,
                                             kind="ExternalInput").ap()
        dd["l1f_wo"] = nc.dram_tensor("l1f_wo", [1024, 1024], F32, kind="ExternalInput").ap()
        dd["l1f_n1w"] = nc.dram_tensor("l1f_n1w", [128, 8], F32, kind="ExternalInput").ap()
        dd["l1f_qnw"] = nc.dram_tensor("l1f_qnw", [128, 1], F32, kind="ExternalInput").ap()
        dd["l1f_knw"] = nc.dram_tensor("l1f_knw", [128, 1], F32, kind="ExternalInput").ap()
    return dd


def _rope_pm():
    pm = np.zeros((128, 128), np.float32)
    for blk in range(2):
        for i in range(32):
            d0 = blk * 64 + i
            pm[blk * 64 + i + 32, d0] = -1.0
            pm[d0, blk * 64 + i + 32] = 1.0
    return pm


def _stage_l1a_weights(inputs, half):
    h0 = half * 512
    w = {}
    for nm, key in [("wq", "L1_attn_Wq"), ("wk", "L1_attn_Wk"), ("wv", "L1_attn_Wv")]:
        W = np.asarray(inputs[key], np.float32)
        w["l1a_" + nm] = np.ascontiguousarray(W[h0:h0 + 512, :].T)
    w["l1a_wo"] = np.ascontiguousarray(np.asarray(inputs["L1_attn_Wo"], np.float32).T)
    bq = np.asarray(inputs["L1_attn_bq"], np.float32)[h0:h0 + 512].reshape(4, 128).T
    bk = np.asarray(inputs["L1_attn_bk"], np.float32)[h0:h0 + 512].reshape(4, 128).T
    w["l1a_bqk"] = np.ascontiguousarray(np.concatenate([bq, bk], 1))
    w["l1a_bo"] = np.ascontiguousarray(
        np.asarray(inputs["L1_attn_bo"], np.float32).reshape(8, 128).T)
    w["l1a_bvrow"] = np.ascontiguousarray(
        np.asarray(inputs["L1_attn_bv"], np.float32)[h0:h0 + 512].reshape(1, 512))
    w["ropePM"] = _rope_pm()
    return w


def build_l1a_program(debug=False):
    nc = bass.Bass("TRN2", target_bir_lowering=False)
    xTin = nc.dram_tensor("xT", [1024, 2048], F32, kind="ExternalInput").ap()
    dd = declare_attn_inputs(nc, "l1attn", debug)
    out = nc.dram_tensor("xout", [1024, 2048], F32, kind="ExternalOutput").ap()
    if debug:
        dd["dbg_qw"] = nc.dram_tensor("dbg_qw", [128, 4096], F32, kind="ExternalOutput").ap()
        dd["dbg_kw"] = nc.dram_tensor("dbg_kw", [128, 4096], F32, kind="ExternalOutput").ap()
        dd["dbg_vw"] = nc.dram_tensor("dbg_vw", [128, 32 * 130], F32, kind="ExternalOutput").ap()
        dd["dbg_nmx"] = nc.dram_tensor("dbg_nmx", [1, 4096], F32, kind="ExternalOutput").ap()
        dd["dbg_nmxraw"] = nc.dram_tensor("dbg_nmxraw", [128, 32], F32, kind="ExternalOutput").ap()
        dd["dbg_oag"] = nc.dram_tensor("dbg_oag", [512, 4096], BF16, kind="ExternalOutput").ap()
    with ExitStack() as ctx:
        p = Prog(nc, ctx)
        C = Consts(p, nc, ctx)
        xres = sbuf(nc, ctx, "xres", [128, NCH * 2048], F32)
        r_xres = Res("xres")
        for c in range(NCH):
            p.dma("sp", xres[:, c * 2048:(c + 1) * 2048],
                  xTin.rearrange("(c p) t -> c p t", p=128)[c], writes=[r_xres])
        build_attn_block(p, nc, ctx, C, xres, r_xres, dd, "l1attn")
        r_out = Res("out")
        for c in range(NCH):
            p.dma("sp", out.rearrange("(c p) t -> c p t", p=128)[c],
                  xres[:, c * 2048:(c + 1) * 2048], reads=[r_xres], writes=[r_out])
        p.finish()
    return nc


def run_l1a(inputs, x_in, debug=False):
    nc = build_l1a_program(debug)
    maps = []
    for core in range(NCORES):
        b, h = core // 2, core % 2
        m = _stage_l1a_weights(inputs, h)
        m["xT"] = np.ascontiguousarray(x_in[b, h * 2048:(h + 1) * 2048, :].T)
        m["flag"] = np.full((1, 1), float(h), np.float32)
        maps.append(m)
    res = run_bass_kernel_spmd(nc, maps, core_ids=list(range(NCORES)))
    out = np.zeros((4, 4096, 1024), np.float32)
    for core in range(NCORES):
        b, h = core // 2, core % 2
        out[b, h * 2048:(h + 1) * 2048, :] = np.asarray(res.results[core]["xout"]).T
    if debug:
        return out, res
    return out



def declare_l0f_inputs(nc):
    dd = {}
    dd["f0_bcols"] = nc.dram_tensor("f0_bcols", [128, 64], F32, kind="ExternalInput").ap()
    dd["f0_gbias"] = nc.dram_tensor("f0_gbias", [128, 16], F32, kind="ExternalInput").ap()
    dd["f0_b2row"] = nc.dram_tensor("f0_b2row", [1, 16], F32, kind="ExternalInput").ap()
    dd["f0_ceT"] = nc.dram_tensor("f0_ceT", [1024, 16], F32, kind="ExternalInput").ap()
    dd["f0_g1"] = nc.dram_tensor("f0_g1", [1024, 512], F32, kind="ExternalInput").ap()
    dd["f0_g2"] = nc.dram_tensor("f0_g2", [512, 16], F32, kind="ExternalInput").ap()
    dd["f0_wcols"] = nc.dram_tensor("f0_wcols", [1024, 4096], F32, kind="ExternalInput").ap()
    dd["f0_proj"] = nc.dram_tensor("f0_proj", [4096, 1024], F32, kind="ExternalInput").ap()
    dd["f0y"] = nc.dram_tensor("d_f0y", [1024, 2048], F32).ap()
    return dd


def _stage_l0f_weights(inputs):
    w = {}
    bcols = np.asarray(inputs["L0_ffn_b_cols"], np.float32).reshape(4096)
    projb = np.asarray(inputs["L0_ffn_proj_b"], np.float32)
    lng = np.asarray(inputs["L0_ffn_ln_g"], np.float32)
    lnb = np.asarray(inputs["L0_ffn_ln_b"], np.float32)
    out = np.zeros((128, 64), np.float32)
    for cn in range(32):
        out[:, cn] = bcols[cn * 128:(cn + 1) * 128]
    for c in range(8):
        out[:, 32 + c] = projb[c * 128:(c + 1) * 128]
        out[:, 40 + c] = lng[c * 128:(c + 1) * 128]
        out[:, 48 + c] = lnb[c * 128:(c + 1) * 128]
    w["f0_bcols"] = out
    w["f0_gbias"] = np.ascontiguousarray(
        np.concatenate([np.asarray(inputs["L0_ffn_gate_b1"], np.float32).reshape(4, 128).T,
                        np.zeros((128, 12), np.float32)], 1))
    w["f0_b2row"] = np.asarray(inputs["L0_ffn_gate_b2"], np.float32).reshape(1, 16)
    w["f0_ceT"] = np.ascontiguousarray(np.asarray(inputs["L0_ffn_col_emb"], np.float32).T)
    w["f0_g1"] = np.ascontiguousarray(np.asarray(inputs["L0_ffn_gate_w1"], np.float32))
    w["f0_g2"] = np.ascontiguousarray(np.asarray(inputs["L0_ffn_gate_w2"], np.float32))
    wc = np.asarray(inputs["L0_ffn_W_cols"], np.float32)
    w["f0_wcols"] = np.ascontiguousarray(wc.transpose(1, 0, 2).reshape(1024, 4096))
    w["f0_proj"] = np.ascontiguousarray(np.asarray(inputs["L0_ffn_proj_w"], np.float32))
    return w


def build_l0f_program(debug=False):
    nc = bass.Bass("TRN2", target_bir_lowering=False)
    xTin = nc.dram_tensor("xT", [1024, 2048], F32, kind="ExternalInput").ap()
    dd = declare_l0f_inputs(nc)
    out = nc.dram_tensor("xout", [1024, 2048], F32, kind="ExternalOutput").ap()
    if debug:
        dd["dbg_wtok"] = nc.dram_tensor("dbg_wtok", [128, 256], F32, kind="ExternalOutput").ap()
        dd["dbg_ltok"] = nc.dram_tensor("dbg_ltok", [128, 256], F32, kind="ExternalOutput").ap()
        dd["dbg_y"] = nc.dram_tensor("dbg_y", [1024, 2048], F32, kind="ExternalOutput").ap()
    with ExitStack() as ctx:
        p = Prog(nc, ctx)
        C = Consts(p, nc, ctx)
        xres = sbuf(nc, ctx, "xres", [128, NCH * 2048], F32)
        r_xres = Res("xres")
        for c in range(NCH):
            p.dma("sp", xres[:, c * 2048:(c + 1) * 2048],
                  xTin.rearrange("(c p) t -> c p t", p=128)[c], writes=[r_xres])
        build_l0_ffn(p, nc, ctx, C, xres, r_xres, dd)
        r_out = Res("out")
        for c in range(NCH):
            p.dma("sp", out.rearrange("(c p) t -> c p t", p=128)[c],
                  xres[:, c * 2048:(c + 1) * 2048], reads=[r_xres], writes=[r_out])
        p.finish()
    return nc


def run_l0f(inputs, x_in, debug=False):
    nc = build_l0f_program(debug)
    w = _stage_l0f_weights(inputs)
    maps = []
    for core in range(NCORES):
        b, h = core // 2, core % 2
        m = dict(w)
        m["xT"] = np.ascontiguousarray(x_in[b, h * 2048:(h + 1) * 2048, :].T)
        maps.append(m)
    res = run_bass_kernel_spmd(nc, maps, core_ids=list(range(NCORES)))
    out = np.zeros((4, 4096, 1024), np.float32)
    for core in range(NCORES):
        b, h = core // 2, core % 2
        out[b, h * 2048:(h + 1) * 2048, :] = np.asarray(res.results[core]["xout"]).T
    if debug:
        return out, res
    return out


def declare_l1f_inputs(nc):
    dd = declare_attn_inputs(nc, "l1ffn")
    dd["l1f_n2w"] = nc.dram_tensor("l1f_n2w", [128, 8], F32, kind="ExternalInput").ap()
    dd["xsp"] = nc.dram_tensor("d_xsp", [1024, 2048], F32).ap()
    dd["l1f_gwT"] = nc.dram_tensor("l1f_gwT", [1024, 4], F32, kind="ExternalInput").ap()
    for nm, shp in [("e0_up", (1024, 2048)), ("e0_down", (2048, 1024)),
                    ("e1_w1", (1024, 2048)), ("e1_w2", (1024, 2048)),
                    ("e1_down", (2048, 1024)), ("e2_l1", (1024, 2048)),
                    ("e2_l2", (2048, 2048)), ("e2_l3", (2048, 2048)),
                    ("e2_l4", (2048, 1024)), ("e3_down", (1024, 1024)),
                    ("e3_up", (1024, 2048)), ("e3_out", (2048, 1024))]:
        dd["l1f_" + nm] = nc.dram_tensor("l1f_" + nm, list(shp), F32,
                                         kind="ExternalInput").ap()
    return dd


def _stage_l1f_weights(inputs, half):
    h0 = half * 512
    w = {}
    for nm, key in [("wq", "L1_ffn_wq"), ("wk", "L1_ffn_wk"), ("wv", "L1_ffn_wv")]:
        W = np.asarray(inputs[key], np.float32)
        w["l1f_" + nm] = np.ascontiguousarray(W[h0:h0 + 512, :].T)
    w["l1f_wo"] = np.ascontiguousarray(np.asarray(inputs["L1_ffn_wo"], np.float32).T)
    w["l1f_n1w"] = np.ascontiguousarray(
        np.asarray(inputs["L1_ffn_n1_w"], np.float32).reshape(8, 128).T)
    w["l1f_n2w"] = np.ascontiguousarray(
        np.asarray(inputs["L1_ffn_n2_w"], np.float32).reshape(8, 128).T)
    w["l1f_qnw"] = np.ascontiguousarray(
        np.tile(np.asarray(inputs["L1_ffn_qn_w"], np.float32), 2).reshape(128, 1))
    w["l1f_knw"] = np.ascontiguousarray(
        np.tile(np.asarray(inputs["L1_ffn_kn_w"], np.float32), 2).reshape(128, 1))
    w["l1f_gwT"] = np.ascontiguousarray(np.asarray(inputs["L1_ffn_gate_w"], np.float32).T)
    for nm in ["e0_up", "e0_down", "e1_w1", "e1_w2", "e1_down", "e2_l1", "e2_l2",
               "e2_l3", "e2_l4", "e3_down", "e3_up", "e3_out"]:
        w["l1f_" + nm] = np.ascontiguousarray(
            np.asarray(inputs["L1_ffn_" + nm], np.float32).T)
    return w


def build_l1f_program(debug=False):
    nc = bass.Bass("TRN2", target_bir_lowering=False)
    xTin = nc.dram_tensor("xT", [1024, 2048], F32, kind="ExternalInput").ap()
    dd = declare_l1f_inputs(nc)
    out = nc.dram_tensor("xout", [1024, 2048], F32, kind="ExternalOutput").ap()
    dbgx = None
    if debug:
        dbgx = nc.dram_tensor("dbg_xmid", [1024, 2048], F32, kind="ExternalOutput").ap()
        dd["dbg_wts"] = nc.dram_tensor("dbg_wts", [128, 64], F32, kind="ExternalOutput").ap()
    with ExitStack() as ctx:
        p = Prog(nc, ctx)
        C = Consts(p, nc, ctx)
        xres = sbuf(nc, ctx, "xres", [128, NCH * 2048], F32)
        r_xres = Res("xres")
        for c in range(NCH):
            p.dma("sp", xres[:, c * 2048:(c + 1) * 2048],
                  xTin.rearrange("(c p) t -> c p t", p=128)[c], writes=[r_xres])
        build_attn_block(p, nc, ctx, C, xres, r_xres, dd, "l1ffn")
        if dbgx is not None:
            for c in range(NCH):
                p.dma("sp", dbgx.rearrange("(c p) t -> c p t", p=128)[c],
                      xres[:, c * 2048:(c + 1) * 2048], reads=[r_xres], writes=[Res("dx%d" % c)])
        build_l1_moe(p, nc, ctx, C, xres, r_xres, dd)
        r_out = Res("out")
        for c in range(NCH):
            p.dma("sp", out.rearrange("(c p) t -> c p t", p=128)[c],
                  xres[:, c * 2048:(c + 1) * 2048], reads=[r_xres], writes=[r_out])
        p.finish()
    return nc


def run_l1f(inputs, x_in, debug=False):
    nc = build_l1f_program(debug)
    maps = []
    for core in range(NCORES):
        b, h = core // 2, core % 2
        m = _stage_l1f_weights(inputs, h)
        m["xT"] = np.ascontiguousarray(x_in[b, h * 2048:(h + 1) * 2048, :].T)
        m["flag"] = np.full((1, 1), float(h), np.float32)
        maps.append(m)
    res = run_bass_kernel_spmd(nc, maps, core_ids=list(range(NCORES)))
    out = np.zeros((4, 4096, 1024), np.float32)
    for core in range(NCORES):
        b, h = core // 2, core % 2
        out[b, h * 2048:(h + 1) * 2048, :] = np.asarray(res.results[core]["xout"]).T
    if debug:
        return out, res
    return out



def build_fused_program(debug=False):
    nc = bass.Bass("TRN2", target_bir_lowering=False)
    d = {}
    d.update(declare_l0_inputs(nc))
    d["agi"] = nc.dram_tensor("d_agi", [1024, 2048], BF16).ap()
    d["ago"] = nc.dram_tensor("d_ago", [2048, 2048], BF16).ap()
    d["oag"] = nc.dram_tensor("d_oag", [512, 4096], BF16).ap()
    d["oago"] = nc.dram_tensor("d_oago", [1024, 4096], BF16).ap()
    d["mx"] = nc.dram_tensor("d_mx", [1, 4096], F32).ap()
    for nm in ["wq", "wk", "wv"]:
        d["l1a_" + nm] = nc.dram_tensor("l1a_" + nm, [1024, 512], F32,
                                        kind="ExternalInput").ap()
    d["l1a_wo"] = nc.dram_tensor("l1a_wo", [1024, 1024], F32, kind="ExternalInput").ap()
    d["l1a_bqk"] = nc.dram_tensor("l1a_bqk", [128, 8], F32, kind="ExternalInput").ap()
    d["l1a_bo"] = nc.dram_tensor("l1a_bo", [128, 8], F32, kind="ExternalInput").ap()
    d["l1a_bvrow"] = nc.dram_tensor("l1a_bvrow", [1, 512], F32, kind="ExternalInput").ap()
    d["ropePM"] = nc.dram_tensor("ropePM", [128, 128], F32, kind="ExternalInput").ap()
    for nm in ["wq", "wk", "wv"]:
        d["l1f_" + nm] = nc.dram_tensor("l1f_" + nm, [1024, 512], F32,
                                        kind="ExternalInput").ap()
    d["l1f_wo"] = nc.dram_tensor("l1f_wo", [1024, 1024], F32, kind="ExternalInput").ap()
    d["l1f_n1w"] = nc.dram_tensor("l1f_n1w", [128, 8], F32, kind="ExternalInput").ap()
    d["l1f_qnw"] = nc.dram_tensor("l1f_qnw", [128, 1], F32, kind="ExternalInput").ap()
    d["l1f_knw"] = nc.dram_tensor("l1f_knw", [128, 1], F32, kind="ExternalInput").ap()
    d["l1f_n2w"] = nc.dram_tensor("l1f_n2w", [128, 8], F32, kind="ExternalInput").ap()
    d["l1f_gwT"] = nc.dram_tensor("l1f_gwT", [1024, 4], F32, kind="ExternalInput").ap()
    d["xsp"] = nc.dram_tensor("d_xsp", [1024, 2048], F32).ap()
    for nm, shp in [("e0_up", (1024, 2048)), ("e0_down", (2048, 1024)),
                    ("e1_w1", (1024, 2048)), ("e1_w2", (1024, 2048)),
                    ("e1_down", (2048, 1024)), ("e2_l1", (1024, 2048)),
                    ("e2_l2", (2048, 2048)), ("e2_l3", (2048, 2048)),
                    ("e2_l4", (2048, 1024)), ("e3_down", (1024, 1024)),
                    ("e3_up", (1024, 2048)), ("e3_out", (2048, 1024))]:
        d["l1f_" + nm] = nc.dram_tensor("l1f_" + nm, list(shp), F32,
                                        kind="ExternalInput").ap()
    d.update({k: v for k, v in declare_l0f_inputs(nc).items()})
    out = nc.dram_tensor("xout", [1024, 2048], F32, kind="ExternalOutput").ap()

    with ExitStack() as ctx:
        p = Prog(nc, ctx)
        C = Consts(p, nc, ctx)
        xres = sbuf(nc, ctx, "xres", [128, NCH * 2048], F32)
        r_xres = Res("xres")
        def dump(tag):
            if not debug:
                return
            dx = nc.dram_tensor("dbg_" + tag, [1024, 2048], F32, kind="ExternalOutput").ap()
            for c in range(NCH):
                p.dma("sp", dx.rearrange("(c p) t -> c p t", p=128)[c],
                      xres[:, c * 2048:(c + 1) * 2048], reads=[r_xres],
                      writes=[Res("d" + tag + str(c))])
        build_l0_attn(p, nc, ctx, C, xres, r_xres, d, first=True)
        dump("x1")
        build_l0_ffn(p, nc, ctx, C, xres, r_xres, d)
        dump("x2")
        build_attn_block(p, nc, ctx, C, xres, r_xres, d, "l1attn")
        dump("x3")
        build_attn_block(p, nc, ctx, C, xres, r_xres, d, "l1ffn")
        dump("x3b")
        build_l1_moe(p, nc, ctx, C, xres, r_xres, d)
        r_out = Res("out")
        for c in range(NCH):
            p.dma("sp", out.rearrange("(c p) t -> c p t", p=128)[c],
                  xres[:, c * 2048:(c + 1) * 2048], reads=[r_xres], writes=[r_out])
        p.finish()
    return nc


def fused_in_maps(inputs):
    w0 = _stage_l0_weights(inputs)
    wf = _stage_l0f_weights(inputs)
    x = np.asarray(inputs["L0_attn_x"], np.float32)
    maps = []
    for core in range(NCORES):
        b, h = core // 2, core % 2
        m = dict(w0)
        m.update(wf)
        m.update(_stage_l1a_weights(inputs, h))
        m.update(_stage_l1f_weights(inputs, h))
        m["xT"] = np.ascontiguousarray(x[b, h * 2048:(h + 1) * 2048, :].T)
        m["xTov"] = (np.zeros((1024, 3), np.float32) if h == 0 else
                     np.ascontiguousarray(x[b, h * 2048 - 3:h * 2048, :].T))
        m["flag"] = np.full((1, 1), float(h), np.float32)
        maps.append(m)
    return maps


def kernel(**inputs):
    nc = build_fused_program()
    maps = fused_in_maps(inputs)
    res = run_bass_kernel_spmd(nc, maps, core_ids=list(range(NCORES)))
    out = np.zeros((4, 4096, 1024), np.float32)
    for core in range(NCORES):
        b, h = core // 2, core % 2
        out[b, h * 2048:(h + 1) * 2048, :] = np.asarray(res.results[core]["xout"]).T
    return out


# version 06
# speedup vs baseline: 1.5145x; 1.4998x over previous
import numpy as np
import ml_dtypes
import concourse.bass as bass
import concourse.mybir as mybir
from concourse.bass_utils import run_bass_kernel_spmd
from contextlib import ExitStack

F32 = mybir.dt.float32
BF16 = mybir.dt.bfloat16
AF = mybir.ActivationFunctionType
ALU = mybir.AluOpType
AX = mybir.AxisListType
BF = ml_dtypes.bfloat16

NCORES = 8
B, T, DM = 4, 4096, 1024
TL = 2048
NCH = DM // 128
DS, DI = 16, 2048
NDI = DI // 128
CL = 128
NSC = TL // CL
CW = 2056

RO_MEAN, RO_RSTD, RO_MRS = 0, 2048, 4096
RO_OV = 6144
RO_DROW = 6656
RO_DELTA, RO_CD = 0, 2048
RO_CDREV, RO_CDC, RO_NEGCD, RO_CTOT = 4096, 4224, 4352, 4480
RO_SP1 = 4608


class Res:
    __slots__ = ("name", "w", "r")

    def __init__(self, name=""):
        self.name = name
        self.w = None
        self.r = []


class Prog:
    ENGS = ("pe", "act", "dve", "pool", "sp")

    def __init__(self, nc, ctx):
        self.nc = nc
        self.eng = {"pe": nc.tensor, "act": nc.scalar, "dve": nc.vector,
                    "pool": nc.gpsimd, "sp": nc.sync}
        self.sem = {e: ctx.enter_context(nc.semaphore(f"sem_{e}")) for e in self.ENGS}
        self.pos = {e: 0 for e in self.ENGS}
        self.known = {e: {e2: 0 for e2 in self.ENGS} for e in self.ENGS}
        self.strict_same_engine = True

    def emit(self, e, fn, reads=(), writes=(), inc=1, dma=False):
        need = {}

        def _need(dep):
            e2, v, d2 = dep
            if e2 == e == "pe" and not (dma or d2):
                return
            if e2 != e or dma or d2 or self.strict_same_engine:
                if v > need.get(e2, 0):
                    need[e2] = v

        for r in reads:
            if r.w:
                _need(r.w)
        for w in writes:
            if w.w:
                _need(w.w)
            for rr in w.r:
                _need(rr)
        eng = self.eng[e]
        for e2 in sorted(need):
            v = need[e2]
            if self.known[e][e2] < v:
                eng.wait_ge(self.sem[e2], v)
                self.known[e][e2] = v
        last = fn(eng)
        last.then_inc(self.sem[e], inc)
        self.pos[e] += inc
        v = self.pos[e]
        for w in writes:
            w.w = (e, v, dma)
            w.r = []
        for r in reads:
            r.r.append((e, v, dma))
        return v

    def dma(self, q, out, in_, reads=(), writes=(), **kw):
        return self.emit(q, lambda g: g.dma_start(out=out, in_=in_, **kw),
                         reads=reads, writes=writes, inc=16, dma=True)

    def finish(self):
        eng = self.eng["sp"]
        for e2 in self.ENGS:
            if e2 != "sp" and self.known["sp"][e2] < self.pos[e2]:
                eng.wait_ge(self.sem[e2], self.pos[e2])
                self.known["sp"][e2] = self.pos[e2]
        eng.wait_ge(self.sem["sp"], self.pos["sp"])


def collective_barrier(p, d, scr_ap, r_scr, out_res):
    rb = Res("cbar")
    p.emit("pool", lambda e: e.collective_compute(
        "AllReduce", ALU.add, replica_groups=[[2 * i, 2 * i + 1] for i in range(4)],
        ins=[d["bar1"].opt()], outs=[d["bar2"].opt()]),
        reads=[out_res], writes=[rb], inc=1)
    p.dma("sp", scr_ap, d["bar2"][:, :], reads=[rb, r_scr], writes=[r_scr])
    out_res.w = r_scr.w


def sbuf(nc, ctx, name, shape, dt):
    return ctx.enter_context(nc.sbuf_tensor(name, shape, dt, allow_name_mangling=True))


def psumt(nc, ctx, name, shape, dt=F32):
    return ctx.enter_context(nc.psum_tensor(name, shape, dt, allow_name_mangling=True))


class Consts:
    def __init__(self, p, nc, ctx):
        self.onesrow_b = sbuf(nc, ctx, "c_onesrow_b", [1, 128], BF16)
        self.onesrow_f = sbuf(nc, ctx, "c_onesrow_f", [1, 128], F32)
        self.onescol_b = sbuf(nc, ctx, "c_onescol_b", [128, 1], BF16)
        self.onescol_f = sbuf(nc, ctx, "c_onescol_f", [128, 1], F32)
        self.ident = sbuf(nc, ctx, "c_ident", [128, 128], F32)
        self.minf = sbuf(nc, ctx, "c_minf", [128, 128], F32)
        self.eps1 = sbuf(nc, ctx, "c_eps1", [128, 1], F32)
        self.eps2 = sbuf(nc, ctx, "c_eps2", [128, 1], F32)
        self.r = {k: Res("c_" + k) for k in
                  ["onesrow_b", "onesrow_f", "onescol_b", "onescol_f", "ident", "minf",
                   "eps1", "eps2"]}
        r = self.r
        p.emit("dve", lambda e: e.memset(self.onesrow_b[:, :], 1.0), writes=[r["onesrow_b"]])
        p.emit("dve", lambda e: e.memset(self.onesrow_f[:, :], 1.0), writes=[r["onesrow_f"]])
        p.emit("dve", lambda e: e.memset(self.onescol_b[:, :], 1.0), writes=[r["onescol_b"]])
        p.emit("dve", lambda e: e.memset(self.onescol_f[:, :], 1.0), writes=[r["onescol_f"]])
        p.emit("dve", lambda e: e.memset(self.ident[:, :], 1.0), writes=[r["ident"]])
        p.emit("pool", lambda e: e.affine_select(self.ident[:, :], self.ident[:, :],
                                                 [[1, 128]], ALU.is_equal, 0.0,
                                                 base=0, channel_multiplier=-1),
               reads=[r["ident"]], writes=[r["ident"]])
        p.emit("dve", lambda e: e.memset(self.minf[:, :], 1.0e30), writes=[r["minf"]])
        p.emit("pool", lambda e: e.affine_select(self.minf[:, :], self.minf[:, :],
                                                 [[-1, 128]], ALU.is_ge, 0.0,
                                                 base=-1, channel_multiplier=1),
               reads=[r["minf"]], writes=[r["minf"]])
        p.emit("dve", lambda e: e.memset(self.eps1[:, :], 1e-5), writes=[r["eps1"]])
        p.emit("dve", lambda e: e.memset(self.eps2[:, :], 1.1920929e-07), writes=[r["eps2"]])


def bcast_row(p, C, pbank, r_pbank, row_ap, r_row, ncols):
    p.emit("pe", lambda e: e.matmul(pbank[:, 0:ncols], C.onesrow_f[0:1, :], row_ap,
                                    start=True, stop=True),
           reads=[r_row, C.r["onesrow_f"]], writes=[r_pbank])


def fence_res(p, name):
    r = Res(name)
    r.r = [(e, p.pos[e], True) for e in p.ENGS if p.pos[e] > 0]
    return r


def build_l0_attn(p, nc, ctx, C, xres, r_xres, d, first=True, dbg=None):
    dbg = dbg or {}
    ctx = ExitStack()
    rows = sbuf(nc, ctx, "l0_rows", [1, 8704], F32)
    scr = sbuf(nc, ctx, "l0_scr", [128, 512], F32)
    scrb = sbuf(nc, ctx, "l0_scrb", [128, 512], BF16)
    wtile = sbuf(nc, ctx, "l0_wtile", [128, 2 * 1024], BF16)
    xdbl = sbuf(nc, ctx, "l0_xdbl", [128, 2048], F32)
    dbc = sbuf(nc, ctx, "l0_dbc", [128, 2048], F32)
    szt = sbuf(nc, ctx, "l0_szt", [128, 2048], BF16)
    arow = sbuf(nc, ctx, "l0_arow", [1, 16], F32)
    abc = sbuf(nc, ctx, "l0_abc", [128, 16], F32)
    flagc = sbuf(nc, ctx, "l0_flagc", [16, 1], F32)

    pb = [psumt(nc, ctx, f"l0_pb{i}", [128, 512]) for i in range(8)]
    rpb = [Res(f"pb{i}") for i in range(8)]
    R = {k: Res(k) for k in ["rows", "scr", "scrb", "xdbl", "dbc", "szt",
                             "arow", "abc", "flagc",
                             "d_xs", "d_y", "d_sz", "d_ccin", "d_ccout"]}
    R["wtile"] = [Res("wtile0"), Res("wtile1")]

    cd = rows[0:1, RO_CD:RO_CD + 2048]
    dl = rows[0:1, RO_DELTA:RO_DELTA + 2048]

    if first:
        for c in range(NCH):
            p.dma("sp", xres[:, c * 2048:(c + 1) * 2048],
                  d["xT"].rearrange("(c p) t -> c p t", p=128)[c], writes=[r_xres])
    p.dma("sp", arow[:, :], d["Alog"][:, :], writes=[R["arow"]])
    p.emit("act", lambda e: e.activation(arow[:, :], arow[:, :], AF.Exp),
           reads=[R["arow"]], writes=[R["arow"]])
    p.emit("dve", lambda e: e.tensor_scalar(arow[:, :], arow[:, :], -1.0, None, ALU.mult),
           reads=[R["arow"]], writes=[R["arow"]])
    p.emit("pe", lambda e: e.matmul(pb[7][0:128, 0:16], C.onesrow_f[0:1, :], arow[:, :],
                                    start=True, stop=True),
           reads=[R["arow"], C.r["onesrow_f"]], writes=[rpb[7]])
    p.emit("dve", lambda e: e.tensor_copy(abc[:, :], pb[7][0:128, 0:16]),
           reads=[rpb[7]], writes=[R["abc"]])
    p.dma("sp", rows[0:1, RO_OV + 16:RO_OV + 17], d["flag"][:, :], writes=[R["rows"]])
    p.emit("pe", lambda e: e.matmul(pb[7][0:16, 0:1], C.onesrow_f[0:1, 0:16],
                                    rows[0:1, RO_OV + 16:RO_OV + 17], start=True, stop=True),
           reads=[R["rows"], C.r["onesrow_f"]], writes=[rpb[7]])
    p.emit("dve", lambda e: e.tensor_copy(flagc[:, :], pb[7][0:16, 0:1]),
           reads=[rpb[7]], writes=[R["flagc"]])
    p.dma("sp", rows[0:1, RO_DROW:RO_DROW + 2048], d["Drow"][0:1, :], writes=[R["rows"]])
    for tb in range(4):
        bcast_row(p, C, pb[6], rpb[6],
                  rows[0:1, RO_DROW + tb * 512:RO_DROW + (tb + 1) * 512], R["rows"], 512)
        p.emit("dve", lambda e, tb=tb: e.tensor_copy(dbc[:, tb * 512:(tb + 1) * 512],
                                                     pb[6][:, 0:512]),
               reads=[rpb[6]], writes=[R["dbc"]])

    s12 = ExitStack()
    xn = s12.enter_context(nc.sbuf_tensor("l0_xn", [128, NCH * CW], BF16, allow_name_mangling=True))
    xcs = s12.enter_context(nc.sbuf_tensor("l0_xcs", [128, 2052], F32, allow_name_mangling=True))
    xconv = s12.enter_context(nc.sbuf_tensor("l0_xconv", [128, 2048], F32, allow_name_mangling=True))
    xscur = s12.enter_context(nc.sbuf_tensor("l0_xscur", [128, 2048], BF16, allow_name_mangling=True))
    wz = s12.enter_context(nc.sbuf_tensor("l0_wz", [128, 8 * 1024], BF16, allow_name_mangling=True))
    wxp = s12.enter_context(nc.sbuf_tensor("l0_wxp", [128, NDI * 80], BF16, allow_name_mangling=True))
    convw = s12.enter_context(nc.sbuf_tensor("l0_convw", [128, 64], F32, allow_name_mangling=True))
    convb = s12.enter_context(nc.sbuf_tensor("l0_convb", [128, 16], F32, allow_name_mangling=True))
    ovf = s12.enter_context(nc.sbuf_tensor("l0_ovf", [128, NCH * 3], F32, allow_name_mangling=True))
    for k in ["xn", "xcs", "xconv", "xscur", "wz", "wxp", "convw", "convb", "ovf"]:
        R[k] = fence_res(p, k)

    for c in range(NCH):
        p.emit("dve", lambda e, c=c: e.tensor_copy(
            xn[:, c * CW + 3:c * CW + 3 + 2048], xres[:, c * 2048:(c + 1) * 2048]),
            reads=[r_xres], writes=[R["xn"]])
    for c in range(NCH):
        p.dma("sp", ovf[:, c * 3:(c + 1) * 3],
              d["xTov"].rearrange("(c p) t -> c p t", p=128)[c], writes=[R["ovf"]])
    for c in range(NCH):
        p.emit("dve", lambda e, c=c: e.tensor_copy(xn[:, c * CW:c * CW + 3],
                                                   ovf[:, c * 3:(c + 1) * 3]),
               reads=[R["ovf"]], writes=[R["xn"]])

    def ln_block(tok0, n, roff):
        mu = rows[0:1, roff:roff + n]
        rs = rows[0:1, roff + n:roff + 2 * n]
        mr = rows[0:1, roff + 2 * n:roff + 3 * n]
        for s0 in range(0, n, 512):
            w = min(512, n - s0)

            def mmsum(e, s0=s0, w=w):
                last = None
                for c in range(NCH):
                    last = e.matmul(pb[5][0:1, 0:w], C.onescol_b[:, :],
                                    xn[:, c * CW + tok0 + s0:c * CW + tok0 + s0 + w],
                                    start=(c == 0), stop=(c == NCH - 1))
                return last
            p.emit("pe", mmsum, reads=[R["xn"], C.r["onescol_b"]], writes=[rpb[5]])
            for c in range(NCH):
                p.emit("act", lambda e, c=c, s0=s0, w=w: e.activation(
                    scrb[:, 0:w], xn[:, c * CW + tok0 + s0:c * CW + tok0 + s0 + w],
                    AF.Square),
                    reads=[R["xn"]], writes=[R["scrb"]])
                p.emit("pe", lambda e, c=c, w=w: e.matmul(
                    pb[4][0:1, 0:w], C.onescol_b[:, :], scrb[:, 0:w],
                    start=(c == 0), stop=(c == NCH - 1)),
                    reads=[R["scrb"], C.r["onescol_b"]], writes=[rpb[4]])
            p.emit("dve", lambda e, s0=s0, w=w: e.tensor_scalar(
                mu[0:1, s0:s0 + w], pb[5][0:1, 0:w], 1.0 / DM, None, ALU.mult),
                reads=[rpb[5]], writes=[R["rows"]])
            p.emit("dve", lambda e, s0=s0, w=w: e.tensor_scalar(
                rs[0:1, s0:s0 + w], pb[4][0:1, 0:w], 1.0 / DM, None, ALU.mult),
                reads=[rpb[4]], writes=[R["rows"]])
        musq = rows[0:1, RO_DROW:RO_DROW + n]
        p.emit("act", lambda e: e.activation(musq, mu, AF.Square),
               reads=[R["rows"]], writes=[R["rows"]])
        p.emit("dve", lambda e: e.tensor_sub(rs, rs, musq),
               reads=[R["rows"]], writes=[R["rows"]])
        p.emit("act", lambda e: e.activation(rs, rs, AF.Ln, bias=C.eps1[0:1, 0:1]),
               reads=[R["rows"], C.r["eps1"]], writes=[R["rows"]])
        p.emit("act", lambda e: e.activation(rs, rs, AF.Exp, bias=0.0, scale=-0.5),
               reads=[R["rows"]], writes=[R["rows"]])
        p.emit("dve", lambda e: e.tensor_mul(mr, mu, rs),
               reads=[R["rows"]], writes=[R["rows"]])
        nb = (n + 511) // 512
        for tb in range(nb):
            s0, s1 = tb * 512, min((tb + 1) * 512, n)
            w = s1 - s0
            bcast_row(p, C, pb[5], rpb[5], rows[0:1, roff + n + s0:roff + n + s1],
                      R["rows"], w)
            bcast_row(p, C, pb[4], rpb[4], rows[0:1, roff + 2 * n + s0:roff + 2 * n + s1],
                      R["rows"], w)
            for c in range(NCH):
                if tok0 == 0:
                    src_f, r_src = ovf[:, c * 3:(c + 1) * 3], R["ovf"]
                else:
                    src_f, r_src = xres[:, c * 2048 + s0:c * 2048 + s1], r_xres
                dst = xn[:, c * CW + tok0 + s0:c * CW + tok0 + s1]
                p.emit("dve", lambda e, src_f=src_f, w=w: e.tensor_mul(
                    scr[:, 0:w], src_f, pb[5][:, 0:w]),
                    reads=[r_src, rpb[5]], writes=[R["scr"]])
                p.emit("dve", lambda e, dst=dst, w=w: e.tensor_sub(dst, scr[:, 0:w],
                                                                   pb[4][:, 0:w]),
                       reads=[R["scr"], rpb[4]], writes=[R["xn"]])

    ln_block(3, 2048, RO_MEAN)
    ln_block(0, 3, RO_OV)

    if "xn" in dbg:
        p.dma("pool", dbg["xn"][:, :], xn[:, :], reads=[R["xn"]], writes=[Res("dbgxn")])

    for kc in range(NDI):
        p.dma("pool", wxp.rearrange("p (k n) -> p k n", n=80)[:, kc],
              d["Wxp"].rearrange("(k p) n -> k p n", p=128)[kc], writes=[R["wxp"]])
    p.dma("sp", convw[:, :], d["convw"][:, :], writes=[R["convw"]])
    p.dma("sp", convb[:, :], d["convb"][:, :], writes=[R["convb"]])

    for co in range(NDI):
        wt = wtile[:, (co % 2) * 1024:(co % 2) * 1024 + 1024]
        r_wt = R["wtile"][co % 2]
        p.dma("pool", wt.rearrange("p (k n) -> p k n", n=128),
              d["Win"].rearrange("(k p) n -> p k n", p=128)[:, :, co * 128:(co + 1) * 128],
              reads=[r_wt], writes=[r_wt])

        def mm_small(e, wt=wt):
            last = None
            for kc in range(NCH):
                last = e.matmul(pb[0][:, 0:3], wt[:, kc * 128:(kc + 1) * 128],
                                xn[:, kc * CW:kc * CW + 3],
                                start=(kc == 0), stop=(kc == NCH - 1))
            return last
        p.emit("pe", mm_small, reads=[r_wt, R["xn"]], writes=[rpb[0]])
        p.emit("dve", lambda e: e.tensor_copy(xcs[:, 0:3], pb[0][:, 0:3]),
               reads=[rpb[0]], writes=[R["xcs"]])
        for tb in range(4):
            s0 = 3 + tb * 512
            bk = tb % 2

            def mm_xc(e, wt=wt, s0=s0, bk=bk):
                last = None
                for kc in range(NCH):
                    last = e.matmul(pb[bk][:, 0:512], wt[:, kc * 128:(kc + 1) * 128],
                                    xn[:, kc * CW + s0:kc * CW + s0 + 512],
                                    start=(kc == 0), stop=(kc == NCH - 1))
                return last
            p.emit("pe", mm_xc, reads=[r_wt, R["xn"]], writes=[rpb[bk]])
            p.emit("dve", lambda e, s0=s0, bk=bk: e.tensor_copy(
                xcs[:, s0:s0 + 512], pb[bk][:, 0:512]),
                reads=[rpb[bk]], writes=[R["xcs"]])
        p.emit("dve", lambda e, co=co: e.tensor_scalar(
            xconv[:, :], xcs[:, 0:2048], convw[:, co * 4:co * 4 + 1], None, ALU.mult),
            reads=[R["xcs"], R["convw"]], writes=[R["xconv"]])
        for j in range(1, 4):
            p.emit("dve", lambda e, co=co, j=j: e.scalar_tensor_tensor(
                xconv[:, :], xcs[:, j:j + 2048], convw[:, co * 4 + j:co * 4 + j + 1],
                xconv[:, :], ALU.mult, ALU.add),
                reads=[R["xcs"], R["convw"], R["xconv"]], writes=[R["xconv"]])
        if co == 0 and "xcs" in dbg:
            p.dma("pool", dbg["xcs"][:, :], xcs[:, :], reads=[R["xcs"]], writes=[Res("dxcs")])
            p.dma("pool", dbg["xconv"][:, :], xconv[:, :], reads=[R["xconv"]], writes=[Res("dxcv")])
        p.emit("act", lambda e, co=co: e.activation(
            xscur[:, :], xconv[:, :], AF.Silu, bias=convb[:, co:co + 1]),
            reads=[R["xconv"], R["convb"]], writes=[R["xscur"]])
        p.dma("sp", d["xs"][co * 128:(co + 1) * 128, :], xscur[:, :],
              reads=[R["xscur"]], writes=[R["d_xs"]])
        for tb in range(4):
            p.emit("pe", lambda e, co=co, tb=tb: e.matmul(
                pb[2 + tb][0:80, 0:512], wxp[:, co * 80:(co + 1) * 80],
                xscur[:, tb * 512:(tb + 1) * 512],
                start=(co == 0), stop=(co == NDI - 1)),
                reads=[R["wxp"], R["xscur"]], writes=[rpb[2 + tb]])
    for tb in range(4):
        p.emit("dve", lambda e, tb=tb: e.tensor_copy(
            xdbl[0:80, tb * 512:(tb + 1) * 512], pb[2 + tb][0:80, 0:512]),
            reads=[rpb[2 + tb]], writes=[R["xdbl"]])

    for zh in range(2):
        for kc in range(NCH):
            p.dma("pool", wz.rearrange("p (k n) -> p k n", n=1024)[:, kc],
                  d["Win"].rearrange("(k p) n -> k p n", p=128)[kc][:, 2048 + zh * 1024:
                                                                   2048 + (zh + 1) * 1024],
                  reads=[R["wz"]], writes=[R["wz"]])
        for tc in range(NSC):
            for zb in range(2):
                bk = zb % 2

                def mm_z(e, tc=tc, zb=zb, bk=bk):
                    last = None
                    for kc in range(NCH):
                        last = e.matmul(pb[bk][:, 0:512],
                                        xn[:, kc * CW + 3 + tc * 128:kc * CW + 3 + (tc + 1) * 128],
                                        wz[:, kc * 1024 + zb * 512:kc * 1024 + (zb + 1) * 512],
                                        start=(kc == 0), stop=(kc == NCH - 1))
                    return last
                p.emit("pe", mm_z, reads=[R["xn"], R["wz"]], writes=[rpb[bk]])
                p.emit("act", lambda e, zh=zh, zb=zb, bk=bk: e.activation(
                    szt[:, zh * 1024 + zb * 512:zh * 1024 + (zb + 1) * 512],
                    pb[bk][:, 0:512], AF.Silu),
                    reads=[rpb[bk]], writes=[R["szt"]])
            p.dma("sp", d["sz"][tc * 128:(tc + 1) * 128, zh * 1024:(zh + 1) * 1024],
                  szt[:, zh * 1024:(zh + 1) * 1024], reads=[R["szt"]], writes=[R["d_sz"]])

    s12.close()

    s36 = ExitStack()
    hrun = s36.enter_context(nc.sbuf_tensor("l0_hrun", [16, 2048], F32, allow_name_mangling=True))
    hin = s36.enter_context(nc.sbuf_tensor("l0_hin", [16, 2048], F32, allow_name_mangling=True))
    htmp = s36.enter_context(nc.sbuf_tensor("l0_htmp", [16, 2048], BF16, allow_name_mangling=True))
    ccs = s36.enter_context(nc.sbuf_tensor("l0_ccs", [16, 2048], F32, allow_name_mangling=True))
    dpc = s36.enter_context(nc.sbuf_tensor("l0_dpc", [16, 40], F32, allow_name_mangling=True))
    mtf = s36.enter_context(nc.sbuf_tensor("l0_mtf", [128, 128], F32, allow_name_mangling=True))
    macc = s36.enter_context(nc.sbuf_tensor("l0_macc", [128, 128], F32, allow_name_mangling=True))
    mtb = s36.enter_context(nc.sbuf_tensor("l0_mtb", [128, 128], BF16, allow_name_mangling=True))
    esc = s36.enter_context(nc.sbuf_tensor("l0_esc", [128, 2 * 128], F32, allow_name_mangling=True))
    qT = s36.enter_context(nc.sbuf_tensor("l0_qT", [16, 128], BF16))
    rtok = s36.enter_context(nc.sbuf_tensor("l0_rtok", [128, 16], BF16, allow_name_mangling=True))
    bbtok = s36.enter_context(nc.sbuf_tensor("l0_bbtok", [128, 16], F32, allow_name_mangling=True))
    xstok = s36.enter_context(nc.sbuf_tensor("l0_xstok", [128, 2 * 2048], BF16, allow_name_mangling=True))
    ytok = s36.enter_context(nc.sbuf_tensor("l0_ytok", [128, 2048], BF16, allow_name_mangling=True))
    cT = s36.enter_context(nc.sbuf_tensor("l0_cT", [16, 2048], F32))
    cTb = s36.enter_context(nc.sbuf_tensor("l0_cTb", [16, 2048], BF16))
    onehot = s36.enter_context(nc.sbuf_tensor("l0_onehot", [16, 16 * 128], BF16, allow_name_mangling=True))
    bpT = s36.enter_context(nc.sbuf_tensor("l0_bpT", [16, 2048], F32))
    for k in ["hrun", "hin", "htmp", "ccs", "dpc", "mtf", "macc", "mtb", "esc",
              "qT", "rtok", "bbtok", "ytok", "cT", "cTb", "bpT", "onehot"]:
        R[k] = fence_res(p, k)
    p.emit("dve", lambda e: e.memset(onehot[:, :], 1.0),
           reads=[R["onehot"]], writes=[R["onehot"]])
    p.emit("pool", lambda e: e.affine_select(onehot[:, :], onehot[:, :],
                                             [[1, 16], [0, 128]], ALU.is_equal, 0.0,
                                             base=0, channel_multiplier=-1),
           reads=[R["onehot"]], writes=[R["onehot"]])
    R["xstok"] = [fence_res(p, "xstok0"), fence_res(p, "xstok1")]

    p.emit("dve", lambda e: e.tensor_copy(cT[:, :], xdbl[64:80, :]),
           reads=[R["xdbl"]], writes=[R["cT"]])
    p.emit("dve", lambda e: e.tensor_copy(bpT[:, :], xdbl[32:48, :]),
           reads=[R["xdbl"]], writes=[R["bpT"]])
    p.emit("dve", lambda e: e.tensor_copy(cTb[:, :], cT[:, :]),
           reads=[R["cT"]], writes=[R["cTb"]])
    sp1 = rows[0:1, RO_SP1:RO_SP1 + 2048]
    p.emit("act", lambda e: e.activation(sp1, xdbl[0:1, 0:2048], AF.Abs),
           reads=[R["xdbl"]], writes=[R["rows"]])
    p.emit("act", lambda e: e.activation(sp1, sp1, AF.Exp, bias=0.0, scale=-1.0),
           reads=[R["rows"]], writes=[R["rows"]])
    p.emit("act", lambda e: e.activation(sp1, sp1, AF.Ln, bias=1.0),
           reads=[R["rows"]], writes=[R["rows"]])
    p.emit("dve", lambda e: e.tensor_scalar(dl, xdbl[0:1, 0:2048], 0.0, None, ALU.max),
           reads=[R["xdbl"]], writes=[R["rows"]])
    p.emit("dve", lambda e: e.tensor_add(dl, dl, sp1),
           reads=[R["rows"]], writes=[R["rows"]])
    p.emit("dve", lambda e: e.tensor_tensor_scan(cd, dl, dl, 0.0, ALU.add, ALU.max),
           reads=[R["rows"]], writes=[R["rows"]])
    for tc in range(NSC):
        def mk(e, tc=tc):
            dst = rows[0:1, RO_CTOT + tc:RO_CTOT + tc + 1]
            if tc == 0:
                return e.tensor_copy(dst, cd[0:1, CL - 1:CL])
            return e.tensor_sub(dst, cd[0:1, (tc + 1) * CL - 1:(tc + 1) * CL],
                                cd[0:1, tc * CL - 1:tc * CL])
        p.emit("dve", mk, reads=[R["rows"]], writes=[R["rows"]])
    p.emit("pe", lambda e: e.matmul(pb[6][0:16, 0:16], arow[:, :],
                                    rows[0:1, RO_CTOT:RO_CTOT + 16], start=True, stop=True),
           reads=[R["arow"], R["rows"]], writes=[rpb[6]])
    p.emit("act", lambda e: e.activation(dpc[:, 0:16], pb[6][0:16, 0:16], AF.Exp),
           reads=[rpb[6]], writes=[R["dpc"]])
    PO = 20
    p.emit("dve", lambda e: e.memset(dpc[:, PO:PO + 1], 1.0),
           reads=[R["dpc"]], writes=[R["dpc"]])
    for tc in range(1, NSC):
        p.emit("dve", lambda e, tc=tc: e.tensor_mul(
            dpc[:, PO + tc:PO + tc + 1], dpc[:, PO + tc - 1:PO + tc], dpc[:, tc - 1:tc]),
            reads=[R["dpc"]], writes=[R["dpc"]])

    def chunk_prep(tc, xst, r_xst):
        p.dma("sp", xst, d["xs"][:, tc * 128:(tc + 1) * 128], transpose=True,
              reads=[R["d_xs"]], writes=[r_xst])
        p.emit("pe", lambda e, tc=tc: e.transpose(pb[6][0:128, 0:16],
                                                  bpT[:, tc * 128:(tc + 1) * 128],
                                                  C.ident[0:16, 0:16]),
               reads=[R["bpT"], C.r["ident"]], writes=[rpb[6]])
        p.emit("pe", lambda e, tc=tc: e.transpose(pb[7][0:128, 0:1],
                                                  dl[0:1, tc * 128:(tc + 1) * 128],
                                                  C.ident[0:1, 0:1]),
               reads=[R["rows"], C.r["ident"]], writes=[rpb[7]])
        p.emit("dve", lambda e: e.tensor_copy(scr[:, 0:1], pb[7][0:128, 0:1]),
               reads=[rpb[7]], writes=[R["scr"]])
        p.emit("dve", lambda e: e.tensor_scalar(bbtok[:, :], pb[6][0:128, 0:16],
                                                scr[:, 0:1], None, ALU.mult),
               reads=[rpb[6], R["scr"]], writes=[R["bbtok"]])

    def chunk_scan_step(tc, xst, r_xst):
        p.emit("dve", lambda e, tc=tc: e.tensor_scalar(
            rows[0:1, RO_CDREV:RO_CDREV + 128], cd[0:1, tc * 128:(tc + 1) * 128],
            -1.0, cd[0:1, (tc + 1) * 128 - 1:(tc + 1) * 128], ALU.mult, ALU.add),
            reads=[R["rows"]], writes=[R["rows"]])
        p.emit("pe", lambda e: e.matmul(pb[6][0:128, 0:16],
                                        rows[0:1, RO_CDREV:RO_CDREV + 128], arow[:, :],
                                        start=True, stop=True),
               reads=[R["rows"], R["arow"]], writes=[rpb[6]])
        p.emit("act", lambda e: e.activation(esc[:, 0:16], pb[6][0:128, 0:16], AF.Exp),
               reads=[rpb[6]], writes=[R["esc"]])
        p.emit("dve", lambda e: e.tensor_mul(rtok[:, :], esc[:, 0:16], bbtok[:, :]),
               reads=[R["esc"], R["bbtok"]], writes=[R["rtok"]])
        for dbk in range(4):
            p.emit("pe", lambda e, dbk=dbk: e.matmul(
                pb[2 + dbk][0:16, 0:512], rtok[:, :], xst[:, dbk * 512:(dbk + 1) * 512],
                start=True, stop=True),
                reads=[R["rtok"], r_xst], writes=[rpb[2 + dbk]])
        for dbk in range(4):
            p.emit("dve", lambda e, tc=tc, dbk=dbk: e.scalar_tensor_tensor(
                hrun[:, dbk * 512:(dbk + 1) * 512], hrun[:, dbk * 512:(dbk + 1) * 512],
                dpc[:, tc:tc + 1], pb[2 + dbk][0:16, 0:512], ALU.mult, ALU.add),
                reads=[R["hrun"], R["dpc"], rpb[2 + dbk]], writes=[R["hrun"]])

    p.emit("dve", lambda e: e.memset(hrun[:, :], 0.0),
           reads=[R["hrun"]], writes=[R["hrun"]])
    for tc in range(NSC):
        xst = xstok[:, (tc % 2) * 2048:(tc % 2) * 2048 + 2048]
        chunk_prep(tc, xst, R["xstok"][tc % 2])
        chunk_scan_step(tc, xst, R["xstok"][tc % 2])

    if "xdbl" in dbg:
        p.dma("sp", dbg["xdbl"][:, :], xdbl[:, :], reads=[R["xdbl"]], writes=[Res("dbg1")])
        p.dma("sp", dbg["rows"][:, :], rows[:, :], reads=[R["rows"]], writes=[Res("dbg2")])
        p.dma("sp", dbg["hrun"][:, :], hrun[:, :], reads=[R["hrun"]], writes=[Res("dbg3")])

    p.dma("sp", d["ccin"][0:16, :], hrun[:, :], reads=[R["hrun"]], writes=[R["d_ccin"]])
    p.emit("pool", lambda e: e.collective_compute(
        "AllGather", ALU.bypass, replica_groups=[[2 * i, 2 * i + 1] for i in range(4)],
        ins=[d["ccin"].opt()], outs=[d["ccout"].opt()]),
        reads=[R["d_ccin"]], writes=[R["d_ccout"]], inc=1)
    collective_barrier(p, d, scr[0:16, 0:16], R["scr"], R["d_ccout"])
    p.dma("sp", ccs[:, :], d["ccout"][0:16, :], reads=[R["d_ccout"]], writes=[R["ccs"]])
    p.emit("dve", lambda e: e.tensor_scalar(hin[:, :], ccs[:, :], flagc[:, :], None, ALU.mult),
           reads=[R["ccs"], R["flagc"]], writes=[R["hin"]])

    p.emit("dve", lambda e: e.memset(hrun[:, :], 0.0), reads=[R["hrun"]], writes=[R["hrun"]])
    for tc in range(NSC):
        xst = xstok[:, (tc % 2) * 2048:(tc % 2) * 2048 + 2048]
        r_xst = R["xstok"][tc % 2]
        chunk_prep(tc, xst, r_xst)
        p.dma("sp", szt[:, :], d["sz"][tc * 128:(tc + 1) * 128, :],
              reads=[R["d_sz"], R["szt"]], writes=[R["szt"]])
        p.emit("dve", lambda e, tc=tc: e.scalar_tensor_tensor(
            htmp[:, :], hin[:, :], dpc[:, PO + tc:PO + tc + 1], hrun[:, :],
            ALU.mult, ALU.add),
            reads=[R["hin"], R["dpc"], R["hrun"]], writes=[R["htmp"]])
        p.emit("dve", lambda e, tc=tc: e.tensor_scalar(
            rows[0:1, RO_NEGCD:RO_NEGCD + 128], cd[0:1, tc * 128:(tc + 1) * 128],
            -1.0, None, ALU.mult),
            reads=[R["rows"]], writes=[R["rows"]])

        def mmdt(e, tc=tc):
            e.matmul(pb[6][:, 0:128], C.onesrow_f[0:1, :], cd[0:1, tc * 128:(tc + 1) * 128],
                     start=True, stop=False)
            return e.matmul(pb[6][:, 0:128], rows[0:1, RO_NEGCD:RO_NEGCD + 128],
                            C.onesrow_f[0:1, :], start=False, stop=True)
        p.emit("pe", mmdt, reads=[R["rows"], C.r["onesrow_f"]], writes=[rpb[6]])
        p.emit("dve", lambda e: e.tensor_add(mtf[:, :], pb[6][:, 0:128], C.minf[:, :]),
               reads=[rpb[6], C.r["minf"]], writes=[R["mtf"]])
        for n in range(DS):
            eb = esc[:, (n % 2) * 128:(n % 2) * 128 + 128]
            bk = 6 + (n % 2)
            p.emit("pe", lambda e, n=n, tc=tc, bk=bk: e.matmul(
                pb[bk][:, 0:128], onehot[:, n * 128:(n + 1) * 128],
                cTb[:, tc * 128:(tc + 1) * 128], start=True, stop=True),
                reads=[R["onehot"], R["cTb"]], writes=[rpb[bk]])
            p.emit("act", lambda e, n=n, eb=eb: e.activation(
                eb, mtf[:, :], AF.Exp, bias=0.0, scale=abc[:, n:n + 1]),
                reads=[R["mtf"], R["abc"]], writes=[R["esc"]])
            p.emit("dve", lambda e, eb=eb, bk=bk: e.tensor_mul(eb, eb, pb[bk][:, 0:128]),
                   reads=[R["esc"], rpb[bk]], writes=[R["esc"]])
            if n == 0:
                p.emit("dve", lambda e, n=n, eb=eb: e.tensor_scalar(
                    macc[:, :], eb, bbtok[:, n:n + 1], None, ALU.mult),
                    reads=[R["esc"], R["bbtok"]], writes=[R["macc"]])
            elif n < DS - 1:
                p.emit("dve", lambda e, n=n, eb=eb: e.scalar_tensor_tensor(
                    macc[:, :], eb, bbtok[:, n:n + 1], macc[:, :], ALU.mult, ALU.add),
                    reads=[R["esc"], R["bbtok"], R["macc"]], writes=[R["macc"]])
            else:
                p.emit("dve", lambda e, n=n, eb=eb: e.scalar_tensor_tensor(
                    mtb[:, :], eb, bbtok[:, n:n + 1], macc[:, :], ALU.mult, ALU.add),
                    reads=[R["esc"], R["bbtok"], R["macc"]], writes=[R["mtb"]])

        def mkcdc(e, tc=tc):
            dst = rows[0:1, RO_CDC:RO_CDC + 128]
            if tc == 0:
                return e.tensor_copy(dst, cd[0:1, 0:128])
            return e.tensor_scalar(dst, cd[0:1, tc * 128:(tc + 1) * 128], 1.0,
                                   cd[0:1, tc * 128 - 1:tc * 128], ALU.mult, ALU.subtract)
        p.emit("dve", mkcdc, reads=[R["rows"]], writes=[R["rows"]])
        p.emit("pe", lambda e: e.matmul(pb[6][0:16, 0:128], arow[:, :],
                                        rows[0:1, RO_CDC:RO_CDC + 128], start=True, stop=True),
               reads=[R["arow"], R["rows"]], writes=[rpb[6]])
        p.emit("act", lambda e: e.activation(esc[0:16, 0:128], pb[6][0:16, 0:128], AF.Exp),
               reads=[rpb[6]], writes=[R["esc"]])
        p.emit("dve", lambda e, tc=tc: e.tensor_mul(
            qT[:, :], esc[0:16, 0:128], cT[:, tc * 128:(tc + 1) * 128]),
            reads=[R["esc"], R["cT"]], writes=[R["qT"]])
        for dbk in range(4):
            def mmy(e, dbk=dbk):
                e.matmul(pb[2 + dbk][:, 0:512], mtb[:, :], xst[:, dbk * 512:(dbk + 1) * 512],
                         start=True, stop=False)
                return e.matmul(pb[2 + dbk][:, 0:512], qT[:, :],
                                htmp[:, dbk * 512:(dbk + 1) * 512], start=False, stop=True)
            p.emit("pe", mmy, reads=[R["mtb"], R["qT"], r_xst, R["htmp"]],
                   writes=[rpb[2 + dbk]])
        for dbk in range(4):
            p.emit("dve", lambda e, dbk=dbk: e.tensor_mul(
                scr[:, 0:512], xst[:, dbk * 512:(dbk + 1) * 512],
                dbc[:, dbk * 512:(dbk + 1) * 512]),
                reads=[r_xst, R["dbc"]], writes=[R["scr"]])
            p.emit("dve", lambda e, dbk=dbk: e.tensor_add(
                scr[:, 0:512], scr[:, 0:512], pb[2 + dbk][:, 0:512]),
                reads=[R["scr"], rpb[2 + dbk]], writes=[R["scr"]])
            p.emit("dve", lambda e, dbk=dbk: e.tensor_mul(
                ytok[:, dbk * 512:(dbk + 1) * 512], scr[:, 0:512],
                szt[:, dbk * 512:(dbk + 1) * 512]),
                reads=[R["scr"], R["szt"]], writes=[R["ytok"]])
        p.dma("sp", d["y"][tc * 128:(tc + 1) * 128, :], ytok[:, :],
              reads=[R["ytok"]], writes=[R["d_y"]])
        chunk_scan_step(tc, xst, r_xst)

    s36.close()

    s7 = ExitStack()
    yT = s7.enter_context(nc.sbuf_tensor("l0_yT", [128, NDI * 2048], BF16))
    wv2 = s7.enter_context(nc.sbuf_tensor("l0_wv2", [128, 2 * 2048], BF16, allow_name_mangling=True))
    R["yT"] = fence_res(p, "yT")
    R["wv2"] = [fence_res(p, "wv20"), fence_res(p, "wv21")]
    for kc in range(NDI):
        p.dma("sp", yT[:, kc * 2048:(kc + 1) * 2048],
              d["y"][:, kc * 128:(kc + 1) * 128], transpose=True,
              reads=[R["d_y"]], writes=[R["yT"]])
    for do in range(NCH):
        wv = wv2[:, (do % 2) * 2048:(do % 2) * 2048 + 2048]
        r_wv = R["wv2"][do % 2]
        p.dma("pool", wv.rearrange("p (k n) -> p k n", n=128),
              d["Wout"].rearrange("(k p) n -> p k n", p=128)[:, :, do * 128:(do + 1) * 128],
              reads=[r_wv], writes=[r_wv])
        for tb in range(4):
            bk = tb % 2

            def mmo(e, do=do, tb=tb, bk=bk, wv=wv):
                last = None
                for kc in range(NDI):
                    last = e.matmul(pb[bk][:, 0:512], wv[:, kc * 128:(kc + 1) * 128],
                                    yT[:, kc * 2048 + tb * 512:kc * 2048 + (tb + 1) * 512],
                                    start=(kc == 0), stop=(kc == NDI - 1))
                return last
            p.emit("pe", mmo, reads=[r_wv, R["yT"]], writes=[rpb[bk]])
            p.emit("dve", lambda e, do=do, tb=tb, bk=bk: e.tensor_add(
                xres[:, do * 2048 + tb * 512:do * 2048 + (tb + 1) * 512],
                xres[:, do * 2048 + tb * 512:do * 2048 + (tb + 1) * 512],
                pb[bk][:, 0:512]),
                reads=[rpb[bk], r_xres], writes=[r_xres])
    s7.close()
    ctx.close()



def feat_norm(p, nc, C, xres, r_xres, xnb, r_xnb, rows, r_rows, scr, r_scr,
              scrb, r_scrb, pb, rpb, mode, wcol=None, r_wcol=None):
    for c in range(NCH):
        p.emit("dve", lambda e, c=c: e.tensor_copy(
            xnb[:, c * 2048:(c + 1) * 2048], xres[:, c * 2048:(c + 1) * 2048]),
            reads=[r_xres], writes=[r_xnb])
    isln = mode in ("ln", "lnip")
    for s0 in range(0, 2048, 512):
        if isln:
            def mmsum(e, s0=s0):
                last = None
                for c in range(NCH):
                    last = e.matmul(pb[5][0:1, 0:512], C.onescol_b[:, :],
                                    xnb[:, c * 2048 + s0:c * 2048 + s0 + 512],
                                    start=(c == 0), stop=(c == NCH - 1))
                return last
            p.emit("pe", mmsum, reads=[r_xnb, C.r["onescol_b"]], writes=[rpb[5]])
        for c in range(NCH):
            p.emit("act", lambda e, c=c, s0=s0: e.activation(
                scrb[:, 0:512], xnb[:, c * 2048 + s0:c * 2048 + s0 + 512], AF.Square),
                reads=[r_xnb], writes=[r_scrb])
            p.emit("pe", lambda e, c=c: e.matmul(
                pb[4][0:1, 0:512], C.onescol_b[:, :], scrb[:, 0:512],
                start=(c == 0), stop=(c == NCH - 1)),
                reads=[r_scrb, C.r["onescol_b"]], writes=[rpb[4]])
        if isln:
            p.emit("dve", lambda e, s0=s0: e.tensor_scalar(
                rows[0:1, s0:s0 + 512], pb[5][0:1, 0:512], 1.0 / DM, None, ALU.mult),
                reads=[rpb[5]], writes=[r_rows])
        p.emit("dve", lambda e, s0=s0: e.tensor_scalar(
            rows[0:1, 2048 + s0:2048 + s0 + 512], pb[4][0:1, 0:512], 1.0 / DM, None,
            ALU.mult),
            reads=[rpb[4]], writes=[r_rows])
    mu = rows[0:1, 0:2048]
    rs = rows[0:1, 2048:4096]
    if isln:
        p.emit("act", lambda e: e.activation(rows[0:1, 4096:6144], mu, AF.Square),
               reads=[r_rows], writes=[r_rows])
        p.emit("dve", lambda e: e.tensor_sub(rs, rs, rows[0:1, 4096:6144]),
               reads=[r_rows], writes=[r_rows])
        epsap, repcs = C.eps1[0:1, 0:1], [C.r["eps1"]]
    else:
        epsap, repcs = C.eps2[0:1, 0:1], [C.r["eps2"]]
    p.emit("act", lambda e: e.activation(rs, rs, AF.Ln, bias=epsap),
           reads=[r_rows] + repcs, writes=[r_rows])
    p.emit("act", lambda e: e.activation(rs, rs, AF.Exp, bias=0.0, scale=-0.5),
           reads=[r_rows], writes=[r_rows])
    if isln:
        p.emit("dve", lambda e: e.tensor_mul(rows[0:1, 4096:6144], mu, rs),
               reads=[r_rows], writes=[r_rows])
    for s0 in range(0, 2048, 512):
        bcast_row(p, C, pb[5], rpb[5], rs[0:1, s0:s0 + 512], r_rows, 512)
        if isln:
            bcast_row(p, C, pb[4], rpb[4], rows[0:1, 4096 + s0:4096 + s0 + 512],
                      r_rows, 512)
        for c in range(NCH):
            if mode == "lnip":
                dst, r_dst = xres[:, c * 2048 + s0:c * 2048 + s0 + 512], r_xres
            else:
                dst, r_dst = xnb[:, c * 2048 + s0:c * 2048 + s0 + 512], r_xnb
            p.emit("dve", lambda e, c=c, s0=s0: e.tensor_mul(
                scr[:, 0:512], xres[:, c * 2048 + s0:c * 2048 + s0 + 512],
                pb[5][:, 0:512]),
                reads=[r_xres, rpb[5]], writes=[r_scr])
            if isln:
                p.emit("dve", lambda e, dst=dst: e.tensor_sub(dst, scr[:, 0:512],
                                                              pb[4][:, 0:512]),
                       reads=[r_scr, rpb[4]], writes=[r_dst])
            else:
                p.emit("dve", lambda e, c=c, dst=dst: e.tensor_scalar(
                    dst, scr[:, 0:512], wcol[:, c:c + 1], None, ALU.mult),
                    reads=[r_scr, r_wcol], writes=[r_dst])


def build_attn_block(p, nc, ctx, C, xres, r_xres, d, kind):
    import math as _m
    sa = ExitStack()
    rows = sa.enter_context(nc.sbuf_tensor("at_rows", [1, 2560], F32, allow_name_mangling=True))
    scr = sa.enter_context(nc.sbuf_tensor("at_scr", [128, 512], F32, allow_name_mangling=True))
    scrb = sa.enter_context(nc.sbuf_tensor("at_scrb", [128, 512], BF16, allow_name_mangling=True))
    qw = sa.enter_context(nc.sbuf_tensor("at_qw", [128, 4096], BF16, allow_name_mangling=True))
    kw = sa.enter_context(nc.sbuf_tensor("at_kw", [128, 4096], BF16, allow_name_mangling=True))
    vw = sa.enter_context(nc.sbuf_tensor("at_vw", [128, 32 * 130], BF16, allow_name_mangling=True))
    wt3 = sa.enter_context(nc.sbuf_tensor("at_wt3", [128, 3 * 1024], BF16, allow_name_mangling=True))
    wcol = sa.enter_context(nc.sbuf_tensor("at_wcol", [128, 8], F32, allow_name_mangling=True))
    bcols = sa.enter_context(nc.sbuf_tensor("at_bcols", [128, 16], F32, allow_name_mangling=True))
    bvbc = sa.enter_context(nc.sbuf_tensor("at_bvbc", [128, 128], F32, allow_name_mangling=True))
    abf = sa.enter_context(nc.sbuf_tensor("at_abf", [128, 4 * 512], BF16, allow_name_mangling=True))
    rbc = sa.enter_context(nc.sbuf_tensor("at_rbc", [64, 512], F32, allow_name_mangling=True))
    obf = sa.enter_context(nc.sbuf_tensor("at_obf", [64, 512], BF16, allow_name_mangling=True))
    m512 = sa.enter_context(nc.sbuf_tensor("at_m512", [128, 896], BF16, allow_name_mangling=True))
    fsel = sa.enter_context(nc.sbuf_tensor("at_fsel", [128, 2], F32, allow_name_mangling=True))
    gcol = sa.enter_context(nc.sbuf_tensor("at_gcol", [128, 1], F32, allow_name_mangling=True))
    pb = [psumt(nc, sa, f"at_pb{i}", [128, 512]) for i in range(8)]
    rpb = [fence_res(p, f"at_pb{i}") for i in range(8)]
    N = ["rows", "scr", "scrb", "qw", "kw", "vw", "wcol", "bcols",
         "bvbc", "rbc", "obf", "m512", "fsel",
         "gcol", "d_agi", "d_ago", "d_oag", "d_oago", "d_mx"]
    R = {k: fence_res(p, k) for k in N}
    R["wt3"] = [fence_res(p, f"wt3_{i}") for i in range(3)]
    Rabf = [fence_res(p, f"abf{i}") for i in range(4)]

    p.emit("dve", lambda e: e.memset(m512[:, :], 1.0), writes=[R["m512"]])
    p.emit("pool", lambda e: e.affine_select(
        m512[:, :], m512[:, :], [[1, 896]], ALU.is_ge, 0.0,
        base=-384, channel_multiplier=-1),
        reads=[R["m512"]], writes=[R["m512"]])
    p.dma("sp", rows[0:1, 0:1], d["flag"][:, :], writes=[R["rows"]])
    p.emit("pe", lambda e: e.matmul(pb[7][0:128, 0:1], C.onesrow_f[0:1, :],
                                    rows[0:1, 0:1], start=True, stop=True),
           reads=[R["rows"], C.r["onesrow_f"]], writes=[rpb[7]])
    p.emit("dve", lambda e: e.tensor_copy(fsel[:, 1:2], pb[7][0:128, 0:1]),
           reads=[rpb[7]], writes=[R["fsel"]])
    p.emit("dve", lambda e: e.tensor_scalar(fsel[:, 0:1], fsel[:, 1:2], -1.0, 1.0,
                                            ALU.mult, ALU.add),
           reads=[R["fsel"]], writes=[R["fsel"]])

    pfx = "l1a" if kind == "l1attn" else "l1f"
    if kind == "l1ffn":
        p.dma("sp", wcol[:, :], d[pfx + "_n1w"][:, :], writes=[R["wcol"]])
        p.dma("sp", bcols[:, 0:1], d[pfx + "_qnw"][:, :], writes=[R["bcols"]])
        p.dma("sp", bcols[:, 1:2], d[pfx + "_knw"][:, :], writes=[R["bcols"]])
    else:
        p.dma("sp", bcols[:, 0:8], d[pfx + "_bqk"][:, :], writes=[R["bcols"]])
        p.dma("sp", bcols[:, 8:16], d[pfx + "_bo"][:, :], writes=[R["bcols"]])

    sn = ExitStack()
    xnb = sn.enter_context(nc.sbuf_tensor("at_xnb", [128, NCH * 2048], BF16, allow_name_mangling=True))
    rows6 = sn.enter_context(nc.sbuf_tensor("at_rows6", [1, 6144], F32, allow_name_mangling=True))
    R["xnb"] = fence_res(p, "xnb")
    R["rows6"] = fence_res(p, "rows6")
    if kind == "l1ffn":
        R["d_xsp"] = Res("d_xsp")
        for c in range(NCH):
            p.dma("sp", d["xsp"][c * 128:(c + 1) * 128, :],
                  xres[:, c * 2048:(c + 1) * 2048], reads=[r_xres], writes=[R["d_xsp"]])
        feat_norm(p, nc, C, xres, r_xres, xnb, R["xnb"], rows6, R["rows6"], scr,
                  R["scr"], scrb, R["scrb"], pb, rpb, "lnip")
    feat_norm(p, nc, C, xres, r_xres, xnb, R["xnb"], rows6, R["rows6"], scr, R["scr"],
              scrb, R["scrb"], pb, rpb,
              "ln" if kind == "l1attn" else "rms", wcol, R["wcol"])

    for c in range(NCH):
        p.dma("sp", d["agi"][c * 128:(c + 1) * 128, :], xnb[:, c * 2048:(c + 1) * 2048],
              reads=[R["xnb"]], writes=[R["d_agi"]])
    p.emit("pool", lambda e: e.collective_compute(
        "AllGather", ALU.bypass, replica_groups=[[2 * i, 2 * i + 1] for i in range(4)],
        ins=[d["agi"].opt()], outs=[d["ago"].opt()]),
        reads=[R["d_agi"]], writes=[R["d_ago"]], inc=1)
    collective_barrier(p, d, scr[0:16, 0:16], R["scr"], R["d_ago"])
    sn.close()
    xnf = sa.enter_context(nc.sbuf_tensor("at_xnf", [128, NCH * 4096], BF16, allow_name_mangling=True))
    R["xnf"] = fence_res(p, "xnf")
    for c in range(NCH):
        for rr in range(2):
            p.dma("sp", xnf[:, c * 4096 + rr * 2048:c * 4096 + (rr + 1) * 2048],
                  d["ago"][rr * 1024 + c * 128:rr * 1024 + (c + 1) * 128, :],
                  reads=[R["d_ago"]], writes=[R["xnf"]])

    if kind == "l1attn":
        cosb = sa.enter_context(nc.sbuf_tensor("at_cos", [128, 4096], BF16, allow_name_mangling=True))
        sinb = sa.enter_context(nc.sbuf_tensor("at_sin", [128, 4096], BF16, allow_name_mangling=True))
        pmb = sa.enter_context(nc.sbuf_tensor("at_pmb", [128, 128], BF16, allow_name_mangling=True))
        sr = ExitStack()
        angf = sr.enter_context(nc.sbuf_tensor("at_ang", [128, 512], F32, allow_name_mangling=True))
        angi = sr.enter_context(nc.sbuf_tensor("at_angi", [128, 512], mybir.dt.int32, allow_name_mangling=True))
        for k in ["cosb", "sinb", "angf", "angi", "pmb"]:
            R[k] = fence_res(p, k)
        p.dma("pool", pmb[:, :], d["ropePM"][:, :], writes=[R["pmb"]])
        p.dma("sp", rows[0:1, 1024:1536], d[pfx + "_bvrow"][0:1, :],
              reads=[R["rows"]], writes=[R["rows"]])
        p.emit("pool", lambda e: e.iota(rows[0:1, 1536:1600], [[1, 64]], base=0,
                                        channel_multiplier=0,
                                        allow_small_or_imprecise_dtypes=True),
               reads=[R["rows"]], writes=[R["rows"]])
        p.emit("dve", lambda e: e.tensor_scalar(rows[0:1, 1600:1664], rows[0:1, 1536:1600],
                                                32.0, None, ALU.is_ge),
               reads=[R["rows"]], writes=[R["rows"]])
        p.emit("dve", lambda e: e.scalar_tensor_tensor(rows[0:1, 1536:1600],
                                                       rows[0:1, 1600:1664], -32.0,
                                                       rows[0:1, 1536:1600], ALU.mult,
                                                       ALU.add),
               reads=[R["rows"]], writes=[R["rows"]])
        p.emit("act", lambda e: e.activation(rows[0:1, 1536:1600], rows[0:1, 1536:1600],
                                             AF.Exp, bias=0.0,
                                             scale=-2.0 * _m.log(10000.0) / 64.0),
               reads=[R["rows"]], writes=[R["rows"]])

        TWO_PI = 2.0 * _m.pi
        for half in range(2):
            dst, rdst = (sinb, R["sinb"]) if half == 0 else (cosb, R["cosb"])
            for tb in range(8):
                p.emit("pool", lambda e, tb=tb: e.iota(
                    rows[0:1, 0:512], [[1, 512]], base=tb * 512 + 1,
                    channel_multiplier=0, allow_small_or_imprecise_dtypes=True),
                    reads=[R["rows"]], writes=[R["rows"]])
                p.emit("pe", lambda e, tb=tb: e.matmul(
                    pb[6][0:64, 0:512], rows[0:1, 1536:1600], rows[0:1, 0:512],
                    start=True, stop=True),
                    reads=[R["rows"]], writes=[rpb[6]])
                p.emit("dve", lambda e, half=half: e.tensor_scalar(
                    angf[0:64, :], pb[6][0:64, 0:512], 1.0,
                    _m.pi / 2.0 if half else 0.0, ALU.mult, ALU.add),
                    reads=[rpb[6]], writes=[R["angf"]])
                p.emit("dve", lambda e: e.tensor_scalar(
                    scr[0:64, 0:512], pb[6][0:64, 0:512], 1.0 / TWO_PI,
                    0.25 if half else 0.0, ALU.mult, ALU.add),
                    reads=[rpb[6], R["scr"]], writes=[R["scr"]])
                p.emit("dve", lambda e: e.tensor_copy(angi[0:64, :], scr[0:64, 0:512]),
                       reads=[R["scr"]], writes=[R["angi"]])
                p.emit("dve", lambda e: e.tensor_copy(scr[0:64, 0:512], angi[0:64, :]),
                       reads=[R["angi"]], writes=[R["scr"]])
                p.emit("dve", lambda e: e.scalar_tensor_tensor(
                    angf[0:64, :], scr[0:64, 0:512], -TWO_PI, angf[0:64, :],
                    ALU.mult, ALU.add),
                    reads=[R["scr"], R["angf"]], writes=[R["angf"]])
                p.emit("act", lambda e, dst=dst, tb=tb: e.activation(
                    dst[0:64, tb * 512:(tb + 1) * 512], angf[0:64, :], AF.Sin),
                    reads=[R["angf"]], writes=[rdst])
            p.emit("dve", lambda e, dst=dst: e.tensor_copy(dst[64:128, :], dst[0:64, :]),
                   reads=[rdst], writes=[rdst])
        sr.close()
    else:
        hsel2 = sa.enter_context(nc.sbuf_tensor("at_hsel2", [128, 2], BF16, allow_name_mangling=True))
        sel64 = sa.enter_context(nc.sbuf_tensor("at_sel64", [2, 128], F32, allow_name_mangling=True))
        sqrow = sa.enter_context(nc.sbuf_tensor("at_sqrow", [2, 512], F32, allow_name_mangling=True))
        for k in ["hsel2", "sel64", "sqrow"]:
            R[k] = fence_res(p, k)
        p.emit("dve", lambda e: e.memset(hsel2[:, :], 1.0), writes=[R["hsel2"]])
        p.emit("pool", lambda e: e.affine_select(hsel2[:, :], hsel2[:, :],
                                                 [[-64, 2]], ALU.is_ge, 0.0,
                                                 base=0, channel_multiplier=1),
               reads=[R["hsel2"]], writes=[R["hsel2"]])
        p.emit("pool", lambda e: e.affine_select(hsel2[:, :], hsel2[:, :],
                                                 [[64, 2]], ALU.is_ge, 0.0,
                                                 base=63, channel_multiplier=-1),
               reads=[R["hsel2"]], writes=[R["hsel2"]])
        p.emit("dve", lambda e: e.memset(sel64[:, :], 1.0), writes=[R["sel64"]])
        p.emit("pool", lambda e: e.affine_select(sel64[:, :], sel64[:, :],
                                                 [[1, 128]], ALU.is_ge, 0.0,
                                                 base=0, channel_multiplier=-64),
               reads=[R["sel64"]], writes=[R["sel64"]])
        p.emit("pool", lambda e: e.affine_select(sel64[:, :], sel64[:, :],
                                                 [[-1, 128]], ALU.is_ge, 0.0,
                                                 base=63, channel_multiplier=64),
               reads=[R["sel64"]], writes=[R["sel64"]])

    SCALE = 1.0 / 8.0
    for w in range(4):
        for i, nm in enumerate(["wq", "wk", "wv"]):
            p.dma("pool", wt3[:, i * 1024:(i + 1) * 1024].rearrange("p (k n) -> p k n", n=128),
                  d[pfx + "_" + nm].rearrange("(k p) n -> p k n", p=128)[:, :, w * 128:(w + 1) * 128],
                  reads=[R["wt3"][i]], writes=[R["wt3"][i]])
        for tb in range(8):
            for i, (dst, rdst) in enumerate([(qw, R["qw"]), (kw, R["kw"])]):
                bk = i % 2

                def mmqk(e, i=i, tb=tb, bk=bk):
                    last = None
                    for kc in range(NCH):
                        last = e.matmul(pb[bk][:, 0:512],
                                        wt3[:, i * 1024 + kc * 128:i * 1024 + (kc + 1) * 128],
                                        xnf[:, kc * 4096 + tb * 512:kc * 4096 + (tb + 1) * 512],
                                        start=(kc == 0), stop=(kc == NCH - 1))
                    return last
                p.emit("pe", mmqk, reads=[R["wt3"][i], R["xnf"]], writes=[rpb[bk]])
                if kind == "l1attn":
                    p.emit("dve", lambda e, dst=dst, tb=tb, bk=bk, i=i, w=w: e.tensor_scalar(
                        dst[:, tb * 512:(tb + 1) * 512], pb[bk][:, 0:512],
                        bcols[:, i * 4 + w:i * 4 + w + 1], None, ALU.add),
                        reads=[rpb[bk], R["bcols"]], writes=[rdst])
                else:
                    p.emit("act", lambda e, dst=dst, tb=tb, bk=bk: e.activation(
                        dst[:, tb * 512:(tb + 1) * 512], pb[bk][:, 0:512], AF.Copy),
                        reads=[rpb[bk]], writes=[rdst])
        p.emit("dve", lambda e: e.memset(vw[:, :], 1.0), reads=[R["vw"]], writes=[R["vw"]])
        if kind == "l1attn":
            bcast_row(p, C, pb[5], rpb[5], rows[0:1, 1024 + w * 128:1024 + (w + 1) * 128],
                      R["rows"], 128)
            p.emit("dve", lambda e: e.tensor_copy(bvbc[:, :], pb[5][:, 0:128]),
                   reads=[rpb[5]], writes=[R["bvbc"]])
        for kt in range(32):
            def mmv(e, kt=kt, w=w):
                last = None
                for kc in range(NCH):
                    last = e.matmul(pb[2][:, 0:128],
                                    xnf[:, kc * 4096 + kt * 128:kc * 4096 + (kt + 1) * 128],
                                    wt3[:, 2 * 1024 + kc * 128:2 * 1024 + (kc + 1) * 128],
                                    start=(kc == 0), stop=(kc == NCH - 1))
                return last
            p.emit("pe", mmv, reads=[R["wt3"][2], R["xnf"]], writes=[rpb[2]])
            if kind == "l1attn":
                p.emit("dve", lambda e, kt=kt: e.tensor_add(scrb[:, 0:128],
                                                            pb[2][:, 0:128], bvbc[:, :]),
                       reads=[rpb[2], R["bvbc"]], writes=[R["scrb"]])
            else:
                p.emit("dve", lambda e, kt=kt: e.tensor_copy(scrb[:, 0:128],
                                                             pb[2][:, 0:128]),
                       reads=[rpb[2]], writes=[R["scrb"]])
            for hw in range(2):
                p.emit("dve", lambda e, kt=kt, hw=hw: e.tensor_copy(
                    vw[:, kt * 130 + hw * 65:kt * 130 + hw * 65 + 64],
                    scrb[:, hw * 64:hw * 64 + 64]),
                    reads=[R["scrb"]], writes=[R["vw"]])
        if kind == "l1attn":
            for (tw, rtw) in [(qw, R["qw"]), (kw, R["kw"])]:
                for tb in range(8):
                    p.emit("pe", lambda e, tw=tw, tb=tb: e.matmul(
                        pb[3][:, 0:512], pmb[:, :], tw[:, tb * 512:(tb + 1) * 512],
                        start=True, stop=True),
                        reads=[R["pmb"], rtw], writes=[rpb[3]])
                    p.emit("dve", lambda e, tw=tw, tb=tb: e.tensor_mul(
                        scr[:, 0:512], tw[:, tb * 512:(tb + 1) * 512],
                        cosb[:, tb * 512:(tb + 1) * 512]),
                        reads=[rtw, R["cosb"]], writes=[R["scr"]])
                    p.emit("dve", lambda e, tb=tb: e.tensor_mul(
                        scrb[:, 0:512], pb[3][:, 0:512], sinb[:, tb * 512:(tb + 1) * 512]),
                        reads=[rpb[3], R["sinb"]], writes=[R["scrb"]])
                    p.emit("dve", lambda e, tw=tw, tb=tb: e.tensor_add(
                        tw[:, tb * 512:(tb + 1) * 512], scr[:, 0:512], scrb[:, 0:512]),
                        reads=[R["scr"], R["scrb"]], writes=[rtw])
        else:
            for (tw, rtw, wc) in [(qw, R["qw"], 0), (kw, R["kw"], 1)]:
                for tb in range(8):
                    p.emit("act", lambda e, tw=tw, tb=tb: e.activation(
                        scrb[:, 0:512], tw[:, tb * 512:(tb + 1) * 512], AF.Square),
                        reads=[rtw], writes=[R["scrb"]])
                    p.emit("pe", lambda e, tb=tb: e.matmul(
                        pb[3][0:2, 0:512], hsel2[:, :], scrb[:, 0:512],
                        start=True, stop=True),
                        reads=[R["scrb"], R["hsel2"]], writes=[rpb[3]])
                    p.emit("dve", lambda e: e.tensor_scalar(
                        sqrow[:, :], pb[3][0:2, 0:512], 1.0 / 64.0, None, ALU.mult),
                        reads=[rpb[3]], writes=[R["sqrow"]])
                    p.emit("act", lambda e: e.activation(sqrow[:, :], sqrow[:, :], AF.Ln,
                                                         bias=C.eps2[0:2, 0:1]),
                           reads=[R["sqrow"], C.r["eps2"]], writes=[R["sqrow"]])
                    p.emit("act", lambda e: e.activation(sqrow[:, :], sqrow[:, :], AF.Exp,
                                                         bias=0.0, scale=-0.5),
                           reads=[R["sqrow"]], writes=[R["sqrow"]])
                    p.emit("pe", lambda e, tb=tb: e.matmul(
                        pb[3][:, 0:512], sel64[:, :], sqrow[:, :],
                        start=True, stop=True),
                        reads=[R["sqrow"], R["sel64"]], writes=[rpb[3]])
                    p.emit("dve", lambda e, tw=tw, tb=tb: e.tensor_mul(
                        scr[:, 0:512], tw[:, tb * 512:(tb + 1) * 512], pb[3][:, 0:512]),
                        reads=[rtw, rpb[3]], writes=[R["scr"]])
                    p.emit("dve", lambda e, tw=tw, tb=tb, wc=wc: e.tensor_scalar(
                        tw[:, tb * 512:(tb + 1) * 512], scr[:, 0:512],
                        bcols[:, wc:wc + 1], None, ALU.mult),
                        reads=[R["scr"], R["bcols"]], writes=[rtw])
        if "dbg_qw" in d and w == 0:
            p.dma("pool", d["dbg_qw"][:, :], qw[:, :], reads=[R["qw"]], writes=[Res("dbq")])
            p.dma("pool", d["dbg_kw"][:, :], kw[:, :], reads=[R["kw"]], writes=[Res("dbk")])
            p.dma("pool", d["dbg_vw"][:, :], vw[:, :], reads=[R["vw"]], writes=[Res("dbv")])
        for which, buf, rbuf in ((0, qw, R["qw"]), (1, kw, R["kw"])):
            for c8 in range(8):
                sl = abf[:, (c8 % 2) * 512:(c8 % 2) * 512 + 512]
                p.emit("act", lambda e, buf=buf, c8=c8, sl=sl: e.activation(
                    sl, buf[:, c8 * 512:(c8 + 1) * 512], AF.Square,
                    bias=0.0, scale=1.0),
                    reads=[rbuf, Rabf[c8 % 2]], writes=[Rabf[c8 % 2]])
                for hj in range(2):
                    hb2, bkk = hj * 64, (c8 % 2) + 2 * hj
                    p.emit("pe", lambda e, sl=sl, hb2=hb2, bkk=bkk: e.matmul(
                        pb[bkk][0:1, 0:512], C.onescol_b[hb2:hb2 + 64, 0:1],
                        sl[hb2:hb2 + 64, :], start=True, stop=True),
                        reads=[Rabf[c8 % 2], C.r["onescol_b"]], writes=[rpb[bkk]])
                    so = 1026 + which * 16 + hj * 8 + c8
                    p.emit("dve", lambda e, so=so, bkk=bkk: e.reduce_max(
                        rows[0:1, so:so + 1], pb[bkk][0:1, 0:512], axis=AX.X),
                        reads=[rpb[bkk], R["rows"]], writes=[R["rows"]])
        for hw in range(2):
            hb = hw * 64
            p.emit("dve", lambda e, hw=hw: e.reduce_max(
                rows[0:1, 1058:1059], rows[0:1, 1026 + hw * 8:1034 + hw * 8],
                axis=AX.X), reads=[R["rows"]], writes=[R["rows"]])
            p.emit("dve", lambda e, hw=hw: e.reduce_max(
                rows[0:1, 1059:1060], rows[0:1, 1042 + hw * 8:1050 + hw * 8],
                axis=AX.X), reads=[R["rows"]], writes=[R["rows"]])
            p.emit("dve", lambda e: e.tensor_mul(
                rows[0:1, 1058:1059], rows[0:1, 1058:1059], rows[0:1, 1059:1060]),
                reads=[R["rows"]], writes=[R["rows"]])
            p.emit("act", lambda e: e.activation(
                rows[0:1, 1058:1059], rows[0:1, 1058:1059], AF.Sqrt,
                bias=0.0, scale=1.0), reads=[R["rows"]], writes=[R["rows"]])
            p.emit("pe", lambda e: e.matmul(pb[7][0:128, 0:1], C.onesrow_f[0:1, :],
                                            rows[0:1, 1058:1059],
                                            start=True, stop=True),
                   reads=[R["rows"], C.r["onesrow_f"]], writes=[rpb[7]])
            p.emit("dve", lambda e: e.tensor_scalar(gcol[:, 0:1], pb[7][0:128, 0:1],
                                                    -SCALE, None, ALU.mult),
                   reads=[rpb[7]], writes=[R["gcol"]])
            tiles = [(qb, kc) for qb in range(8) for kc in range((qb + 1) * 4)]
            ntl = len(tiles)
            PIPE, DEFER = 3, 3

            def emit_s(i, hb=hb):
                qb, kc = tiles[i]
                bk = kc % 4
                p.emit("pe", lambda e, qb=qb, kc=kc, bk=bk: e.matmul(
                    pb[bk][:, 0:512], kw[hb:hb + 64, kc * 128:(kc + 1) * 128],
                    qw[hb:hb + 64, qb * 512:(qb + 1) * 512],
                    start=True, stop=True),
                    reads=[R["qw"], R["kw"]], writes=[rpb[bk]])

            def emit_exp(i):
                qb, kc = tiles[i]
                bk = kc % 4
                ab = abf[:, bk * 512:bk * 512 + 512]
                p.emit("act", lambda e, ab=ab, bk=bk: e.activation(
                    ab, pb[bk][:, 0:512], AF.Exp, bias=gcol[:, 0:1], scale=SCALE),
                    reads=[rpb[bk], R["gcol"]], writes=[Rabf[bk]])
                dlt = kc - qb * 4
                if dlt >= 0:
                    o0 = 384 - dlt * 128
                    p.emit("dve", lambda e, ab=ab, o0=o0: e.tensor_mul(
                        ab, ab, m512[:, o0:o0 + 512]),
                        reads=[Rabf[bk], R["m512"]], writes=[Rabf[bk]])

            def emit_av(i, hw=hw):
                qb, kc = tiles[i]
                bk = kc % 4
                acb = 4 + 2 * (qb % 2)
                nkc = (qb + 1) * 4
                ab = abf[:, bk * 512:bk * 512 + 512]
                p.emit("pe", lambda e, ab=ab, kc=kc, acb=acb, nkc=nkc: e.matmul(
                    pb[acb][0:65, 0:512],
                    vw[:, kc * 130 + hw * 65:kc * 130 + (hw + 1) * 65],
                    ab, start=(kc == 0), stop=(kc == nkc - 1)),
                    reads=[R["vw"], Rabf[bk]], writes=[rpb[acb]])
                if kc == nkc - 1:
                    so = (qb % 2) * 1280
                    p.emit("dve", lambda e, acb=acb, so=so: e.tensor_copy(
                        rows[0:1, so:so + 512], pb[acb][64:65, 0:512]),
                        reads=[rpb[acb], R["rows"]], writes=[R["rows"]])
                    p.emit("dve", lambda e, so=so: e.reciprocal(
                        rows[0:1, so + 512:so + 1024], rows[0:1, so:so + 512]),
                        reads=[R["rows"]], writes=[R["rows"]])

            def emit_out(i, hw=hw):
                qb, kc = tiles[i]
                nkc = (qb + 1) * 4
                if kc != nkc - 1:
                    return
                acb = 4 + 2 * (qb % 2)
                so = (qb % 2) * 1280
                p.emit("pe", lambda e, so=so: e.matmul(
                    pb[5][0:64, 0:512], C.onesrow_f[0:1, 0:64],
                    rows[0:1, so + 512:so + 1024], start=True, stop=True),
                    reads=[R["rows"], C.r["onesrow_f"]], writes=[rpb[5]])
                p.emit("dve", lambda e: e.tensor_copy(rbc[:, :], pb[5][0:64, 0:512]),
                       reads=[rpb[5], R["rbc"]], writes=[R["rbc"]])
                p.emit("dve", lambda e, acb=acb: e.tensor_mul(
                    obf[:, :], pb[acb][0:64, 0:512], rbc[:, :]),
                    reads=[rpb[acb], R["rbc"], R["obf"]], writes=[R["obf"]])
                p.dma("sp", d["oag"][w * 128 + hw * 64:w * 128 + hw * 64 + 64,
                                     qb * 512:(qb + 1) * 512], obf[:, :],
                      reads=[R["obf"]], writes=[R["d_oag"]])

            for i in range(ntl + PIPE + DEFER):
                if i < ntl:
                    emit_s(i)
                    emit_exp(i)
                j = i - PIPE
                if 0 <= j < ntl:
                    emit_av(j)
                k2 = i - PIPE - DEFER
                if 0 <= k2 < ntl:
                    emit_out(k2)

    if "dbg_oag" in d:
        p.dma("pool", d["dbg_oag"][:, :], d["oag"][:, :],
              reads=[R["d_oag"]], writes=[Res("dbo")])
    p.emit("pool", lambda e: e.collective_compute(
        "AllGather", ALU.bypass, replica_groups=[[2 * i, 2 * i + 1] for i in range(4)],
        ins=[d["oag"].opt()], outs=[d["oago"].opt()]),
        reads=[R["d_oag"]], writes=[R["d_oago"]], inc=1)
    collective_barrier(p, d, scr[0:16, 0:16], R["scr"], R["d_oago"])
    osel, r_osel = xnf, R["xnf"]
    for c in range(NCH):
        p.dma("sp", qw[:, 0:2048], d["oago"][c * 128:(c + 1) * 128, 0:2048],
              reads=[R["d_oago"], R["qw"]], writes=[R["qw"]])
        p.dma("sp", kw[:, 0:2048], d["oago"][c * 128:(c + 1) * 128, 2048:4096],
              reads=[R["d_oago"], R["kw"]], writes=[R["kw"]])
        p.emit("dve", lambda e, c=c: e.tensor_scalar(
            osel[:, c * 2048:(c + 1) * 2048], qw[:, 0:2048], fsel[:, 0:1], None,
            ALU.mult),
            reads=[R["qw"], R["fsel"], r_osel], writes=[r_osel])
        p.emit("dve", lambda e, c=c: e.scalar_tensor_tensor(
            osel[:, c * 2048:(c + 1) * 2048], kw[:, 0:2048], fsel[:, 1:2],
            osel[:, c * 2048:(c + 1) * 2048], ALU.mult, ALU.add),
            reads=[R["kw"], R["fsel"], r_osel], writes=[r_osel])
    do_start = 0 if kind == "l1attn" else 2
    for do in range(do_start, NCH):
        i3 = do % 3
        p.dma("pool", wt3[:, i3 * 1024:(i3 + 1) * 1024].rearrange("p (k n) -> p k n", n=128),
              d[pfx + "_wo"].rearrange("(k p) n -> p k n", p=128)[:, :, do * 128:(do + 1) * 128],
              reads=[R["wt3"][i3]], writes=[R["wt3"][i3]])
        for tb in range(4):
            bk = tb % 2

            def mmo(e, do=do, tb=tb, bk=bk, i3=i3):
                last = None
                for kc in range(NCH):
                    last = e.matmul(pb[bk][:, 0:512],
                                    wt3[:, i3 * 1024 + kc * 128:i3 * 1024 + (kc + 1) * 128],
                                    osel[:, kc * 2048 + tb * 512:kc * 2048 + (tb + 1) * 512],
                                    start=(kc == 0), stop=(kc == NCH - 1))
                return last
            p.emit("pe", mmo, reads=[R["wt3"][i3], r_osel], writes=[rpb[bk]])
            if kind == "l1attn":
                p.emit("dve", lambda e, do=do, bk=bk: e.tensor_scalar(
                    scr[:, 0:512], pb[bk][:, 0:512],
                    bcols[:, 8 + do:8 + do + 1], None, ALU.add),
                    reads=[rpb[bk], R["bcols"]], writes=[R["scr"]])
                p.emit("dve", lambda e, do=do, tb=tb: e.tensor_add(
                    xres[:, do * 2048 + tb * 512:do * 2048 + (tb + 1) * 512],
                    xres[:, do * 2048 + tb * 512:do * 2048 + (tb + 1) * 512],
                    scr[:, 0:512]),
                    reads=[R["scr"], r_xres], writes=[r_xres])
            else:
                p.emit("dve", lambda e, do=do, tb=tb, bk=bk: e.tensor_add(
                    xres[:, do * 2048 + tb * 512:do * 2048 + (tb + 1) * 512],
                    xres[:, do * 2048 + tb * 512:do * 2048 + (tb + 1) * 512],
                    pb[bk][:, 0:512]),
                    reads=[rpb[bk], r_xres], writes=[r_xres])
    sa.close()



def build_l0_ffn(p, nc, ctx, C, xres, r_xres, d):
    sf = ExitStack()
    rows = sf.enter_context(nc.sbuf_tensor("f0_rows", [1, 8224], F32, allow_name_mangling=True))
    rsbc = sf.enter_context(nc.sbuf_tensor("f0_rsbc", [128, 512], F32, allow_name_mangling=True))
    mrbc = sf.enter_context(nc.sbuf_tensor("f0_mrbc", [128, 512], F32, allow_name_mangling=True))
    scr = sf.enter_context(nc.sbuf_tensor("f0_scr", [128, 512], F32, allow_name_mangling=True))
    scr2 = sf.enter_context(nc.sbuf_tensor("f0_scr2", [128, 512], F32, allow_name_mangling=True))
    scrb = sf.enter_context(nc.sbuf_tensor("f0_scrb", [128, 512], BF16, allow_name_mangling=True))
    xnb = sf.enter_context(nc.sbuf_tensor("f0_xnb", [128, NCH * 2048], BF16, allow_name_mangling=True))
    wTb = sf.enter_context(nc.sbuf_tensor("f0_wTb", [16, 2048], BF16))
    onehot = sf.enter_context(nc.sbuf_tensor("f0_onehot", [16, 16 * 128], BF16, allow_name_mangling=True))
    bcolt = sf.enter_context(nc.sbuf_tensor("f0_bcolt", [128, 64], F32, allow_name_mangling=True))
    gbias = sf.enter_context(nc.sbuf_tensor("f0_gbias_s", [128, 16], F32, allow_name_mangling=True))
    ceT = sf.enter_context(nc.sbuf_tensor("f0_ceT_s", [128, NCH * 16], BF16))
    rcebc = sf.enter_context(nc.sbuf_tensor("f0_rcebc", [128, 16], F32, allow_name_mangling=True))
    wtile = sf.enter_context(nc.sbuf_tensor("f0_wtile", [128, 2 * 1024], BF16, allow_name_mangling=True))
    pb = [psumt(nc, sf, f"f0_pb{i}", [128, 512]) for i in range(8)]
    rpb = [fence_res(p, f"f0_pb{i}") for i in range(8)]
    N = ["rows", "scr", "scr2", "scrb", "xnb", "wtok", "ltok", "onehot", "bcolt",
         "gbias", "ceT", "rcebc", "rsbc", "mrbc"]
    R = {k: fence_res(p, k) for k in N}
    R["wtile"] = [fence_res(p, "f0_wt0"), fence_res(p, "f0_wt1")]

    p.emit("dve", lambda e: e.memset(onehot[:, :], 1.0), writes=[R["onehot"]])
    p.emit("pool", lambda e: e.affine_select(onehot[:, :], onehot[:, :],
                                             [[1, 16], [0, 128]], ALU.is_equal, 0.0,
                                             base=0, channel_multiplier=-1),
           reads=[R["onehot"]], writes=[R["onehot"]])
    p.dma("sp", bcolt[:, :], d["f0_bcols"][:, :], writes=[R["bcolt"]])
    p.dma("sp", gbias[:, :], d["f0_gbias"][:, :], writes=[R["gbias"]])
    p.dma("pool", ceT.rearrange("p (k n) -> p k n", n=16),
          d["f0_ceT"].rearrange("(k p) n -> p k n", p=128), writes=[R["ceT"]])

    feat_norm(p, nc, C, xres, r_xres, xnb, R["xnb"], rows, R["rows"], scr, R["scr"],
              scrb, R["scrb"], pb, rpb, "ln")

    for s0 in range(0, 2048, 512):
        for c in range(NCH):
            p.emit("act", lambda e, c=c, s0=s0: e.activation(
                scrb[:, 0:512], xnb[:, c * 2048 + s0:c * 2048 + s0 + 512], AF.Square),
                reads=[R["xnb"]], writes=[R["scrb"]])
            p.emit("pe", lambda e, c=c: e.matmul(pb[4][0:1, 0:512], C.onescol_b[:, :],
                                                 scrb[:, 0:512], start=(c == 0),
                                                 stop=(c == NCH - 1)),
                   reads=[R["scrb"], C.r["onescol_b"]], writes=[rpb[4]])
        p.emit("dve", lambda e, s0=s0: e.tensor_copy(rows[0:1, 6144 + s0:6144 + s0 + 512],
                                                     pb[4][0:1, 0:512]),
               reads=[rpb[4]], writes=[R["rows"]])
    rx = rows[0:1, 6144:8192]
    p.emit("act", lambda e: e.activation(rx, rx, AF.Ln), reads=[R["rows"]], writes=[R["rows"]])
    p.emit("act", lambda e: e.activation(rx, rx, AF.Exp, bias=0.0, scale=-0.5),
           reads=[R["rows"]], writes=[R["rows"]])
    for c in range(NCH):
        p.emit("act", lambda e, c=c: e.activation(scrb[:, 0:16],
                                                  ceT[:, c * 16:(c + 1) * 16], AF.Square),
               reads=[R["ceT"]], writes=[R["scrb"]])
        p.emit("pe", lambda e, c=c: e.matmul(pb[4][0:1, 0:16], C.onescol_b[:, :],
                                             scrb[:, 0:16], start=(c == 0),
                                             stop=(c == NCH - 1)),
               reads=[R["scrb"], C.r["onescol_b"]], writes=[rpb[4]])
    rc = rows[0:1, 8192:8208]
    p.emit("dve", lambda e: e.tensor_copy(rc, pb[4][0:1, 0:16]),
           reads=[rpb[4]], writes=[R["rows"]])
    p.emit("act", lambda e: e.activation(rc, rc, AF.Ln), reads=[R["rows"]], writes=[R["rows"]])
    p.emit("act", lambda e: e.activation(rc, rc, AF.Exp, bias=0.0, scale=-0.5),
           reads=[R["rows"]], writes=[R["rows"]])
    bcast_row(p, C, pb[5], rpb[5], rc, R["rows"], 16)
    p.emit("dve", lambda e: e.tensor_copy(rcebc[:, :], pb[5][:, 0:16]),
           reads=[rpb[5]], writes=[R["rcebc"]])

    xn32 = sf.enter_context(nc.sbuf_tensor("f0_xn32", [128, 8 * 512], F32, allow_name_mangling=True))
    sw = ExitStack()
    wtok = sw.enter_context(nc.sbuf_tensor("f0_wtok", [128, 16 * 16], F32, allow_name_mangling=True))
    ltok = sw.enter_context(nc.sbuf_tensor("f0_ltok", [128, 16 * 16], F32, allow_name_mangling=True))
    R["xn32"] = fence_res(p, "xn32")
    sg = ExitStack()
    hT = sg.enter_context(nc.sbuf_tensor("f0_hT", [128, 4 * 2048], F32))
    wt32 = sg.enter_context(nc.sbuf_tensor("f0_wt32", [128, 1024], F32, allow_name_mangling=True))
    R["hT"] = fence_res(p, "hT")
    R["wt32"] = fence_res(p, "wt32")
    for tb in range(4):
        bcast_row(p, C, pb[4], rpb[4], rows[0:1, 2048 + tb * 512:2048 + (tb + 1) * 512],
                  R["rows"], 512)
        p.emit("dve", lambda e: e.tensor_copy(rsbc[:, :], pb[4][:, 0:512]),
               reads=[rpb[4]], writes=[R["rsbc"]])
        bcast_row(p, C, pb[4], rpb[4], rows[0:1, 4096 + tb * 512:4096 + (tb + 1) * 512],
                  R["rows"], 512)
        p.emit("dve", lambda e: e.tensor_copy(mrbc[:, :], pb[4][:, 0:512]),
               reads=[rpb[4]], writes=[R["mrbc"]])
        for kc in range(NCH):
            p.emit("dve", lambda e, kc=kc, tb=tb: e.tensor_mul(
                xn32[:, kc * 512:(kc + 1) * 512],
                xres[:, kc * 2048 + tb * 512:kc * 2048 + (tb + 1) * 512], rsbc[:, :]),
                reads=[r_xres, R["rsbc"], R["xn32"]], writes=[R["xn32"]])
            p.emit("dve", lambda e, kc=kc: e.tensor_sub(
                xn32[:, kc * 512:(kc + 1) * 512], xn32[:, kc * 512:(kc + 1) * 512],
                mrbc[:, :]),
                reads=[R["mrbc"], R["xn32"]], writes=[R["xn32"]])
        for ho in range(4):
            p.dma("sp", wt32.rearrange("p (k n) -> p k n", n=128),
                  d["f0_g1"].rearrange("(k p) n -> p k n", p=128)[:, :, ho * 128:(ho + 1) * 128],
                  reads=[R["wt32"]], writes=[R["wt32"]])
            bk = ho % 2

            def mmh(e, ho=ho, tb=tb, bk=bk):
                last = None
                for kc in range(NCH):
                    last = e.matmul(pb[bk][:, 0:512], wt32[:, kc * 128:(kc + 1) * 128],
                                    xn32[:, kc * 512:(kc + 1) * 512],
                                    start=(kc == 0), stop=(kc == NCH - 1))
                return last
            p.emit("pe", mmh, reads=[R["wt32"], R["xn32"]], writes=[rpb[bk]])
            p.emit("act", lambda e, ho=ho, tb=tb, bk=bk: e.activation(
                hT[:, ho * 2048 + tb * 512:ho * 2048 + (tb + 1) * 512], pb[bk][:, 0:512],
                AF.Gelu, bias=gbias[:, ho:ho + 1]),
                reads=[rpb[bk], R["gbias"]], writes=[R["hT"]])
    g2t = sg.enter_context(nc.sbuf_tensor("f0_g2t", [128, 64], F32, allow_name_mangling=True))
    R["g2t"] = fence_res(p, "g2t")
    p.dma("sp", g2t.rearrange("p (k n) -> p k n", n=16),
          d["f0_g2"].rearrange("(k p) n -> p k n", p=128), writes=[R["g2t"]])
    p.dma("sp", rows[0:1, 8208:8224], d["f0_b2row"][0:1, :], reads=[R["rows"]],
          writes=[R["rows"]])
    bcast_row(p, C, pb[5], rpb[5], rows[0:1, 8208:8224], R["rows"], 16)
    b2bc = sg.enter_context(nc.sbuf_tensor("f0_b2bc", [128, 16], F32, allow_name_mangling=True))
    R["b2bc"] = fence_res(p, "b2bc")
    p.emit("dve", lambda e: e.tensor_copy(b2bc[:, :], pb[5][:, 0:16]),
           reads=[rpb[5]], writes=[R["b2bc"]])

    for tt in range(16):
        def mmsim(e, tt=tt):
            last = None
            for kc in range(NCH):
                last = e.matmul(pb[6][:, 0:16],
                                xnb[:, kc * 2048 + tt * 128:kc * 2048 + (tt + 1) * 128],
                                ceT[:, kc * 16:(kc + 1) * 16],
                                start=(kc == 0), stop=(kc == NCH - 1))
            return last
        p.emit("pe", mmsim, reads=[R["xnb"], R["ceT"]], writes=[rpb[6]])
        def mmgv(e, tt=tt):
            last = None
            for kc in range(4):
                last = e.matmul(pb[7][:, 0:16],
                                hT[:, kc * 2048 + tt * 128:kc * 2048 + (tt + 1) * 128],
                                g2t[:, kc * 16:(kc + 1) * 16],
                                start=(kc == 0), stop=(kc == 3))
            return last
        p.emit("pe", mmgv, reads=[R["hT"], R["g2t"]], writes=[rpb[7]])
        p.emit("pe", lambda e, tt=tt: e.transpose(pb[5][0:128, 0:1],
                                                  rx[0:1, tt * 128:(tt + 1) * 128],
                                                  C.ident[0:1, 0:1]),
               reads=[R["rows"], C.r["ident"]], writes=[rpb[5]])
        p.emit("dve", lambda e: e.tensor_copy(scr2[:, 0:1], pb[5][0:128, 0:1]),
               reads=[rpb[5]], writes=[R["scr2"]])
        p.emit("dve", lambda e: e.tensor_scalar(scr[:, 0:16], pb[6][:, 0:16],
                                                scr2[:, 0:1], None, ALU.mult),
               reads=[rpb[6], R["scr2"]], writes=[R["scr"]])
        p.emit("dve", lambda e: e.tensor_mul(scr[:, 0:16], scr[:, 0:16], rcebc[:, :]),
               reads=[R["rcebc"], R["scr"]], writes=[R["scr"]])
        p.emit("dve", lambda e: e.tensor_add(scr2[:, 0:16], pb[7][:, 0:16], b2bc[:, :]),
               reads=[rpb[7], R["b2bc"]], writes=[R["scr2"]])
        p.emit("act", lambda e: e.activation(scr2[:, 0:16], scr2[:, 0:16], AF.Sigmoid),
               reads=[R["scr2"]], writes=[R["scr2"]])
        p.emit("dve", lambda e: e.tensor_add(scr[:, 0:16], scr[:, 0:16], scr2[:, 0:16]),
               reads=[R["scr"], R["scr2"]], writes=[R["scr"]])
        p.emit("dve", lambda e, tt=tt: e.tensor_copy(ltok[:, tt * 16:(tt + 1) * 16],
                                                     scr[:, 0:16]),
               reads=[R["scr"]], writes=[R["ltok"]])
        p.emit("dve", lambda e: e.reduce_max(scr2[:, 0:1], scr[:, 0:16], axis=AX.X,
                                             negate=True),
               reads=[R["scr"]], writes=[R["scr2"]])
        p.emit("act", lambda e: e.activation(scr2[:, 1:17], scr[:, 0:16], AF.Exp,
                                             bias=scr2[:, 0:1],
                                             accum_out=scr2[:, 17:18]),
               reads=[R["scr"], R["scr2"]], writes=[R["scr2"]])
        p.emit("dve", lambda e: e.reciprocal(scr2[:, 18:19], scr2[:, 17:18]),
               reads=[R["scr2"]], writes=[R["scr2"]])
        p.emit("dve", lambda e: e.tensor_scalar(scr2[:, 19:20], scr2[:, 0:1], -1.0,
                                                None, ALU.mult),
               reads=[R["scr2"]], writes=[R["scr2"]])
        p.emit("dve", lambda e: e.tensor_scalar(scr[:, 16:32], scr[:, 0:16],
                                                scr2[:, 19:20], None, ALU.is_equal),
               reads=[R["scr"], R["scr2"]], writes=[R["scr"]])
        p.emit("dve", lambda e: e.tensor_scalar(scr[:, 32:48], scr2[:, 1:17],
                                                scr2[:, 18:19], None, ALU.mult),
               reads=[R["scr2"], R["scr"]], writes=[R["scr"]])
        p.emit("dve", lambda e, tt=tt: e.tensor_mul(wtok[:, tt * 16:(tt + 1) * 16],
                                                    scr[:, 16:32], scr[:, 32:48]),
               reads=[R["scr"]], writes=[R["wtok"]])
    sg.close()
    if "dbg_wtok" in d:
        p.dma("sp", d["dbg_wtok"][:, :], wtok[:, :], reads=[R["wtok"]], writes=[Res("dw")])
        p.dma("sp", d["dbg_ltok"][:, :], ltok[:, :], reads=[R["ltok"]], writes=[Res("dl")])
    dump_y = "dbg_y" in d
    R["wTb"] = fence_res(p, "wTb")
    for tt in range(16):
        p.emit("pe", lambda e, tt=tt: e.transpose(pb[5][0:16, 0:128],
                                                  wtok[:, tt * 16:(tt + 1) * 16],
                                                  C.ident[:, :]),
               reads=[R["wtok"], C.r["ident"]], writes=[rpb[5]])
        p.emit("dve", lambda e, tt=tt: e.tensor_copy(wTb[:, tt * 128:(tt + 1) * 128],
                                                     pb[5][0:16, 0:128]),
               reads=[rpb[5]], writes=[R["wTb"]])
    sw.close()

    sc = ExitStack()
    xnlo = sc.enter_context(nc.sbuf_tensor("f0_xnlo", [128, 8 * 512], BF16, allow_name_mangling=True))
    R["xnlo"] = fence_res(p, "xnlo")
    colsw = sc.enter_context(nc.sbuf_tensor("f0_colsw", [128, 4 * 512], F32, allow_name_mangling=True))
    yacc = sc.enter_context(nc.sbuf_tensor("f0_yacc", [128, 8 * 512], F32, allow_name_mangling=True))
    chl = sc.enter_context(nc.sbuf_tensor("f0_chl", [128, 2 * 512], BF16, allow_name_mangling=True))
    R["colsw"] = fence_res(p, "colsw")
    R["yacc"] = fence_res(p, "yacc")
    R["chl"] = fence_res(p, "chl")
    R["d_f0y"] = Res("d_f0y")
    for tb in range(4):
        bcast_row(p, C, pb[4], rpb[4], rows[0:1, 2048 + tb * 512:2048 + (tb + 1) * 512],
                  R["rows"], 512)
        p.emit("dve", lambda e: e.tensor_copy(rsbc[:, :], pb[4][:, 0:512]),
               reads=[rpb[4]], writes=[R["rsbc"]])
        bcast_row(p, C, pb[4], rpb[4], rows[0:1, 4096 + tb * 512:4096 + (tb + 1) * 512],
                  R["rows"], 512)
        p.emit("dve", lambda e: e.tensor_copy(mrbc[:, :], pb[4][:, 0:512]),
               reads=[rpb[4]], writes=[R["mrbc"]])
        for kc in range(NCH):
            p.emit("dve", lambda e, kc=kc, tb=tb: e.tensor_mul(
                xn32[:, kc * 512:(kc + 1) * 512],
                xres[:, kc * 2048 + tb * 512:kc * 2048 + (tb + 1) * 512], rsbc[:, :]),
                reads=[r_xres, R["rsbc"], R["xn32"]], writes=[R["xn32"]])
            p.emit("dve", lambda e, kc=kc: e.tensor_sub(
                xn32[:, kc * 512:(kc + 1) * 512], xn32[:, kc * 512:(kc + 1) * 512],
                mrbc[:, :]),
                reads=[R["mrbc"], R["xn32"]], writes=[R["xn32"]])
            p.emit("dve", lambda e, kc=kc, tb=tb: e.tensor_sub(
                xnlo[:, kc * 512:(kc + 1) * 512], xn32[:, kc * 512:(kc + 1) * 512],
                xnb[:, kc * 2048 + tb * 512:kc * 2048 + (tb + 1) * 512]),
                reads=[R["xn32"], R["xnb"], R["xnlo"]], writes=[R["xnlo"]])
        for half in range(8):
            for ci in range(4):
                cn = half * 4 + ci
                wt = wtile[:, (cn % 2) * 1024:(cn % 2) * 1024 + 1024]
                r_wt = R["wtile"][cn % 2]
                p.dma("pool", wt.rearrange("p (k n) -> p k n", n=128),
                      d["f0_wcols"].rearrange("(k p) n -> p k n", p=128)[:, :, cn * 128:(cn + 1) * 128],
                      reads=[r_wt], writes=[r_wt])
                bk = cn % 2

                def mmc(e, cn=cn, tb=tb, bk=bk, wt=wt):
                    last = None
                    for kc in range(NCH):
                        e.matmul(pb[bk][:, 0:512], wt[:, kc * 128:(kc + 1) * 128],
                                 xnb[:, kc * 2048 + tb * 512:kc * 2048 + (tb + 1) * 512],
                                 start=(kc == 0), stop=False)
                        last = e.matmul(pb[bk][:, 0:512], wt[:, kc * 128:(kc + 1) * 128],
                                        xnlo[:, kc * 512:(kc + 1) * 512],
                                        start=False, stop=(kc == NCH - 1))
                    return last
                p.emit("pe", mmc, reads=[r_wt, R["xnb"], R["xnlo"]], writes=[rpb[bk]])
                p.emit("act", lambda e, cn=cn, bk=bk: e.activation(
                    scr[:, 0:512], pb[bk][:, 0:512], AF.Gelu, bias=bcolt[:, cn:cn + 1]),
                    reads=[rpb[bk], R["bcolt"]], writes=[R["scr"]])
                cix = cn // 2
                p.emit("pe", lambda e, cix=cix, tb=tb: e.matmul(
                    pb[5][:, 0:512], onehot[:, cix * 128:(cix + 1) * 128],
                    wTb[:, tb * 512:(tb + 1) * 512], start=True, stop=True),
                    reads=[R["onehot"], R["wTb"]], writes=[rpb[5]])
                p.emit("dve", lambda e, ci=ci: e.tensor_mul(
                    colsw[:, ci * 512:(ci + 1) * 512], scr[:, 0:512], pb[5][:, 0:512]),
                    reads=[R["scr"], rpb[5]], writes=[R["colsw"]])
            for ci in range(4):
                cn = half * 4 + ci
                wt = wtile[:, (cn % 2) * 1024:(cn % 2) * 1024 + 1024]
                r_wt = R["wtile"][cn % 2]
                p.dma("pool", wt[:, 0:1024],
                      d["f0_proj"].rearrange("(k p) n -> k p n", p=128)[cn],
                      reads=[r_wt], writes=[r_wt])
                p.emit("dve", lambda e, ci=ci: e.tensor_copy(
                    chl[:, 0:512], colsw[:, ci * 512:(ci + 1) * 512]),
                    reads=[R["colsw"], R["chl"]], writes=[R["chl"]])
                p.emit("dve", lambda e, ci=ci: e.tensor_sub(
                    chl[:, 512:1024], colsw[:, ci * 512:(ci + 1) * 512], chl[:, 0:512]),
                    reads=[R["colsw"], R["chl"]], writes=[R["chl"]])

                def mmp(e, cn=cn, half=half, ci=ci, wt=wt):
                    last = None
                    for do in range(NCH):
                        e.matmul(pb[do][:, 0:512], wt[:, do * 128:(do + 1) * 128],
                                 chl[:, 0:512],
                                 start=(ci == 0), stop=False)
                        last = e.matmul(pb[do][:, 0:512], wt[:, do * 128:(do + 1) * 128],
                                        chl[:, 512:1024], start=False,
                                        stop=(ci == 3))
                    return last
                p.emit("pe", mmp, reads=[r_wt, R["chl"]],
                       writes=[rpb[do2] for do2 in range(NCH)])
            for do in range(NCH):
                if half == 0:
                    p.emit("dve", lambda e, do=do: e.tensor_copy(
                        yacc[:, do * 512:(do + 1) * 512], pb[do][:, 0:512]),
                        reads=[rpb[do]], writes=[R["yacc"]])
                else:
                    p.emit("dve", lambda e, do=do: e.tensor_add(
                        yacc[:, do * 512:(do + 1) * 512],
                        yacc[:, do * 512:(do + 1) * 512], pb[do][:, 0:512]),
                        reads=[rpb[do], R["yacc"]], writes=[R["yacc"]])
        for do in range(NCH):
            p.emit("dve", lambda e, do=do: e.tensor_scalar(
                scr[:, 0:512], yacc[:, do * 512:(do + 1) * 512],
                bcolt[:, 32 + do:32 + do + 1], None, ALU.add),
                reads=[R["yacc"], R["bcolt"]], writes=[R["scr"]])
            p.emit("dve", lambda e, do=do, tb=tb: e.tensor_mul(
                scr2[:, 0:512], xres[:, do * 2048 + tb * 512:do * 2048 + (tb + 1) * 512],
                rsbc[:, :]),
                reads=[r_xres, R["rsbc"], R["scr2"]], writes=[R["scr2"]])
            p.emit("dve", lambda e: e.tensor_sub(scr2[:, 0:512], scr2[:, 0:512],
                                                 mrbc[:, :]),
                   reads=[R["scr2"], R["mrbc"]], writes=[R["scr2"]])
            p.emit("dve", lambda e: e.tensor_add(scr[:, 0:512], scr[:, 0:512],
                                                 scr2[:, 0:512]),
                   reads=[R["scr"], R["scr2"]], writes=[R["scr"]])
            p.dma("sp", d["f0y"][do * 128:(do + 1) * 128, tb * 512:(tb + 1) * 512],
                  scr[:, 0:512], reads=[R["scr"]], writes=[R["d_f0y"]])
            p.emit("pe", lambda e, do=do: e.matmul(pb[0][0:1, 0:512], C.onescol_f[:, :],
                                                   scr[:, 0:512], start=(do == 0),
                                                   stop=(do == NCH - 1)),
                   reads=[R["scr"], C.r["onescol_f"]], writes=[rpb[0]])
            p.emit("act", lambda e: e.activation(scr2[:, 0:512], scr[:, 0:512], AF.Square),
                   reads=[R["scr"]], writes=[R["scr2"]])
            p.emit("pe", lambda e, do=do: e.matmul(pb[1][0:1, 0:512], C.onescol_f[:, :],
                                                   scr2[:, 0:512], start=(do == 0),
                                                   stop=(do == NCH - 1)),
                   reads=[R["scr2"], C.r["onescol_f"]], writes=[rpb[1]])
        p.emit("dve", lambda e, tb=tb: e.tensor_scalar(
            rows[0:1, 2048 + tb * 512:2048 + (tb + 1) * 512], pb[0][0:1, 0:512],
            1.0 / DM, None, ALU.mult),
            reads=[rpb[0]], writes=[R["rows"]])
        p.emit("dve", lambda e, tb=tb: e.tensor_scalar(
            rows[0:1, 4096 + tb * 512:4096 + (tb + 1) * 512], pb[1][0:1, 0:512],
            1.0 / DM, None, ALU.mult),
            reads=[rpb[1]], writes=[R["rows"]])
    if "dbg_y" in d:
        p.dma("pool", d["dbg_y"][:, :], d["f0y"][:, :], reads=[R["d_f0y"]],
              writes=[Res("dy0")])
    muy = rows[0:1, 2048:4096]
    rsy = rows[0:1, 4096:6144]
    p.emit("act", lambda e: e.activation(rows[0:1, 0:2048], muy, AF.Square),
           reads=[R["rows"]], writes=[R["rows"]])
    p.emit("dve", lambda e: e.tensor_sub(rsy, rsy, rows[0:1, 0:2048]),
           reads=[R["rows"]], writes=[R["rows"]])
    p.emit("act", lambda e: e.activation(rsy, rsy, AF.Ln, bias=C.eps1[0:1, 0:1]),
           reads=[R["rows"], C.r["eps1"]], writes=[R["rows"]])
    p.emit("act", lambda e: e.activation(rsy, rsy, AF.Exp, bias=0.0, scale=-0.5),
           reads=[R["rows"]], writes=[R["rows"]])
    p.emit("dve", lambda e: e.tensor_mul(rows[0:1, 0:2048], muy, rsy),
           reads=[R["rows"]], writes=[R["rows"]])
    for s0 in range(0, 2048, 512):
        bcast_row(p, C, pb[5], rpb[5], rsy[0:1, s0:s0 + 512], R["rows"], 512)
        bcast_row(p, C, pb[4], rpb[4], rows[0:1, s0:s0 + 512], R["rows"], 512)
        for c in range(NCH):
            p.dma("sp", scr2[:, 0:512], d["f0y"][c * 128:(c + 1) * 128, s0:s0 + 512],
                  reads=[R["d_f0y"], R["scr2"]], writes=[R["scr2"]])
            p.emit("dve", lambda e, c=c, s0=s0: e.tensor_mul(
                scr[:, 0:512], scr2[:, 0:512], pb[5][:, 0:512]),
                reads=[R["scr2"], rpb[5]], writes=[R["scr"]])
            p.emit("dve", lambda e: e.tensor_sub(scr[:, 0:512], scr[:, 0:512],
                                                 pb[4][:, 0:512]),
                   reads=[R["scr"], rpb[4]], writes=[R["scr"]])
            p.emit("dve", lambda e, c=c: e.tensor_scalar(
                scr[:, 0:512], scr[:, 0:512], bcolt[:, 40 + c:40 + c + 1],
                bcolt[:, 48 + c:48 + c + 1], ALU.mult, ALU.add),
                reads=[R["scr"], R["bcolt"]], writes=[R["scr"]])
            p.emit("dve", lambda e, c=c, s0=s0: e.tensor_add(
                xres[:, c * 2048 + s0:c * 2048 + s0 + 512],
                xres[:, c * 2048 + s0:c * 2048 + s0 + 512], scr[:, 0:512]),
                reads=[R["scr"], r_xres], writes=[r_xres])
    sc.close()
    sf.close()


def build_l1_moe(p, nc, ctx, C, xres, r_xres, d):
    sf = ExitStack()
    rows = sf.enter_context(nc.sbuf_tensor("mo_rows", [1, 6144], F32, allow_name_mangling=True))
    scr = sf.enter_context(nc.sbuf_tensor("mo_scr", [128, 512], F32, allow_name_mangling=True))
    scr2 = sf.enter_context(nc.sbuf_tensor("mo_scr2", [128, 512], F32, allow_name_mangling=True))
    scrb = sf.enter_context(nc.sbuf_tensor("mo_scrb", [128, 512], BF16, allow_name_mangling=True))
    xf = sf.enter_context(nc.sbuf_tensor("mo_xf", [128, NCH * 2048], BF16, allow_name_mangling=True))
    n2col = sf.enter_context(nc.sbuf_tensor("mo_n2col", [128, 8], F32, allow_name_mangling=True))
    gwt = sf.enter_context(nc.sbuf_tensor("mo_gwt", [128, 32], BF16, allow_name_mangling=True))
    wts = sf.enter_context(nc.sbuf_tensor("mo_wts", [128, 16 * 4], F32, allow_name_mangling=True))
    wT4 = sf.enter_context(nc.sbuf_tensor("mo_wT4", [4, 2048], F32))
    oh4 = sf.enter_context(nc.sbuf_tensor("mo_oh4", [4, 4 * 128], F32, allow_name_mangling=True))
    wtile = sf.enter_context(nc.sbuf_tensor("mo_wtile", [128, 2 * 1024], BF16, allow_name_mangling=True))
    wbcs = sf.enter_context(nc.sbuf_tensor("mo_wbcs", [128, 512], F32, allow_name_mangling=True))
    pb = [psumt(nc, sf, f"mo_pb{i}", [128, 512]) for i in range(8)]
    rpb = [fence_res(p, f"mo_pb{i}") for i in range(8)]
    N = ["rows", "scr", "scr2", "scrb", "xf", "n2col", "gwt", "wts", "wT4", "oh4",
         "wbcs"]
    R = {k: fence_res(p, k) for k in N}
    R["wtile"] = [fence_res(p, "mo_wt0"), fence_res(p, "mo_wt1")]

    p.dma("sp", n2col[:, :], d["l1f_n2w"][:, :], writes=[R["n2col"]])
    p.dma("pool", gwt.rearrange("p (k n) -> p k n", n=4),
          d["l1f_gwT"].rearrange("(k p) n -> p k n", p=128), writes=[R["gwt"]])
    p.emit("dve", lambda e: e.memset(oh4[:, :], 1.0), writes=[R["oh4"]])
    p.emit("pool", lambda e: e.affine_select(oh4[:, :], oh4[:, :],
                                             [[1, 4], [0, 128]], ALU.is_equal, 0.0,
                                             base=0, channel_multiplier=-1),
           reads=[R["oh4"]], writes=[R["oh4"]])

    feat_norm(p, nc, C, xres, r_xres, xf, R["xf"], rows, R["rows"], scr, R["scr"],
              scrb, R["scrb"], pb, rpb, "rms", n2col, R["n2col"])

    for tt in range(16):
        def mmg(e, tt=tt):
            last = None
            for kc in range(NCH):
                last = e.matmul(pb[6][:, 0:4],
                                xf[:, kc * 2048 + tt * 128:kc * 2048 + (tt + 1) * 128],
                                gwt[:, kc * 4:(kc + 1) * 4],
                                start=(kc == 0), stop=(kc == NCH - 1))
            return last
        p.emit("pe", mmg, reads=[R["xf"], R["gwt"]], writes=[rpb[6]])
        p.emit("dve", lambda e: e.tensor_copy(scr[:, 0:4], pb[6][:, 0:4]),
               reads=[rpb[6]], writes=[R["scr"]])
        p.emit("dve", lambda e: e.reduce_max(scr[:, 4:5], scr[:, 0:4], axis=AX.X,
                                             negate=True),
               reads=[R["scr"]], writes=[R["scr"]])
        p.emit("act", lambda e: e.activation(scr[:, 5:9], scr[:, 0:4], AF.Exp,
                                             bias=scr[:, 4:5], accum_out=scr[:, 9:10]),
               reads=[R["scr"]], writes=[R["scr"]])
        p.emit("dve", lambda e: e.tensor_scalar(scr[:, 10:11], scr[:, 4:5], -1.0,
                                                None, ALU.mult),
               reads=[R["scr"]], writes=[R["scr"]])
        p.emit("dve", lambda e: e.tensor_scalar(scr[:, 11:15], scr[:, 0:4],
                                                scr[:, 10:11], None, ALU.is_equal),
               reads=[R["scr"]], writes=[R["scr"]])
        p.emit("dve", lambda e: e.scalar_tensor_tensor(scr[:, 15:19], scr[:, 11:15],
                                                       -1.0e30, scr[:, 0:4],
                                                       ALU.mult, ALU.add),
               reads=[R["scr"]], writes=[R["scr"]])
        p.emit("dve", lambda e: e.reduce_max(scr[:, 19:20], scr[:, 15:19], axis=AX.X),
               reads=[R["scr"]], writes=[R["scr"]])
        p.emit("dve", lambda e: e.tensor_scalar(scr[:, 20:24], scr[:, 15:19],
                                                scr[:, 19:20], None, ALU.is_equal),
               reads=[R["scr"]], writes=[R["scr"]])
        p.emit("dve", lambda e: e.tensor_add(scr[:, 24:25], scr[:, 19:20], scr[:, 4:5]),
               reads=[R["scr"]], writes=[R["scr"]])
        p.emit("act", lambda e: e.activation(scr[:, 24:25], scr[:, 24:25], AF.Exp),
               reads=[R["scr"]], writes=[R["scr"]])
        p.emit("dve", lambda e: e.tensor_scalar(scr[:, 25:26], scr[:, 9:10], 1.0e-8,
                                                1.0, ALU.mult, ALU.add),
               reads=[R["scr"]], writes=[R["scr"]])
        p.emit("dve", lambda e: e.tensor_add(scr[:, 25:26], scr[:, 25:26], scr[:, 24:25]),
               reads=[R["scr"]], writes=[R["scr"]])
        p.emit("dve", lambda e: e.reciprocal(scr[:, 26:27], scr[:, 25:26]),
               reads=[R["scr"]], writes=[R["scr"]])
        p.emit("dve", lambda e: e.tensor_mul(scr[:, 27:28], scr[:, 24:25], scr[:, 26:27]),
               reads=[R["scr"]], writes=[R["scr"]])
        p.emit("dve", lambda e: e.tensor_scalar(scr[:, 28:32], scr[:, 11:15],
                                                scr[:, 26:27], None, ALU.mult),
               reads=[R["scr"]], writes=[R["scr"]])
        p.emit("dve", lambda e, tt=tt: e.scalar_tensor_tensor(
            wts[:, tt * 4:(tt + 1) * 4], scr[:, 20:24], scr[:, 27:28], scr[:, 28:32],
            ALU.mult, ALU.add),
            reads=[R["scr"]], writes=[R["wts"]])
    if "dbg_wts" in d:
        p.dma("sp", d["dbg_wts"][:, :], wts[:, :], reads=[R["wts"]], writes=[Res("dwts")])
    for tt in range(16):
        p.emit("pe", lambda e, tt=tt: e.transpose(pb[5][0:4, 0:128],
                                                  wts[:, tt * 4:(tt + 1) * 4],
                                                  C.ident[:, :]),
               reads=[R["wts"], C.r["ident"]], writes=[rpb[5]])
        p.emit("dve", lambda e, tt=tt: e.tensor_copy(wT4[:, tt * 128:(tt + 1) * 128],
                                                     pb[5][0:4, 0:128]),
               reads=[rpb[5]], writes=[R["wT4"]])

    se = ExitStack()
    h1 = se.enter_context(nc.sbuf_tensor("mo_h1", [128, 16 * 1024], BF16, allow_name_mangling=True))
    h2 = se.enter_context(nc.sbuf_tensor("mo_h2", [128, 16 * 1024], BF16, allow_name_mangling=True))
    R["h1"] = fence_res(p, "h1")
    R["h2"] = fence_res(p, "h2")

    def mm_layer(dst, r_dst, src, r_src, src_nch, wkey, nout_ch, tk, act, tb2n=2):
        nh = (src_nch + 7) // 8
        for ho in range(nout_ch):
            wt = wtile[:, (ho % 2) * 1024:(ho % 2) * 1024 + 1024]
            r_wt = R["wtile"][ho % 2]
            for half in range(nh):
                kn = min(8, src_nch - half * 8)
                p.dma("pool", wt.rearrange("p (k n) -> p k n", n=128)[:, 0:kn],
                      d[wkey].rearrange("(k p) n -> p k n", p=128)[:, half * 8:half * 8 + kn,
                                                                  ho * 128:(ho + 1) * 128],
                      reads=[r_wt], writes=[r_wt])
                for tb in range(tb2n):
                    bk = tb % 2

                    def mml(e, ho=ho, tb=tb, bk=bk, wt=wt, half=half, kn=kn):
                        last = None
                        for kc in range(kn):
                            kk = half * 8 + kc
                            if src is xf:
                                sl = xf[:, kk * 2048 + tk * 1024 + tb * 512:
                                        kk * 2048 + tk * 1024 + (tb + 1) * 512]
                            else:
                                sl = src[:, kk * 1024 + tb * 512:kk * 1024 + (tb + 1) * 512]
                            last = e.matmul(pb[bk][:, 0:512], wt[:, kc * 128:(kc + 1) * 128],
                                            sl, start=(half == 0 and kc == 0),
                                            stop=(half == nh - 1 and kc == kn - 1))
                        return last
                    p.emit("pe", mml, reads=[r_wt, r_src], writes=[rpb[bk]])
            fn = {"gelu": AF.Gelu, "silu": AF.Silu}.get(act, AF.Copy)
            for tb in range(tb2n):
                bk = tb % 2
                p.emit("act", lambda e, ho=ho, tb=tb, bk=bk, fn=fn: e.activation(
                    dst[:, ho * 1024 + tb * 512:ho * 1024 + (tb + 1) * 512],
                    pb[bk][:, 0:512], fn),
                    reads=[rpb[bk]], writes=[r_dst])

    def down_proj(src, r_src, src_nch, wkey, tk, eidx):
        for do in range(2, NCH):
            wt = wtile[:, (do % 2) * 1024:(do % 2) * 1024 + 1024]
            r_wt = R["wtile"][do % 2]
            bk2 = 6 + do % 2
            nh = (src_nch + 7) // 8
            for half in range(nh):
                kn = min(8, src_nch - half * 8)
                p.dma("pool", wt.rearrange("p (k n) -> p k n", n=128)[:, 0:kn],
                      d[wkey].rearrange("(k p) n -> p k n", p=128)[:, half * 8:half * 8 + kn,
                                                                  do * 128:(do + 1) * 128],
                      reads=[r_wt], writes=[r_wt])
                for tb in range(2):
                    def mmd(e, do=do, half=half, kn=kn, tb=tb, wt=wt):
                        last = None
                        for kc in range(kn):
                            last = e.matmul(
                                pb[4 + tb][:, 0:512],
                                wt[:, kc * 128:(kc + 1) * 128],
                                src[:, (half * 8 + kc) * 1024 + tb * 512:
                                    (half * 8 + kc) * 1024 + (tb + 1) * 512],
                                start=(half == 0 and kc == 0),
                                stop=(half == nh - 1 and kc == kn - 1))
                        return last
                    p.emit("pe", mmd, reads=[r_wt, r_src], writes=[rpb[4 + tb]])
            for tb in range(2):
                ts0 = tk * 1024 + tb * 512
                p.emit("pe", lambda e, eidx=eidx, ts0=ts0: e.matmul(
                    pb[6][:, 0:512], oh4[:, eidx * 128:(eidx + 1) * 128],
                    wT4[:, ts0:ts0 + 512], start=True, stop=True),
                    reads=[R["oh4"], R["wT4"]], writes=[rpb[6]])
                p.emit("dve", lambda e: e.tensor_copy(wbcs[:, 0:512], pb[6][:, 0:512]),
                       reads=[rpb[6]], writes=[R["wbcs"]])
                p.emit("dve", lambda e, tb=tb: e.tensor_mul(scr[:, 0:512],
                                                            pb[4 + tb][:, 0:512],
                                                            wbcs[:, 0:512]),
                       reads=[rpb[4 + tb], R["wbcs"]], writes=[R["scr"]])
                p.emit("dve", lambda e, do=do, ts0=ts0: e.tensor_add(
                    xres[:, do * 2048 + ts0:do * 2048 + ts0 + 512],
                    xres[:, do * 2048 + ts0:do * 2048 + ts0 + 512], scr[:, 0:512]),
                    reads=[R["scr"], r_xres], writes=[r_xres])

    for tk in range(2):
        mm_layer(h1, R["h1"], xf, R["xf"], NCH, "l1f_e0_up", 16, tk, "gelu")
        down_proj(h1, R["h1"], 16, "l1f_e0_down", tk, 0)
        mm_layer(h1, R["h1"], xf, R["xf"], NCH, "l1f_e1_w1", 16, tk, "silu")
        mm_layer(h2, R["h2"], xf, R["xf"], NCH, "l1f_e1_w2", 16, tk, "none")
        for hc in range(16):
            for tb in range(2):
                p.emit("dve", lambda e, hc=hc, tb=tb: e.tensor_mul(
                    h1[:, hc * 1024 + tb * 512:hc * 1024 + (tb + 1) * 512],
                    h1[:, hc * 1024 + tb * 512:hc * 1024 + (tb + 1) * 512],
                    h2[:, hc * 1024 + tb * 512:hc * 1024 + (tb + 1) * 512]),
                    reads=[R["h1"], R["h2"]], writes=[R["h1"]])
        down_proj(h1, R["h1"], 16, "l1f_e1_down", tk, 1)
        mm_layer(h1, R["h1"], xf, R["xf"], NCH, "l1f_e2_l1", 16, tk, "gelu")
        mm_layer(h2, R["h2"], h1, R["h1"], 16, "l1f_e2_l2", 16, tk, "gelu")
        mm_layer(h1, R["h1"], h2, R["h2"], 16, "l1f_e2_l3", 16, tk, "gelu")
        down_proj(h1, R["h1"], 16, "l1f_e2_l4", tk, 2)
        mm_layer(h1, R["h1"], xf, R["xf"], NCH, "l1f_e3_down", 8, tk, "gelu")
        mm_layer(h2, R["h2"], h1, R["h1"], 8, "l1f_e3_up", 16, tk, "gelu")
        down_proj(h2, R["h2"], 16, "l1f_e3_out", tk, 3)
    se.close()
    R["d_xsp2"] = Res("d_xsp2")
    R["d_xsp2"].w = None
    for c in range(NCH):
        for tb in range(4):
            p.dma("sp", scr2[:, 0:512], d["xsp"][c * 128:(c + 1) * 128, tb * 512:(tb + 1) * 512],
                  reads=[R["scr2"]], writes=[R["scr2"]])
            p.emit("dve", lambda e, c=c, tb=tb: e.tensor_add(
                xres[:, c * 2048 + tb * 512:c * 2048 + (tb + 1) * 512],
                xres[:, c * 2048 + tb * 512:c * 2048 + (tb + 1) * 512], scr2[:, 0:512]),
                reads=[R["scr2"], r_xres], writes=[r_xres])
    sf.close()


def _stage_l0_weights(inputs):
    w = {}
    w["Win"] = np.ascontiguousarray(np.asarray(inputs["L0_attn_W_in"], np.float32))
    cw = np.asarray(inputs["L0_attn_conv_w"], np.float32)[:, 0, :]
    w["convw"] = np.ascontiguousarray(cw.reshape(16, 128, 4).transpose(1, 0, 2).reshape(128, 64))
    cb = np.asarray(inputs["L0_attn_conv_b"], np.float32)
    w["convb"] = np.ascontiguousarray(cb.reshape(16, 128).T)
    wx = np.asarray(inputs["L0_attn_W_x"], np.float32)
    wxp = np.zeros((2048, 80), np.float32)
    wxp[:, 0] = wx[:, 0]
    wxp[:, 32:48] = wx[:, 1:17]
    wxp[:, 64:80] = wx[:, 17:33]
    w["Wxp"] = wxp
    w["Alog"] = np.ascontiguousarray(np.asarray(inputs["L0_attn_A_log"], np.float32).reshape(1, 16))
    w["Drow"] = np.ascontiguousarray(np.asarray(inputs["L0_attn_D"], np.float32).reshape(1, 2048))
    w["Wout"] = np.ascontiguousarray(np.asarray(inputs["L0_attn_W_out"], np.float32))
    return w


def declare_l0_inputs(nc, debug=False):
    dd = {}
    dd["xT"] = nc.dram_tensor("xT", [1024, 2048], F32, kind="ExternalInput").ap()
    dd["xTov"] = nc.dram_tensor("xTov", [1024, 3], F32, kind="ExternalInput").ap()
    dd["Win"] = nc.dram_tensor("Win", [1024, 4096], F32, kind="ExternalInput").ap()
    dd["convw"] = nc.dram_tensor("convw", [128, 64], F32, kind="ExternalInput").ap()
    dd["convb"] = nc.dram_tensor("convb", [128, 16], F32, kind="ExternalInput").ap()
    dd["Wxp"] = nc.dram_tensor("Wxp", [2048, 80], F32, kind="ExternalInput").ap()
    dd["Alog"] = nc.dram_tensor("Alog", [1, 16], F32, kind="ExternalInput").ap()
    dd["Drow"] = nc.dram_tensor("Drow", [1, 2048], F32, kind="ExternalInput").ap()
    dd["Wout"] = nc.dram_tensor("Wout", [2048, 1024], F32, kind="ExternalInput").ap()
    dd["flag"] = nc.dram_tensor("flag", [1, 1], F32, kind="ExternalInput").ap()
    kind = "ExternalOutput" if debug else "Internal"
    dd["xs"] = nc.dram_tensor("d_xs", [2048, 2048], BF16, kind=kind).ap()
    dd["y"] = nc.dram_tensor("d_y", [2048, 2048], BF16, kind=kind).ap()
    dd["sz"] = nc.dram_tensor("d_sz", [2048, 2048], BF16, kind=kind).ap()
    dd["ccin"] = nc.dram_tensor("d_ccin", [16, 2048], F32).ap()
    dd["ccout"] = nc.dram_tensor("d_ccout", [32, 2048], F32).ap()
    dd["bar1"] = nc.dram_tensor("d_bar1", [16, 16], F32).ap()
    dd["bar2"] = nc.dram_tensor("d_bar2", [16, 16], F32).ap()
    return dd


def build_l0_program(debug=False):
    nc = bass.Bass("TRN2", target_bir_lowering=False)
    dd = declare_l0_inputs(nc, debug)
    out = nc.dram_tensor("xout", [1024, 2048], F32, kind="ExternalOutput").ap()
    dbg = {}
    if debug:
        dbg["xdbl"] = nc.dram_tensor("dbg_xdbl", [128, 2048], F32, kind="ExternalOutput").ap()
        dbg["rows"] = nc.dram_tensor("dbg_rows", [1, 8704], F32, kind="ExternalOutput").ap()
        dbg["hrun"] = nc.dram_tensor("dbg_hrun", [16, 2048], F32, kind="ExternalOutput").ap()
        dbg["xn"] = nc.dram_tensor("dbg_xn", [128, 8 * 2056], F32, kind="ExternalOutput").ap()
        dbg["xcs"] = nc.dram_tensor("dbg_xcs", [128, 2052], F32, kind="ExternalOutput").ap()
        dbg["xconv"] = nc.dram_tensor("dbg_xconv", [128, 2048], F32, kind="ExternalOutput").ap()
    with ExitStack() as ctx:
        p = Prog(nc, ctx)
        C = Consts(p, nc, ctx)
        xres = sbuf(nc, ctx, "xres", [128, NCH * 2048], F32)
        r_xres = Res("xres")
        hooks = build_l0_attn(p, nc, ctx, C, xres, r_xres, dd, first=True, dbg=dbg)
        r_out = Res("out")
        for c in range(NCH):
            p.dma("sp", out.rearrange("(c p) t -> c p t", p=128)[c],
                  xres[:, c * 2048:(c + 1) * 2048], reads=[r_xres], writes=[r_out])
        p.finish()
    return nc


def l0_in_maps(inputs):
    w = _stage_l0_weights(inputs)
    x = np.asarray(inputs["L0_attn_x"], np.float32)
    maps = []
    for core in range(NCORES):
        b, h = core // 2, core % 2
        m = dict(w)
        m["xT"] = np.ascontiguousarray(x[b, h * 2048:(h + 1) * 2048, :].T)
        m["xTov"] = (np.zeros((1024, 3), np.float32) if h == 0 else
                     np.ascontiguousarray(x[b, h * 2048 - 3:h * 2048, :].T))
        m["flag"] = np.full((1, 1), float(h), np.float32)
        maps.append(m)
    return maps



def declare_attn_inputs(nc, kind, debug=False):
    dd = {}
    dd["flag"] = nc.dram_tensor("flag", [1, 1], F32, kind="ExternalInput").ap()
    dd["agi"] = nc.dram_tensor("d_agi", [1024, 2048], BF16).ap()
    dd["ago"] = nc.dram_tensor("d_ago", [2048, 2048], BF16).ap()
    dd["oag"] = nc.dram_tensor("d_oag", [512, 4096], BF16).ap()
    dd["oago"] = nc.dram_tensor("d_oago", [1024, 4096], BF16).ap()
    dd["mx"] = nc.dram_tensor("d_mx", [1, 4096], F32).ap()
    dd["bar1"] = nc.dram_tensor("d_bar1", [16, 16], F32).ap()
    dd["bar2"] = nc.dram_tensor("d_bar2", [16, 16], F32).ap()
    if kind == "l1attn":
        for nm in ["wq", "wk", "wv"]:
            dd["l1a_" + nm] = nc.dram_tensor("l1a_" + nm, [1024, 512], F32,
                                             kind="ExternalInput").ap()
        dd["l1a_wo"] = nc.dram_tensor("l1a_wo", [1024, 1024], F32, kind="ExternalInput").ap()
        dd["l1a_bqk"] = nc.dram_tensor("l1a_bqk", [128, 8], F32, kind="ExternalInput").ap()
        dd["l1a_bo"] = nc.dram_tensor("l1a_bo", [128, 8], F32, kind="ExternalInput").ap()
        dd["l1a_bvrow"] = nc.dram_tensor("l1a_bvrow", [1, 512], F32, kind="ExternalInput").ap()
        dd["ropePM"] = nc.dram_tensor("ropePM", [128, 128], F32, kind="ExternalInput").ap()
    else:
        for nm in ["wq", "wk", "wv"]:
            dd["l1f_" + nm] = nc.dram_tensor("l1f_" + nm, [1024, 512], F32,
                                             kind="ExternalInput").ap()
        dd["l1f_wo"] = nc.dram_tensor("l1f_wo", [1024, 1024], F32, kind="ExternalInput").ap()
        dd["l1f_n1w"] = nc.dram_tensor("l1f_n1w", [128, 8], F32, kind="ExternalInput").ap()
        dd["l1f_qnw"] = nc.dram_tensor("l1f_qnw", [128, 1], F32, kind="ExternalInput").ap()
        dd["l1f_knw"] = nc.dram_tensor("l1f_knw", [128, 1], F32, kind="ExternalInput").ap()
    return dd


def _rope_pm():
    pm = np.zeros((128, 128), np.float32)
    for blk in range(2):
        for i in range(32):
            d0 = blk * 64 + i
            pm[blk * 64 + i + 32, d0] = -1.0
            pm[d0, blk * 64 + i + 32] = 1.0
    return pm


def _stage_l1a_weights(inputs, half):
    h0 = half * 512
    w = {}
    for nm, key in [("wq", "L1_attn_Wq"), ("wk", "L1_attn_Wk"), ("wv", "L1_attn_Wv")]:
        W = np.asarray(inputs[key], np.float32)
        w["l1a_" + nm] = np.ascontiguousarray(W[h0:h0 + 512, :].T)
    w["l1a_wo"] = np.ascontiguousarray(np.asarray(inputs["L1_attn_Wo"], np.float32).T)
    bq = np.asarray(inputs["L1_attn_bq"], np.float32)[h0:h0 + 512].reshape(4, 128).T
    bk = np.asarray(inputs["L1_attn_bk"], np.float32)[h0:h0 + 512].reshape(4, 128).T
    w["l1a_bqk"] = np.ascontiguousarray(np.concatenate([bq, bk], 1))
    w["l1a_bo"] = np.ascontiguousarray(
        np.asarray(inputs["L1_attn_bo"], np.float32).reshape(8, 128).T)
    w["l1a_bvrow"] = np.ascontiguousarray(
        np.asarray(inputs["L1_attn_bv"], np.float32)[h0:h0 + 512].reshape(1, 512))
    w["ropePM"] = _rope_pm()
    return w


def build_l1a_program(debug=False):
    nc = bass.Bass("TRN2", target_bir_lowering=False)
    xTin = nc.dram_tensor("xT", [1024, 2048], F32, kind="ExternalInput").ap()
    dd = declare_attn_inputs(nc, "l1attn", debug)
    out = nc.dram_tensor("xout", [1024, 2048], F32, kind="ExternalOutput").ap()
    if debug:
        dd["dbg_qw"] = nc.dram_tensor("dbg_qw", [128, 4096], F32, kind="ExternalOutput").ap()
        dd["dbg_kw"] = nc.dram_tensor("dbg_kw", [128, 4096], F32, kind="ExternalOutput").ap()
        dd["dbg_vw"] = nc.dram_tensor("dbg_vw", [128, 32 * 130], F32, kind="ExternalOutput").ap()
        dd["dbg_nmx"] = nc.dram_tensor("dbg_nmx", [1, 4096], F32, kind="ExternalOutput").ap()
        dd["dbg_nmxraw"] = nc.dram_tensor("dbg_nmxraw", [128, 32], F32, kind="ExternalOutput").ap()
        dd["dbg_oag"] = nc.dram_tensor("dbg_oag", [512, 4096], BF16, kind="ExternalOutput").ap()
    with ExitStack() as ctx:
        p = Prog(nc, ctx)
        C = Consts(p, nc, ctx)
        xres = sbuf(nc, ctx, "xres", [128, NCH * 2048], F32)
        r_xres = Res("xres")
        for c in range(NCH):
            p.dma("sp", xres[:, c * 2048:(c + 1) * 2048],
                  xTin.rearrange("(c p) t -> c p t", p=128)[c], writes=[r_xres])
        build_attn_block(p, nc, ctx, C, xres, r_xres, dd, "l1attn")
        r_out = Res("out")
        for c in range(NCH):
            p.dma("sp", out.rearrange("(c p) t -> c p t", p=128)[c],
                  xres[:, c * 2048:(c + 1) * 2048], reads=[r_xres], writes=[r_out])
        p.finish()
    return nc


def run_l1a(inputs, x_in, debug=False):
    nc = build_l1a_program(debug)
    maps = []
    for core in range(NCORES):
        b, h = core // 2, core % 2
        m = _stage_l1a_weights(inputs, h)
        m["xT"] = np.ascontiguousarray(x_in[b, h * 2048:(h + 1) * 2048, :].T)
        m["flag"] = np.full((1, 1), float(h), np.float32)
        maps.append(m)
    res = run_bass_kernel_spmd(nc, maps, core_ids=list(range(NCORES)))
    out = np.zeros((4, 4096, 1024), np.float32)
    for core in range(NCORES):
        b, h = core // 2, core % 2
        out[b, h * 2048:(h + 1) * 2048, :] = np.asarray(res.results[core]["xout"]).T
    if debug:
        return out, res
    return out



def declare_l0f_inputs(nc):
    dd = {}
    dd["f0_bcols"] = nc.dram_tensor("f0_bcols", [128, 64], F32, kind="ExternalInput").ap()
    dd["f0_gbias"] = nc.dram_tensor("f0_gbias", [128, 16], F32, kind="ExternalInput").ap()
    dd["f0_b2row"] = nc.dram_tensor("f0_b2row", [1, 16], F32, kind="ExternalInput").ap()
    dd["f0_ceT"] = nc.dram_tensor("f0_ceT", [1024, 16], F32, kind="ExternalInput").ap()
    dd["f0_g1"] = nc.dram_tensor("f0_g1", [1024, 512], F32, kind="ExternalInput").ap()
    dd["f0_g2"] = nc.dram_tensor("f0_g2", [512, 16], F32, kind="ExternalInput").ap()
    dd["f0_wcols"] = nc.dram_tensor("f0_wcols", [1024, 4096], F32, kind="ExternalInput").ap()
    dd["f0_proj"] = nc.dram_tensor("f0_proj", [4096, 1024], F32, kind="ExternalInput").ap()
    dd["f0y"] = nc.dram_tensor("d_f0y", [1024, 2048], F32).ap()
    return dd


def _stage_l0f_weights(inputs):
    w = {}
    bcols = np.asarray(inputs["L0_ffn_b_cols"], np.float32).reshape(4096)
    projb = np.asarray(inputs["L0_ffn_proj_b"], np.float32)
    lng = np.asarray(inputs["L0_ffn_ln_g"], np.float32)
    lnb = np.asarray(inputs["L0_ffn_ln_b"], np.float32)
    out = np.zeros((128, 64), np.float32)
    for cn in range(32):
        out[:, cn] = bcols[cn * 128:(cn + 1) * 128]
    for c in range(8):
        out[:, 32 + c] = projb[c * 128:(c + 1) * 128]
        out[:, 40 + c] = lng[c * 128:(c + 1) * 128]
        out[:, 48 + c] = lnb[c * 128:(c + 1) * 128]
    w["f0_bcols"] = out
    w["f0_gbias"] = np.ascontiguousarray(
        np.concatenate([np.asarray(inputs["L0_ffn_gate_b1"], np.float32).reshape(4, 128).T,
                        np.zeros((128, 12), np.float32)], 1))
    w["f0_b2row"] = np.asarray(inputs["L0_ffn_gate_b2"], np.float32).reshape(1, 16)
    w["f0_ceT"] = np.ascontiguousarray(np.asarray(inputs["L0_ffn_col_emb"], np.float32).T)
    w["f0_g1"] = np.ascontiguousarray(np.asarray(inputs["L0_ffn_gate_w1"], np.float32))
    w["f0_g2"] = np.ascontiguousarray(np.asarray(inputs["L0_ffn_gate_w2"], np.float32))
    wc = np.asarray(inputs["L0_ffn_W_cols"], np.float32)
    w["f0_wcols"] = np.ascontiguousarray(wc.transpose(1, 0, 2).reshape(1024, 4096))
    w["f0_proj"] = np.ascontiguousarray(np.asarray(inputs["L0_ffn_proj_w"], np.float32))
    return w


def build_l0f_program(debug=False):
    nc = bass.Bass("TRN2", target_bir_lowering=False)
    xTin = nc.dram_tensor("xT", [1024, 2048], F32, kind="ExternalInput").ap()
    dd = declare_l0f_inputs(nc)
    out = nc.dram_tensor("xout", [1024, 2048], F32, kind="ExternalOutput").ap()
    if debug:
        dd["dbg_wtok"] = nc.dram_tensor("dbg_wtok", [128, 256], F32, kind="ExternalOutput").ap()
        dd["dbg_ltok"] = nc.dram_tensor("dbg_ltok", [128, 256], F32, kind="ExternalOutput").ap()
        dd["dbg_y"] = nc.dram_tensor("dbg_y", [1024, 2048], F32, kind="ExternalOutput").ap()
    with ExitStack() as ctx:
        p = Prog(nc, ctx)
        C = Consts(p, nc, ctx)
        xres = sbuf(nc, ctx, "xres", [128, NCH * 2048], F32)
        r_xres = Res("xres")
        for c in range(NCH):
            p.dma("sp", xres[:, c * 2048:(c + 1) * 2048],
                  xTin.rearrange("(c p) t -> c p t", p=128)[c], writes=[r_xres])
        build_l0_ffn(p, nc, ctx, C, xres, r_xres, dd)
        r_out = Res("out")
        for c in range(NCH):
            p.dma("sp", out.rearrange("(c p) t -> c p t", p=128)[c],
                  xres[:, c * 2048:(c + 1) * 2048], reads=[r_xres], writes=[r_out])
        p.finish()
    return nc


def run_l0f(inputs, x_in, debug=False):
    nc = build_l0f_program(debug)
    w = _stage_l0f_weights(inputs)
    maps = []
    for core in range(NCORES):
        b, h = core // 2, core % 2
        m = dict(w)
        m["xT"] = np.ascontiguousarray(x_in[b, h * 2048:(h + 1) * 2048, :].T)
        maps.append(m)
    res = run_bass_kernel_spmd(nc, maps, core_ids=list(range(NCORES)))
    out = np.zeros((4, 4096, 1024), np.float32)
    for core in range(NCORES):
        b, h = core // 2, core % 2
        out[b, h * 2048:(h + 1) * 2048, :] = np.asarray(res.results[core]["xout"]).T
    if debug:
        return out, res
    return out


def declare_l1f_inputs(nc):
    dd = declare_attn_inputs(nc, "l1ffn")
    dd["l1f_n2w"] = nc.dram_tensor("l1f_n2w", [128, 8], F32, kind="ExternalInput").ap()
    dd["xsp"] = nc.dram_tensor("d_xsp", [1024, 2048], F32).ap()
    dd["l1f_gwT"] = nc.dram_tensor("l1f_gwT", [1024, 4], F32, kind="ExternalInput").ap()
    for nm, shp in [("e0_up", (1024, 2048)), ("e0_down", (2048, 1024)),
                    ("e1_w1", (1024, 2048)), ("e1_w2", (1024, 2048)),
                    ("e1_down", (2048, 1024)), ("e2_l1", (1024, 2048)),
                    ("e2_l2", (2048, 2048)), ("e2_l3", (2048, 2048)),
                    ("e2_l4", (2048, 1024)), ("e3_down", (1024, 1024)),
                    ("e3_up", (1024, 2048)), ("e3_out", (2048, 1024))]:
        dd["l1f_" + nm] = nc.dram_tensor("l1f_" + nm, list(shp), F32,
                                         kind="ExternalInput").ap()
    return dd


def _stage_l1f_weights(inputs, half):
    h0 = half * 512
    w = {}
    for nm, key in [("wq", "L1_ffn_wq"), ("wk", "L1_ffn_wk"), ("wv", "L1_ffn_wv")]:
        W = np.asarray(inputs[key], np.float32)
        w["l1f_" + nm] = np.ascontiguousarray(W[h0:h0 + 512, :].T)
    w["l1f_wo"] = np.ascontiguousarray(np.asarray(inputs["L1_ffn_wo"], np.float32).T)
    w["l1f_n1w"] = np.ascontiguousarray(
        np.asarray(inputs["L1_ffn_n1_w"], np.float32).reshape(8, 128).T)
    w["l1f_n2w"] = np.ascontiguousarray(
        np.asarray(inputs["L1_ffn_n2_w"], np.float32).reshape(8, 128).T)
    w["l1f_qnw"] = np.ascontiguousarray(
        np.tile(np.asarray(inputs["L1_ffn_qn_w"], np.float32), 2).reshape(128, 1))
    w["l1f_knw"] = np.ascontiguousarray(
        np.tile(np.asarray(inputs["L1_ffn_kn_w"], np.float32), 2).reshape(128, 1))
    w["l1f_gwT"] = np.ascontiguousarray(np.asarray(inputs["L1_ffn_gate_w"], np.float32).T)
    for nm in ["e0_up", "e0_down", "e1_w1", "e1_w2", "e1_down", "e2_l1", "e2_l2",
               "e2_l3", "e2_l4", "e3_down", "e3_up", "e3_out"]:
        w["l1f_" + nm] = np.ascontiguousarray(
            np.asarray(inputs["L1_ffn_" + nm], np.float32).T)
    return w


def build_l1f_program(debug=False):
    nc = bass.Bass("TRN2", target_bir_lowering=False)
    xTin = nc.dram_tensor("xT", [1024, 2048], F32, kind="ExternalInput").ap()
    dd = declare_l1f_inputs(nc)
    out = nc.dram_tensor("xout", [1024, 2048], F32, kind="ExternalOutput").ap()
    dbgx = None
    if debug:
        dbgx = nc.dram_tensor("dbg_xmid", [1024, 2048], F32, kind="ExternalOutput").ap()
        dd["dbg_wts"] = nc.dram_tensor("dbg_wts", [128, 64], F32, kind="ExternalOutput").ap()
    with ExitStack() as ctx:
        p = Prog(nc, ctx)
        C = Consts(p, nc, ctx)
        xres = sbuf(nc, ctx, "xres", [128, NCH * 2048], F32)
        r_xres = Res("xres")
        for c in range(NCH):
            p.dma("sp", xres[:, c * 2048:(c + 1) * 2048],
                  xTin.rearrange("(c p) t -> c p t", p=128)[c], writes=[r_xres])
        build_attn_block(p, nc, ctx, C, xres, r_xres, dd, "l1ffn")
        if dbgx is not None:
            for c in range(NCH):
                p.dma("sp", dbgx.rearrange("(c p) t -> c p t", p=128)[c],
                      xres[:, c * 2048:(c + 1) * 2048], reads=[r_xres], writes=[Res("dx%d" % c)])
        build_l1_moe(p, nc, ctx, C, xres, r_xres, dd)
        r_out = Res("out")
        for c in range(NCH):
            p.dma("sp", out.rearrange("(c p) t -> c p t", p=128)[c],
                  xres[:, c * 2048:(c + 1) * 2048], reads=[r_xres], writes=[r_out])
        p.finish()
    return nc


def run_l1f(inputs, x_in, debug=False):
    nc = build_l1f_program(debug)
    maps = []
    for core in range(NCORES):
        b, h = core // 2, core % 2
        m = _stage_l1f_weights(inputs, h)
        m["xT"] = np.ascontiguousarray(x_in[b, h * 2048:(h + 1) * 2048, :].T)
        m["flag"] = np.full((1, 1), float(h), np.float32)
        maps.append(m)
    res = run_bass_kernel_spmd(nc, maps, core_ids=list(range(NCORES)))
    out = np.zeros((4, 4096, 1024), np.float32)
    for core in range(NCORES):
        b, h = core // 2, core % 2
        out[b, h * 2048:(h + 1) * 2048, :] = np.asarray(res.results[core]["xout"]).T
    if debug:
        return out, res
    return out



def build_fused_program(debug=False):
    nc = bass.Bass("TRN2", target_bir_lowering=False)
    d = {}
    d.update(declare_l0_inputs(nc))
    d["agi"] = nc.dram_tensor("d_agi", [1024, 2048], BF16).ap()
    d["ago"] = nc.dram_tensor("d_ago", [2048, 2048], BF16).ap()
    d["oag"] = nc.dram_tensor("d_oag", [512, 4096], BF16).ap()
    d["oago"] = nc.dram_tensor("d_oago", [1024, 4096], BF16).ap()
    d["mx"] = nc.dram_tensor("d_mx", [1, 4096], F32).ap()
    for nm in ["wq", "wk", "wv"]:
        d["l1a_" + nm] = nc.dram_tensor("l1a_" + nm, [1024, 512], F32,
                                        kind="ExternalInput").ap()
    d["l1a_wo"] = nc.dram_tensor("l1a_wo", [1024, 1024], F32, kind="ExternalInput").ap()
    d["l1a_bqk"] = nc.dram_tensor("l1a_bqk", [128, 8], F32, kind="ExternalInput").ap()
    d["l1a_bo"] = nc.dram_tensor("l1a_bo", [128, 8], F32, kind="ExternalInput").ap()
    d["l1a_bvrow"] = nc.dram_tensor("l1a_bvrow", [1, 512], F32, kind="ExternalInput").ap()
    d["ropePM"] = nc.dram_tensor("ropePM", [128, 128], F32, kind="ExternalInput").ap()
    for nm in ["wq", "wk", "wv"]:
        d["l1f_" + nm] = nc.dram_tensor("l1f_" + nm, [1024, 512], F32,
                                        kind="ExternalInput").ap()
    d["l1f_wo"] = nc.dram_tensor("l1f_wo", [1024, 1024], F32, kind="ExternalInput").ap()
    d["l1f_n1w"] = nc.dram_tensor("l1f_n1w", [128, 8], F32, kind="ExternalInput").ap()
    d["l1f_qnw"] = nc.dram_tensor("l1f_qnw", [128, 1], F32, kind="ExternalInput").ap()
    d["l1f_knw"] = nc.dram_tensor("l1f_knw", [128, 1], F32, kind="ExternalInput").ap()
    d["l1f_n2w"] = nc.dram_tensor("l1f_n2w", [128, 8], F32, kind="ExternalInput").ap()
    d["l1f_gwT"] = nc.dram_tensor("l1f_gwT", [1024, 4], F32, kind="ExternalInput").ap()
    d["xsp"] = nc.dram_tensor("d_xsp", [1024, 2048], F32).ap()
    for nm, shp in [("e0_up", (1024, 2048)), ("e0_down", (2048, 1024)),
                    ("e1_w1", (1024, 2048)), ("e1_w2", (1024, 2048)),
                    ("e1_down", (2048, 1024)), ("e2_l1", (1024, 2048)),
                    ("e2_l2", (2048, 2048)), ("e2_l3", (2048, 2048)),
                    ("e2_l4", (2048, 1024)), ("e3_down", (1024, 1024)),
                    ("e3_up", (1024, 2048)), ("e3_out", (2048, 1024))]:
        d["l1f_" + nm] = nc.dram_tensor("l1f_" + nm, list(shp), F32,
                                        kind="ExternalInput").ap()
    d.update({k: v for k, v in declare_l0f_inputs(nc).items()})
    out = nc.dram_tensor("xout", [1024, 2048], F32, kind="ExternalOutput").ap()

    with ExitStack() as ctx:
        p = Prog(nc, ctx)
        C = Consts(p, nc, ctx)
        xres = sbuf(nc, ctx, "xres", [128, NCH * 2048], F32)
        r_xres = Res("xres")
        def dump(tag):
            if not debug:
                return
            dx = nc.dram_tensor("dbg_" + tag, [1024, 2048], F32, kind="ExternalOutput").ap()
            for c in range(NCH):
                p.dma("sp", dx.rearrange("(c p) t -> c p t", p=128)[c],
                      xres[:, c * 2048:(c + 1) * 2048], reads=[r_xres],
                      writes=[Res("d" + tag + str(c))])
        build_l0_attn(p, nc, ctx, C, xres, r_xres, d, first=True)
        dump("x1")
        build_l0_ffn(p, nc, ctx, C, xres, r_xres, d)
        dump("x2")
        build_attn_block(p, nc, ctx, C, xres, r_xres, d, "l1attn")
        dump("x3")
        build_attn_block(p, nc, ctx, C, xres, r_xres, d, "l1ffn")
        dump("x3b")
        build_l1_moe(p, nc, ctx, C, xres, r_xres, d)
        r_out = Res("out")
        for c in range(NCH):
            p.dma("sp", out.rearrange("(c p) t -> c p t", p=128)[c],
                  xres[:, c * 2048:(c + 1) * 2048], reads=[r_xres], writes=[r_out])
        p.finish()
    return nc


def fused_in_maps(inputs):
    w0 = _stage_l0_weights(inputs)
    wf = _stage_l0f_weights(inputs)
    x = np.asarray(inputs["L0_attn_x"], np.float32)
    maps = []
    for core in range(NCORES):
        b, h = core // 2, core % 2
        m = dict(w0)
        m.update(wf)
        m.update(_stage_l1a_weights(inputs, h))
        m.update(_stage_l1f_weights(inputs, h))
        m["xT"] = np.ascontiguousarray(x[b, h * 2048:(h + 1) * 2048, :].T)
        m["xTov"] = (np.zeros((1024, 3), np.float32) if h == 0 else
                     np.ascontiguousarray(x[b, h * 2048 - 3:h * 2048, :].T))
        m["flag"] = np.full((1, 1), float(h), np.float32)
        maps.append(m)
    return maps


def kernel(**inputs):
    nc = build_fused_program()
    maps = fused_in_maps(inputs)
    res = run_bass_kernel_spmd(nc, maps, core_ids=list(range(NCORES)))
    out = np.zeros((4, 4096, 1024), np.float32)
    for core in range(NCORES):
        b, h = core // 2, core % 2
        out[b, h * 2048:(h + 1) * 2048, :] = np.asarray(res.results[core]["xout"]).T
    return out
